# version 08
# baseline (speedup 1.0000x reference)
import numpy as np
import concourse.bass as bass
import concourse.mybir as mybir
from concourse.bass_utils import run_bass_kernel_spmd

F32 = mybir.dt.float32
BF16 = mybir.dt.bfloat16
A = mybir.AluOpType
AF = mybir.ActivationFunctionType
AX = mybir.AxisListType

B, L, D = 4, 8192, 1024
NCORES = 8
TPC = L // 2
PRE = TPC
XROWS = PRE + TPC
NLN = XROWS // 128
HID = 4096
NTILE = TPC // 128
GTILES = 8
NGRP = NTILE // GTILES
SUBT = 4
ATILE = 256
NAT = XROWS // ATILE
CS = 64
NCH = ATILE // CS
BK = 32
LN_EPS = 1e-5
FP64_GADGET = True

MAGIC23 = float(1.5 * 2 ** 23)
MAGIC33 = float(1.5 * 2 ** 33)
MAGIC52 = float(1.5 * 2 ** 52)


class Sem:
    def __init__(self, nc, name):
        self.h = nc.alloc_semaphore(name)
        self.n = 0


class Prog:
    ENGINES = ("sync", "scalar", "vector", "tensor", "gpsimd")

    def __init__(self, nc):
        self.nc = nc
        self.streams = {e: [] for e in self.ENGINES}

    def on(self, eng):
        def deco(fn):
            self.streams[eng].append(fn)
            return None
        return deco

    def emit(self):
        with self.nc.Block() as blk:
            for eng in self.ENGINES:
                thunks = self.streams[eng]
                getattr(blk, eng)(lambda e, th=thunks: [f(e) for f in th])


def emit_ladder_m(v, am, p, scr, m23, m33, m52, nblk, lanes=4):
    nb4 = nblk // lanes
    sl = [slice(l * nb4, (l + 1) * nb4) for l in range(lanes)]
    for l in range(lanes):
        v.memset(p[:, sl[l]], float(2.0 ** -34))
    for k in (32, 16, 8, 4, 2, 1):
        for l in range(lanes):
            v.tensor_single_scalar(scr[:, sl[l]], p[:, sl[l]], float(2.0 ** k), A.mult)
        for l in range(lanes):
            v.tensor_tensor(scr[:, sl[l]], am[:, sl[l]], scr[:, sl[l]], A.is_ge)
        for l in range(lanes):
            v.tensor_scalar(scr[:, sl[l]], scr[:, sl[l]], float(2.0 ** k - 1.0), 1.0,
                            A.mult, A.add)
        for l in range(lanes):
            v.tensor_tensor(p[:, sl[l]], p[:, sl[l]], scr[:, sl[l]], A.mult)
    for l in range(lanes):
        v.tensor_single_scalar(m23[:, sl[l]], p[:, sl[l]], MAGIC23 / 8.0, A.mult)
    for l in range(lanes):
        v.tensor_single_scalar(m33[:, sl[l]], p[:, sl[l]], MAGIC33 / 8.0, A.mult)
    for l in range(lanes):
        v.tensor_single_scalar(m52[:, sl[l]], p[:, sl[l]], MAGIC52 / 8.0, A.mult)


def emit_quant(v, src_ap, dst_ap, m23, m33, m52, tmp, tmp2, nblk, P=128):
    s3 = src_ap.rearrange("p (b c) -> p b c", b=nblk, c=BK)
    d3 = dst_ap.rearrange("p (b c) -> p b c", b=nblk, c=BK)
    t3 = tmp.rearrange("p (b c) -> p b c", b=nblk, c=BK)
    u3 = tmp2.rearrange("p (b c) -> p b c", b=nblk, c=BK)
    mb23 = m23[0:P, 0:nblk].unsqueeze(2).broadcast_to([P, nblk, BK])
    v.tensor_tensor(t3, s3, mb23, A.add)
    if not FP64_GADGET:
        v.tensor_tensor(d3, t3, mb23, A.subtract)
        return
    v.tensor_tensor(t3, t3, mb23, A.subtract)
    mb52 = m52[0:P, 0:nblk].unsqueeze(2).broadcast_to([P, nblk, BK])
    v.tensor_tensor(u3, s3, mb52, A.add)
    v.tensor_tensor(u3, u3, mb52, A.subtract)
    v.tensor_tensor(t3, t3, u3, A.add)
    mb33 = m33[0:P, 0:nblk].unsqueeze(2).broadcast_to([P, nblk, BK])
    v.tensor_tensor(u3, s3, mb33, A.add)
    v.tensor_tensor(u3, u3, mb33, A.subtract)
    v.tensor_tensor(d3, t3, u3, A.subtract)


def build_program():
    nc = bass.Bass("TRN2", target_bir_lowering=False)
    pr = Prog(nc)

    x_in = nc.dram_tensor("x", [XROWS, D], F32, kind="ExternalInput").ap()
    w_fc = nc.dram_tensor("w_fc", [HID, D], F32, kind="ExternalInput").ap()
    b_fc = nc.dram_tensor("b_fc", [1, HID], F32, kind="ExternalInput").ap()
    w_pj = nc.dram_tensor("w_proj", [D, HID], F32, kind="ExternalInput").ap()
    b_pj = nc.dram_tensor("b_proj", [1, D], F32, kind="ExternalInput").ap()
    ones_in = nc.dram_tensor("ones_row", [1, 128], F32, kind="ExternalInput").ap()
    wqkv_in = nc.dram_tensor("wqkvT", [D, 3076], F32, kind="ExternalInput").ap()
    c4cols_in = nc.dram_tensor("c4cols", [128, 96], F32, kind="ExternalInput").ap()
    f31cols_in = nc.dram_tensor("f31cols", [128, 256], F32, kind="ExternalInput").ap()
    f3cols_in = nc.dram_tensor("f3cols", [128, 32], F32, kind="ExternalInput").ap()
    dmask_in = nc.dram_tensor("dmask", [128, 32], F32, kind="ExternalInput").ap()
    umask_in = nc.dram_tensor("umask", [128, 64], F32, kind="ExternalInput").ap()
    lmask_in = nc.dram_tensor("lmask", [128, 64], F32, kind="ExternalInput").ap()
    amask_in = nc.dram_tensor("amask", [128, 64], F32, kind="ExternalInput").ap()
    eye_in = nc.dram_tensor("eye128", [128, 64], F32, kind="ExternalInput").ap()
    ones2_in = nc.dram_tensor("ones2", [128, 64], F32, kind="ExternalInput").ap()
    fgw1_in = nc.dram_tensor("fgw1T", [1792, 1024], F32, kind="ExternalInput").ap()
    fgb1_in = nc.dram_tensor("fgb1r", [1, 1024], F32, kind="ExternalInput").ap()
    fgw2_in = nc.dram_tensor("fgw2T", [1024, 4], F32, kind="ExternalInput").ap()
    fgb2_in = nc.dram_tensor("fgb2r", [1, 4], F32, kind="ExternalInput").ap()
    ltemp_in = nc.dram_tensor("ltemp", [1, 1], F32, kind="ExternalInput").ap()
    onw_in = nc.dram_tensor("onwcols", [128, 8], F32, kind="ExternalInput").ap()
    oprj_in = nc.dram_tensor("oprojT", [D, D], F32, kind="ExternalInput").ap()
    zeros_in = nc.dram_tensor("zeros128", [128, 1024], F32, kind="ExternalInput").ap()
    out = nc.dram_tensor("out", [TPC, D], F32, kind="ExternalOutput").ap()
    q_hbm = nc.dram_tensor("q_hbm", [D, XROWS], BF16).ap()
    k_hbm = nc.dram_tensor("k_hbm", [D, XROWS], BF16).ap()
    v_hbm = nc.dram_tensor("v_hbm", [D, XROWS + 8], BF16).ap()
    sh_hbm = nc.dram_tensor("sh_hbm", [D, TPC], BF16).ap()
    lo_hbm = nc.dram_tensor("lo_hbm", [D, TPC], BF16).ap()
    mn_hbm = nc.dram_tensor("mn_hbm", [TPC, 768], BF16).ap()
    gh_hbm = nc.dram_tensor("gh_hbm", [TPC, 1024], BF16).ap()
    o_hbm = nc.dram_tensor("o_hbm", [TPC, D], BF16).ap()

    xn_tm = nc.dram_tensor("xn_tm", [64 + XROWS, D], BF16).ap()
    g_tm = nc.dram_tensor("g_tm", [TPC, D], BF16).ap()
    xnq_tm = nc.dram_tensor("xnq_tm", [TPC, D], BF16).ap()
    hq_tm = nc.dram_tensor("hq_tm", [TPC, HID], BF16).ap()
    wqfc_hbm = nc.dram_tensor("wqfc_hbm", [HID, D], BF16).ap()
    wqpj_hbm = nc.dram_tensor("wqpj_hbm", [D, HID], BF16).ap()

    sb = nc.alloc_sbuf_tensor
    wbig = sb("wbig", [128, 8 * HID], BF16)
    bfc_row = sb("bfc_row", [1, HID], BF16)
    bpj_row = sb("bpj_row", [1, D], BF16)
    ones = sb("ones", [1, 128], BF16)

    NX = 2
    xt = [sb(f"xt{i}", [128, D], F32) for i in range(NX)]
    xnt = [sb(f"xnt{i}", [128, D], F32) for i in range(NX)]
    xnb = [sb(f"xnb{i}", [128, D], BF16) for i in range(NX)]
    xqb = [sb(f"xqb{i}", [128, D], BF16) for i in range(NX)]
    lnscr = sb("lnscr", [128, D], F32)
    qt1 = sb("qt1", [128, D], F32)
    qt2 = sb("qt2", [128, D], F32)
    stat = [sb(f"stat{i}", [128, 12], F32) for i in range(NX)]
    xam = [sb(f"xam{i}", [128, 32], F32) for i in range(NX)]
    xp = [sb(f"xp{i}", [128, 32], F32) for i in range(NX)]
    xscr = sb("xscr", [128, 32], F32)
    xm23 = [sb(f"xm23{i}", [128, 32], F32) for i in range(NX)]
    xm33 = [sb(f"xm33{i}", [128, 32], F32) for i in range(NX)]
    xm52 = [sb(f"xm52{i}", [128, 32], F32) for i in range(NX)]

    xnqf = sb("xnqf", [128, 8 * 1024], BF16)
    hbuf = sb("hbuf", [128, 2048], F32)
    hqb = sb("hqb", [128, 2048], BF16)
    ht1 = sb("ht1", [128, 2048], F32)
    ht2 = sb("ht2", [128, 2048], F32)
    NH = 2
    ham = [sb(f"ham{i}", [128, 64], F32) for i in range(NH)]
    hp = [sb(f"hp{i}", [128, 64], F32) for i in range(NH)]
    hscr = sb("hscr", [128, 64], F32)
    hm23 = [sb(f"hm23{i}", [128, 64], F32) for i in range(NH)]
    hm33 = [sb(f"hm33{i}", [128, 64], F32) for i in range(NH)]
    hm52 = [sb(f"hm52{i}", [128, 64], F32) for i in range(NH)]

    NHQ = 8
    hqT = [sb(f"hqT{i}", [128, SUBT * 128], BF16) for i in range(NHQ)]
    gbuf = [sb(f"gbuf{i}", [128, D], F32) for i in range(SUBT)]
    x2 = [sb(f"x2_{i}", [128, D], F32) for i in range(2)]
    obuf = [sb(f"obuf{i}", [128, D], F32) for i in range(2)]

    psum = [nc.alloc_psum_tensor(f"ps{i}", [128, 512], F32) for i in range(8)]

    wsrc = ht1
    wt1 = ht2
    wt2 = hbuf
    wout = hqb
    wam, wp_, wscr = ham[0], hp[0], hscr
    wm23, wm33, wm52 = hm23[0], hm33[0], hm52[0]

    wfcT_v = wbig[:].rearrange("p (c n) -> p c n", c=8, n=HID)
    wpjT_v = wbig[:].rearrange("p (c n) -> p c n", c=32, n=D)
    xnqf_v = xnqf[:].rearrange("p (c n) -> p c n", c=8, n=1024)

    S = lambda name: Sem(nc, name)
    s_cst = S("s_cst")
    s_wld = S("s_wld"); s_wam = S("s_wam"); s_wpool = S("s_wpool")
    s_wqd = S("s_wqd"); s_wst = S("s_wst"); s_wtr = S("s_wtr"); s_wtr2 = S("s_wtr2")
    s_xp = [S("s_xp0"), S("s_xp1")]
    s_xstp = [S("s_xstp0"), S("s_xstp1")]
    s_htr8 = [S(f"s_htr8_{i}") for i in range(8)]
    s_x = S("s_x"); s_st1 = S("s_st1"); s_pmath = S("s_pmath"); s_sqrt = S("s_sqrt")
    s_lnv = S("s_lnv"); s_xpool = S("s_xpool"); s_xq = S("s_xq"); s_xst = S("s_xst")
    s_xtr = S("s_xtr"); s_fc = S("s_fc"); s_gel = S("s_gel")
    s_ham = S("s_ham"); s_hpool = S("s_hpool"); s_hq = S("s_hq"); s_hst = S("s_hst")
    s_htr = S("s_htr"); s_pjk = S("s_pjk"); s_sil = S("s_sil")
    s_x2 = S("s_x2"); s_ores = S("s_ores"); s_out = S("s_out")

    @pr.on("gpsimd")
    def _(g):
        g.dma_start(out=ones[:], in_=ones_in[:]).then_inc(s_cst.h, 16)
        g.dma_start(out=bfc_row[:], in_=b_fc[:]).then_inc(s_cst.h, 16)
        g.dma_start(out=bpj_row[:], in_=b_pj[:]).then_inc(s_cst.h, 16)
    s_cst.n += 48

    WTILES = [("fc", i, 0, 1024, 32) for i in range(32)] + \
             [("pj", i, hh, 2048, 64) for i in range(8) for hh in range(2)]
    wqd_done = {}

    for wi, (kind, i, hh, ncol, nblk) in enumerate(WTILES):
        src = w_fc if kind == "fc" else w_pj
        dst = wqfc_hbm if kind == "fc" else wqpj_hbm
        col0 = hh * 2048

        @pr.on("sync")
        def _(sy, src=src, i=i, col0=col0, ncol=ncol, wi=wi):
            if wi >= 1:
                sy.wait_ge(s_wqd.h, wqd_done[wi - 1])
            sy.dma_start(out=wsrc[:, 0:ncol],
                         in_=src[i * 128:(i + 1) * 128, col0:col0 + ncol]).then_inc(s_wld.h, 16)
        s_wld.n += 16

        @pr.on("vector")
        def _(v, wld_n=s_wld.n, nblk=nblk, ncol=ncol):
            v.wait_ge(s_wld.h, wld_n)
            v.tensor_reduce(wam[:, 0:nblk].unsqueeze(2),
                            wsrc[:, 0:ncol].rearrange("p (b c) -> p b c", b=nblk, c=BK),
                            AX.X, A.max, apply_absolute_value=True)
            emit_ladder_m(v, wam, wp_, wscr, wm23, wm33, wm52, nblk)
            emit_quant(v, wsrc[:, 0:ncol], wout[:, 0:ncol],
                       wm23, wm33, wm52, wt1[:, 0:ncol], wt2[:, 0:ncol], nblk)
            v.tensor_copy(wscr[0:1, 1:2], wscr[0:1, 1:2]).then_inc(s_wqd.h, 1)
        s_wqd.n += 1
        wqd_done[wi] = s_wqd.n

        @pr.on("gpsimd")
        def _(g, dst=dst, i=i, col0=col0, ncol=ncol, wqd_n=s_wqd.n):
            g.wait_ge(s_wqd.h, wqd_n)
            g.dma_start(out=dst[i * 128:(i + 1) * 128, col0:col0 + ncol],
                        in_=wout[:, 0:ncol]).then_inc(s_wst.h, 16)
        s_wst.n += 16

    @pr.on("scalar")
    def _(sc, wst_n=s_wst.n):
        sc.wait_ge(s_wst.h, wst_n)
        for fc_ in range(8):
            sc.dma_start_transpose(wfcT_v[:, fc_, :],
                                   wqfc_hbm[:, fc_ * 128:(fc_ + 1) * 128]).then_inc(s_wtr.h, 16)
    s_wtr.n += 8 * 16

    lnv_done = {}
    xq_done = {}
    xst_done = {}

    for t in range(NLN):
        bi = t % NX
        row0 = t * 128
        own = t >= NLN - NTILE
        orow0 = (t - (NLN - NTILE)) * 128

        @pr.on("sync")
        def _(sy, t=t, bi=bi, row0=row0):
            if t >= NX:
                sy.wait_ge(s_xq.h, xq_done[t - NX])
            sy.dma_start(out=xt[bi][:],
                         in_=x_in[row0:row0 + 128, :]).then_inc(s_xp[bi].h, 16)
        s_xp[bi].n += 16

        @pr.on("scalar")
        def _(sc, t=t, bi=bi, x_n=s_xp[bi].n):
            sc.wait_ge(s_xp[bi].h, x_n)
            if t >= NX:
                sc.wait_ge(s_lnv.h, lnv_done[t - NX])
            sc.activation(lnscr[:], xt[bi][:], AF.Copy, accum_out=stat[bi][:, 0:1])
            sc.activation(lnscr[:], xt[bi][:], AF.Square,
                          accum_out=stat[bi][:, 1:2]).then_inc(s_st1.h, 1)
        s_st1.n += 1

        @pr.on("gpsimd")
        def _(g, bi=bi, st1_n=s_st1.n):
            g.wait_ge(s_st1.h, st1_n)
            g.tensor_single_scalar(stat[bi][:, 2:3], stat[bi][:, 0:1], 1.0 / D, A.mult)
            g.tensor_single_scalar(stat[bi][:, 3:4], stat[bi][:, 1:2], 1.0 / D, A.mult)
            g.tensor_tensor(stat[bi][:, 4:5], stat[bi][:, 2:3], stat[bi][:, 2:3], A.mult)
            g.tensor_tensor(stat[bi][:, 3:4], stat[bi][:, 3:4], stat[bi][:, 4:5], A.subtract)
            g.tensor_single_scalar(stat[bi][:, 3:4], stat[bi][:, 3:4], LN_EPS,
                                   A.add).then_inc(s_pmath.h, 1)
        s_pmath.n += 1

        @pr.on("scalar")
        def _(sc, bi=bi, pmath_n=s_pmath.n):
            sc.wait_ge(s_pmath.h, pmath_n)
            sc.activation(stat[bi][:, 5:6], stat[bi][:, 3:4], AF.Sqrt).then_inc(s_sqrt.h, 1)
        s_sqrt.n += 1

        @pr.on("vector")
        def _(v, t=t, bi=bi, sqrt_n=s_sqrt.n):
            v.wait_ge(s_sqrt.h, sqrt_n)
            v.reciprocal(stat[bi][:, 6:7], stat[bi][:, 5:6])
            v.tensor_copy(xscr[0:1, 0:1], xscr[0:1, 0:1])
            v.tensor_copy(xscr[0:1, 1:2], xscr[0:1, 1:2])
            v.tensor_copy(xscr[0:1, 2:3], xscr[0:1, 2:3])
            v.tensor_scalar(xnt[bi][:], xt[bi][:], stat[bi][:, 2:3], stat[bi][:, 6:7],
                            A.subtract, A.mult)
            v.tensor_copy(xnb[bi][:], xnt[bi][:])
            v.tensor_reduce(xam[bi][:, 0:32].unsqueeze(2),
                            xnt[bi][:].rearrange("p (b c) -> p b c", b=32, c=BK),
                            AX.X, A.max, apply_absolute_value=True).then_inc(s_lnv.h, 1)
        s_lnv.n += 1
        lnv_done[t] = s_lnv.n

        @pr.on("vector")
        def _(v, t=t, bi=bi):
            if t >= NX:
                _xd = xst_done[t - NX]
                v.wait_ge(s_xstp[0].h, _xd[0])
                v.wait_ge(s_xstp[1].h, _xd[1])
            emit_ladder_m(v, xam[bi], xp[bi], xscr, xm23[bi], xm33[bi], xm52[bi], 32)
            emit_quant(v, xnt[bi][:], xqb[bi][:], xm23[bi], xm33[bi], xm52[bi],
                       qt1[:], qt2[:], 32)
            v.tensor_copy(xscr[0:1, 4:5], xscr[0:1, 4:5]).then_inc(s_xq.h, 1)
        s_xq.n += 1
        xq_done[t] = s_xq.n

        @pr.on("gpsimd")
        def _(g, bi=bi, row0=row0, own=own, orow0=orow0, xq_n=s_xq.n):
            g.wait_ge(s_xq.h, xq_n)
            g.dma_start(out=xn_tm[64 + row0:64 + row0 + 128, :],
                        in_=xnb[bi][:]).then_inc(s_xstp[bi].h, 16)
            if own:
                g.dma_start(out=xnq_tm[orow0:orow0 + 128, :],
                            in_=xqb[bi][:]).then_inc(s_xstp[bi].h, 16)
        s_xstp[bi].n += 16 + (16 if own else 0)
        xst_done[t] = (s_xstp[0].n, s_xstp[1].n)

    fc_done = {}
    gel_done = {}
    hq_done = {}
    hst_done = {}
    xtr_done = {}
    hpool_done = {}

    def prev_half(key):
        grp, tl, half = key
        if half:
            return (grp, tl, 0)
        if tl:
            return (grp, tl - 1, 1)
        if grp:
            return (grp - 1, GTILES - 1, 1)
        return None

    for grp in range(NGRP):
        @pr.on("scalar")
        def _(sc, grp=grp, wtr_n=s_wtr.n, xst_n=None):
            _xd = xst_done[(NLN - NTILE) + grp * GTILES + GTILES - 1]
            sc.wait_ge(s_xstp[0].h, _xd[0])
            sc.wait_ge(s_xstp[1].h, _xd[1])
            sc.wait_ge(s_wtr.h, wtr_n)
            if grp >= 1:
                sc.wait_ge(s_fc.h, fc_done[(grp - 1, GTILES - 1, 1)])
            for fc_ in range(8):
                sc.dma_start_transpose(
                    xnqf_v[:, fc_, :],
                    xnq_tm[grp * 1024:(grp + 1) * 1024,
                           fc_ * 128:(fc_ + 1) * 128]).then_inc(s_xtr.h, 16)
        s_xtr.n += 8 * 16
        xtr_done[grp] = s_xtr.n

        for tl in range(GTILES):
            tok = tl * 128
            for half in range(2):
                key = (grp, tl, half)

                @pr.on("tensor")
                def _(te, key=key, tok=tok, half=half, xtr_n=xtr_done[grp]):
                    te.wait_ge(s_xtr.h, xtr_n)
                    te.wait_ge(s_cst.h, 48)
                    pk = prev_half(key)
                    if pk in gel_done:
                        te.wait_ge(s_gel.h, gel_done[pk])
                    for nb in range(4):
                        hid0 = half * 2048 + nb * 512
                        te.matmul(psum[nb][:], ones[:],
                                  bfc_row[0:1, hid0:hid0 + 512], start=True, stop=False)
                    ins = None
                    for kc in range(8):
                        for nb in range(4):
                            hid0 = half * 2048 + nb * 512
                            ins = te.matmul(psum[nb][:],
                                            xnqf_v[:, kc, tok:tok + 128],
                                            wfcT_v[:, kc, hid0:hid0 + 512],
                                            start=False, stop=(kc == 7))
                    ins.then_inc(s_fc.h, 1)
                s_fc.n += 1
                fc_done[key] = s_fc.n

                @pr.on("scalar")
                def _(sc, key=key, fc_n=s_fc.n):
                    sc.wait_ge(s_fc.h, fc_n)
                    pk = prev_half(key)
                    if pk in hq_done:
                        sc.wait_ge(s_hq.h, hq_done[pk])
                    for nb in range(4):
                        sc.activation(hbuf[:, nb * 512:(nb + 1) * 512], psum[nb][:],
                                      AF.Gelu).then_inc(s_gel.h, 1)
                s_gel.n += 4
                gel_done[key] = s_gel.n

                @pr.on("vector")
                def _(v, key=key, half=half, gel_n=s_gel.n):
                    v.wait_ge(s_gel.h, gel_n)
                    pk = prev_half(key)
                    if pk in hst_done:
                        v.wait_ge(s_hst.h, hst_done[pk])
                    v.tensor_reduce(ham[half][:, 0:64].unsqueeze(2),
                                    hbuf[:].rearrange("p (b c) -> p b c", b=64, c=BK),
                                    AX.X, A.max, apply_absolute_value=True)
                    emit_ladder_m(v, ham[half], hp[half], hscr,
                                  hm23[half], hm33[half], hm52[half], 64)
                    emit_quant(v, hbuf[:], hqb[:], hm23[half], hm33[half], hm52[half],
                               ht1[:], ht2[:], 64)
                    v.tensor_copy(hscr[0:1, 1:2], hscr[0:1, 1:2]).then_inc(s_hq.h, 1)
                s_hq.n += 1
                hq_done[key] = s_hq.n

                @pr.on("gpsimd")
                def _(g, key=key, half=half, hq_n=s_hq.n):
                    grp_, tl_, _ = key
                    g.wait_ge(s_hq.h, hq_n)
                    r0 = grp_ * 1024 + tl_ * 128
                    g.dma_start(out=hq_tm[r0:r0 + 128, half * 2048:(half + 1) * 2048],
                                in_=hqb[:]).then_inc(s_hst.h, 16)
                s_hst.n += 16
                hst_done[key] = s_hst.n

    last_fc = fc_done[(NGRP - 1, GTILES - 1, 1)]
    last_gel = gel_done[(NGRP - 1, GTILES - 1, 1)]

    @pr.on("scalar")
    def _(sc, last_fc=last_fc):
        sc.wait_ge(s_fc.h, last_fc)
        for hc in range(32):
            sc.dma_start_transpose(wpjT_v[:, hc, :],
                                   wqpj_hbm[:, hc * 128:(hc + 1) * 128]).then_inc(s_wtr2.h, 16)
    s_wtr2.n += 32 * 16

    htr_done = {}
    pjk_done = {}
    sil_done = {}
    ores_done = {}
    sgi = 0
    ckidx = 0

    for grp in range(NGRP):
        for sub in range(GTILES // SUBT):
            tok0 = sub * SUBT * 128
            ck0 = ckidx
            for kc in range(32):
                @pr.on("scalar")
                def _(sc, grp=grp, kc=kc, tok0=tok0, ck=ckidx,
                      hst_n=hst_done[(grp, GTILES - 1, 1)]):
                    sc.wait_ge(s_hst.h, hst_n)
                    if ck >= NHQ:
                        sc.wait_ge(s_pjk.h, pjk_done[ck - NHQ])
                    sc.dma_start_transpose(
                        hqT[ck % NHQ][:],
                        hq_tm[grp * 1024 + tok0:grp * 1024 + tok0 + SUBT * 128,
                              kc * 128:(kc + 1) * 128]).then_inc(s_htr8[ck % 8].h, 16)
                s_htr8[ckidx % 8].n += 16
                htr_done[ckidx] = (ckidx % 8, s_htr8[ckidx % 8].n)
                ckidx += 1

            @pr.on("tensor")
            def _(te, sgi=sgi, ck0=ck0, last_gel=last_gel, wtr2_n=s_wtr2.n):
                te.wait_ge(s_wtr2.h, wtr2_n)
                if sgi == 0:
                    te.wait_ge(s_gel.h, last_gel)
                else:
                    te.wait_ge(s_sil.h, sil_done[sgi - 1])
                for st_ in range(SUBT):
                    for pb in range(2):
                        te.matmul(psum[st_ * 2 + pb][:], ones[:],
                                  bpj_row[0:1, pb * 512:(pb + 1) * 512],
                                  start=True, stop=False)
                for kc in range(32):
                    _hs, _hn = htr_done[ck0 + kc]
                    te.wait_ge(s_htr8[_hs].h, _hn)
                    ins = None
                    for st_ in range(SUBT):
                        for pb in range(2):
                            ins = te.matmul(psum[st_ * 2 + pb][:],
                                            hqT[(ck0 + kc) % NHQ][:, st_ * 128:st_ * 128 + 128],
                                            wpjT_v[:, kc, pb * 512:(pb + 1) * 512],
                                            start=False, stop=(kc == 31))
                    ins.then_inc(s_pjk.h, 1)
            for kc in range(32):
                pjk_done[ck0 + kc] = s_pjk.n + kc + 1
            s_pjk.n += 32

            @pr.on("scalar")
            def _(sc, sgi=sgi, pjk_n=s_pjk.n):
                sc.wait_ge(s_pjk.h, pjk_n)
                if sgi >= 1:
                    sc.wait_ge(s_ores.h, ores_done[sgi * SUBT - 1])
                for st_ in range(SUBT):
                    for pb in range(2):
                        sc.activation(gbuf[st_][:, pb * 512:(pb + 1) * 512],
                                      psum[st_ * 2 + pb][:], AF.Silu).then_inc(s_sil.h, 1)
            s_sil.n += 2 * SUBT
            sil_done[sgi] = s_sil.n

            for st_ in range(SUBT):
                t_own = grp * GTILES + sub * SUBT + st_

                @pr.on("vector")
                def _(v, sgi=sgi, st_=st_, t_own=t_own, sil_n=sil_done[sgi]):
                    v.wait_ge(s_sil.h, sil_n)
                    if t_own >= 2:
                        v.wait_ge(s_out.h, (t_own - 1) * 16)
                    v.tensor_copy(obuf[st_ % 2][:].bitcast(BF16)[:, 0:D],
                                  gbuf[st_][:]).then_inc(s_ores.h, 1)
                s_ores.n += 1
                ores_done[t_own] = s_ores.n

                @pr.on("gpsimd")
                def _(g, t_own=t_own, st_=st_, ores_n=s_ores.n):
                    g.wait_ge(s_ores.h, ores_n)
                    r0 = t_own * 128
                    g.dma_start(out=g_tm[r0:r0 + 128, :],
                                in_=obuf[st_ % 2][:].bitcast(BF16)[:, 0:D]).then_inc(s_out.h, 16)
                s_out.n += 16
            sgi += 1

    class St:
        def __init__(self, name, eng, dma):
            self.name, self.eng, self.dma = name, eng, dma
            self.sem = nc.alloc_semaphore("tk_" + name)
            if dma:
                self.dsem = nc.alloc_semaphore("tkd_" + name)
            self.n = 0
            self.q = 0

        def tok(self):
            return (self.sem, self.n)

    sT = St("T", "tensor", False)
    sV = St("V", "vector", False)
    sC = St("C", "scalar", False)
    sCQ = St("CQ", "scalar", True)
    sG = St("G", "gpsimd", False)
    sGQ = St("GQ", "gpsimd", True)
    sY = St("Y", "sync", True)

    _psem = {}

    def step(stream, deps, body, ndma=0, key=None):
        if stream.dma:
            key = key or (stream.name + "_d")
            kk = (stream.name, key)
            if kk not in _psem:
                _psem[kk] = Sem(nc, "pd_" + stream.name + "_" + key)
            ps = _psem[kk]
            ps.n += 16 * ndma
            post = ps.n
            sem = ps.h
        else:
            stream.n += 1
            post = stream.n
            sem = stream.sem
        isdma = stream.dma

        def thunk(e, deps=tuple(deps), body=body, sem=sem):
            for (s, c) in deps:
                if c > 0:
                    e.wait_ge(s, c)

            def fin(instr):
                instr.then_inc(sem, 16 if isdma else 1)
                return instr
            body(e, fin)
        pr.streams[stream.eng].append(thunk)
        return (sem, post)

    B_DONE = [(s_xstp[0].h, s_xstp[0].n), (s_xstp[1].h, s_xstp[1].n),
              (s_out.h, s_out.n)]

    wbigb = wbig[:]
    wqkvT = wbigb[:, 0:24608].rearrange("p (c n) -> p c n", c=8, n=3076)
    c4st = wbigb[:, 24608:27680].rearrange("p (i n) -> p i n", i=96, n=32)
    f3st = wbigb[:, 27680:28704].rearrange("p (i n) -> p i n", i=32, n=32)
    gfin = wbigb[:, 28704:29728]
    o_tmb = wbigb[:, 29728:31776].rearrange("p (s n) -> p s n", s=2, n=1024)
    wstage = wbigb[:, 31776:32768]
    f31st = xnqf[:].rearrange("p (i n) -> p i n", i=256, n=32)
    XW = 304
    xnfm = hbuf[:].bitcast(BF16).rearrange("p (c n) -> p c n", c=8, n=512)
    ht1b = ht1[:].bitcast(BF16)
    pre_ring = ht1b[:, 0:1216].rearrange("p (r n) -> p r n", r=4, n=304)
    vst = ht1b[:, 1216:1816].rearrange("p (r n) -> p r n", r=2, n=300)
    ht2b = ht2[:].bitcast(BF16)
    rep4 = ht2b[:, 0:2400].rearrange("p (r g n) -> p r g n", r=2, g=4, n=300)
    repf = ht2b[:, 2400:3536].rearrange("p (g n) -> p g n", g=4, n=284)
    qfm = xt[0][:].bitcast(BF16).rearrange("p (c n) -> p c n", c=8, n=256)
    kfm = xt[1][:].bitcast(BF16).rearrange("p (c n) -> p c n", c=8, n=256)
    qtm = qt1[:].bitcast(BF16).rearrange("p (s n) -> p s n", s=2, n=1024)
    ktm = qt2[:].bitcast(BF16).rearrange("p (s n) -> p s n", s=2, n=1024)
    vtm = obuf[0][:].bitcast(BF16).rearrange("p (s n) -> p s n", s=2, n=1024)
    dtm = obuf[1][:].bitcast(BF16).rearrange("p (s n) -> p s n", s=2, n=1024)
    vbt = lnscr[:].bitcast(BF16).rearrange("p (s n) -> p s n", s=2, n=1024)
    ghst = vbt
    Ssb = x2[0][:].bitcast(BF16).rearrange("p (h n) -> p h n", h=4, n=512)
    scb = x2[1][:].bitcast(BF16)
    atp = [scb[:, i * 64:(i + 1) * 64] for i in range(6)]
    alo = [scb[:, 384 + i * 64:384 + (i + 1) * 64] for i in range(2)]
    zy = [scb[:, 512 + i * 256:512 + (i + 1) * 256] for i in range(2)]
    upr = scb[:, 1024:1280]
    colm = [scb[:, 1280 + i * 64:1280 + (i + 1) * 64] for i in range(2)]
    rowbuf = scb[:, 1408:1536]
    attn_sb = scb[:, 1536:1600]
    slevv = xnb[1][:].rearrange("p (r n) -> p r n", r=4, n=256)
    mstg = [xqb[0][:], xqb[1][:]]
    grd = xnb[0][:].rearrange("p (r n) -> p r n", r=8, n=128)
    sltm = hqb[:].rearrange("p (i n) -> p i n", i=2, n=1024)
    x_fin = xnt[1][:]
    comb = [xnt[0][:].bitcast(BF16)[:, 0:1024],
            xnt[0][:].bitcast(BF16)[:, 1024:2048]]
    o2buf = xqb[0][:]
    oprjT = [gbuf[i // 2][:].bitcast(BF16)[:, (i % 2) * 1024:(i % 2 + 1) * 1024]
             for i in range(8)]
    fgw1r = [hqT[i][:] for i in range(8)]
    ofs32 = qt1[:]
    zrof = qt2[:]

    arsn = sb("arsn", [128, 48], F32)
    wsm = sb("wsm", [128, 24], F32)
    s_fin = sb("s_fin", [128, D], F32)
    outst = lnscr[:]
    vgscr = sb("vgscr", [128, 8], F32)
    cst128 = sb("cst128", [128, 192], F32)
    eye128 = sb("eye128_sb", [128, 64], F32)
    ones2 = sb("ones2_sb", [128, 64], BF16)
    dmask_sb = sb("dmask_s2", [128, 32], F32)
    ccols = s_fin[:, 0:384]
    onw_sb = sb("onw_sb", [128, 8], F32)
    fgb1_sb = sb("fgb1_sb", [1, 1024], BF16)
    fgb2_sb = sb("fgb2_sb", [1, 4], BF16)
    fgw2_sb = sb("fgw2_sb", [128, 32], BF16)
    invt = sb("invt", [1, 8], F32)
    invt_col = sb("invt_col", [128, 1], F32)

    fgw1_bhbm = nc.dram_tensor("fgw1_bhbm", [1792, 1024], BF16).ap()
    pre_hbm = nc.dram_tensor("pre_hbm", [512, 304], BF16).ap()
    invt_hbm = nc.dram_tensor("invt_hbm", [1, 1], F32).ap()

    def VG(v, n=3):
        for i in range(n):
            v.tensor_copy(vgscr[0:1, i:i + 1], vgscr[0:1, i:i + 1])

    def _wq_loads(g, fin):
        for kc in range(8):
            fin(g.dma_start(out=wqkvT[:, kc, :],
                            in_=wqkv_in[kc * 128:(kc + 1) * 128, :]))
        fin(g.dma_start(out=fgb1_sb[:], in_=fgb1_in[:]))
        fin(g.dma_start(out=fgb2_sb[:], in_=fgb2_in[:]))
        for kc in range(8):
            fin(g.dma_start(out=fgw2_sb[:, kc * 4:kc * 4 + 4],
                            in_=fgw2_in[kc * 128:(kc + 1) * 128, :]))
        fin(g.dma_start(out=ones2[:], in_=ones2_in[:]))
    t_wq = step(sGQ, B_DONE, _wq_loads, ndma=19, key="wld")

    def _consts(sy, fin):
        fin(sy.dma_start(out=cst128[:, 0:64], in_=umask_in[:]))
        fin(sy.dma_start(out=cst128[:, 64:128], in_=lmask_in[:]))
        fin(sy.dma_start(out=cst128[:, 128:192], in_=amask_in[:]))
        fin(sy.dma_start(out=eye128[:], in_=eye_in[:]))
        fin(sy.dma_start(out=dmask_sb[:], in_=dmask_in[:]))
        fin(sy.dma_start(out=ccols[:, 0:96], in_=c4cols_in[:]))
        fin(sy.dma_start(out=ccols[:, 96:352], in_=f31cols_in[:]))
        fin(sy.dma_start(out=ccols[:, 352:384], in_=f3cols_in[:]))
        fin(sy.dma_start(out=onw_sb[:], in_=onw_in[:]))
        fin(sy.dma_start(out=invt[0:1, 0:1], in_=ltemp_in[:]))
        fin(sy.dma_start(out=xn_tm[0:64, :],
                         in_=zeros_in[0:64, 0:512].bitcast(BF16)))
        zb = zeros_in[0:128, :].bitcast(BF16)
        VROW = XROWS + 8
        for ch in range(8):
            i_ap = bass.AP(zb.tensor, 0, [[2048, 128], [0, 5], [1, 1640]])
            o_ap = bass.AP(v_hbm.tensor, ch * 128 * VROW,
                           [[VROW, 128], [1640, 5], [1, 1640]])
            fin(sy.dma_start(out=o_ap, in_=i_ap))
    t_cst = step(sY, B_DONE, _consts, ndma=19, key="cst")

    def _convst(v, fin):
        for i in range(96):
            v.tensor_scalar(c4st[:, i, :], dmask_sb[:], ccols[:, i:i + 1], None, A.mult)
        for i in range(256):
            v.tensor_scalar(f31st[:, i, :], dmask_sb[:], ccols[:, 96 + i:97 + i],
                            None, A.mult)
        for i in range(32):
            v.tensor_scalar(f3st[:, i, :], dmask_sb[:], ccols[:, 352 + i:353 + i],
                            None, A.mult)
        v.memset(zrof[:], 0.0)
        VG(v)
        for h in range(4):
            v.tensor_copy(Ssb[:, h, :], zrof[:, 0:512])
        fin(v.tensor_copy(vgscr[0:1, 7:8], vgscr[0:1, 7:8]))
    t_cvst = step(sV, [t_cst] + B_DONE, _convst)

    t_pv, t_pg = t_cvst, t_wq
    for i in range(14):
        t_rd = step(sY, [t_pv, t_pg],
                    lambda sy, fin, i=i: fin(sy.dma_start(
                        out=ofs32[:], in_=fgw1_in[i * 128:(i + 1) * 128, :])),
                    ndma=1, key="fgrd")
        t_pv = step(sV, [t_rd],
                    lambda v, fin: fin(v.tensor_copy(wstage[:, 0:992] if False
                                                     else gfin[:, 0:1024], ofs32[:])))
        t_pg = step(sGQ, [t_pv],
                    lambda g, fin, i=i: fin(g.dma_start(
                        out=fgw1_bhbm[i * 128:(i + 1) * 128, :],
                        in_=gfin[:, 0:1024])), ndma=1, key="fgst")
    t_fgw1 = t_pg

    for kc in range(8):
        t_rd = step(sY, [t_pv, t_pg],
                    lambda sy, fin, kc=kc: fin(sy.dma_start(
                        out=ofs32[:], in_=oprj_in[kc * 128:(kc + 1) * 128, :])),
                    ndma=1, key="oprd")
        t_pv = step(sV, [t_rd],
                    lambda v, fin, kc=kc: fin(v.tensor_scalar(
                        oprjT[kc][:], ofs32[:], onw_sb[:, kc:kc + 1], None, A.mult)))
    t_oprep = t_pv

    def _sp1(sc, fin):
        def CG():
            for _ in range(3):
                sc.activation(rowbuf[0:1, 0:1], invt[0:1, 0:1], AF.Copy)
        sc.activation(invt[0:1, 1:2], invt[0:1, 0:1], AF.Abs)
        sc.activation(invt[0:1, 4:5], invt[0:1, 0:1], AF.Relu)
        CG()
        sc.activation(invt[0:1, 2:3], invt[0:1, 1:2], AF.Exp, scale=-1.0)
        CG()
        fin(sc.activation(invt[0:1, 3:4], invt[0:1, 2:3], AF.Ln, bias=1.0))
    t_sp1 = step(sC, [t_cst], _sp1)
    def _sp2(v, fin):
        v.tensor_tensor(invt[0:1, 5:6], invt[0:1, 3:4], invt[0:1, 4:5], A.add)
        VG(v)
        v.tensor_single_scalar(invt[0:1, 6:7], invt[0:1, 5:6], 1e-4, A.add)
        VG(v)
        v.reciprocal(invt[0:1, 7:8], invt[0:1, 6:7])
        VG(v)
        fin(v.tensor_copy(vgscr[0:1, 6:7], vgscr[0:1, 6:7]))
    t_sp2 = step(sV, [t_sp1], _sp2)
    def _sp3(g, fin):
        fin(g.dma_start(out=invt_hbm[:], in_=invt[0:1, 7:8]))
    t_sp3a = step(sGQ, [t_sp2], _sp3, ndma=1, key="invt")

    def _sp4(g, fin):
        in_ap = bass.AP(invt_hbm.tensor, 0, [[0, 128], [1, 1]])
        fin(g.dma_start(out=invt_col[:], in_=in_ap))
    t_sp3 = step(sGQ, [t_sp3a], _sp4, ndma=1, key="invt")

    NOWN = NAT // 2
    t_tile_prev = t_cvst
    t_last_out = t_oprep

    for t in range(NAT):
        own = t >= NOWN
        tok0 = t * ATILE
        ot0 = tok0 - PRE

        def _xnrd(sc, fin, tok0=tok0):
            for c in range(8):
                fin(sc.dma_start_transpose(
                    xnfm[:, c, 0:XW],
                    xn_tm[tok0 + 16:tok0 + 16 + XW, c * 128:(c + 1) * 128]))
        t_xn = step(sCQ, B_DONE[0:2] + [t_tile_prev, t_cvst], _xnrd, ndma=8, key="xnr")

        chunk_list = ([("q", c) for c in range(8)] if own else []) + \
                     [("k", c) for c in range(8)] + [("v", c) for c in range(8)]
        t_sil = {}
        t_evac = {}
        t_conv = {}
        t_w = {}
        pi = 0
        for (kind, c) in chunk_list:
            mc = {"q": 0, "k": 8, "v": 16}[kind] + c
            bank = 6 + pi % 2
            ring = pi % 4
            par = pi % 2

            def _proj(te, fin, mc=mc, bank=bank):
                ins = None
                for kc in range(8):
                    ins = te.matmul(psum[bank][:, 0:XW],
                                    wqkvT[:, kc, mc * 128:(mc + 1) * 128],
                                    xnfm[:, kc, 0:XW],
                                    start=(kc == 0), stop=(kc == 7))
                fin(ins)
            deps = [t_xn, t_wq]
            if pi - 2 in t_sil:
                deps.append(t_sil[pi - 2])
            t_p = step(sT, deps, _proj)

            def _evpre(sc, fin, bank=bank, ring=ring):
                fin(sc.activation(pre_ring[:, ring, :], psum[bank][:, 0:XW], AF.Copy))
            t_e = step(sC, [t_p], _evpre)
            t_evac[pi] = t_e

            def _prew(g, fin, ring=ring):
                fin(g.dma_start(out=pre_hbm[ring * 128:(ring + 1) * 128, :],
                                in_=pre_ring[:, ring, :]))
            deps_pw = [t_e]
            if pi - 4 in t_conv:
                deps_pw.append(t_conv[pi - 4])
            t_pw = step(sGQ, deps_pw, _prew, ndma=1, key=f"prew{ring}")

            def _reprd(g, fin, kind=kind, ring=ring, par=par):
                off = 1 if kind == "v" else 45
                W = 300 if kind == "v" else 256
                for g4 in range(4):
                    in_ap = bass.AP(pre_hbm.tensor,
                                    (ring * 128 + 32 * g4) * 304 + off,
                                    [[1, 4], [304, 32], [1, W]])
                    fin(g.dma_start(out=rep4[:, par, g4, 0:W], in_=in_ap))
            deps_rr = [t_pw]
            if pi - 2 in t_conv:
                deps_rr.append(t_conv[pi - 2])
            t_rr = step(sGQ, deps_rr, _reprd, ndma=4, key=f"rep{par}")

            def _conv(te, fin, kind=kind, c=c, par=par, bank=bank):
                base = {"q": 0, "k": 32, "v": 64}[kind]
                W = 300 if kind == "v" else 256
                ins = None
                for g4 in range(4):
                    ins = te.matmul(psum[bank][32 * g4:32 * g4 + 32, 0:W],
                                    c4st[:, base + c * 4 + g4, :],
                                    rep4[:, par, g4, 0:W],
                                    start=True, stop=True,
                                    tile_position=(0, 32 * g4))
                fin(ins)
            t_c = step(sT, [t_e, t_rr], _conv)
            t_conv[pi] = t_c

            def _silu(sc, fin, kind=kind, c=c, bank=bank, par=par):
                if kind == "q":
                    fin(sc.activation(qfm[:, c, :], psum[bank][:, 0:256], AF.Silu))
                elif kind == "k":
                    fin(sc.activation(kfm[:, c, :], psum[bank][:, 0:256], AF.Silu))
                else:
                    fin(sc.activation(vst[:, par, :], psum[bank][:, 0:300], AF.Silu))
            deps_s = [t_c]
            if kind == "v" and pi - 2 in t_w:
                deps_s.append(t_w[pi - 2])
            t_s = step(sC, deps_s, _silu)
            t_sil[pi] = t_s

            def _whbm(g, fin, kind=kind, c=c, tok0=tok0, par=par, t=t):
                if kind == "v":
                    if t == 0:
                        fin(g.dma_start(out=v_hbm[c * 128:(c + 1) * 128, 0:256],
                                        in_=vst[:, par, 44:300]))
                    else:
                        fin(g.dma_start(
                            out=v_hbm[c * 128:(c + 1) * 128, tok0 - 44:tok0 + 256],
                            in_=vst[:, par, :]))
                else:
                    src = qfm if kind == "q" else kfm
                    hb = q_hbm if kind == "q" else k_hbm
                    fin(g.dma_start(out=hb[c * 128:(c + 1) * 128, tok0:tok0 + 256],
                                    in_=src[:, c, :]))
            wkey = f"vw{c % 2}" if kind == "v" else "whbm"
            t_wr = step(sGQ, [t_s], _whbm, ndma=1, key=wkey)
            t_w[pi] = t_wr
            pi += 1

        t_beta = {}
        for sub in range(2):
            bank = 6 + sub

            def _bproj(te, fin, sub=sub, bank=bank):
                ins = None
                for kc in range(8):
                    ins = te.matmul(psum[bank][:, 0:4],
                                    xnfm[:, kc, 48 + sub * 128:48 + sub * 128 + 128],
                                    wqkvT[:, kc, 3072:3076],
                                    start=(kc == 0), stop=(kc == 7))
                fin(ins)
            t_bp = step(sT, [t_xn, t_wq, t_sil[pi - 2 + sub]], _bproj)

            def _bev(sc, fin, sub=sub, bank=bank):
                fin(sc.activation(arsn[:, sub * 24 + 8:sub * 24 + 12],
                                  psum[bank][:, 0:4], AF.Sigmoid))
            t_beta[sub] = step(sC, [t_bp], _bev)

        t_fir = {}
        if own:
            for c in range(8):
                def _repf(g, fin, c=c, tok0=tok0):
                    VROW = XROWS + 8
                    for g4 in range(4):
                        in_ap = bass.AP(v_hbm.tensor,
                                        (c * 128 + 32 * g4) * VROW + tok0 - 30,
                                        [[1, 4], [VROW, 32], [1, 284]])
                        fin(g.dma_start(out=repf[:, g4, :], in_=in_ap))
                deps_rf = [t_w[16 + c]]
                if ("mm", c - 1) in t_fir:
                    deps_rf.append(t_fir[("mm", c - 1)])
                t_rf = step(sGQ, deps_rf, _repf, ndma=4, key="repf")

                def _firmm(te, fin, c=c):
                    ins = None
                    for g4 in range(4):
                        te.matmul(psum[6][32 * g4:32 * g4 + 32, 0:256],
                                  f3st[:, c * 4 + g4, :], repf[:, g4, 28:284],
                                  start=True, stop=True, tile_position=(0, 32 * g4))
                    for tb in range(8):
                        for g4 in range(4):
                            ins = te.matmul(psum[7][32 * g4:32 * g4 + 32, 0:256],
                                            f31st[:, (c * 4 + g4) * 8 + tb, :],
                                            repf[:, g4, 4 * tb:4 * tb + 256],
                                            start=(tb == 0), stop=(tb == 7),
                                            tile_position=(0, 32 * g4))
                    fin(ins)
                deps_fm = [t_rf, t_beta[1]]
                if ("ev", c - 1) in t_fir:
                    deps_fm.append(t_fir[("ev", c - 1)])
                t_fm = step(sT, deps_fm, _firmm)
                t_fir[("mm", c)] = t_fm

                def _firev(sc, fin, c=c):
                    sc.activation(slevv[:, (2 * c) % 4, :], psum[6][:, 0:256], AF.Copy)
                    fin(sc.activation(slevv[:, (2 * c + 1) % 4, :], psum[7][:, 0:256],
                                      AF.Copy))
                deps_fe = [t_fm]
                if ("w", c - 2) in t_fir:
                    deps_fe.append(t_fir[("w", c - 2)])
                t_fe = step(sC, deps_fe, _firev)
                t_fir[("ev", c)] = t_fe

                def _firw(g, fin, c=c, ot0=ot0):
                    fin(g.dma_start(out=sh_hbm[c * 128:(c + 1) * 128, ot0:ot0 + 256],
                                    in_=slevv[:, (2 * c) % 4, :]))
                    fin(g.dma_start(out=lo_hbm[c * 128:(c + 1) * 128, ot0:ot0 + 256],
                                    in_=slevv[:, (2 * c + 1) % 4, :]))
                t_fw = step(sGQ, [t_fe], _firw, ndma=2, key="firw")
                t_fir[("w", c)] = t_fw

        t_sub = {}
        for sub in range(2):
            def _tmrd(sc, fin, sub=sub, tok0=tok0, own=own):
                t0 = tok0 + sub * 128
                fin(sc.dma_start_transpose(ktm[:, sub, :], k_hbm[:, t0:t0 + 128]))
                fin(sc.dma_start_transpose(vtm[:, sub, :], v_hbm[:, t0:t0 + 128]))
                if own:
                    fin(sc.dma_start_transpose(qtm[:, sub, :], q_hbm[:, t0:t0 + 128]))
            t_rd = step(sCQ, [t_w[pi - 9], t_w[pi - 2], t_w[pi - 1]], _tmrd,
                        ndma=3 if own else 2, key="tmrd")

            def _l2a(v, fin, sub=sub, own=own):
                base = sub * 24
                v.tensor_tensor(o2buf[:].bitcast(BF16)[:, 0:1024], ktm[:, sub, :],
                                ktm[:, sub, :], A.mult)
                VG(v)
                v.tensor_reduce(arsn[:, base + 4:base + 8].unsqueeze(2),
                                o2buf[:].bitcast(BF16)[:, 0:1024]
                                .rearrange("p (h n) -> p h n", h=4, n=256),
                                AX.X, A.add)
                VG(v)
                v.tensor_scalar(arsn[:, base + 4:base + 8],
                                arsn[:, base + 4:base + 8], 1e-12, None, A.add)
                if own:
                    v.tensor_tensor(o2buf[:].bitcast(BF16)[:, 0:1024], qtm[:, sub, :],
                                    qtm[:, sub, :], A.mult)
                    VG(v)
                    v.tensor_reduce(arsn[:, base + 0:base + 4].unsqueeze(2),
                                    o2buf[:].bitcast(BF16)[:, 0:1024]
                                    .rearrange("p (h n) -> p h n", h=4, n=256),
                                    AX.X, A.add)
                    VG(v)
                    v.tensor_scalar(arsn[:, base + 0:base + 4],
                                    arsn[:, base + 0:base + 4], 1e-12, None, A.add)
                fin(v.tensor_copy(vgscr[0:1, 4:5], vgscr[0:1, 4:5]))
            t_l1 = step(sV, [t_rd, t_beta[sub]], _l2a)

            def _sqr(sc, fin, sub=sub, own=own):
                base = sub * 24
                ins = sc.activation(arsn[:, base + 4:base + 8],
                                    arsn[:, base + 4:base + 8], AF.Sqrt)
                if own:
                    ins = sc.activation(arsn[:, base + 0:base + 4],
                                        arsn[:, base + 0:base + 4], AF.Sqrt)
                fin(ins)
            t_l2 = step(sC, [t_l1], _sqr)

            def _l3(v, fin, sub=sub, own=own):
                base = sub * 24
                v.reciprocal(arsn[:, base + 4:base + 8], arsn[:, base + 4:base + 8])
                if own:
                    v.reciprocal(arsn[:, base + 0:base + 4], arsn[:, base + 0:base + 4])
                VG(v)
                v.tensor_tensor(arsn[:, base + 12:base + 16],
                                arsn[:, base + 8:base + 12],
                                arsn[:, base + 4:base + 8], A.mult)
                VG(v)
                v.tensor_scalar(arsn[:, base + 16:base + 20],
                                arsn[:, base + 12:base + 16], -1.0, None, A.mult)
                VG(v)
                for h in range(4):
                    v.tensor_scalar(ktm[:, sub, h * 256:(h + 1) * 256],
                                    ktm[:, sub, h * 256:(h + 1) * 256],
                                    arsn[:, base + 4 + h:base + 5 + h], None, A.mult)
                for h in range(4):
                    v.tensor_scalar(vbt[:, sub, h * 256:(h + 1) * 256],
                                    vtm[:, sub, h * 256:(h + 1) * 256],
                                    arsn[:, base + 8 + h:base + 9 + h], None, A.mult)
                fin(v.tensor_copy(vgscr[0:1, 3:4], vgscr[0:1, 3:4]))
            t_sub[sub] = step(sV, [t_l2], _l3)

        t_sev = t_tile_prev
        for c in range(NCH):
            sub = c // 2
            rb = (c % 2) * 64
            cl = c * 64
            base = sub * 24
            for h in range(4):
                first = (t == 0 and c == 0)

                def _trp(te, fin, rb=rb, base=base, h=h, sub=sub):
                    te.transpose(psum[4][0:1, 0:64],
                                 arsn[rb:rb + 64, base + 4 + h:base + 5 + h],
                                 eye128[rb:rb + 64, 0:64],
                                 tile_position=(rb, 0))
                    fin(te.transpose(psum[4][0:1, 64:128],
                                     arsn[rb:rb + 64, base + 12 + h:base + 13 + h],
                                     eye128[rb:rb + 64, 0:64],
                                     tile_position=(rb, 0)))
                t_tr = step(sT, [t_sub[sub], t_sev], _trp)

                def _rev(sc, fin, rb=rb):
                    fin(sc.activation(rowbuf[rb:rb + 1, 0:128],
                                      psum[4][0:1, 0:128], AF.Copy))
                t_re = step(sC, [t_tr], _rev)

                def _bc(te, fin, rb=rb):
                    te.matmul(psum[4][rb:rb + 64, 0:64], ones2[rb:rb + 1, 0:64],
                              rowbuf[rb:rb + 1, 64:128], start=True, stop=True,
                              tile_position=(rb, rb))
                    fin(te.matmul(psum[4][rb:rb + 64, 64:128], ones2[rb:rb + 1, 0:64],
                                  rowbuf[rb:rb + 1, 0:64], start=True, stop=True,
                                  tile_position=(rb, rb)))
                t_b2 = step(sT, [t_re], _bc)

                def _cm(v, fin, rb=rb):
                    v.tensor_tensor(colm[0][rb:rb + 64, :], cst128[rb:rb + 64, 0:64],
                                    psum[4][rb:rb + 64, 0:64], A.mult)
                    fin(v.tensor_tensor(colm[1][rb:rb + 64, :],
                                        cst128[rb:rb + 64, 64:128],
                                        psum[4][rb:rb + 64, 64:128], A.mult))
                t_cm = step(sV, [t_b2], _cm)

                def _a0(te, fin, c=c, h=h, cl=cl, rb=rb):
                    ins = None
                    for kk in range(2):
                        ins = te.matmul(psum[5][rb:rb + 64, 0:64],
                                        kfm[:, 2 * h + kk, cl:cl + 64],
                                        kfm[:, 2 * h + kk, cl:cl + 64],
                                        start=(kk == 0), stop=(kk == 1),
                                        tile_position=(0, rb))
                    fin(ins)
                t_a0 = step(sT, [t_cm], _a0)

                def _am(v, fin, rb=rb, base=base, h=h):
                    v.scalar_tensor_tensor(atp[0][rb:rb + 64, :],
                                           psum[5][rb:rb + 64, 0:64],
                                           arsn[rb:rb + 64, base + 4 + h:base + 5 + h],
                                           colm[0][rb:rb + 64, :], A.mult, A.mult)
                    fin(v.scalar_tensor_tensor(alo[0][rb:rb + 64, :],
                                               psum[5][rb:rb + 64, 0:64],
                                               arsn[rb:rb + 64, base + 12 + h:base + 13 + h],
                                               colm[1][rb:rb + 64, :], A.mult, A.mult))
                t_am = step(sV, [t_a0], _am)

                t_pw = t_am
                for lvl in range(5):
                    def _pm(te, fin, lvl=lvl, rb=rb):
                        te.matmul(psum[4][rb:rb + 64, 0:64], alo[lvl % 2][rb:rb + 64, :],
                                  atp[lvl][rb:rb + 64, :], start=True, stop=True,
                                  tile_position=(rb, rb))
                        if lvl < 4:
                            fin(te.matmul(psum[4][rb:rb + 64, 64:128],
                                          atp[lvl][rb:rb + 64, :],
                                          alo[lvl % 2][rb:rb + 64, :],
                                          start=True, stop=True,
                                          tile_position=(rb, rb)))
                        else:
                            fin(te.nop())
                    t_pm = step(sT, [t_pw], _pm)

                    def _pe(sc, fin, lvl=lvl, rb=rb):
                        ins = sc.activation(atp[lvl + 1][rb:rb + 64, :],
                                            psum[4][rb:rb + 64, 0:64], AF.Copy)
                        if lvl < 4:
                            ins = sc.activation(alo[(lvl + 1) % 2][rb:rb + 64, :],
                                                psum[4][rb:rb + 64, 64:128], AF.Copy)
                        fin(ins)
                    t_pw = step(sC, [t_pm], _pe)

                def _zm(te, fin, c=c, h=h, cl=cl, rb=rb):
                    ins = None
                    for kk in range(2):
                        ins = te.matmul(psum[5][rb:rb + 64, 0:256],
                                        kfm[:, 2 * h + kk, cl:cl + 64],
                                        Ssb[:, h, kk * 256:(kk + 1) * 256],
                                        start=(kk == 0), stop=(kk == 1),
                                        tile_position=(0, rb))
                    fin(ins)
                t_zm = step(sT, [t_am, t_sev], _zm)

                def _ze(v, fin, rb=rb, base=base, h=h, sub=sub):
                    fin(v.scalar_tensor_tensor(
                        zy[0][rb:rb + 64, :], psum[5][rb:rb + 64, 0:256],
                        arsn[rb:rb + 64, base + 16 + h:base + 17 + h],
                        vbt[:, sub, h * 256:(h + 1) * 256][rb:rb + 64, :],
                        A.mult, A.add))
                t_ze = step(sV, [t_zm, t_pw], _ze)

                t_ap = t_ze
                src_i = 0
                for l in (5, 4, 3, 2, 1, 0):
                    def _am2(te, fin, l=l, src_i=src_i, rb=rb):
                        fin(te.matmul(psum[5][rb:rb + 64, 0:256], atp[l][rb:rb + 64, :],
                                      zy[src_i][rb:rb + 64, :], start=True, stop=True,
                                      tile_position=(rb, rb)))
                    t_am2 = step(sT, [t_ap], _am2)

                    def _av(v, fin, l=l, src_i=src_i, rb=rb):
                        dst = zy[1 - src_i] if l > 0 else upr
                        fin(v.tensor_tensor(dst[rb:rb + 64, :], zy[src_i][rb:rb + 64, :],
                                            psum[5][rb:rb + 64, 0:256],
                                            A.add if l > 0 else A.subtract))
                    t_ap = step(sV, [t_am2], _av)
                    if l > 0:
                        src_i = 1 - src_i

                if own:
                    def _at(te, fin, c=c, h=h, cl=cl, rb=rb):
                        ins = None
                        for kk in range(2):
                            ins = te.matmul(psum[4][rb:rb + 64, 0:64],
                                            kfm[:, 2 * h + kk, cl:cl + 64],
                                            qfm[:, 2 * h + kk, cl:cl + 64],
                                            start=(kk == 0), stop=(kk == 1),
                                            tile_position=(0, rb))
                        fin(ins)
                    t_at = step(sT, [t_ap], _at)

                    def _atv(v, fin, rb=rb, base=base, h=h):
                        fin(v.scalar_tensor_tensor(
                            attn_sb[rb:rb + 64, :], psum[4][rb:rb + 64, 0:64],
                            arsn[rb:rb + 64, base + 4 + h:base + 5 + h],
                            cst128[rb:rb + 64, 128:192], A.mult, A.mult))
                    t_atv = step(sV, [t_at], _atv)

                    def _om(te, fin, c=c, h=h, cl=cl, rb=rb):
                        for kk in range(2):
                            te.matmul(psum[5][rb:rb + 64, 0:256],
                                      qfm[:, 2 * h + kk, cl:cl + 64],
                                      Ssb[:, h, kk * 256:(kk + 1) * 256],
                                      start=(kk == 0), stop=False,
                                      tile_position=(0, rb))
                        fin(te.matmul(psum[5][rb:rb + 64, 0:256],
                                      attn_sb[rb:rb + 64, :], upr[rb:rb + 64, :],
                                      start=False, stop=True,
                                      tile_position=(rb, rb)))
                    t_om = step(sT, [t_atv], _om)

                    def _de(v, fin, rb=rb, base=base, h=h, sub=sub):
                        fin(v.tensor_scalar(
                            dtm[:, sub, h * 256:(h + 1) * 256][rb:rb + 64, :],
                            psum[5][rb:rb + 64, 0:256],
                            arsn[rb:rb + 64, base + 0 + h:base + 1 + h], None, A.mult))
                    t_pre_s = step(sV, [t_om], _de)
                else:
                    t_pre_s = t_ap

                def _su(te, fin, h=h, sub=sub, rb=rb, first=first):
                    ins = None
                    for kk in range(2):
                        ins = te.matmul(psum[h][:, kk * 256:(kk + 1) * 256],
                                        ktm[:, sub, h * 256 + kk * 128:
                                            h * 256 + (kk + 1) * 128][rb:rb + 64, :],
                                        upr[rb:rb + 64, :],
                                        start=first, stop=True,
                                        tile_position=(rb, 0))
                    fin(ins)
                t_su = step(sT, [t_pre_s], _su)

                def _se(sc, fin, h=h):
                    fin(sc.activation(Ssb[:, h, :], psum[h][:, 0:512], AF.Copy))
                t_sev = step(sC, [t_su], _se)

        if own:
            t_m = {}
            for sub in range(2):
                def _slrd(sc, fin, sub=sub, ot0=ot0):
                    t0 = ot0 + sub * 128
                    fin(sc.dma_start_transpose(sltm[:, 0, :], sh_hbm[:, t0:t0 + 128]))
                    fin(sc.dma_start_transpose(sltm[:, 1, :], lo_hbm[:, t0:t0 + 128]))
                deps_sr = [t_fir[("w", 7)], t_sev]
                if sub == 1:
                    deps_sr.append(t_m[0])
                t_sr = step(sCQ, deps_sr, _slrd, ndma=2, key="slrd")

                def _mn(v, fin, sub=sub):
                    st = mstg[sub]
                    for (src, col) in ((sltm[:, 0, :], 0), (sltm[:, 1, :], 256),
                                       (dtm[:, sub, :], 512)):
                        v.tensor_tensor(st[:, 2 * col:2 * col + 512]
                                        if False else st[:, col * 2:col * 2 + 2][:, 0:0]
                                        if False else st[:, col:col + 256],
                                        src[:, 0:256], src[:, 256:512], A.add)
                        v.tensor_tensor(o2buf[:].bitcast(BF16)[:, 0:256],
                                        src[:, 512:768], src[:, 768:1024], A.add)
                        VG(v)
                        v.tensor_tensor(st[:, col:col + 256], st[:, col:col + 256],
                                        o2buf[:].bitcast(BF16)[:, 0:256], A.add)
                        VG(v)
                        v.tensor_scalar(st[:, col:col + 256], st[:, col:col + 256],
                                        0.25, None, A.mult)
                    fin(v.tensor_copy(vgscr[0:1, 1:2], vgscr[0:1, 1:2]))
                t_mv = step(sV, [t_sr], _mn)

                def _mw(g, fin, sub=sub, ot0=ot0):
                    fin(g.dma_start(
                        out=mn_hbm[ot0 + sub * 128:ot0 + sub * 128 + 128, :],
                        in_=mstg[sub][:, 0:768]))
                t_m[sub] = step(sGQ, [t_mv], _mw, ndma=1, key="mnw")

            t_gmm = {}
            t_gr = {}
            for kc in range(14):
                def _fgr(g, fin, kc=kc):
                    fin(g.dma_start(out=fgw1r[2 * (kc % 4)][:],
                                    in_=fgw1_bhbm[kc * 128:(kc + 1) * 128, 0:512]))
                    fin(g.dma_start(out=fgw1r[2 * (kc % 4) + 1][:],
                                    in_=fgw1_bhbm[kc * 128:(kc + 1) * 128, 512:1024]))
                depf = [t_fgw1]
                if kc - 4 in t_gmm:
                    depf.append(t_gmm[kc - 4])
                t_fg = step(sGQ, depf, _fgr, ndma=2, key=f"fgw1{kc % 4}")

                if kc >= 8:
                    def _mr(sc, fin, kc=kc, ot0=ot0):
                        for sub in range(2):
                            fin(sc.dma_start_transpose(
                                grd[:, 2 * (kc % 2) + sub, :],
                                mn_hbm[ot0 + sub * 128:ot0 + sub * 128 + 128,
                                       (kc - 8) * 128:(kc - 7) * 128]))
                    depm = [t_m[0], t_m[1]]
                    if kc - 2 in t_gmm:
                        depm.append(t_gmm[kc - 2])
                    t_gr[kc] = step(sCQ, depm, _mr, ndma=2, key=f"mnr{kc % 2}")

                def _gm(te, fin, kc=kc):
                    ins = None
                    for sub in range(2):
                        if kc < 8:
                            lhsT = xnfm[:, kc, 48 + sub * 128:48 + sub * 128 + 128]
                        else:
                            lhsT = grd[:, 2 * (kc % 2) + sub, :]
                        for nb in range(2):
                            bank = 4 + sub * 2 + nb
                            if kc == 0:
                                te.matmul(psum[bank][:, 0:512], ones[0:1, 0:128],
                                          fgb1_sb[0:1, nb * 512:(nb + 1) * 512],
                                          start=True, stop=False)
                            ins = te.matmul(psum[bank][:, 0:512], lhsT,
                                            fgw1r[2 * (kc % 4) + nb][:],
                                            start=False, stop=(kc == 13))
                    fin(ins)
                depsg = [t_fg, t_sev]
                if kc >= 8:
                    depsg.append(t_gr[kc])
                t_gmm[kc] = step(sT, depsg, _gm)

            t_gh = {}
            for sub in range(2):
                def _ge(sc, fin, sub=sub):
                    ins = None
                    for nb in range(2):
                        bank = 4 + sub * 2 + nb
                        ins = sc.activation(ghst[:, sub, nb * 512:(nb + 1) * 512],
                                            psum[bank][:, 0:512], AF.Gelu)
                    fin(ins)
                t_ge = step(sC, [t_gmm[13], sV.tok()], _ge)

                def _gw(g, fin, sub=sub, ot0=ot0):
                    fin(g.dma_start(
                        out=gh_hbm[ot0 + sub * 128:ot0 + sub * 128 + 128, :],
                        in_=ghst[:, sub, :]))
                t_gh[sub] = step(sGQ, [t_ge], _gw, ndma=1, key="ghw")

            t_ow = {}
            for sub in range(2):
                def _gr2(sc, fin, sub=sub, ot0=ot0):
                    for kc in range(8):
                        fin(sc.dma_start_transpose(
                            grd[:, kc, :],
                            gh_hbm[ot0 + sub * 128:ot0 + sub * 128 + 128,
                                   kc * 128:(kc + 1) * 128]))
                t_g2 = step(sCQ, [t_gh[sub], sT.tok()], _gr2, ndma=8, key="ghr")

                def _lm(te, fin, sub=sub):
                    te.matmul(psum[4][:, 0:4], ones[0:1, 0:128], fgb2_sb[0:1, 0:4],
                              start=True, stop=False)
                    ins = None
                    for kc in range(8):
                        ins = te.matmul(psum[4][:, 0:4], grd[:, kc, :],
                                        fgw2_sb[:, kc * 4:kc * 4 + 4],
                                        start=False, stop=(kc == 7))
                    fin(ins)
                t_lg = step(sT, [t_g2, t_sp3], _lm)

                def _s1(v, fin):
                    v.tensor_copy(wsm[:, 0:4], psum[4][:, 0:4])
                    VG(v)
                    v.tensor_reduce(wsm[:, 4:5].unsqueeze(2),
                                    wsm[:, 0:4].unsqueeze(1), AX.X, A.max)
                    VG(v)
                    fin(v.tensor_scalar(wsm[:, 5:9], wsm[:, 0:4], wsm[:, 4:5], None,
                                        A.subtract))
                t_s1 = step(sV, [t_lg], _s1)

                def _s2(sc, fin):
                    fin(sc.activation(wsm[:, 9:13], wsm[:, 5:9], AF.Exp,
                                      scale=invt_col[:, 0:1]))
                t_s2 = step(sC, [t_s1], _s2)

                def _s3(v, fin):
                    v.tensor_reduce(wsm[:, 13:14].unsqueeze(2),
                                    wsm[:, 9:13].unsqueeze(1), AX.X, A.add)
                    VG(v)
                    v.reciprocal(wsm[:, 14:15], wsm[:, 13:14])
                    VG(v)
                    v.tensor_scalar(wsm[:, 15:19], wsm[:, 9:13], wsm[:, 14:15],
                                    None, A.mult)
                    VG(v)
                    v.tensor_scalar(wsm[:, 19:20], wsm[:, 18:19], 0.05, None, A.mult)
                    VG(v)
                    v.tensor_scalar(wsm[:, 15:17], wsm[:, 15:17], wsm[:, 19:20],
                                    None, A.max)
                    VG(v)
                    v.tensor_reduce(wsm[:, 20:21].unsqueeze(2),
                                    wsm[:, 15:19].unsqueeze(1), AX.X, A.add)
                    VG(v)
                    v.reciprocal(wsm[:, 21:22], wsm[:, 20:21])
                    VG(v)
                    fin(v.tensor_scalar(wsm[:, 15:19], wsm[:, 15:19], wsm[:, 21:22],
                                        None, A.mult))
                t_s3 = step(sV, [t_s2], _s3)

                def _sl2(sc, fin, sub=sub, ot0=ot0):
                    t0 = ot0 + sub * 128
                    fin(sc.dma_start_transpose(sltm[:, 0, :], sh_hbm[:, t0:t0 + 128]))
                    fin(sc.dma_start_transpose(sltm[:, 1, :], lo_hbm[:, t0:t0 + 128]))
                t_sl2 = step(sCQ, [sV.tok(), t_lg], _sl2, ndma=2, key="slrd")

                def _cb(v, fin, sub=sub):
                    VG(v)
                    v.tensor_scalar(comb[0][:], sltm[:, 1, :], wsm[:, 16:17], None,
                                    A.mult)
                    VG(v)
                    v.scalar_tensor_tensor(comb[1][:], sltm[:, 0, :], wsm[:, 15:16],
                                           comb[0][:], A.mult, A.add)
                    VG(v)
                    v.scalar_tensor_tensor(comb[0][:], dtm[:, sub, :], wsm[:, 17:18],
                                           comb[1][:], A.mult, A.add)
                    VG(v)
                    v.scalar_tensor_tensor(comb[1][:], vtm[:, sub, :], wsm[:, 18:19],
                                           comb[0][:], A.mult, A.add)
                    v.tensor_tensor(comb[0][:], comb[1][:], comb[1][:], A.mult)
                    VG(v)
                    v.tensor_reduce(wsm[:, 4:8].unsqueeze(2),
                                    comb[0][:].rearrange("p (h n) -> p h n",
                                                         h=4, n=256),
                                    AX.X, A.add)
                    VG(v)
                    fin(v.tensor_scalar(wsm[:, 4:8], wsm[:, 4:8], 1.0 / 256.0, 1e-5,
                                        A.mult, A.add))
                t_c1 = step(sV, [t_s3, t_sl2], _cb)

                def _rq(sc, fin):
                    fin(sc.activation(wsm[:, 8:12], wsm[:, 4:8], AF.Sqrt))
                t_c2 = step(sC, [t_c1], _rq)

                def _c3(v, fin, sub=sub):
                    v.reciprocal(wsm[:, 8:12], wsm[:, 8:12])
                    VG(v)
                    for h in range(4):
                        v.tensor_scalar(o_tmb[:, sub, h * 256:(h + 1) * 256],
                                        comb[1][:, h * 256:(h + 1) * 256],
                                        wsm[:, 8 + h:9 + h], None, A.mult)
                    fin(v.tensor_copy(vgscr[0:1, 0:1], vgscr[0:1, 0:1]))
                t_c3 = step(sV, [t_c2], _c3)

                def _ow2(g, fin, sub=sub, ot0=ot0):
                    fin(g.dma_start(
                        out=o_hbm[ot0 + sub * 128:ot0 + sub * 128 + 128, :],
                        in_=o_tmb[:, sub, :]))
                t_ow[sub] = step(sGQ, [t_c3], _ow2, ndma=1, key="ow")

            for sub in range(2):
                def _or(sc, fin, sub=sub, ot0=ot0):
                    for kc in range(8):
                        fin(sc.dma_start_transpose(
                            grd[:, kc, :],
                            o_hbm[ot0 + sub * 128:ot0 + sub * 128 + 128,
                                  kc * 128:(kc + 1) * 128]))
                t_or = step(sCQ, [t_ow[sub], sT.tok()], _or, ndma=8, key="ghr")

                def _op(te, fin, sub=sub):
                    ins = None
                    for kc in range(8):
                        for nb in range(2):
                            ins = te.matmul(psum[4 + nb][:, 0:512], grd[:, kc, :],
                                            oprjT[kc][:, nb * 512:(nb + 1) * 512],
                                            start=(kc == 0), stop=(kc == 7))
                    fin(ins)
                t_op = step(sT, [t_or, t_oprep, t_last_out], _op)

                def _sv(sc, fin):
                    sc.activation(s_fin[:, 0:512], psum[4][:, 0:512], AF.Copy)
                    fin(sc.activation(s_fin[:, 512:1024], psum[5][:, 0:512], AF.Copy))
                t_se2 = step(sC, [t_op], _sv)

                def _xr(sy, fin, sub=sub, tok0=tok0):
                    fin(sy.dma_start(
                        out=x_fin[:],
                        in_=x_in[tok0 + sub * 128:tok0 + sub * 128 + 128, :]))
                t_xf = step(sY, [t_se2], _xr, ndma=1, key="xr")

                def _gr3(g, fin, sub=sub, ot0=ot0):
                    fin(g.dma_start(out=gfin[:, 0:1024],
                                    in_=g_tm[ot0 + sub * 128:ot0 + sub * 128 + 128, :]))
                t_gf = step(sGQ, [t_se2], _gr3, ndma=1, key="gr3")

                def _fo(v, fin):
                    v.tensor_tensor(outst[:], s_fin[:], gfin[:, 0:1024], A.mult)
                    VG(v)
                    fin(v.tensor_tensor(outst[:], outst[:], x_fin[:], A.add))
                t_fo = step(sV, [t_se2, t_xf, t_gf], _fo)

                def _out(g, fin, sub=sub, ot0=ot0):
                    fin(g.dma_start(
                        out=out[ot0 + sub * 128:ot0 + sub * 128 + 128, :],
                        in_=outst[:]))
                t_last_out = step(sGQ, [t_fo], _out, ndma=1, key="outw")

            t_tile_prev = t_last_out
        else:
            t_tile_prev = t_sev

    def _fbar(g, fin):
        fin(g.nop())
    _all_d = [(ps.h, ps.n) for ps in _psem.values()]
    step(sG, [t_tile_prev] + B_DONE + _all_d, _fbar)

    pr.emit()
    return nc


_PROGRAM_CACHE = {}


def _get_program():
    if "nc" not in _PROGRAM_CACHE:
        _PROGRAM_CACHE["nc"] = build_program()
    return _PROGRAM_CACHE["nc"]


def make_in_maps(inputs):
    x = np.asarray(inputs["L0_attn_hidden_states"], dtype=np.float32)
    w_fc = np.ascontiguousarray(np.asarray(inputs["L0_ffn_w_fc"], dtype=np.float32))
    b_fc = np.ascontiguousarray(
        np.asarray(inputs["L0_ffn_b_fc"], dtype=np.float32).reshape(1, HID))
    w_pj = np.ascontiguousarray(np.asarray(inputs["L0_ffn_w_proj"], dtype=np.float32))
    b_pj = np.ascontiguousarray(
        np.asarray(inputs["L0_ffn_b_proj"], dtype=np.float32).reshape(1, D))
    ones = np.ones((1, 128), np.float32)
    qw = np.asarray(inputs["L0_attn_q_proj_w"], np.float32)
    kw = np.asarray(inputs["L0_attn_k_proj_w"], np.float32)
    vw = np.asarray(inputs["L0_attn_v_proj_w"], np.float32)
    bw = np.asarray(inputs["L0_attn_b_proj_w"], np.float32)
    wqkvT = np.ascontiguousarray(
        np.concatenate([qw.T, kw.T, vw.T, bw.T], axis=1))
    qc = np.asarray(inputs["L0_attn_q_conv_w"], np.float32)
    kc_ = np.asarray(inputs["L0_attn_k_conv_w"], np.float32)
    vc = np.asarray(inputs["L0_attn_v_conv_w"], np.float32)
    firs = np.asarray(inputs["L0_attn_fir_short_w"], np.float32).reshape(1024, 3)
    firl = np.asarray(inputs["L0_attn_fir_long_w"], np.float32).reshape(1024, 31)

    def convcols(w, ntap):
        outc = np.zeros((128, 32), np.float32)
        for i in range(32):
            for a in range(min(4, ntap)):
                outc[32 * a:32 * a + 32, i] = w[i * 32:i * 32 + 32, a]
        return outc
    c4cols = np.concatenate([convcols(qc, 4), convcols(kc_, 4), convcols(vc, 4)],
                            axis=1)
    f31cols = np.zeros((128, 256), np.float32)
    for i in range(32):
        for tb in range(8):
            for a in range(4):
                j = 4 * tb + a
                if j <= 30:
                    f31cols[32 * a:32 * a + 32, i * 8 + tb] =                         firl[i * 32:i * 32 + 32, j]
    f3cols = convcols(firs, 3)
    dmask = np.zeros((128, 32), np.float32)
    for a in range(4):
        dmask[32 * a:32 * a + 32, :] = np.eye(32, dtype=np.float32)
    idx = np.arange(64)
    umask = np.tile((idx[:, None] < idx[None, :]).astype(np.float32), (2, 1))
    lmask = np.tile((idx[:, None] > idx[None, :]).astype(np.float32), (2, 1))
    amask = np.tile((idx[:, None] <= idx[None, :]).astype(np.float32), (2, 1))
    eye128 = np.tile(np.eye(64, dtype=np.float32), (2, 1))
    ones2 = np.ones((128, 64), np.float32)
    fgw1T = np.ascontiguousarray(np.asarray(inputs["L0_attn_fg_w1"], np.float32).T)
    fgb1r = np.ascontiguousarray(
        np.asarray(inputs["L0_attn_fg_b1"], np.float32).reshape(1, 1024))
    fgw2T = np.ascontiguousarray(np.asarray(inputs["L0_attn_fg_w2"], np.float32).T)
    fgb2r = np.ascontiguousarray(
        np.asarray(inputs["L0_attn_fg_b2"], np.float32).reshape(1, 4))
    ltemp = np.asarray(inputs["L0_attn_logit_temp"], np.float32).reshape(1, 1)
    onw = np.asarray(inputs["L0_attn_o_norm_w"], np.float32)
    onwcols = np.zeros((128, 8), np.float32)
    for kc2 in range(8):
        onwcols[:, kc2] = onw[(kc2 * 128 + np.arange(128)) % 256]
    oprojT = np.ascontiguousarray(np.asarray(inputs["L0_attn_o_proj_w"], np.float32).T)
    zeros128 = np.zeros((128, 1024), np.float32)
    attn_ins = {"wqkvT": wqkvT, "c4cols": c4cols, "f31cols": f31cols,
                "f3cols": f3cols, "dmask": dmask, "umask": umask, "lmask": lmask,
                "amask": amask, "eye128": eye128, "ones2": ones2, "fgw1T": fgw1T,
                "fgb1r": fgb1r, "fgw2T": fgw2T, "fgb2r": fgb2r, "ltemp": ltemp,
                "onwcols": onwcols, "oprojT": oprojT, "zeros128": zeros128}
    in_maps = []
    for c in range(NCORES):
        b, h = c // 2, c % 2
        if h == 0:
            xs = np.concatenate([np.zeros((PRE, D), np.float32), x[b, :TPC]], axis=0)
        else:
            xs = np.ascontiguousarray(x[b])
        m = {"x": xs, "w_fc": w_fc, "b_fc": b_fc,
             "w_proj": w_pj, "b_proj": b_pj, "ones_row": ones}
        m.update(attn_ins)
        in_maps.append(m)
    return in_maps


def kernel(**inputs):
    nc = _get_program()
    in_maps = make_in_maps(inputs)
    res = run_bass_kernel_spmd(nc, in_maps, core_ids=list(range(NCORES)))
    outp = np.empty((B, L, D), np.float32)
    for c in range(NCORES):
        b, h = c // 2, c % 2
        outp[b, h * TPC:(h + 1) * TPC] = res.results[c]["out"]
    return outp


# version 09
# speedup vs baseline: 1.2821x; 1.2821x over previous
import numpy as np
import concourse.bass as bass
import concourse.mybir as mybir
from concourse.bass_utils import run_bass_kernel_spmd

F32 = mybir.dt.float32
BF16 = mybir.dt.bfloat16
A = mybir.AluOpType
AF = mybir.ActivationFunctionType
AX = mybir.AxisListType

B, L, D = 4, 8192, 1024
NCORES = 8
TPC = L // 2
PRE = TPC
XROWS = PRE + TPC
NLN = XROWS // 128
HID = 4096
NTILE = TPC // 128
GTILES = 8
NGRP = NTILE // GTILES
SUBT = 4
ATILE = 256
NAT = XROWS // ATILE
CS = 64
NCH = ATILE // CS
BK = 32
LN_EPS = 1e-5
FP64_GADGET = True

MAGIC23 = float(1.5 * 2 ** 23)
MAGIC33 = float(1.5 * 2 ** 33)
MAGIC52 = float(1.5 * 2 ** 52)


class Sem:
    def __init__(self, nc, name):
        self.h = nc.alloc_semaphore(name)
        self.n = 0


class Prog:
    ENGINES = ("sync", "scalar", "vector", "tensor", "gpsimd")

    def __init__(self, nc):
        self.nc = nc
        self.streams = {e: [] for e in self.ENGINES}

    def on(self, eng):
        def deco(fn):
            self.streams[eng].append(fn)
            return None
        return deco

    def emit(self):
        with self.nc.Block() as blk:
            for eng in self.ENGINES:
                thunks = self.streams[eng]
                getattr(blk, eng)(lambda e, th=thunks: [f(e) for f in th])


def emit_ladder_m(v, am, p, scr, m23, m33, m52, nblk, lanes=4):
    nb4 = nblk // lanes
    sl = [slice(l * nb4, (l + 1) * nb4) for l in range(lanes)]
    for l in range(lanes):
        v.memset(p[:, sl[l]], float(2.0 ** -34))
    for k in (32, 16, 8, 4, 2, 1):
        for l in range(lanes):
            v.tensor_single_scalar(scr[:, sl[l]], p[:, sl[l]], float(2.0 ** k), A.mult)
        for l in range(lanes):
            v.tensor_tensor(scr[:, sl[l]], am[:, sl[l]], scr[:, sl[l]], A.is_ge)
        for l in range(lanes):
            v.tensor_scalar(scr[:, sl[l]], scr[:, sl[l]], float(2.0 ** k - 1.0), 1.0,
                            A.mult, A.add)
        for l in range(lanes):
            v.tensor_tensor(p[:, sl[l]], p[:, sl[l]], scr[:, sl[l]], A.mult)
    for l in range(lanes):
        v.tensor_single_scalar(m23[:, sl[l]], p[:, sl[l]], MAGIC23 / 8.0, A.mult)
    for l in range(lanes):
        v.tensor_single_scalar(m33[:, sl[l]], p[:, sl[l]], MAGIC33 / 8.0, A.mult)
    for l in range(lanes):
        v.tensor_single_scalar(m52[:, sl[l]], p[:, sl[l]], MAGIC52 / 8.0, A.mult)


def emit_quant(v, src_ap, dst_ap, m23, m33, m52, tmp, tmp2, nblk, P=128):
    s3 = src_ap.rearrange("p (b c) -> p b c", b=nblk, c=BK)
    d3 = dst_ap.rearrange("p (b c) -> p b c", b=nblk, c=BK)
    t3 = tmp.rearrange("p (b c) -> p b c", b=nblk, c=BK)
    u3 = tmp2.rearrange("p (b c) -> p b c", b=nblk, c=BK)
    mb23 = m23[0:P, 0:nblk].unsqueeze(2).broadcast_to([P, nblk, BK])
    v.tensor_tensor(t3, s3, mb23, A.add)
    if not FP64_GADGET:
        v.tensor_tensor(d3, t3, mb23, A.subtract)
        return
    v.tensor_tensor(t3, t3, mb23, A.subtract)
    mb52 = m52[0:P, 0:nblk].unsqueeze(2).broadcast_to([P, nblk, BK])
    v.tensor_tensor(u3, s3, mb52, A.add)
    v.tensor_tensor(u3, u3, mb52, A.subtract)
    v.tensor_tensor(t3, t3, u3, A.add)
    mb33 = m33[0:P, 0:nblk].unsqueeze(2).broadcast_to([P, nblk, BK])
    v.tensor_tensor(u3, s3, mb33, A.add)
    v.tensor_tensor(u3, u3, mb33, A.subtract)
    v.tensor_tensor(d3, t3, u3, A.subtract)


def build_program():
    nc = bass.Bass("TRN2", target_bir_lowering=False)
    pr = Prog(nc)

    x_in = nc.dram_tensor("x", [XROWS, D], F32, kind="ExternalInput").ap()
    w_fc = nc.dram_tensor("w_fc", [HID, D], F32, kind="ExternalInput").ap()
    b_fc = nc.dram_tensor("b_fc", [1, HID], F32, kind="ExternalInput").ap()
    w_pj = nc.dram_tensor("w_proj", [D, HID], F32, kind="ExternalInput").ap()
    b_pj = nc.dram_tensor("b_proj", [1, D], F32, kind="ExternalInput").ap()
    ones_in = nc.dram_tensor("ones_row", [1, 128], F32, kind="ExternalInput").ap()
    wqkv_in = nc.dram_tensor("wqkvT", [D, 3076], F32, kind="ExternalInput").ap()
    c4cols_in = nc.dram_tensor("c4cols", [128, 96], F32, kind="ExternalInput").ap()
    f31cols_in = nc.dram_tensor("f31cols", [128, 256], F32, kind="ExternalInput").ap()
    f3cols_in = nc.dram_tensor("f3cols", [128, 32], F32, kind="ExternalInput").ap()
    dmask_in = nc.dram_tensor("dmask", [128, 32], F32, kind="ExternalInput").ap()
    umask_in = nc.dram_tensor("umask", [128, 128], F32, kind="ExternalInput").ap()
    lmask_in = nc.dram_tensor("lmask", [128, 128], F32, kind="ExternalInput").ap()
    amask_in = nc.dram_tensor("amask", [128, 128], F32, kind="ExternalInput").ap()
    eye_in = nc.dram_tensor("eye128", [128, 128], F32, kind="ExternalInput").ap()
    ones2_in = nc.dram_tensor("ones2", [128, 64], F32, kind="ExternalInput").ap()
    fgw1_in = nc.dram_tensor("fgw1T", [1792, 1024], F32, kind="ExternalInput").ap()
    fgb1_in = nc.dram_tensor("fgb1r", [1, 1024], F32, kind="ExternalInput").ap()
    fgw2_in = nc.dram_tensor("fgw2T", [1024, 4], F32, kind="ExternalInput").ap()
    fgb2_in = nc.dram_tensor("fgb2r", [1, 4], F32, kind="ExternalInput").ap()
    ltemp_in = nc.dram_tensor("ltemp", [1, 1], F32, kind="ExternalInput").ap()
    onw_in = nc.dram_tensor("onwcols", [128, 8], F32, kind="ExternalInput").ap()
    oprj_in = nc.dram_tensor("oprojT", [D, D], F32, kind="ExternalInput").ap()
    zeros_in = nc.dram_tensor("zeros128", [128, 1024], F32, kind="ExternalInput").ap()
    out = nc.dram_tensor("out", [TPC, D], F32, kind="ExternalOutput").ap()
    q_hbm = nc.dram_tensor("q_hbm", [D, XROWS], BF16).ap()
    k_hbm = nc.dram_tensor("k_hbm", [D, XROWS], BF16).ap()
    v_hbm = nc.dram_tensor("v_hbm", [D, XROWS + 8], BF16).ap()
    sh_hbm = nc.dram_tensor("sh_hbm", [D, TPC], BF16).ap()
    lo_hbm = nc.dram_tensor("lo_hbm", [D, TPC], BF16).ap()
    mn_hbm = nc.dram_tensor("mn_hbm", [TPC, 768], BF16).ap()
    gh_hbm = nc.dram_tensor("gh_hbm", [TPC, 1024], BF16).ap()
    o_hbm = nc.dram_tensor("o_hbm", [TPC, D], BF16).ap()

    xn_tm = nc.dram_tensor("xn_tm", [64 + XROWS, D], BF16).ap()
    g_tm = nc.dram_tensor("g_tm", [TPC, D], BF16).ap()
    xnq_tm = nc.dram_tensor("xnq_tm", [TPC, D], BF16).ap()
    hq_tm = nc.dram_tensor("hq_tm", [TPC, HID], BF16).ap()
    wqfc_hbm = nc.dram_tensor("wqfc_hbm", [HID, D], BF16).ap()
    wqpj_hbm = nc.dram_tensor("wqpj_hbm", [D, HID], BF16).ap()

    sb = nc.alloc_sbuf_tensor
    wbig = sb("wbig", [128, 8 * HID], BF16)
    bfc_row = sb("bfc_row", [1, HID], BF16)
    bpj_row = sb("bpj_row", [1, D], BF16)
    ones = sb("ones", [1, 128], BF16)

    NX = 2
    xt = [sb(f"xt{i}", [128, D], F32) for i in range(NX)]
    xnt = [sb(f"xnt{i}", [128, D], F32) for i in range(NX)]
    xnb = [sb(f"xnb{i}", [128, D], BF16) for i in range(NX)]
    xqb = [sb(f"xqb{i}", [128, D], BF16) for i in range(NX)]
    lnscr = sb("lnscr", [128, D], F32)
    qt1 = sb("qt1", [128, D], F32)
    qt2 = sb("qt2", [128, D], F32)
    stat = [sb(f"stat{i}", [128, 12], F32) for i in range(NX)]
    xam = [sb(f"xam{i}", [128, 32], F32) for i in range(NX)]
    xp = [sb(f"xp{i}", [128, 32], F32) for i in range(NX)]
    xscr = sb("xscr", [128, 32], F32)
    xm23 = [sb(f"xm23{i}", [128, 32], F32) for i in range(NX)]
    xm33 = [sb(f"xm33{i}", [128, 32], F32) for i in range(NX)]
    xm52 = [sb(f"xm52{i}", [128, 32], F32) for i in range(NX)]

    xnqf = sb("xnqf", [128, 8 * 1024], BF16)
    hbuf = sb("hbuf", [128, 2048], F32)
    hqb = sb("hqb", [128, 2048], BF16)
    ht1 = sb("ht1", [128, 2048], F32)
    ht2 = sb("ht2", [128, 2048], F32)
    NH = 2
    ham = [sb(f"ham{i}", [128, 64], F32) for i in range(NH)]
    hp = [sb(f"hp{i}", [128, 64], F32) for i in range(NH)]
    hscr = sb("hscr", [128, 64], F32)
    hm23 = [sb(f"hm23{i}", [128, 64], F32) for i in range(NH)]
    hm33 = [sb(f"hm33{i}", [128, 64], F32) for i in range(NH)]
    hm52 = [sb(f"hm52{i}", [128, 64], F32) for i in range(NH)]

    NHQ = 8
    hqT = [sb(f"hqT{i}", [128, SUBT * 128], BF16) for i in range(NHQ)]
    gbuf = [sb(f"gbuf{i}", [128, D], F32) for i in range(SUBT)]
    x2 = [sb(f"x2_{i}", [128, D], F32) for i in range(2)]
    obuf = [sb(f"obuf{i}", [128, D], F32) for i in range(2)]

    psum = [nc.alloc_psum_tensor(f"ps{i}", [128, 512], F32) for i in range(8)]

    wsrc = ht1
    wt1 = ht2
    wt2 = hbuf
    wout = hqb
    wam, wp_, wscr = ham[0], hp[0], hscr
    wm23, wm33, wm52 = hm23[0], hm33[0], hm52[0]

    wfcT_v = wbig[:].rearrange("p (c n) -> p c n", c=8, n=HID)
    wpjT_v = wbig[:].rearrange("p (c n) -> p c n", c=32, n=D)
    xnqf_v = xnqf[:].rearrange("p (c n) -> p c n", c=8, n=1024)

    S = lambda name: Sem(nc, name)
    s_cst = S("s_cst")
    s_wld = S("s_wld"); s_wam = S("s_wam"); s_wpool = S("s_wpool")
    s_wqd = S("s_wqd"); s_wst = S("s_wst"); s_wtr = S("s_wtr"); s_wtr2 = S("s_wtr2")
    s_xp = [S("s_xp0"), S("s_xp1")]
    s_xstp = [S("s_xstp0"), S("s_xstp1")]
    s_htr8 = [S(f"s_htr8_{i}") for i in range(8)]
    s_x = S("s_x"); s_st1 = S("s_st1"); s_pmath = S("s_pmath"); s_sqrt = S("s_sqrt")
    s_lnv = S("s_lnv"); s_xpool = S("s_xpool"); s_xq = S("s_xq"); s_xst = S("s_xst")
    s_xtr = S("s_xtr"); s_fc = S("s_fc"); s_gel = S("s_gel")
    s_ham = S("s_ham"); s_hpool = S("s_hpool"); s_hq = S("s_hq"); s_hst = S("s_hst")
    s_htr = S("s_htr"); s_pjk = S("s_pjk"); s_sil = S("s_sil")
    s_x2 = S("s_x2"); s_ores = S("s_ores"); s_out = S("s_out")

    @pr.on("gpsimd")
    def _(g):
        g.dma_start(out=ones[:], in_=ones_in[:]).then_inc(s_cst.h, 16)
        g.dma_start(out=bfc_row[:], in_=b_fc[:]).then_inc(s_cst.h, 16)
        g.dma_start(out=bpj_row[:], in_=b_pj[:]).then_inc(s_cst.h, 16)
    s_cst.n += 48

    WTILES = [("fc", i, 0, 1024, 32) for i in range(32)] + \
             [("pj", i, hh, 2048, 64) for i in range(8) for hh in range(2)]
    wqd_done = {}

    for wi, (kind, i, hh, ncol, nblk) in enumerate(WTILES):
        src = w_fc if kind == "fc" else w_pj
        dst = wqfc_hbm if kind == "fc" else wqpj_hbm
        col0 = hh * 2048

        @pr.on("sync")
        def _(sy, src=src, i=i, col0=col0, ncol=ncol, wi=wi):
            if wi >= 1:
                sy.wait_ge(s_wqd.h, wqd_done[wi - 1])
            sy.dma_start(out=wsrc[:, 0:ncol],
                         in_=src[i * 128:(i + 1) * 128, col0:col0 + ncol]).then_inc(s_wld.h, 16)
        s_wld.n += 16

        @pr.on("vector")
        def _(v, wld_n=s_wld.n, nblk=nblk, ncol=ncol):
            v.wait_ge(s_wld.h, wld_n)
            v.tensor_reduce(wam[:, 0:nblk].unsqueeze(2),
                            wsrc[:, 0:ncol].rearrange("p (b c) -> p b c", b=nblk, c=BK),
                            AX.X, A.max, apply_absolute_value=True)
            emit_ladder_m(v, wam, wp_, wscr, wm23, wm33, wm52, nblk)
            emit_quant(v, wsrc[:, 0:ncol], wout[:, 0:ncol],
                       wm23, wm33, wm52, wt1[:, 0:ncol], wt2[:, 0:ncol], nblk)
            v.tensor_copy(wscr[0:1, 1:2], wscr[0:1, 1:2]).then_inc(s_wqd.h, 1)
        s_wqd.n += 1
        wqd_done[wi] = s_wqd.n

        @pr.on("gpsimd")
        def _(g, dst=dst, i=i, col0=col0, ncol=ncol, wqd_n=s_wqd.n):
            g.wait_ge(s_wqd.h, wqd_n)
            g.dma_start(out=dst[i * 128:(i + 1) * 128, col0:col0 + ncol],
                        in_=wout[:, 0:ncol]).then_inc(s_wst.h, 16)
        s_wst.n += 16

    @pr.on("scalar")
    def _(sc, wst_n=s_wst.n):
        sc.wait_ge(s_wst.h, wst_n)
        for fc_ in range(8):
            sc.dma_start_transpose(wfcT_v[:, fc_, :],
                                   wqfc_hbm[:, fc_ * 128:(fc_ + 1) * 128]).then_inc(s_wtr.h, 16)
    s_wtr.n += 8 * 16

    lnv_done = {}
    xq_done = {}
    xst_done = {}

    for t in range(NLN):
        bi = t % NX
        row0 = t * 128
        own = t >= NLN - NTILE
        orow0 = (t - (NLN - NTILE)) * 128

        @pr.on("sync")
        def _(sy, t=t, bi=bi, row0=row0):
            if t >= NX:
                sy.wait_ge(s_xq.h, xq_done[t - NX])
            sy.dma_start(out=xt[bi][:],
                         in_=x_in[row0:row0 + 128, :]).then_inc(s_xp[bi].h, 16)
        s_xp[bi].n += 16

        @pr.on("scalar")
        def _(sc, t=t, bi=bi, x_n=s_xp[bi].n):
            sc.wait_ge(s_xp[bi].h, x_n)
            if t >= NX:
                sc.wait_ge(s_lnv.h, lnv_done[t - NX])
            sc.activation(lnscr[:], xt[bi][:], AF.Copy, accum_out=stat[bi][:, 0:1])
            sc.activation(lnscr[:], xt[bi][:], AF.Square,
                          accum_out=stat[bi][:, 1:2]).then_inc(s_st1.h, 1)
        s_st1.n += 1

        @pr.on("gpsimd")
        def _(g, bi=bi, st1_n=s_st1.n):
            g.wait_ge(s_st1.h, st1_n)
            g.tensor_single_scalar(stat[bi][:, 2:3], stat[bi][:, 0:1], 1.0 / D, A.mult)
            g.tensor_single_scalar(stat[bi][:, 3:4], stat[bi][:, 1:2], 1.0 / D, A.mult)
            g.tensor_tensor(stat[bi][:, 4:5], stat[bi][:, 2:3], stat[bi][:, 2:3], A.mult)
            g.tensor_tensor(stat[bi][:, 3:4], stat[bi][:, 3:4], stat[bi][:, 4:5], A.subtract)
            g.tensor_single_scalar(stat[bi][:, 3:4], stat[bi][:, 3:4], LN_EPS,
                                   A.add).then_inc(s_pmath.h, 1)
        s_pmath.n += 1

        @pr.on("scalar")
        def _(sc, bi=bi, pmath_n=s_pmath.n):
            sc.wait_ge(s_pmath.h, pmath_n)
            sc.activation(stat[bi][:, 5:6], stat[bi][:, 3:4], AF.Sqrt).then_inc(s_sqrt.h, 1)
        s_sqrt.n += 1

        @pr.on("vector")
        def _(v, t=t, bi=bi, sqrt_n=s_sqrt.n):
            v.wait_ge(s_sqrt.h, sqrt_n)
            v.reciprocal(stat[bi][:, 6:7], stat[bi][:, 5:6])
            v.tensor_copy(xscr[0:1, 0:1], xscr[0:1, 0:1])
            v.tensor_copy(xscr[0:1, 1:2], xscr[0:1, 1:2])
            v.tensor_copy(xscr[0:1, 2:3], xscr[0:1, 2:3])
            v.tensor_scalar(xnt[bi][:], xt[bi][:], stat[bi][:, 2:3], stat[bi][:, 6:7],
                            A.subtract, A.mult)
            v.tensor_copy(xnb[bi][:], xnt[bi][:])
            v.tensor_reduce(xam[bi][:, 0:32].unsqueeze(2),
                            xnt[bi][:].rearrange("p (b c) -> p b c", b=32, c=BK),
                            AX.X, A.max, apply_absolute_value=True).then_inc(s_lnv.h, 1)
        s_lnv.n += 1
        lnv_done[t] = s_lnv.n

        @pr.on("vector")
        def _(v, t=t, bi=bi):
            if t >= NX:
                _xd = xst_done[t - NX]
                v.wait_ge(s_xstp[0].h, _xd[0])
                v.wait_ge(s_xstp[1].h, _xd[1])
            emit_ladder_m(v, xam[bi], xp[bi], xscr, xm23[bi], xm33[bi], xm52[bi], 32)
            emit_quant(v, xnt[bi][:], xqb[bi][:], xm23[bi], xm33[bi], xm52[bi],
                       qt1[:], qt2[:], 32)
            v.tensor_copy(xscr[0:1, 4:5], xscr[0:1, 4:5]).then_inc(s_xq.h, 1)
        s_xq.n += 1
        xq_done[t] = s_xq.n

        @pr.on("gpsimd")
        def _(g, bi=bi, row0=row0, own=own, orow0=orow0, xq_n=s_xq.n):
            g.wait_ge(s_xq.h, xq_n)
            g.dma_start(out=xn_tm[64 + row0:64 + row0 + 128, :],
                        in_=xnb[bi][:]).then_inc(s_xstp[bi].h, 16)
            if own:
                g.dma_start(out=xnq_tm[orow0:orow0 + 128, :],
                            in_=xqb[bi][:]).then_inc(s_xstp[bi].h, 16)
        s_xstp[bi].n += 16 + (16 if own else 0)
        xst_done[t] = (s_xstp[0].n, s_xstp[1].n)

    fc_done = {}
    gel_done = {}
    hq_done = {}
    hst_done = {}
    xtr_done = {}
    hpool_done = {}

    def prev_half(key):
        grp, tl, half = key
        if half:
            return (grp, tl, 0)
        if tl:
            return (grp, tl - 1, 1)
        if grp:
            return (grp - 1, GTILES - 1, 1)
        return None

    for grp in range(NGRP):
        @pr.on("scalar")
        def _(sc, grp=grp, wtr_n=s_wtr.n, xst_n=None):
            _xd = xst_done[(NLN - NTILE) + grp * GTILES + GTILES - 1]
            sc.wait_ge(s_xstp[0].h, _xd[0])
            sc.wait_ge(s_xstp[1].h, _xd[1])
            sc.wait_ge(s_wtr.h, wtr_n)
            if grp >= 1:
                sc.wait_ge(s_fc.h, fc_done[(grp - 1, GTILES - 1, 1)])
            for fc_ in range(8):
                sc.dma_start_transpose(
                    xnqf_v[:, fc_, :],
                    xnq_tm[grp * 1024:(grp + 1) * 1024,
                           fc_ * 128:(fc_ + 1) * 128]).then_inc(s_xtr.h, 16)
        s_xtr.n += 8 * 16
        xtr_done[grp] = s_xtr.n

        for tl in range(GTILES):
            tok = tl * 128
            for half in range(2):
                key = (grp, tl, half)

                @pr.on("tensor")
                def _(te, key=key, tok=tok, half=half, xtr_n=xtr_done[grp]):
                    te.wait_ge(s_xtr.h, xtr_n)
                    te.wait_ge(s_cst.h, 48)
                    pk = prev_half(key)
                    if pk in gel_done:
                        te.wait_ge(s_gel.h, gel_done[pk])
                    for nb in range(4):
                        hid0 = half * 2048 + nb * 512
                        te.matmul(psum[nb][:], ones[:],
                                  bfc_row[0:1, hid0:hid0 + 512], start=True, stop=False)
                    ins = None
                    for kc in range(8):
                        for nb in range(4):
                            hid0 = half * 2048 + nb * 512
                            ins = te.matmul(psum[nb][:],
                                            xnqf_v[:, kc, tok:tok + 128],
                                            wfcT_v[:, kc, hid0:hid0 + 512],
                                            start=False, stop=(kc == 7))
                    ins.then_inc(s_fc.h, 1)
                s_fc.n += 1
                fc_done[key] = s_fc.n

                @pr.on("scalar")
                def _(sc, key=key, fc_n=s_fc.n):
                    sc.wait_ge(s_fc.h, fc_n)
                    pk = prev_half(key)
                    if pk in hq_done:
                        sc.wait_ge(s_hq.h, hq_done[pk])
                    for nb in range(4):
                        sc.activation(hbuf[:, nb * 512:(nb + 1) * 512], psum[nb][:],
                                      AF.Gelu).then_inc(s_gel.h, 1)
                s_gel.n += 4
                gel_done[key] = s_gel.n

                @pr.on("vector")
                def _(v, key=key, half=half, gel_n=s_gel.n):
                    v.wait_ge(s_gel.h, gel_n)
                    pk = prev_half(key)
                    if pk in hst_done:
                        v.wait_ge(s_hst.h, hst_done[pk])
                    v.tensor_reduce(ham[half][:, 0:64].unsqueeze(2),
                                    hbuf[:].rearrange("p (b c) -> p b c", b=64, c=BK),
                                    AX.X, A.max, apply_absolute_value=True)
                    emit_ladder_m(v, ham[half], hp[half], hscr,
                                  hm23[half], hm33[half], hm52[half], 64)
                    emit_quant(v, hbuf[:], hqb[:], hm23[half], hm33[half], hm52[half],
                               ht1[:], ht2[:], 64)
                    v.tensor_copy(hscr[0:1, 1:2], hscr[0:1, 1:2]).then_inc(s_hq.h, 1)
                s_hq.n += 1
                hq_done[key] = s_hq.n

                @pr.on("gpsimd")
                def _(g, key=key, half=half, hq_n=s_hq.n):
                    grp_, tl_, _ = key
                    g.wait_ge(s_hq.h, hq_n)
                    r0 = grp_ * 1024 + tl_ * 128
                    g.dma_start(out=hq_tm[r0:r0 + 128, half * 2048:(half + 1) * 2048],
                                in_=hqb[:]).then_inc(s_hst.h, 16)
                s_hst.n += 16
                hst_done[key] = s_hst.n

    last_fc = fc_done[(NGRP - 1, GTILES - 1, 1)]
    last_gel = gel_done[(NGRP - 1, GTILES - 1, 1)]

    @pr.on("scalar")
    def _(sc, last_fc=last_fc):
        sc.wait_ge(s_fc.h, last_fc)
        for hc in range(32):
            sc.dma_start_transpose(wpjT_v[:, hc, :],
                                   wqpj_hbm[:, hc * 128:(hc + 1) * 128]).then_inc(s_wtr2.h, 16)
    s_wtr2.n += 32 * 16

    htr_done = {}
    pjk_done = {}
    sil_done = {}
    ores_done = {}
    sgi = 0
    ckidx = 0

    for grp in range(NGRP):
        for sub in range(GTILES // SUBT):
            tok0 = sub * SUBT * 128
            ck0 = ckidx
            for kc in range(32):
                @pr.on("scalar")
                def _(sc, grp=grp, kc=kc, tok0=tok0, ck=ckidx,
                      hst_n=hst_done[(grp, GTILES - 1, 1)]):
                    sc.wait_ge(s_hst.h, hst_n)
                    if ck >= NHQ:
                        sc.wait_ge(s_pjk.h, pjk_done[ck - NHQ])
                    sc.dma_start_transpose(
                        hqT[ck % NHQ][:],
                        hq_tm[grp * 1024 + tok0:grp * 1024 + tok0 + SUBT * 128,
                              kc * 128:(kc + 1) * 128]).then_inc(s_htr8[ck % 8].h, 16)
                s_htr8[ckidx % 8].n += 16
                htr_done[ckidx] = (ckidx % 8, s_htr8[ckidx % 8].n)
                ckidx += 1

            @pr.on("tensor")
            def _(te, sgi=sgi, ck0=ck0, last_gel=last_gel, wtr2_n=s_wtr2.n):
                te.wait_ge(s_wtr2.h, wtr2_n)
                if sgi == 0:
                    te.wait_ge(s_gel.h, last_gel)
                else:
                    te.wait_ge(s_sil.h, sil_done[sgi - 1])
                for st_ in range(SUBT):
                    for pb in range(2):
                        te.matmul(psum[st_ * 2 + pb][:], ones[:],
                                  bpj_row[0:1, pb * 512:(pb + 1) * 512],
                                  start=True, stop=False)
                for kc in range(32):
                    _hs, _hn = htr_done[ck0 + kc]
                    te.wait_ge(s_htr8[_hs].h, _hn)
                    ins = None
                    for st_ in range(SUBT):
                        for pb in range(2):
                            ins = te.matmul(psum[st_ * 2 + pb][:],
                                            hqT[(ck0 + kc) % NHQ][:, st_ * 128:st_ * 128 + 128],
                                            wpjT_v[:, kc, pb * 512:(pb + 1) * 512],
                                            start=False, stop=(kc == 31))
                    ins.then_inc(s_pjk.h, 1)
            for kc in range(32):
                pjk_done[ck0 + kc] = s_pjk.n + kc + 1
            s_pjk.n += 32

            @pr.on("scalar")
            def _(sc, sgi=sgi, pjk_n=s_pjk.n):
                sc.wait_ge(s_pjk.h, pjk_n)
                if sgi >= 1:
                    sc.wait_ge(s_ores.h, ores_done[sgi * SUBT - 1])
                for st_ in range(SUBT):
                    for pb in range(2):
                        sc.activation(gbuf[st_][:, pb * 512:(pb + 1) * 512],
                                      psum[st_ * 2 + pb][:], AF.Silu).then_inc(s_sil.h, 1)
            s_sil.n += 2 * SUBT
            sil_done[sgi] = s_sil.n

            for st_ in range(SUBT):
                t_own = grp * GTILES + sub * SUBT + st_

                @pr.on("vector")
                def _(v, sgi=sgi, st_=st_, t_own=t_own, sil_n=sil_done[sgi]):
                    v.wait_ge(s_sil.h, sil_n)
                    if t_own >= 2:
                        v.wait_ge(s_out.h, (t_own - 1) * 16)
                    v.tensor_copy(obuf[st_ % 2][:].bitcast(BF16)[:, 0:D],
                                  gbuf[st_][:]).then_inc(s_ores.h, 1)
                s_ores.n += 1
                ores_done[t_own] = s_ores.n

                @pr.on("gpsimd")
                def _(g, t_own=t_own, st_=st_, ores_n=s_ores.n):
                    g.wait_ge(s_ores.h, ores_n)
                    r0 = t_own * 128
                    g.dma_start(out=g_tm[r0:r0 + 128, :],
                                in_=obuf[st_ % 2][:].bitcast(BF16)[:, 0:D]).then_inc(s_out.h, 16)
                s_out.n += 16
            sgi += 1

    class St:
        def __init__(self, name, eng, dma):
            self.name, self.eng, self.dma = name, eng, dma
            self.sem = nc.alloc_semaphore("tk_" + name)
            if dma:
                self.dsem = nc.alloc_semaphore("tkd_" + name)
            self.n = 0
            self.q = 0

        def tok(self):
            return (self.sem, self.n)

    sT = St("T", "tensor", False)
    sV = St("V", "vector", False)
    sC = St("C", "scalar", False)
    sCQ = St("CQ", "scalar", True)
    sG = St("G", "gpsimd", False)
    sGQ = St("GQ", "gpsimd", True)
    sY = St("Y", "sync", True)

    _psem = {}

    def step(stream, deps, body, ndma=0, key=None):
        if stream.dma:
            key = key or (stream.name + "_d")
            kk = (stream.name, key)
            if kk not in _psem:
                _psem[kk] = Sem(nc, "pd_" + stream.name + "_" + key)
            ps = _psem[kk]
            ps.n += 16 * ndma
            post = ps.n
            sem = ps.h
        else:
            stream.n += 1
            post = stream.n
            sem = stream.sem
        isdma = stream.dma

        def thunk(e, deps=tuple(deps), body=body, sem=sem):
            for (s, c) in deps:
                if c > 0:
                    e.wait_ge(s, c)

            def fin(instr):
                instr.then_inc(sem, 16 if isdma else 1)
                return instr
            body(e, fin)
        pr.streams[stream.eng].append(thunk)
        return (sem, post)

    B_DONE = [(s_xstp[0].h, s_xstp[0].n), (s_xstp[1].h, s_xstp[1].n),
              (s_out.h, s_out.n)]

    wbigb = wbig[:]
    wqkvT = wbigb[:, 0:24608].rearrange("p (c n) -> p c n", c=8, n=3076)
    c4st = wbigb[:, 24608:27680].rearrange("p (i n) -> p i n", i=96, n=32)
    f3st = wbigb[:, 27680:28704].rearrange("p (i n) -> p i n", i=32, n=32)
    gfin = wbigb[:, 28704:29728]
    o_tmb = wbigb[:, 29728:31776].rearrange("p (s n) -> p s n", s=2, n=1024)
    wstage = wbigb[:, 31776:32768]
    f31st = xnqf[:].rearrange("p (i n) -> p i n", i=256, n=32)
    XW = 304
    xnfm = hbuf[:].bitcast(BF16).rearrange("p (c n) -> p c n", c=8, n=512)
    ht1b = ht1[:].bitcast(BF16)
    pre_ring = ht1b[:, 0:1216].rearrange("p (r n) -> p r n", r=4, n=304)
    vst = ht1b[:, 1216:1816].rearrange("p (r n) -> p r n", r=2, n=300)
    ht2b = ht2[:].bitcast(BF16)
    rep4 = ht2b[:, 0:2400].rearrange("p (r g n) -> p r g n", r=2, g=4, n=300)
    repf = ht2b[:, 2400:3536].rearrange("p (g n) -> p g n", g=4, n=284)
    qfm = xt[0][:].bitcast(BF16).rearrange("p (c n) -> p c n", c=8, n=256)
    kfm = xt[1][:].bitcast(BF16).rearrange("p (c n) -> p c n", c=8, n=256)
    qtm = qt1[:].bitcast(BF16).rearrange("p (s n) -> p s n", s=2, n=1024)
    ktm = qt2[:].bitcast(BF16).rearrange("p (s n) -> p s n", s=2, n=1024)
    vtm = obuf[0][:].bitcast(BF16).rearrange("p (s n) -> p s n", s=2, n=1024)
    dtm = obuf[1][:].bitcast(BF16).rearrange("p (s n) -> p s n", s=2, n=1024)
    vbt = lnscr[:].bitcast(BF16).rearrange("p (s n) -> p s n", s=2, n=1024)
    ghst = vbt
    Ssb = x2[0][:].bitcast(BF16).rearrange("p (h n) -> p h n", h=4, n=512)
    scb = x2[1][:].bitcast(BF16)
    zbuf = scb[:, 0:1024].rearrange("p (h n) -> p h n", h=4, n=256)
    cmt = scb[:, 1024:2048].rearrange("p (i n) -> p i n", i=8, n=128)
    slevv = xnb[1][:].rearrange("p (r n) -> p r n", r=4, n=256)
    mstg = [xqb[0][:], xqb[1][:]]
    grd = xnb[0][:].rearrange("p (r n) -> p r n", r=8, n=128)
    sltm = hqb[:].rearrange("p (i n) -> p i n", i=2, n=1024)
    x_fin = xnt[1][:]
    comb = [xnt[0][:].bitcast(BF16)[:, 0:1024],
            xnt[0][:].bitcast(BF16)[:, 1024:2048]]
    o2buf = xqb[0][:]
    oprjT = [gbuf[i // 2][:].bitcast(BF16)[:, (i % 2) * 1024:(i % 2 + 1) * 1024]
             for i in range(8)]
    fgw1r = [hqT[i][:] for i in range(8)]
    ofs32 = qt1[:]
    zrof = qt2[:]

    arsn = sb("arsn", [128, 48], F32)
    wsm = sb("wsm", [128, 24], F32)
    s_fin = sb("s_fin", [128, D], F32)
    outst = lnscr[:]
    vgscr = sb("vgscr", [128, 8], F32)
    cst128 = sb("cst128", [128, 384], F32)
    eye128 = sb("eye128_sb", [128, 128], F32)
    ones2 = sb("ones2_sb", [128, 64], BF16)
    dmask_sb = sb("dmask_s2", [128, 32], F32)
    ccols = s_fin[:, 0:384]
    onw_sb = sb("onw_sb", [128, 8], F32)
    fgb1_sb = sb("fgb1_sb", [1, 1024], BF16)
    fgb2_sb = sb("fgb2_sb", [1, 4], BF16)
    fgw2_sb = sb("fgw2_sb", [128, 32], BF16)
    invt = sb("invt", [1, 8], F32)
    invt_col = sb("invt_col", [128, 1], F32)
    qch_sb = ht1b[:, 1824:3872]
    Qch = [qch_sb[:, i * 512:(i + 1) * 512].rearrange("p (h n) -> p h n",
                                                      h=4, n=128)
           for i in range(4)]
    QchF = [qch_sb[:, i * 512:(i + 1) * 512] for i in range(4)]
    sfb = s_fin[:].bitcast(BF16)
    Gsb = [sfb[:, i * 512:(i + 1) * 512].rearrange("p (h n) -> p h n",
                                                   h=4, n=128)
           for i in range(2)]
    GsbF = [sfb[:, i * 512:(i + 1) * 512] for i in range(2)]
    attnT = [sfb[:, 1024 + i * 512:1024 + (i + 1) * 512]
             .rearrange("p (h n) -> p h n", h=4, n=128)
             for i in range(2)]
    ub_sb = gfin
    ubuf = ub_sb.rearrange("p (h n) -> p h n", h=4, n=256)
    rowb = wbigb[0:1, 29728:30752]
    eye_bf = ht2b[:, 3936:4064]

    fgw1_bhbm = nc.dram_tensor("fgw1_bhbm", [1792, 1024], BF16).ap()
    pre_hbm = nc.dram_tensor("pre_hbm", [512, 304], BF16).ap()
    invt_hbm = nc.dram_tensor("invt_hbm", [1, 1], F32).ap()

    def VG(v, n=3):
        for i in range(n):
            v.tensor_copy(vgscr[0:1, i:i + 1], vgscr[0:1, i:i + 1])

    def _wq_loads(g, fin):
        for kc in range(8):
            fin(g.dma_start(out=wqkvT[:, kc, :],
                            in_=wqkv_in[kc * 128:(kc + 1) * 128, :]))
        fin(g.dma_start(out=fgb1_sb[:], in_=fgb1_in[:]))
        fin(g.dma_start(out=fgb2_sb[:], in_=fgb2_in[:]))
        for kc in range(8):
            fin(g.dma_start(out=fgw2_sb[:, kc * 4:kc * 4 + 4],
                            in_=fgw2_in[kc * 128:(kc + 1) * 128, :]))
        fin(g.dma_start(out=ones2[:], in_=ones2_in[:]))
    t_wq = step(sGQ, B_DONE, _wq_loads, ndma=19, key="wld")

    def _consts(sy, fin):
        fin(sy.dma_start(out=cst128[:, 0:128], in_=umask_in[:]))
        fin(sy.dma_start(out=cst128[:, 128:256], in_=lmask_in[:]))
        fin(sy.dma_start(out=cst128[:, 256:384], in_=amask_in[:]))
        fin(sy.dma_start(out=eye128[:], in_=eye_in[:]))
        fin(sy.dma_start(out=dmask_sb[:], in_=dmask_in[:]))
        fin(sy.dma_start(out=ccols[:, 0:96], in_=c4cols_in[:]))
        fin(sy.dma_start(out=ccols[:, 96:352], in_=f31cols_in[:]))
        fin(sy.dma_start(out=ccols[:, 352:384], in_=f3cols_in[:]))
        fin(sy.dma_start(out=onw_sb[:], in_=onw_in[:]))
        fin(sy.dma_start(out=invt[0:1, 0:1], in_=ltemp_in[:]))
        fin(sy.dma_start(out=xn_tm[0:64, :],
                         in_=zeros_in[0:64, 0:512].bitcast(BF16)))
        zb = zeros_in[0:128, :].bitcast(BF16)
        VROW = XROWS + 8
        for ch in range(8):
            i_ap = bass.AP(zb.tensor, 0, [[2048, 128], [0, 5], [1, 1640]])
            o_ap = bass.AP(v_hbm.tensor, ch * 128 * VROW,
                           [[VROW, 128], [1640, 5], [1, 1640]])
            fin(sy.dma_start(out=o_ap, in_=i_ap))
    t_cst = step(sY, B_DONE, _consts, ndma=19, key="cst")

    def _convst(v, fin):
        for i in range(96):
            v.tensor_scalar(c4st[:, i, :], dmask_sb[:], ccols[:, i:i + 1], None, A.mult)
        for i in range(256):
            v.tensor_scalar(f31st[:, i, :], dmask_sb[:], ccols[:, 96 + i:97 + i],
                            None, A.mult)
        for i in range(32):
            v.tensor_scalar(f3st[:, i, :], dmask_sb[:], ccols[:, 352 + i:353 + i],
                            None, A.mult)
        v.tensor_copy(eye_bf[:], eye128[:])
        v.memset(zrof[:], 0.0)
        VG(v)
        for h in range(4):
            v.tensor_copy(Ssb[:, h, :], zrof[:, 0:512])
        fin(v.tensor_copy(vgscr[0:1, 7:8], vgscr[0:1, 7:8]))
    t_cvst = step(sV, [t_cst] + B_DONE, _convst)

    t_pv, t_pg = t_cvst, t_wq
    for i in range(14):
        t_rd = step(sY, [t_pv, t_pg],
                    lambda sy, fin, i=i: fin(sy.dma_start(
                        out=ofs32[:], in_=fgw1_in[i * 128:(i + 1) * 128, :])),
                    ndma=1, key="fgrd")
        t_pv = step(sV, [t_rd],
                    lambda v, fin: fin(v.tensor_copy(wstage[:, 0:992] if False
                                                     else gfin[:, 0:1024], ofs32[:])))
        t_pg = step(sGQ, [t_pv],
                    lambda g, fin, i=i: fin(g.dma_start(
                        out=fgw1_bhbm[i * 128:(i + 1) * 128, :],
                        in_=gfin[:, 0:1024])), ndma=1, key="fgst")
    t_fgw1 = t_pg

    for kc in range(8):
        t_rd = step(sY, [t_pv, t_pg],
                    lambda sy, fin, kc=kc: fin(sy.dma_start(
                        out=ofs32[:], in_=oprj_in[kc * 128:(kc + 1) * 128, :])),
                    ndma=1, key="oprd")
        t_pv = step(sV, [t_rd],
                    lambda v, fin, kc=kc: fin(v.tensor_scalar(
                        oprjT[kc][:], ofs32[:], onw_sb[:, kc:kc + 1], None, A.mult)))
    t_oprep = t_pv

    def _sp1(sc, fin):
        def CG():
            for _ in range(3):
                sc.activation(rowb[0:1, 0:1], invt[0:1, 0:1], AF.Copy)
        sc.activation(invt[0:1, 1:2], invt[0:1, 0:1], AF.Abs)
        sc.activation(invt[0:1, 4:5], invt[0:1, 0:1], AF.Relu)
        CG()
        sc.activation(invt[0:1, 2:3], invt[0:1, 1:2], AF.Exp, scale=-1.0)
        CG()
        fin(sc.activation(invt[0:1, 3:4], invt[0:1, 2:3], AF.Ln, bias=1.0))
    t_sp1 = step(sC, [t_cst], _sp1)
    def _sp2(v, fin):
        v.tensor_tensor(invt[0:1, 5:6], invt[0:1, 3:4], invt[0:1, 4:5], A.add)
        VG(v)
        v.tensor_single_scalar(invt[0:1, 6:7], invt[0:1, 5:6], 1e-4, A.add)
        VG(v)
        v.reciprocal(invt[0:1, 7:8], invt[0:1, 6:7])
        VG(v)
        fin(v.tensor_copy(vgscr[0:1, 6:7], vgscr[0:1, 6:7]))
    t_sp2 = step(sV, [t_sp1], _sp2)
    def _sp3(g, fin):
        fin(g.dma_start(out=invt_hbm[:], in_=invt[0:1, 7:8]))
    t_sp3a = step(sGQ, [t_sp2], _sp3, ndma=1, key="invt")

    def _sp4(g, fin):
        in_ap = bass.AP(invt_hbm.tensor, 0, [[0, 128], [1, 1]])
        fin(g.dma_start(out=invt_col[:], in_=in_ap))
    t_sp3 = step(sGQ, [t_sp3a], _sp4, ndma=1, key="invt")

    NOWN = NAT // 2
    t_tile_prev = t_cvst
    t_last_out = t_oprep

    for t in range(NAT):
        own = t >= NOWN
        tok0 = t * ATILE
        ot0 = tok0 - PRE

        def _xnrd(sc, fin, tok0=tok0):
            for c in range(8):
                fin(sc.dma_start_transpose(
                    xnfm[:, c, 0:XW],
                    xn_tm[tok0 + 16:tok0 + 16 + XW, c * 128:(c + 1) * 128]))
        t_xn = step(sCQ, B_DONE[0:2] + [t_tile_prev, t_cvst], _xnrd, ndma=8, key="xnr")

        chunk_list = ([("q", c) for c in range(8)] if own else []) + \
                     [("k", c) for c in range(8)] + [("v", c) for c in range(8)]
        t_sil = {}
        t_evac = {}
        t_conv = {}
        t_w = {}
        pi = 0
        for (kind, c) in chunk_list:
            mc = {"q": 0, "k": 8, "v": 16}[kind] + c
            bank = 6 + pi % 2
            ring = pi % 4
            par = pi % 2

            def _proj(te, fin, mc=mc, bank=bank):
                ins = None
                for kc in range(8):
                    ins = te.matmul(psum[bank][:, 0:XW],
                                    wqkvT[:, kc, mc * 128:(mc + 1) * 128],
                                    xnfm[:, kc, 0:XW],
                                    start=(kc == 0), stop=(kc == 7))
                fin(ins)
            deps = [t_xn, t_wq]
            if pi - 2 in t_sil:
                deps.append(t_sil[pi - 2])
            t_p = step(sT, deps, _proj)

            def _evpre(sc, fin, bank=bank, ring=ring):
                fin(sc.activation(pre_ring[:, ring, :], psum[bank][:, 0:XW], AF.Copy))
            t_e = step(sC, [t_p], _evpre)
            t_evac[pi] = t_e

            def _prew(g, fin, ring=ring):
                fin(g.dma_start(out=pre_hbm[ring * 128:(ring + 1) * 128, :],
                                in_=pre_ring[:, ring, :]))
            deps_pw = [t_e]
            if pi - 4 in t_conv:
                deps_pw.append(t_conv[pi - 4])
            t_pw = step(sGQ, deps_pw, _prew, ndma=1, key=f"prew{ring}")

            def _reprd(g, fin, kind=kind, ring=ring, par=par):
                off = 1 if kind == "v" else 45
                W = 300 if kind == "v" else 256
                for g4 in range(4):
                    in_ap = bass.AP(pre_hbm.tensor,
                                    (ring * 128 + 32 * g4) * 304 + off,
                                    [[1, 4], [304, 32], [1, W]])
                    fin(g.dma_start(out=rep4[:, par, g4, 0:W], in_=in_ap))
            deps_rr = [t_pw]
            if pi - 2 in t_conv:
                deps_rr.append(t_conv[pi - 2])
            t_rr = step(sGQ, deps_rr, _reprd, ndma=4, key=f"rep{par}")

            def _conv(te, fin, kind=kind, c=c, par=par, bank=bank):
                base = {"q": 0, "k": 32, "v": 64}[kind]
                W = 300 if kind == "v" else 256
                ins = None
                for g4 in range(4):
                    ins = te.matmul(psum[bank][32 * g4:32 * g4 + 32, 0:W],
                                    c4st[:, base + c * 4 + g4, :],
                                    rep4[:, par, g4, 0:W],
                                    start=True, stop=True,
                                    tile_position=(0, 32 * g4))
                fin(ins)
            t_c = step(sT, [t_e, t_rr], _conv)
            t_conv[pi] = t_c

            def _silu(sc, fin, kind=kind, c=c, bank=bank, par=par):
                if kind == "q":
                    fin(sc.activation(qfm[:, c, :], psum[bank][:, 0:256], AF.Silu))
                elif kind == "k":
                    fin(sc.activation(kfm[:, c, :], psum[bank][:, 0:256], AF.Silu))
                else:
                    fin(sc.activation(vst[:, par, :], psum[bank][:, 0:300], AF.Silu))
            deps_s = [t_c]
            if kind == "v" and pi - 2 in t_w:
                deps_s.append(t_w[pi - 2])
            t_s = step(sC, deps_s, _silu)
            t_sil[pi] = t_s

            def _whbm(g, fin, kind=kind, c=c, tok0=tok0, par=par, t=t):
                if kind == "v":
                    if t == 0:
                        fin(g.dma_start(out=v_hbm[c * 128:(c + 1) * 128, 0:256],
                                        in_=vst[:, par, 44:300]))
                    else:
                        fin(g.dma_start(
                            out=v_hbm[c * 128:(c + 1) * 128, tok0 - 44:tok0 + 256],
                            in_=vst[:, par, :]))
                else:
                    src = qfm if kind == "q" else kfm
                    hb = q_hbm if kind == "q" else k_hbm
                    fin(g.dma_start(out=hb[c * 128:(c + 1) * 128, tok0:tok0 + 256],
                                    in_=src[:, c, :]))
            wkey = f"vw{c % 2}" if kind == "v" else "whbm"
            t_wr = step(sGQ, [t_s], _whbm, ndma=1, key=wkey)
            t_w[pi] = t_wr
            pi += 1

        t_beta = {}
        for sub in range(2):
            bank = 6 + sub

            def _bproj(te, fin, sub=sub, bank=bank):
                ins = None
                for kc in range(8):
                    ins = te.matmul(psum[bank][:, 0:4],
                                    xnfm[:, kc, 48 + sub * 128:48 + sub * 128 + 128],
                                    wqkvT[:, kc, 3072:3076],
                                    start=(kc == 0), stop=(kc == 7))
                fin(ins)
            t_bp = step(sT, [t_xn, t_wq, t_sil[pi - 2 + sub]], _bproj)

            def _bev(sc, fin, sub=sub, bank=bank):
                fin(sc.activation(arsn[:, sub * 24 + 8:sub * 24 + 12],
                                  psum[bank][:, 0:4], AF.Sigmoid))
            t_beta[sub] = step(sC, [t_bp], _bev)

        t_fir = {}
        if own:
            for c in range(8):
                def _repf(g, fin, c=c, tok0=tok0):
                    VROW = XROWS + 8
                    for g4 in range(4):
                        in_ap = bass.AP(v_hbm.tensor,
                                        (c * 128 + 32 * g4) * VROW + tok0 - 30,
                                        [[1, 4], [VROW, 32], [1, 284]])
                        fin(g.dma_start(out=repf[:, g4, :], in_=in_ap))
                deps_rf = [t_w[16 + c]]
                if ("mm", c - 1) in t_fir:
                    deps_rf.append(t_fir[("mm", c - 1)])
                t_rf = step(sGQ, deps_rf, _repf, ndma=4, key="repf")

                def _firmm(te, fin, c=c):
                    ins = None
                    for g4 in range(4):
                        te.matmul(psum[6][32 * g4:32 * g4 + 32, 0:256],
                                  f3st[:, c * 4 + g4, :], repf[:, g4, 28:284],
                                  start=True, stop=True, tile_position=(0, 32 * g4))
                    for tb in range(8):
                        for g4 in range(4):
                            ins = te.matmul(psum[7][32 * g4:32 * g4 + 32, 0:256],
                                            f31st[:, (c * 4 + g4) * 8 + tb, :],
                                            repf[:, g4, 4 * tb:4 * tb + 256],
                                            start=(tb == 0), stop=(tb == 7),
                                            tile_position=(0, 32 * g4))
                    fin(ins)
                deps_fm = [t_rf, t_beta[1]]
                if ("ev", c - 1) in t_fir:
                    deps_fm.append(t_fir[("ev", c - 1)])
                t_fm = step(sT, deps_fm, _firmm)
                t_fir[("mm", c)] = t_fm

                def _firev(sc, fin, c=c):
                    sc.activation(slevv[:, (2 * c) % 4, :], psum[6][:, 0:256], AF.Copy)
                    fin(sc.activation(slevv[:, (2 * c + 1) % 4, :], psum[7][:, 0:256],
                                      AF.Copy))
                deps_fe = [t_fm]
                if ("w", c - 2) in t_fir:
                    deps_fe.append(t_fir[("w", c - 2)])
                t_fe = step(sC, deps_fe, _firev)
                t_fir[("ev", c)] = t_fe

                def _firw(g, fin, c=c, ot0=ot0):
                    fin(g.dma_start(out=sh_hbm[c * 128:(c + 1) * 128, ot0:ot0 + 256],
                                    in_=slevv[:, (2 * c) % 4, :]))
                    fin(g.dma_start(out=lo_hbm[c * 128:(c + 1) * 128, ot0:ot0 + 256],
                                    in_=slevv[:, (2 * c + 1) % 4, :]))
                t_fw = step(sGQ, [t_fe], _firw, ndma=2, key="firw")
                t_fir[("w", c)] = t_fw

        t_sub = {}
        for sub in range(2):
            def _tmrd(sc, fin, sub=sub, tok0=tok0, own=own):
                t0 = tok0 + sub * 128
                fin(sc.dma_start_transpose(ktm[:, sub, :], k_hbm[:, t0:t0 + 128]))
                fin(sc.dma_start_transpose(vtm[:, sub, :], v_hbm[:, t0:t0 + 128]))
                if own:
                    fin(sc.dma_start_transpose(qtm[:, sub, :], q_hbm[:, t0:t0 + 128]))
            t_rd = step(sCQ, [t_w[pi - 9], t_w[pi - 2], t_w[pi - 1]], _tmrd,
                        ndma=3 if own else 2, key="tmrd")

            def _l2a(v, fin, sub=sub, own=own):
                base = sub * 24
                v.tensor_tensor(o2buf[:].bitcast(BF16)[:, 0:1024], ktm[:, sub, :],
                                ktm[:, sub, :], A.mult)
                VG(v)
                v.tensor_reduce(arsn[:, base + 4:base + 8].unsqueeze(2),
                                o2buf[:].bitcast(BF16)[:, 0:1024]
                                .rearrange("p (h n) -> p h n", h=4, n=256),
                                AX.X, A.add)
                VG(v)
                v.tensor_scalar(arsn[:, base + 4:base + 8],
                                arsn[:, base + 4:base + 8], 1e-12, None, A.add)
                if own:
                    v.tensor_tensor(o2buf[:].bitcast(BF16)[:, 0:1024], qtm[:, sub, :],
                                    qtm[:, sub, :], A.mult)
                    VG(v)
                    v.tensor_reduce(arsn[:, base + 0:base + 4].unsqueeze(2),
                                    o2buf[:].bitcast(BF16)[:, 0:1024]
                                    .rearrange("p (h n) -> p h n", h=4, n=256),
                                    AX.X, A.add)
                    VG(v)
                    v.tensor_scalar(arsn[:, base + 0:base + 4],
                                    arsn[:, base + 0:base + 4], 1e-12, None, A.add)
                fin(v.tensor_copy(vgscr[0:1, 4:5], vgscr[0:1, 4:5]))
            t_l1 = step(sV, [t_rd, t_beta[sub]], _l2a)

            def _sqr(sc, fin, sub=sub, own=own):
                base = sub * 24
                ins = sc.activation(arsn[:, base + 4:base + 8],
                                    arsn[:, base + 4:base + 8], AF.Sqrt)
                if own:
                    ins = sc.activation(arsn[:, base + 0:base + 4],
                                        arsn[:, base + 0:base + 4], AF.Sqrt)
                fin(ins)
            t_l2 = step(sC, [t_l1], _sqr)

            def _l3(v, fin, sub=sub, own=own):
                base = sub * 24
                v.reciprocal(arsn[:, base + 4:base + 8], arsn[:, base + 4:base + 8])
                if own:
                    v.reciprocal(arsn[:, base + 0:base + 4], arsn[:, base + 0:base + 4])
                VG(v)
                v.tensor_tensor(arsn[:, base + 12:base + 16],
                                arsn[:, base + 8:base + 12],
                                arsn[:, base + 4:base + 8], A.mult)
                VG(v)
                v.tensor_scalar(arsn[:, base + 16:base + 20],
                                arsn[:, base + 12:base + 16], -1.0, None, A.mult)
                VG(v)
                for h in range(4):
                    v.tensor_scalar(ktm[:, sub, h * 256:(h + 1) * 256],
                                    ktm[:, sub, h * 256:(h + 1) * 256],
                                    arsn[:, base + 4 + h:base + 5 + h], None, A.mult)
                for h in range(4):
                    v.tensor_scalar(vbt[:, sub, h * 256:(h + 1) * 256],
                                    vtm[:, sub, h * 256:(h + 1) * 256],
                                    arsn[:, base + 8 + h:base + 9 + h], None, A.mult)
                fin(v.tensor_copy(vgscr[0:1, 3:4], vgscr[0:1, 3:4]))
            t_sub[sub] = step(sV, [t_l2], _l3)

        t_sev = t_tile_prev
        t_de = {}
        t_pre_done = t_tile_prev
        for sub in range(2):
            c0 = sub * 128
            base = sub * 24
            par = sub
            first = (t == 0 and sub == 0)

            def _trp(te, fin, base=base):
                ins = None
                for h in range(4):
                    te.transpose(psum[4][0:1, h * 128:(h + 1) * 128],
                                 arsn[:, base + 12 + h:base + 13 + h], eye128[:])
                    ins = te.transpose(psum[5][0:1, h * 128:(h + 1) * 128],
                                       arsn[:, base + 4 + h:base + 5 + h], eye128[:])
                fin(ins)
            t_tr = step(sT, [t_sub[sub], t_pre_done], _trp)

            def _re(sc, fin):
                sc.activation(rowb[0:1, 0:512], psum[4][0:1, 0:512], AF.Copy)
                fin(sc.activation(rowb[0:1, 512:1024], psum[5][0:1, 0:512], AF.Copy))
            t_rw = step(sC, [t_tr], _re)

            def _bc(te, fin):
                ins = None
                for h in range(4):
                    te.matmul(psum[4][:, h * 128:(h + 1) * 128], ones[0:1, 0:128],
                              rowb[0:1, h * 128:(h + 1) * 128], start=True, stop=True)
                    ins = te.matmul(psum[5][:, h * 128:(h + 1) * 128],
                                    ones[0:1, 0:128],
                                    rowb[0:1, 512 + h * 128:512 + (h + 1) * 128],
                                    start=True, stop=True)
                fin(ins)
            t_b2 = step(sT, [t_rw], _bc)

            def _cmk(v, fin):
                for h in range(4):
                    v.tensor_tensor(cmt[:, h, :], cst128[:, 0:128],
                                    psum[4][:, h * 128:(h + 1) * 128], A.mult)
                    v.tensor_tensor(cmt[:, 4 + h, :], cst128[:, 128:256],
                                    psum[5][:, h * 128:(h + 1) * 128], A.mult)
                fin(v.tensor_copy(vgscr[0:1, 5:6], vgscr[0:1, 5:6]))
            t_cm = step(sV, [t_b2], _cmk)

            def _a0(te, fin, c0=c0, own=own):
                ins = None
                for h in range(4):
                    for kk in range(2):
                        ins = te.matmul(psum[4][:, h * 128:(h + 1) * 128],
                                        kfm[:, 2 * h + kk, c0:c0 + 128],
                                        kfm[:, 2 * h + kk, c0:c0 + 128],
                                        start=(kk == 0), stop=(kk == 1))
                if own:
                    for h in range(4):
                        for kk in range(2):
                            ins = te.matmul(psum[5][:, h * 128:(h + 1) * 128],
                                            kfm[:, 2 * h + kk, c0:c0 + 128],
                                            qfm[:, 2 * h + kk, c0:c0 + 128],
                                            start=(kk == 0), stop=(kk == 1))
                fin(ins)
            t_a0 = step(sT, [t_cm], _a0)

            def _am(v, fin, base=base, par=par, own=own):
                for h in range(4):
                    v.scalar_tensor_tensor(Qch[0][:, h, :],
                                           psum[4][:, h * 128:(h + 1) * 128],
                                           arsn[:, base + 4 + h:base + 5 + h],
                                           cmt[:, h, :], A.mult, A.mult)
                for h in range(4):
                    v.scalar_tensor_tensor(Qch[1][:, h, :],
                                           psum[4][:, h * 128:(h + 1) * 128],
                                           arsn[:, base + 12 + h:base + 13 + h],
                                           cmt[:, 4 + h, :], A.mult, A.mult)
                for h in range(4):
                    v.tensor_tensor(Gsb[par][:, h, :], eye_bf[:], Qch[0][:, h, :],
                                    A.subtract)
                if own:
                    for h in range(4):
                        v.scalar_tensor_tensor(attnT[par][:, h, :],
                                               psum[5][:, h * 128:(h + 1) * 128],
                                               arsn[:, base + 4 + h:base + 5 + h],
                                               cst128[:, 256:384], A.mult, A.mult)
                fin(v.tensor_copy(vgscr[0:1, 5:6], vgscr[0:1, 5:6]))
            t_pw = step(sV, [t_a0], _am)

            for lvl in range(6):
                qs = lvl % 2
                def _la(te, fin, qs=qs):
                    ins = None
                    for h in range(4):
                        te.matmul(psum[4][:, h * 128:(h + 1) * 128],
                                  Qch[2 * qs + 1][:, h, :], Qch[2 * qs][:, h, :],
                                  start=True, stop=True)
                        ins = te.matmul(psum[5][:, h * 128:(h + 1) * 128],
                                        Qch[2 * qs][:, h, :], Qch[2 * qs + 1][:, h, :],
                                        start=True, stop=True)
                    fin(ins)
                t_la = step(sT, [t_pw], _la)

                def _lb(sc, fin, qs=qs):
                    sc.activation(QchF[2 - 2 * qs][:], psum[4][:, 0:512], AF.Copy)
                    fin(sc.activation(QchF[3 - 2 * qs][:], psum[5][:, 0:512],
                                      AF.Copy))
                t_lb = step(sC, [t_la], _lb)

                def _lc(te, fin, qs=qs, par=par):
                    ins = None
                    for h in range(4):
                        ins = te.matmul(psum[4][:, h * 128:(h + 1) * 128],
                                        Qch[3 - 2 * qs][:, h, :], Gsb[par][:, h, :],
                                        start=True, stop=True)
                    fin(ins)
                t_lc = step(sT, [t_lb], _lc)

                def _ld(v, fin, par=par):
                    fin(v.tensor_tensor(GsbF[par][:], GsbF[par][:],
                                        psum[4][:, 0:512], A.add))
                t_pw = step(sV, [t_lc], _ld)
            t_pre_done = t_pw

            bdeps = [t_pw, t_sev]
            if own:
                bdeps.append(t_fir[("ev", 7)])

            def _zm(te, fin, c0=c0):
                ins = None
                for h in range(4):
                    for kk in range(2):
                        ins = te.matmul(psum[6 + h // 2][:, (h % 2) * 256:
                                                         (h % 2) * 256 + 256],
                                        kfm[:, 2 * h + kk, c0:c0 + 128],
                                        Ssb[:, h, kk * 256:(kk + 1) * 256],
                                        start=(kk == 0), stop=(kk == 1))
                fin(ins)
            t_zm = step(sT, bdeps, _zm)

            def _ze(v, fin, base=base, sub=sub):
                for h in range(4):
                    v.scalar_tensor_tensor(zbuf[:, h, :],
                                           psum[6 + h // 2][:, (h % 2) * 256:
                                                            (h % 2) * 256 + 256],
                                           arsn[:, base + 16 + h:base + 17 + h],
                                           vbt[:, sub, h * 256:(h + 1) * 256],
                                           A.mult, A.add)
                fin(v.tensor_copy(vgscr[0:1, 5:6], vgscr[0:1, 5:6]))
            t_ze = step(sV, [t_zm], _ze)

            def _umm(te, fin, par=par):
                ins = None
                for h in range(4):
                    ins = te.matmul(psum[6 + h // 2][:, (h % 2) * 256:
                                                     (h % 2) * 256 + 256],
                                    Gsb[par][:, h, :], zbuf[:, h, :],
                                    start=True, stop=True)
                fin(ins)
            t_um = step(sT, [t_ze], _umm)

            def _ue(sc, fin):
                sc.activation(ub_sb[:, 0:512], psum[6][:, 0:512], AF.Copy)
                fin(sc.activation(ub_sb[:, 512:1024], psum[7][:, 0:512],
                                  AF.Copy))
            t_ue = step(sC, [t_um], _ue)

            def _sm(te, fin, sub=sub, c0=c0, par=par, own=own, first=first):
                ins = None
                for h in range(4):
                    for kk in range(2):
                        ins = te.matmul(psum[h][:, kk * 256:(kk + 1) * 256],
                                        ktm[:, sub, h * 256 + kk * 128:
                                            h * 256 + kk * 128 + 128],
                                        ubuf[:, h, :], start=first, stop=True)
                if own:
                    for h in range(4):
                        for kk in range(2):
                            te.matmul(psum[6 + h // 2][:, (h % 2) * 256:
                                                       (h % 2) * 256 + 256],
                                      qfm[:, 2 * h + kk, c0:c0 + 128],
                                      Ssb[:, h, kk * 256:(kk + 1) * 256],
                                      start=(kk == 0), stop=False)
                        ins = te.matmul(psum[6 + h // 2][:, (h % 2) * 256:
                                                         (h % 2) * 256 + 256],
                                        attnT[par][:, h, :], ubuf[:, h, :],
                                        start=False, stop=True)
                fin(ins)
            t_su = step(sT, [t_ue], _sm)

            def _se(sc, fin):
                ins = None
                for h in range(4):
                    ins = sc.activation(Ssb[:, h, :], psum[h][:, 0:512], AF.Copy)
                fin(ins)
            t_sev = step(sC, [t_su], _se)

            if own:
                def _de(v, fin, base=base, sub=sub):
                    for h in range(4):
                        v.tensor_scalar(dtm[:, sub, h * 256:(h + 1) * 256],
                                        psum[6 + h // 2][:, (h % 2) * 256:
                                                         (h % 2) * 256 + 256],
                                        arsn[:, base + h:base + h + 1],
                                        None, A.mult)
                    fin(v.tensor_copy(vgscr[0:1, 5:6], vgscr[0:1, 5:6]))
                t_de[sub] = step(sV, [t_su], _de)

        if own:
            t_m = {}
            for sub in range(2):
                def _slrd(sc, fin, sub=sub, ot0=ot0):
                    t0 = ot0 + sub * 128
                    fin(sc.dma_start_transpose(sltm[:, 0, :], sh_hbm[:, t0:t0 + 128]))
                    fin(sc.dma_start_transpose(sltm[:, 1, :], lo_hbm[:, t0:t0 + 128]))
                deps_sr = [t_fir[("w", 7)], t_sev]
                if sub == 1:
                    deps_sr.append(t_m[0])
                t_sr = step(sCQ, deps_sr, _slrd, ndma=2, key="slrd")

                def _mn(v, fin, sub=sub):
                    st = mstg[sub]
                    for (src, col) in ((sltm[:, 0, :], 0), (sltm[:, 1, :], 256),
                                       (dtm[:, sub, :], 512)):
                        v.tensor_tensor(st[:, 2 * col:2 * col + 512]
                                        if False else st[:, col * 2:col * 2 + 2][:, 0:0]
                                        if False else st[:, col:col + 256],
                                        src[:, 0:256], src[:, 256:512], A.add)
                        v.tensor_tensor(o2buf[:].bitcast(BF16)[:, 0:256],
                                        src[:, 512:768], src[:, 768:1024], A.add)
                        VG(v)
                        v.tensor_tensor(st[:, col:col + 256], st[:, col:col + 256],
                                        o2buf[:].bitcast(BF16)[:, 0:256], A.add)
                        VG(v)
                        v.tensor_scalar(st[:, col:col + 256], st[:, col:col + 256],
                                        0.25, None, A.mult)
                    fin(v.tensor_copy(vgscr[0:1, 1:2], vgscr[0:1, 1:2]))
                t_mv = step(sV, [t_sr, t_de[sub]], _mn)

                def _mw(g, fin, sub=sub, ot0=ot0):
                    fin(g.dma_start(
                        out=mn_hbm[ot0 + sub * 128:ot0 + sub * 128 + 128, :],
                        in_=mstg[sub][:, 0:768]))
                t_m[sub] = step(sGQ, [t_mv], _mw, ndma=1, key="mnw")

            t_gmm = {}
            t_gr = {}
            for kc in range(14):
                def _fgr(g, fin, kc=kc):
                    fin(g.dma_start(out=fgw1r[2 * (kc % 4)][:],
                                    in_=fgw1_bhbm[kc * 128:(kc + 1) * 128, 0:512]))
                    fin(g.dma_start(out=fgw1r[2 * (kc % 4) + 1][:],
                                    in_=fgw1_bhbm[kc * 128:(kc + 1) * 128, 512:1024]))
                depf = [t_fgw1]
                if kc - 4 in t_gmm:
                    depf.append(t_gmm[kc - 4])
                t_fg = step(sGQ, depf, _fgr, ndma=2, key=f"fgw1{kc % 4}")

                if kc >= 8:
                    def _mr(sc, fin, kc=kc, ot0=ot0):
                        for sub in range(2):
                            fin(sc.dma_start_transpose(
                                grd[:, 2 * (kc % 2) + sub, :],
                                mn_hbm[ot0 + sub * 128:ot0 + sub * 128 + 128,
                                       (kc - 8) * 128:(kc - 7) * 128]))
                    depm = [t_m[0], t_m[1]]
                    if kc - 2 in t_gmm:
                        depm.append(t_gmm[kc - 2])
                    t_gr[kc] = step(sCQ, depm, _mr, ndma=2, key=f"mnr{kc % 2}")

                def _gm(te, fin, kc=kc):
                    ins = None
                    for sub in range(2):
                        if kc < 8:
                            lhsT = xnfm[:, kc, 48 + sub * 128:48 + sub * 128 + 128]
                        else:
                            lhsT = grd[:, 2 * (kc % 2) + sub, :]
                        for nb in range(2):
                            bank = 4 + sub * 2 + nb
                            if kc == 0:
                                te.matmul(psum[bank][:, 0:512], ones[0:1, 0:128],
                                          fgb1_sb[0:1, nb * 512:(nb + 1) * 512],
                                          start=True, stop=False)
                            ins = te.matmul(psum[bank][:, 0:512], lhsT,
                                            fgw1r[2 * (kc % 4) + nb][:],
                                            start=False, stop=(kc == 13))
                    fin(ins)
                depsg = [t_fg, t_sev]
                if kc >= 8:
                    depsg.append(t_gr[kc])
                t_gmm[kc] = step(sT, depsg, _gm)

            t_gh = {}
            for sub in range(2):
                def _ge(sc, fin, sub=sub):
                    ins = None
                    for nb in range(2):
                        bank = 4 + sub * 2 + nb
                        ins = sc.activation(ghst[:, sub, nb * 512:(nb + 1) * 512],
                                            psum[bank][:, 0:512], AF.Gelu)
                    fin(ins)
                t_ge = step(sC, [t_gmm[13], sV.tok()], _ge)

                def _gw(g, fin, sub=sub, ot0=ot0):
                    fin(g.dma_start(
                        out=gh_hbm[ot0 + sub * 128:ot0 + sub * 128 + 128, :],
                        in_=ghst[:, sub, :]))
                t_gh[sub] = step(sGQ, [t_ge], _gw, ndma=1, key="ghw")

            t_ow = {}
            for sub in range(2):
                def _gr2(sc, fin, sub=sub, ot0=ot0):
                    for kc in range(8):
                        fin(sc.dma_start_transpose(
                            grd[:, kc, :],
                            gh_hbm[ot0 + sub * 128:ot0 + sub * 128 + 128,
                                   kc * 128:(kc + 1) * 128]))
                t_g2 = step(sCQ, [t_gh[sub], sT.tok()], _gr2, ndma=8, key="ghr")

                def _lm(te, fin, sub=sub):
                    te.matmul(psum[4][:, 0:4], ones[0:1, 0:128], fgb2_sb[0:1, 0:4],
                              start=True, stop=False)
                    ins = None
                    for kc in range(8):
                        ins = te.matmul(psum[4][:, 0:4], grd[:, kc, :],
                                        fgw2_sb[:, kc * 4:kc * 4 + 4],
                                        start=False, stop=(kc == 7))
                    fin(ins)
                t_lg = step(sT, [t_g2, t_sp3], _lm)

                def _s1(v, fin):
                    v.tensor_copy(wsm[:, 0:4], psum[4][:, 0:4])
                    VG(v)
                    v.tensor_reduce(wsm[:, 4:5].unsqueeze(2),
                                    wsm[:, 0:4].unsqueeze(1), AX.X, A.max)
                    VG(v)
                    fin(v.tensor_scalar(wsm[:, 5:9], wsm[:, 0:4], wsm[:, 4:5], None,
                                        A.subtract))
                t_s1 = step(sV, [t_lg], _s1)

                def _s2(sc, fin):
                    fin(sc.activation(wsm[:, 9:13], wsm[:, 5:9], AF.Exp,
                                      scale=invt_col[:, 0:1]))
                t_s2 = step(sC, [t_s1], _s2)

                def _s3(v, fin):
                    v.tensor_reduce(wsm[:, 13:14].unsqueeze(2),
                                    wsm[:, 9:13].unsqueeze(1), AX.X, A.add)
                    VG(v)
                    v.reciprocal(wsm[:, 14:15], wsm[:, 13:14])
                    VG(v)
                    v.tensor_scalar(wsm[:, 15:19], wsm[:, 9:13], wsm[:, 14:15],
                                    None, A.mult)
                    VG(v)
                    v.tensor_scalar(wsm[:, 19:20], wsm[:, 18:19], 0.05, None, A.mult)
                    VG(v)
                    v.tensor_scalar(wsm[:, 15:17], wsm[:, 15:17], wsm[:, 19:20],
                                    None, A.max)
                    VG(v)
                    v.tensor_reduce(wsm[:, 20:21].unsqueeze(2),
                                    wsm[:, 15:19].unsqueeze(1), AX.X, A.add)
                    VG(v)
                    v.reciprocal(wsm[:, 21:22], wsm[:, 20:21])
                    VG(v)
                    fin(v.tensor_scalar(wsm[:, 15:19], wsm[:, 15:19], wsm[:, 21:22],
                                        None, A.mult))
                t_s3 = step(sV, [t_s2], _s3)

                def _sl2(sc, fin, sub=sub, ot0=ot0):
                    t0 = ot0 + sub * 128
                    fin(sc.dma_start_transpose(sltm[:, 0, :], sh_hbm[:, t0:t0 + 128]))
                    fin(sc.dma_start_transpose(sltm[:, 1, :], lo_hbm[:, t0:t0 + 128]))
                t_sl2 = step(sCQ, [sV.tok(), t_lg], _sl2, ndma=2, key="slrd")

                def _cb(v, fin, sub=sub):
                    VG(v)
                    v.tensor_scalar(comb[0][:], sltm[:, 1, :], wsm[:, 16:17], None,
                                    A.mult)
                    VG(v)
                    v.scalar_tensor_tensor(comb[1][:], sltm[:, 0, :], wsm[:, 15:16],
                                           comb[0][:], A.mult, A.add)
                    VG(v)
                    v.scalar_tensor_tensor(comb[0][:], dtm[:, sub, :], wsm[:, 17:18],
                                           comb[1][:], A.mult, A.add)
                    VG(v)
                    v.scalar_tensor_tensor(comb[1][:], vtm[:, sub, :], wsm[:, 18:19],
                                           comb[0][:], A.mult, A.add)
                    v.tensor_tensor(comb[0][:], comb[1][:], comb[1][:], A.mult)
                    VG(v)
                    v.tensor_reduce(wsm[:, 4:8].unsqueeze(2),
                                    comb[0][:].rearrange("p (h n) -> p h n",
                                                         h=4, n=256),
                                    AX.X, A.add)
                    VG(v)
                    fin(v.tensor_scalar(wsm[:, 4:8], wsm[:, 4:8], 1.0 / 256.0, 1e-5,
                                        A.mult, A.add))
                t_c1 = step(sV, [t_s3, t_sl2, t_de[sub]], _cb)

                def _rq(sc, fin):
                    fin(sc.activation(wsm[:, 8:12], wsm[:, 4:8], AF.Sqrt))
                t_c2 = step(sC, [t_c1], _rq)

                def _c3(v, fin, sub=sub):
                    v.reciprocal(wsm[:, 8:12], wsm[:, 8:12])
                    VG(v)
                    for h in range(4):
                        v.tensor_scalar(o_tmb[:, sub, h * 256:(h + 1) * 256],
                                        comb[1][:, h * 256:(h + 1) * 256],
                                        wsm[:, 8 + h:9 + h], None, A.mult)
                    fin(v.tensor_copy(vgscr[0:1, 0:1], vgscr[0:1, 0:1]))
                t_c3 = step(sV, [t_c2], _c3)

                def _ow2(g, fin, sub=sub, ot0=ot0):
                    fin(g.dma_start(
                        out=o_hbm[ot0 + sub * 128:ot0 + sub * 128 + 128, :],
                        in_=o_tmb[:, sub, :]))
                t_ow[sub] = step(sGQ, [t_c3], _ow2, ndma=1, key="ow")

            for sub in range(2):
                def _or(sc, fin, sub=sub, ot0=ot0):
                    for kc in range(8):
                        fin(sc.dma_start_transpose(
                            grd[:, kc, :],
                            o_hbm[ot0 + sub * 128:ot0 + sub * 128 + 128,
                                  kc * 128:(kc + 1) * 128]))
                t_or = step(sCQ, [t_ow[sub], sT.tok()], _or, ndma=8, key="ghr")

                def _op(te, fin, sub=sub):
                    ins = None
                    for kc in range(8):
                        for nb in range(2):
                            ins = te.matmul(psum[4 + nb][:, 0:512], grd[:, kc, :],
                                            oprjT[kc][:, nb * 512:(nb + 1) * 512],
                                            start=(kc == 0), stop=(kc == 7))
                    fin(ins)
                t_op = step(sT, [t_or, t_oprep, t_last_out], _op)

                def _sv(sc, fin):
                    sc.activation(s_fin[:, 0:512], psum[4][:, 0:512], AF.Copy)
                    fin(sc.activation(s_fin[:, 512:1024], psum[5][:, 0:512], AF.Copy))
                t_se2 = step(sC, [t_op], _sv)

                def _xr(sy, fin, sub=sub, tok0=tok0):
                    fin(sy.dma_start(
                        out=x_fin[:],
                        in_=x_in[tok0 + sub * 128:tok0 + sub * 128 + 128, :]))
                t_xf = step(sY, [t_se2], _xr, ndma=1, key="xr")

                def _gr3(g, fin, sub=sub, ot0=ot0):
                    fin(g.dma_start(out=gfin[:, 0:1024],
                                    in_=g_tm[ot0 + sub * 128:ot0 + sub * 128 + 128, :]))
                t_gf = step(sGQ, [t_se2], _gr3, ndma=1, key="gr3")

                def _fo(v, fin):
                    v.tensor_tensor(outst[:], s_fin[:], gfin[:, 0:1024], A.mult)
                    VG(v)
                    fin(v.tensor_tensor(outst[:], outst[:], x_fin[:], A.add))
                t_fo = step(sV, [t_se2, t_xf, t_gf], _fo)

                def _out(g, fin, sub=sub, ot0=ot0):
                    fin(g.dma_start(
                        out=out[ot0 + sub * 128:ot0 + sub * 128 + 128, :],
                        in_=outst[:]))
                t_last_out = step(sGQ, [t_fo], _out, ndma=1, key="outw")

            t_tile_prev = t_last_out
        else:
            t_tile_prev = t_sev

    def _fbar(g, fin):
        fin(g.nop())
    _all_d = [(ps.h, ps.n) for ps in _psem.values()]
    step(sG, [t_tile_prev] + B_DONE + _all_d, _fbar)

    pr.emit()
    return nc


_PROGRAM_CACHE = {}


def _get_program():
    if "nc" not in _PROGRAM_CACHE:
        _PROGRAM_CACHE["nc"] = build_program()
    return _PROGRAM_CACHE["nc"]


def make_in_maps(inputs):
    x = np.asarray(inputs["L0_attn_hidden_states"], dtype=np.float32)
    w_fc = np.ascontiguousarray(np.asarray(inputs["L0_ffn_w_fc"], dtype=np.float32))
    b_fc = np.ascontiguousarray(
        np.asarray(inputs["L0_ffn_b_fc"], dtype=np.float32).reshape(1, HID))
    w_pj = np.ascontiguousarray(np.asarray(inputs["L0_ffn_w_proj"], dtype=np.float32))
    b_pj = np.ascontiguousarray(
        np.asarray(inputs["L0_ffn_b_proj"], dtype=np.float32).reshape(1, D))
    ones = np.ones((1, 128), np.float32)
    qw = np.asarray(inputs["L0_attn_q_proj_w"], np.float32)
    kw = np.asarray(inputs["L0_attn_k_proj_w"], np.float32)
    vw = np.asarray(inputs["L0_attn_v_proj_w"], np.float32)
    bw = np.asarray(inputs["L0_attn_b_proj_w"], np.float32)
    wqkvT = np.ascontiguousarray(
        np.concatenate([qw.T, kw.T, vw.T, bw.T], axis=1))
    qc = np.asarray(inputs["L0_attn_q_conv_w"], np.float32)
    kc_ = np.asarray(inputs["L0_attn_k_conv_w"], np.float32)
    vc = np.asarray(inputs["L0_attn_v_conv_w"], np.float32)
    firs = np.asarray(inputs["L0_attn_fir_short_w"], np.float32).reshape(1024, 3)
    firl = np.asarray(inputs["L0_attn_fir_long_w"], np.float32).reshape(1024, 31)

    def convcols(w, ntap):
        outc = np.zeros((128, 32), np.float32)
        for i in range(32):
            for a in range(min(4, ntap)):
                outc[32 * a:32 * a + 32, i] = w[i * 32:i * 32 + 32, a]
        return outc
    c4cols = np.concatenate([convcols(qc, 4), convcols(kc_, 4), convcols(vc, 4)],
                            axis=1)
    f31cols = np.zeros((128, 256), np.float32)
    for i in range(32):
        for tb in range(8):
            for a in range(4):
                j = 4 * tb + a
                if j <= 30:
                    f31cols[32 * a:32 * a + 32, i * 8 + tb] =                         firl[i * 32:i * 32 + 32, j]
    f3cols = convcols(firs, 3)
    dmask = np.zeros((128, 32), np.float32)
    for a in range(4):
        dmask[32 * a:32 * a + 32, :] = np.eye(32, dtype=np.float32)
    idx = np.arange(128)
    umask = (idx[:, None] < idx[None, :]).astype(np.float32)
    lmask = (idx[:, None] > idx[None, :]).astype(np.float32)
    amask = (idx[:, None] <= idx[None, :]).astype(np.float32)
    eye128 = np.eye(128, dtype=np.float32)
    ones2 = np.ones((128, 64), np.float32)
    fgw1T = np.ascontiguousarray(np.asarray(inputs["L0_attn_fg_w1"], np.float32).T)
    fgb1r = np.ascontiguousarray(
        np.asarray(inputs["L0_attn_fg_b1"], np.float32).reshape(1, 1024))
    fgw2T = np.ascontiguousarray(np.asarray(inputs["L0_attn_fg_w2"], np.float32).T)
    fgb2r = np.ascontiguousarray(
        np.asarray(inputs["L0_attn_fg_b2"], np.float32).reshape(1, 4))
    ltemp = np.asarray(inputs["L0_attn_logit_temp"], np.float32).reshape(1, 1)
    onw = np.asarray(inputs["L0_attn_o_norm_w"], np.float32)
    onwcols = np.zeros((128, 8), np.float32)
    for kc2 in range(8):
        onwcols[:, kc2] = onw[(kc2 * 128 + np.arange(128)) % 256]
    oprojT = np.ascontiguousarray(np.asarray(inputs["L0_attn_o_proj_w"], np.float32).T)
    zeros128 = np.zeros((128, 1024), np.float32)
    attn_ins = {"wqkvT": wqkvT, "c4cols": c4cols, "f31cols": f31cols,
                "f3cols": f3cols, "dmask": dmask, "umask": umask, "lmask": lmask,
                "amask": amask, "eye128": eye128, "ones2": ones2, "fgw1T": fgw1T,
                "fgb1r": fgb1r, "fgw2T": fgw2T, "fgb2r": fgb2r, "ltemp": ltemp,
                "onwcols": onwcols, "oprojT": oprojT, "zeros128": zeros128}
    in_maps = []
    for c in range(NCORES):
        b, h = c // 2, c % 2
        if h == 0:
            xs = np.concatenate([np.zeros((PRE, D), np.float32), x[b, :TPC]], axis=0)
        else:
            xs = np.ascontiguousarray(x[b])
        m = {"x": xs, "w_fc": w_fc, "b_fc": b_fc,
             "w_proj": w_pj, "b_proj": b_pj, "ones_row": ones}
        m.update(attn_ins)
        in_maps.append(m)
    return in_maps


def kernel(**inputs):
    nc = _get_program()
    in_maps = make_in_maps(inputs)
    res = run_bass_kernel_spmd(nc, in_maps, core_ids=list(range(NCORES)))
    outp = np.empty((B, L, D), np.float32)
    for c in range(NCORES):
        b, h = c // 2, c % 2
        outp[b, h * TPC:(h + 1) * TPC] = res.results[c]["out"]
    return outp


# version 10
# speedup vs baseline: 1.3543x; 1.0563x over previous
import numpy as np
import concourse.bass as bass
import concourse.mybir as mybir
from concourse.bass_utils import run_bass_kernel_spmd

F32 = mybir.dt.float32
BF16 = mybir.dt.bfloat16
A = mybir.AluOpType
AF = mybir.ActivationFunctionType
AX = mybir.AxisListType

B, L, D = 4, 8192, 1024
NCORES = 8
TPC = L // 2
PRE = TPC
XROWS = PRE + TPC
NLN = XROWS // 128
HID = 4096
NTILE = TPC // 128
GTILES = 8
NGRP = NTILE // GTILES
SUBT = 4
ATILE = 256
NAT = XROWS // ATILE
CS = 64
NCH = ATILE // CS
BK = 32
LN_EPS = 1e-5
FP64_GADGET = True

MAGIC23 = float(1.5 * 2 ** 23)
MAGIC33 = float(1.5 * 2 ** 33)
MAGIC52 = float(1.5 * 2 ** 52)


class Sem:
    def __init__(self, nc, name):
        self.h = nc.alloc_semaphore(name)
        self.n = 0


class Prog:
    ENGINES = ("sync", "scalar", "vector", "tensor", "gpsimd")

    def __init__(self, nc):
        self.nc = nc
        self.streams = {e: [] for e in self.ENGINES}

    def on(self, eng):
        def deco(fn):
            self.streams[eng].append(fn)
            return None
        return deco

    def emit(self):
        with self.nc.Block() as blk:
            for eng in self.ENGINES:
                thunks = self.streams[eng]
                getattr(blk, eng)(lambda e, th=thunks: [f(e) for f in th])


def emit_ladder_m(v, am, p, scr, m23, m33, m52, nblk, lanes=4):
    nb4 = nblk // lanes
    sl = [slice(l * nb4, (l + 1) * nb4) for l in range(lanes)]
    for l in range(lanes):
        v.memset(p[:, sl[l]], float(2.0 ** -34))
    for k in (32, 16, 8, 4, 2, 1):
        for l in range(lanes):
            v.tensor_single_scalar(scr[:, sl[l]], p[:, sl[l]], float(2.0 ** k), A.mult)
        for l in range(lanes):
            v.tensor_tensor(scr[:, sl[l]], am[:, sl[l]], scr[:, sl[l]], A.is_ge)
        for l in range(lanes):
            v.tensor_scalar(scr[:, sl[l]], scr[:, sl[l]], float(2.0 ** k - 1.0), 1.0,
                            A.mult, A.add)
        for l in range(lanes):
            v.tensor_tensor(p[:, sl[l]], p[:, sl[l]], scr[:, sl[l]], A.mult)
    for l in range(lanes):
        v.tensor_single_scalar(m23[:, sl[l]], p[:, sl[l]], MAGIC23 / 8.0, A.mult)
    for l in range(lanes):
        v.tensor_single_scalar(m33[:, sl[l]], p[:, sl[l]], MAGIC33 / 8.0, A.mult)
    for l in range(lanes):
        v.tensor_single_scalar(m52[:, sl[l]], p[:, sl[l]], MAGIC52 / 8.0, A.mult)


def emit_quant(v, src_ap, dst_ap, m23, m33, m52, tmp, tmp2, nblk, P=128):
    s3 = src_ap.rearrange("p (b c) -> p b c", b=nblk, c=BK)
    d3 = dst_ap.rearrange("p (b c) -> p b c", b=nblk, c=BK)
    t3 = tmp.rearrange("p (b c) -> p b c", b=nblk, c=BK)
    u3 = tmp2.rearrange("p (b c) -> p b c", b=nblk, c=BK)
    mb23 = m23[0:P, 0:nblk].unsqueeze(2).broadcast_to([P, nblk, BK])
    v.tensor_tensor(t3, s3, mb23, A.add)
    if not FP64_GADGET:
        v.tensor_tensor(d3, t3, mb23, A.subtract)
        return
    v.tensor_tensor(t3, t3, mb23, A.subtract)
    mb52 = m52[0:P, 0:nblk].unsqueeze(2).broadcast_to([P, nblk, BK])
    v.tensor_tensor(u3, s3, mb52, A.add)
    v.tensor_tensor(u3, u3, mb52, A.subtract)
    v.tensor_tensor(t3, t3, u3, A.add)
    mb33 = m33[0:P, 0:nblk].unsqueeze(2).broadcast_to([P, nblk, BK])
    v.tensor_tensor(u3, s3, mb33, A.add)
    v.tensor_tensor(u3, u3, mb33, A.subtract)
    v.tensor_tensor(d3, t3, u3, A.subtract)


def build_program():
    nc = bass.Bass("TRN2", target_bir_lowering=False)
    pr = Prog(nc)

    x_in = nc.dram_tensor("x", [XROWS, D], F32, kind="ExternalInput").ap()
    w_fc = nc.dram_tensor("w_fc", [HID, D], F32, kind="ExternalInput").ap()
    b_fc = nc.dram_tensor("b_fc", [1, HID], F32, kind="ExternalInput").ap()
    w_pj = nc.dram_tensor("w_proj", [D, HID], F32, kind="ExternalInput").ap()
    b_pj = nc.dram_tensor("b_proj", [1, D], F32, kind="ExternalInput").ap()
    ones_in = nc.dram_tensor("ones_row", [1, 128], F32, kind="ExternalInput").ap()
    wqkv_in = nc.dram_tensor("wqkvT", [D, 3076], F32, kind="ExternalInput").ap()
    c4cols_in = nc.dram_tensor("c4cols", [128, 96], F32, kind="ExternalInput").ap()
    f31cols_in = nc.dram_tensor("f31cols", [128, 256], F32, kind="ExternalInput").ap()
    f3cols_in = nc.dram_tensor("f3cols", [128, 32], F32, kind="ExternalInput").ap()
    dmask_in = nc.dram_tensor("dmask", [128, 32], F32, kind="ExternalInput").ap()
    umask_in = nc.dram_tensor("umask", [128, 128], F32, kind="ExternalInput").ap()
    lmask_in = nc.dram_tensor("lmask", [128, 128], F32, kind="ExternalInput").ap()
    amask_in = nc.dram_tensor("amask", [128, 128], F32, kind="ExternalInput").ap()
    eye_in = nc.dram_tensor("eye128", [128, 128], F32, kind="ExternalInput").ap()
    ones2_in = nc.dram_tensor("ones2", [128, 64], F32, kind="ExternalInput").ap()
    fgw1_in = nc.dram_tensor("fgw1T", [1792, 1024], F32, kind="ExternalInput").ap()
    fgb1_in = nc.dram_tensor("fgb1r", [1, 1024], F32, kind="ExternalInput").ap()
    fgw2_in = nc.dram_tensor("fgw2T", [1024, 4], F32, kind="ExternalInput").ap()
    fgb2_in = nc.dram_tensor("fgb2r", [1, 4], F32, kind="ExternalInput").ap()
    ltemp_in = nc.dram_tensor("ltemp", [1, 1], F32, kind="ExternalInput").ap()
    onw_in = nc.dram_tensor("onwcols", [128, 8], F32, kind="ExternalInput").ap()
    oprj_in = nc.dram_tensor("oprojT", [D, D], F32, kind="ExternalInput").ap()
    zeros_in = nc.dram_tensor("zeros128", [128, 1024], F32, kind="ExternalInput").ap()
    out = nc.dram_tensor("out", [TPC, D], F32, kind="ExternalOutput").ap()
    q_hbm = nc.dram_tensor("q_hbm", [D, XROWS], BF16).ap()
    k_hbm = nc.dram_tensor("k_hbm", [D, XROWS], BF16).ap()
    v_hbm = nc.dram_tensor("v_hbm", [D, XROWS + 8], BF16).ap()
    sh_hbm = nc.dram_tensor("sh_hbm", [D, TPC], BF16).ap()
    lo_hbm = nc.dram_tensor("lo_hbm", [D, TPC], BF16).ap()
    mn_hbm = nc.dram_tensor("mn_hbm", [TPC, 768], BF16).ap()
    gh_hbm = nc.dram_tensor("gh_hbm", [TPC, 1024], BF16).ap()
    o_hbm = nc.dram_tensor("o_hbm", [TPC, D], BF16).ap()

    xn_tm = nc.dram_tensor("xn_tm", [64 + XROWS, D], BF16).ap()
    g_tm = nc.dram_tensor("g_tm", [TPC, D], BF16).ap()
    xnq_tm = nc.dram_tensor("xnq_tm", [TPC, D], BF16).ap()
    hq_tm = nc.dram_tensor("hq_tm", [TPC, HID], BF16).ap()
    wqfc_hbm = nc.dram_tensor("wqfc_hbm", [HID, D], BF16).ap()
    wqpj_hbm = nc.dram_tensor("wqpj_hbm", [D, HID], BF16).ap()

    sb = nc.alloc_sbuf_tensor
    wbig = sb("wbig", [128, 8 * HID], BF16)
    bfc_row = sb("bfc_row", [1, HID], BF16)
    bpj_row = sb("bpj_row", [1, D], BF16)
    ones = sb("ones", [1, 128], BF16)

    NX = 2
    xt = [sb(f"xt{i}", [128, D], F32) for i in range(NX)]
    xnt = [sb(f"xnt{i}", [128, D], F32) for i in range(NX)]
    xnb = [sb(f"xnb{i}", [128, D], BF16) for i in range(NX)]
    xqb = [sb(f"xqb{i}", [128, D], BF16) for i in range(NX)]
    lnscr = sb("lnscr", [128, D], F32)
    qt1 = sb("qt1", [128, D], F32)
    qt2 = sb("qt2", [128, D], F32)
    stat = [sb(f"stat{i}", [128, 12], F32) for i in range(NX)]
    xam = [sb(f"xam{i}", [128, 32], F32) for i in range(NX)]
    xp = [sb(f"xp{i}", [128, 32], F32) for i in range(NX)]
    xscr = sb("xscr", [128, 32], F32)
    xm23 = [sb(f"xm23{i}", [128, 32], F32) for i in range(NX)]
    xm33 = [sb(f"xm33{i}", [128, 32], F32) for i in range(NX)]
    xm52 = [sb(f"xm52{i}", [128, 32], F32) for i in range(NX)]

    xnqf = sb("xnqf", [128, 8 * 1024], BF16)
    hbuf = sb("hbuf", [128, 2048], F32)
    hqb = sb("hqb", [128, 2048], BF16)
    ht1 = sb("ht1", [128, 2048], F32)
    ht2 = sb("ht2", [128, 2048], F32)
    NH = 2
    ham = [sb(f"ham{i}", [128, 64], F32) for i in range(NH)]
    hp = [sb(f"hp{i}", [128, 64], F32) for i in range(NH)]
    hscr = sb("hscr", [128, 64], F32)
    hm23 = [sb(f"hm23{i}", [128, 64], F32) for i in range(NH)]
    hm33 = [sb(f"hm33{i}", [128, 64], F32) for i in range(NH)]
    hm52 = [sb(f"hm52{i}", [128, 64], F32) for i in range(NH)]

    NHQ = 8
    hqT = [sb(f"hqT{i}", [128, SUBT * 128], BF16) for i in range(NHQ)]
    gbuf = [sb(f"gbuf{i}", [128, D], F32) for i in range(SUBT)]
    x2 = [sb(f"x2_{i}", [128, D], F32) for i in range(2)]
    obuf = [sb(f"obuf{i}", [128, D], F32) for i in range(2)]

    psum = [nc.alloc_psum_tensor(f"ps{i}", [128, 512], F32) for i in range(8)]

    wsrc = ht1
    wt1 = ht2
    wt2 = hbuf
    wout = hqb
    wam, wp_, wscr = ham[0], hp[0], hscr
    wm23, wm33, wm52 = hm23[0], hm33[0], hm52[0]

    wfcT_v = wbig[:].rearrange("p (c n) -> p c n", c=8, n=HID)
    wpjT_v = wbig[:].rearrange("p (c n) -> p c n", c=32, n=D)
    xnqf_v = xnqf[:].rearrange("p (c n) -> p c n", c=8, n=1024)

    S = lambda name: Sem(nc, name)
    s_cst = S("s_cst")
    s_wld = S("s_wld"); s_wam = S("s_wam"); s_wpool = S("s_wpool")
    s_wqd = S("s_wqd"); s_wst = S("s_wst"); s_wtr = S("s_wtr"); s_wtr2 = S("s_wtr2")
    s_xp = [S("s_xp0"), S("s_xp1")]
    s_xstp = [S("s_xstp0"), S("s_xstp1")]
    s_htr8 = [S(f"s_htr8_{i}") for i in range(8)]
    s_x = S("s_x"); s_st1 = S("s_st1"); s_pmath = S("s_pmath"); s_sqrt = S("s_sqrt")
    s_lnv = S("s_lnv"); s_xpool = S("s_xpool"); s_xq = S("s_xq"); s_xst = S("s_xst")
    s_xtr = S("s_xtr"); s_fc = S("s_fc"); s_gel = S("s_gel")
    s_ham = S("s_ham"); s_hpool = S("s_hpool"); s_hq = S("s_hq"); s_hst = S("s_hst")
    s_htr = S("s_htr"); s_pjk = S("s_pjk"); s_sil = S("s_sil")
    s_x2 = S("s_x2"); s_ores = S("s_ores"); s_out = S("s_out")

    @pr.on("gpsimd")
    def _(g):
        g.dma_start(out=ones[:], in_=ones_in[:]).then_inc(s_cst.h, 16)
        g.dma_start(out=bfc_row[:], in_=b_fc[:]).then_inc(s_cst.h, 16)
        g.dma_start(out=bpj_row[:], in_=b_pj[:]).then_inc(s_cst.h, 16)
    s_cst.n += 48

    WTILES = [("fc", i, 0, 1024, 32) for i in range(32)] + \
             [("pj", i, hh, 2048, 64) for i in range(8) for hh in range(2)]
    wqd_done = {}

    for wi, (kind, i, hh, ncol, nblk) in enumerate(WTILES):
        src = w_fc if kind == "fc" else w_pj
        dst = wqfc_hbm if kind == "fc" else wqpj_hbm
        col0 = hh * 2048

        @pr.on("sync")
        def _(sy, src=src, i=i, col0=col0, ncol=ncol, wi=wi):
            if wi >= 1:
                sy.wait_ge(s_wqd.h, wqd_done[wi - 1])
            sy.dma_start(out=wsrc[:, 0:ncol],
                         in_=src[i * 128:(i + 1) * 128, col0:col0 + ncol]).then_inc(s_wld.h, 16)
        s_wld.n += 16

        @pr.on("vector")
        def _(v, wld_n=s_wld.n, nblk=nblk, ncol=ncol):
            v.wait_ge(s_wld.h, wld_n)
            v.tensor_reduce(wam[:, 0:nblk].unsqueeze(2),
                            wsrc[:, 0:ncol].rearrange("p (b c) -> p b c", b=nblk, c=BK),
                            AX.X, A.max, apply_absolute_value=True)
            emit_ladder_m(v, wam, wp_, wscr, wm23, wm33, wm52, nblk)
            emit_quant(v, wsrc[:, 0:ncol], wout[:, 0:ncol],
                       wm23, wm33, wm52, wt1[:, 0:ncol], wt2[:, 0:ncol], nblk)
            v.tensor_copy(wscr[0:1, 1:2], wscr[0:1, 1:2]).then_inc(s_wqd.h, 1)
        s_wqd.n += 1
        wqd_done[wi] = s_wqd.n

        @pr.on("gpsimd")
        def _(g, dst=dst, i=i, col0=col0, ncol=ncol, wqd_n=s_wqd.n):
            g.wait_ge(s_wqd.h, wqd_n)
            g.dma_start(out=dst[i * 128:(i + 1) * 128, col0:col0 + ncol],
                        in_=wout[:, 0:ncol]).then_inc(s_wst.h, 16)
        s_wst.n += 16

    @pr.on("scalar")
    def _(sc, wst_n=s_wst.n):
        sc.wait_ge(s_wst.h, wst_n)
        for fc_ in range(8):
            sc.dma_start_transpose(wfcT_v[:, fc_, :],
                                   wqfc_hbm[:, fc_ * 128:(fc_ + 1) * 128]).then_inc(s_wtr.h, 16)
    s_wtr.n += 8 * 16

    lnv_done = {}
    xq_done = {}
    xst_done = {}

    for t in range(NLN):
        bi = t % NX
        row0 = t * 128
        own = t >= NLN - NTILE
        orow0 = (t - (NLN - NTILE)) * 128

        @pr.on("sync")
        def _(sy, t=t, bi=bi, row0=row0):
            if t >= NX:
                sy.wait_ge(s_xq.h, xq_done[t - NX])
            sy.dma_start(out=xt[bi][:],
                         in_=x_in[row0:row0 + 128, :]).then_inc(s_xp[bi].h, 16)
        s_xp[bi].n += 16

        @pr.on("scalar")
        def _(sc, t=t, bi=bi, x_n=s_xp[bi].n):
            sc.wait_ge(s_xp[bi].h, x_n)
            if t >= NX:
                sc.wait_ge(s_lnv.h, lnv_done[t - NX])
            sc.activation(lnscr[:], xt[bi][:], AF.Copy, accum_out=stat[bi][:, 0:1])
            sc.activation(lnscr[:], xt[bi][:], AF.Square,
                          accum_out=stat[bi][:, 1:2]).then_inc(s_st1.h, 1)
        s_st1.n += 1

        @pr.on("gpsimd")
        def _(g, bi=bi, st1_n=s_st1.n):
            g.wait_ge(s_st1.h, st1_n)
            g.tensor_single_scalar(stat[bi][:, 2:3], stat[bi][:, 0:1], 1.0 / D, A.mult)
            g.tensor_single_scalar(stat[bi][:, 3:4], stat[bi][:, 1:2], 1.0 / D, A.mult)
            g.tensor_tensor(stat[bi][:, 4:5], stat[bi][:, 2:3], stat[bi][:, 2:3], A.mult)
            g.tensor_tensor(stat[bi][:, 3:4], stat[bi][:, 3:4], stat[bi][:, 4:5], A.subtract)
            g.tensor_single_scalar(stat[bi][:, 3:4], stat[bi][:, 3:4], LN_EPS,
                                   A.add).then_inc(s_pmath.h, 1)
        s_pmath.n += 1

        @pr.on("scalar")
        def _(sc, bi=bi, pmath_n=s_pmath.n):
            sc.wait_ge(s_pmath.h, pmath_n)
            sc.activation(stat[bi][:, 5:6], stat[bi][:, 3:4], AF.Sqrt).then_inc(s_sqrt.h, 1)
        s_sqrt.n += 1

        @pr.on("vector")
        def _(v, t=t, bi=bi, sqrt_n=s_sqrt.n):
            v.wait_ge(s_sqrt.h, sqrt_n)
            v.reciprocal(stat[bi][:, 6:7], stat[bi][:, 5:6])
            v.tensor_copy(xscr[0:1, 0:1], xscr[0:1, 0:1])
            v.tensor_copy(xscr[0:1, 1:2], xscr[0:1, 1:2])
            v.tensor_copy(xscr[0:1, 2:3], xscr[0:1, 2:3])
            v.tensor_scalar(xnt[bi][:], xt[bi][:], stat[bi][:, 2:3], stat[bi][:, 6:7],
                            A.subtract, A.mult)
            v.tensor_copy(xnb[bi][:], xnt[bi][:])
            v.tensor_reduce(xam[bi][:, 0:32].unsqueeze(2),
                            xnt[bi][:].rearrange("p (b c) -> p b c", b=32, c=BK),
                            AX.X, A.max, apply_absolute_value=True).then_inc(s_lnv.h, 1)
        s_lnv.n += 1
        lnv_done[t] = s_lnv.n

        @pr.on("vector")
        def _(v, t=t, bi=bi):
            if t >= NX:
                _xd = xst_done[t - NX]
                v.wait_ge(s_xstp[0].h, _xd[0])
                v.wait_ge(s_xstp[1].h, _xd[1])
            emit_ladder_m(v, xam[bi], xp[bi], xscr, xm23[bi], xm33[bi], xm52[bi], 32)
            emit_quant(v, xnt[bi][:], xqb[bi][:], xm23[bi], xm33[bi], xm52[bi],
                       qt1[:], qt2[:], 32)
            v.tensor_copy(xscr[0:1, 4:5], xscr[0:1, 4:5]).then_inc(s_xq.h, 1)
        s_xq.n += 1
        xq_done[t] = s_xq.n

        @pr.on("gpsimd")
        def _(g, bi=bi, row0=row0, own=own, orow0=orow0, xq_n=s_xq.n):
            g.wait_ge(s_xq.h, xq_n)
            g.dma_start(out=xn_tm[64 + row0:64 + row0 + 128, :],
                        in_=xnb[bi][:]).then_inc(s_xstp[bi].h, 16)
            if own:
                g.dma_start(out=xnq_tm[orow0:orow0 + 128, :],
                            in_=xqb[bi][:]).then_inc(s_xstp[bi].h, 16)
        s_xstp[bi].n += 16 + (16 if own else 0)
        xst_done[t] = (s_xstp[0].n, s_xstp[1].n)

    fc_done = {}
    gel_done = {}
    hq_done = {}
    hst_done = {}
    xtr_done = {}
    hpool_done = {}

    def prev_half(key):
        grp, tl, half = key
        if half:
            return (grp, tl, 0)
        if tl:
            return (grp, tl - 1, 1)
        if grp:
            return (grp - 1, GTILES - 1, 1)
        return None

    for grp in range(NGRP):
        @pr.on("scalar")
        def _(sc, grp=grp, wtr_n=s_wtr.n, xst_n=None):
            _xd = xst_done[(NLN - NTILE) + grp * GTILES + GTILES - 1]
            sc.wait_ge(s_xstp[0].h, _xd[0])
            sc.wait_ge(s_xstp[1].h, _xd[1])
            sc.wait_ge(s_wtr.h, wtr_n)
            if grp >= 1:
                sc.wait_ge(s_fc.h, fc_done[(grp - 1, GTILES - 1, 1)])
            for fc_ in range(8):
                sc.dma_start_transpose(
                    xnqf_v[:, fc_, :],
                    xnq_tm[grp * 1024:(grp + 1) * 1024,
                           fc_ * 128:(fc_ + 1) * 128]).then_inc(s_xtr.h, 16)
        s_xtr.n += 8 * 16
        xtr_done[grp] = s_xtr.n

        for tl in range(GTILES):
            tok = tl * 128
            for half in range(2):
                key = (grp, tl, half)

                @pr.on("tensor")
                def _(te, key=key, tok=tok, half=half, xtr_n=xtr_done[grp]):
                    te.wait_ge(s_xtr.h, xtr_n)
                    te.wait_ge(s_cst.h, 48)
                    pk = prev_half(key)
                    if pk in gel_done:
                        te.wait_ge(s_gel.h, gel_done[pk])
                    for nb in range(4):
                        hid0 = half * 2048 + nb * 512
                        te.matmul(psum[nb][:], ones[:],
                                  bfc_row[0:1, hid0:hid0 + 512], start=True, stop=False)
                    ins = None
                    for kc in range(8):
                        for nb in range(4):
                            hid0 = half * 2048 + nb * 512
                            ins = te.matmul(psum[nb][:],
                                            xnqf_v[:, kc, tok:tok + 128],
                                            wfcT_v[:, kc, hid0:hid0 + 512],
                                            start=False, stop=(kc == 7))
                    ins.then_inc(s_fc.h, 1)
                s_fc.n += 1
                fc_done[key] = s_fc.n

                @pr.on("scalar")
                def _(sc, key=key, fc_n=s_fc.n):
                    sc.wait_ge(s_fc.h, fc_n)
                    pk = prev_half(key)
                    if pk in hq_done:
                        sc.wait_ge(s_hq.h, hq_done[pk])
                    for nb in range(4):
                        sc.activation(hbuf[:, nb * 512:(nb + 1) * 512], psum[nb][:],
                                      AF.Gelu).then_inc(s_gel.h, 1)
                s_gel.n += 4
                gel_done[key] = s_gel.n

                @pr.on("vector")
                def _(v, key=key, half=half, gel_n=s_gel.n):
                    v.wait_ge(s_gel.h, gel_n)
                    pk = prev_half(key)
                    if pk in hst_done:
                        v.wait_ge(s_hst.h, hst_done[pk])
                    v.tensor_reduce(ham[half][:, 0:64].unsqueeze(2),
                                    hbuf[:].rearrange("p (b c) -> p b c", b=64, c=BK),
                                    AX.X, A.max, apply_absolute_value=True)
                    emit_ladder_m(v, ham[half], hp[half], hscr,
                                  hm23[half], hm33[half], hm52[half], 64)
                    emit_quant(v, hbuf[:], hqb[:], hm23[half], hm33[half], hm52[half],
                               ht1[:], ht2[:], 64)
                    v.tensor_copy(hscr[0:1, 1:2], hscr[0:1, 1:2]).then_inc(s_hq.h, 1)
                s_hq.n += 1
                hq_done[key] = s_hq.n

                @pr.on("gpsimd")
                def _(g, key=key, half=half, hq_n=s_hq.n):
                    grp_, tl_, _ = key
                    g.wait_ge(s_hq.h, hq_n)
                    r0 = grp_ * 1024 + tl_ * 128
                    g.dma_start(out=hq_tm[r0:r0 + 128, half * 2048:(half + 1) * 2048],
                                in_=hqb[:]).then_inc(s_hst.h, 16)
                s_hst.n += 16
                hst_done[key] = s_hst.n

    last_fc = fc_done[(NGRP - 1, GTILES - 1, 1)]
    last_gel = gel_done[(NGRP - 1, GTILES - 1, 1)]

    @pr.on("scalar")
    def _(sc, last_fc=last_fc):
        sc.wait_ge(s_fc.h, last_fc)
        for hc in range(32):
            sc.dma_start_transpose(wpjT_v[:, hc, :],
                                   wqpj_hbm[:, hc * 128:(hc + 1) * 128]).then_inc(s_wtr2.h, 16)
    s_wtr2.n += 32 * 16

    htr_done = {}
    pjk_done = {}
    sil_done = {}
    ores_done = {}
    sgi = 0
    ckidx = 0

    for grp in range(NGRP):
        for sub in range(GTILES // SUBT):
            tok0 = sub * SUBT * 128
            ck0 = ckidx
            for kc in range(32):
                @pr.on("scalar")
                def _(sc, grp=grp, kc=kc, tok0=tok0, ck=ckidx,
                      hst_n=hst_done[(grp, GTILES - 1, 1)]):
                    sc.wait_ge(s_hst.h, hst_n)
                    if ck >= NHQ:
                        sc.wait_ge(s_pjk.h, pjk_done[ck - NHQ])
                    sc.dma_start_transpose(
                        hqT[ck % NHQ][:],
                        hq_tm[grp * 1024 + tok0:grp * 1024 + tok0 + SUBT * 128,
                              kc * 128:(kc + 1) * 128]).then_inc(s_htr8[ck % 8].h, 16)
                s_htr8[ckidx % 8].n += 16
                htr_done[ckidx] = (ckidx % 8, s_htr8[ckidx % 8].n)
                ckidx += 1

            @pr.on("tensor")
            def _(te, sgi=sgi, ck0=ck0, last_gel=last_gel, wtr2_n=s_wtr2.n):
                te.wait_ge(s_wtr2.h, wtr2_n)
                if sgi == 0:
                    te.wait_ge(s_gel.h, last_gel)
                else:
                    te.wait_ge(s_sil.h, sil_done[sgi - 1])
                for st_ in range(SUBT):
                    for pb in range(2):
                        te.matmul(psum[st_ * 2 + pb][:], ones[:],
                                  bpj_row[0:1, pb * 512:(pb + 1) * 512],
                                  start=True, stop=False)
                for kc in range(32):
                    _hs, _hn = htr_done[ck0 + kc]
                    te.wait_ge(s_htr8[_hs].h, _hn)
                    ins = None
                    for st_ in range(SUBT):
                        for pb in range(2):
                            ins = te.matmul(psum[st_ * 2 + pb][:],
                                            hqT[(ck0 + kc) % NHQ][:, st_ * 128:st_ * 128 + 128],
                                            wpjT_v[:, kc, pb * 512:(pb + 1) * 512],
                                            start=False, stop=(kc == 31))
                    ins.then_inc(s_pjk.h, 1)
            for kc in range(32):
                pjk_done[ck0 + kc] = s_pjk.n + kc + 1
            s_pjk.n += 32

            @pr.on("scalar")
            def _(sc, sgi=sgi, pjk_n=s_pjk.n):
                sc.wait_ge(s_pjk.h, pjk_n)
                if sgi >= 1:
                    sc.wait_ge(s_ores.h, ores_done[sgi * SUBT - 1])
                for st_ in range(SUBT):
                    for pb in range(2):
                        sc.activation(gbuf[st_][:, pb * 512:(pb + 1) * 512],
                                      psum[st_ * 2 + pb][:], AF.Silu).then_inc(s_sil.h, 1)
            s_sil.n += 2 * SUBT
            sil_done[sgi] = s_sil.n

            for st_ in range(SUBT):
                t_own = grp * GTILES + sub * SUBT + st_

                @pr.on("vector")
                def _(v, sgi=sgi, st_=st_, t_own=t_own, sil_n=sil_done[sgi]):
                    v.wait_ge(s_sil.h, sil_n)
                    if t_own >= 2:
                        v.wait_ge(s_out.h, (t_own - 1) * 16)
                    v.tensor_copy(obuf[st_ % 2][:].bitcast(BF16)[:, 0:D],
                                  gbuf[st_][:]).then_inc(s_ores.h, 1)
                s_ores.n += 1
                ores_done[t_own] = s_ores.n

                @pr.on("gpsimd")
                def _(g, t_own=t_own, st_=st_, ores_n=s_ores.n):
                    g.wait_ge(s_ores.h, ores_n)
                    r0 = t_own * 128
                    g.dma_start(out=g_tm[r0:r0 + 128, :],
                                in_=obuf[st_ % 2][:].bitcast(BF16)[:, 0:D]).then_inc(s_out.h, 16)
                s_out.n += 16
            sgi += 1

    class St:
        def __init__(self, name, eng, dma):
            self.name, self.eng, self.dma = name, eng, dma
            self.sem = nc.alloc_semaphore("tk_" + name)
            if dma:
                self.dsem = nc.alloc_semaphore("tkd_" + name)
            self.n = 0
            self.q = 0

        def tok(self):
            return (self.sem, self.n)

    sT = St("T", "tensor", False)
    sV = St("V", "vector", False)
    sC = St("C", "scalar", False)
    sCQ = St("CQ", "scalar", True)
    sG = St("G", "gpsimd", False)
    sGQ = St("GQ", "gpsimd", True)
    sY = St("Y", "sync", True)

    _psem = {}

    def step(stream, deps, body, ndma=0, key=None):
        if stream.dma:
            key = key or (stream.name + "_d")
            kk = (stream.name, key)
            if kk not in _psem:
                _psem[kk] = Sem(nc, "pd_" + stream.name + "_" + key)
            ps = _psem[kk]
            ps.n += 16 * ndma
            post = ps.n
            sem = ps.h
        else:
            stream.n += 1
            post = stream.n
            sem = stream.sem
        isdma = stream.dma

        def thunk(e, deps=tuple(deps), body=body, sem=sem):
            for (s, c) in deps:
                if c > 0:
                    e.wait_ge(s, c)

            def fin(instr):
                instr.then_inc(sem, 16 if isdma else 1)
                return instr
            body(e, fin)
        pr.streams[stream.eng].append(thunk)
        return (sem, post)

    B_DONE = [(s_xstp[0].h, s_xstp[0].n), (s_xstp[1].h, s_xstp[1].n),
              (s_out.h, s_out.n)]

    wbigb = wbig[:]
    wqkvT = wbigb[:, 0:24608].rearrange("p (c n) -> p c n", c=8, n=3076)
    c4st = wbigb[:, 24608:27680].rearrange("p (i n) -> p i n", i=96, n=32)
    f3st = wbigb[:, 27680:28704].rearrange("p (i n) -> p i n", i=32, n=32)
    gfin = wbigb[:, 28704:29728]
    o_tmb = wbigb[:, 29728:31776].rearrange("p (s n) -> p s n", s=2, n=1024)
    wstage = wbigb[:, 31776:32768]
    f31st = xnqf[:].rearrange("p (i n) -> p i n", i=256, n=32)
    XW = 304
    xnfm = hbuf[:].bitcast(BF16).rearrange("p (c n) -> p c n", c=8, n=512)
    ht1b = ht1[:].bitcast(BF16)
    pre_ring = ht1b[:, 0:1216].rearrange("p (r n) -> p r n", r=4, n=304)
    vst = ht1b[:, 1216:1816].rearrange("p (r n) -> p r n", r=2, n=300)
    ht2b = ht2[:].bitcast(BF16)
    rep4 = ht2b[:, 0:2400].rearrange("p (r g n) -> p r g n", r=2, g=4, n=300)
    repf = ht2b[:, 2400:3536].rearrange("p (g n) -> p g n", g=4, n=284)
    qfm = xt[0][:].bitcast(BF16).rearrange("p (c n) -> p c n", c=8, n=256)
    kfm = xt[1][:].bitcast(BF16).rearrange("p (c n) -> p c n", c=8, n=256)
    qtm = qt1[:].bitcast(BF16).rearrange("p (s n) -> p s n", s=2, n=1024)
    ktm = qt2[:].bitcast(BF16).rearrange("p (s n) -> p s n", s=2, n=1024)
    vtm = obuf[0][:].bitcast(BF16).rearrange("p (s n) -> p s n", s=2, n=1024)
    dtm = obuf[1][:].bitcast(BF16).rearrange("p (s n) -> p s n", s=2, n=1024)
    vbt = lnscr[:].bitcast(BF16).rearrange("p (s n) -> p s n", s=2, n=1024)
    ghst = vbt
    Ssb = x2[0][:].bitcast(BF16).rearrange("p (h n) -> p h n", h=4, n=512)
    scb = x2[1][:].bitcast(BF16)
    zbuf = scb[:, 0:1024].rearrange("p (h n) -> p h n", h=4, n=256)
    cmt = scb[:, 1024:2048].rearrange("p (i n) -> p i n", i=8, n=128)
    slevv = xnb[1][:].rearrange("p (r n) -> p r n", r=4, n=256)
    mstg = [xqb[0][:], xqb[1][:]]
    grd = xnb[0][:].rearrange("p (r n) -> p r n", r=8, n=128)
    sltm = hqb[:].rearrange("p (i n) -> p i n", i=2, n=1024)
    x_fin = xnt[1][:]
    comb = [xnt[0][:].bitcast(BF16)[:, 0:1024],
            xnt[0][:].bitcast(BF16)[:, 1024:2048]]
    o2buf = xqb[0][:]
    oprjT = [gbuf[i // 2][:].bitcast(BF16)[:, (i % 2) * 1024:(i % 2 + 1) * 1024]
             for i in range(8)]
    fgw1r = [hqT[i][:] for i in range(8)]
    ofs32 = qt1[:]
    zrof = qt2[:]

    arsn = sb("arsn", [128, 48], F32)
    wsm = sb("wsm", [128, 24], F32)
    s_fin = sb("s_fin", [128, D], F32)
    outst = lnscr[:]
    vgscr = sb("vgscr", [128, 8], F32)
    cst128 = sb("cst128", [128, 384], F32)
    eye128 = sb("eye128_sb", [128, 128], F32)
    ones2 = sb("ones2_sb", [128, 64], BF16)
    dmask_sb = sb("dmask_s2", [128, 32], F32)
    ccols = s_fin[:, 0:384]
    onw_sb = sb("onw_sb", [128, 8], F32)
    fgb1_sb = sb("fgb1_sb", [1, 1024], BF16)
    fgb2_sb = sb("fgb2_sb", [1, 4], BF16)
    fgw2_sb = sb("fgw2_sb", [128, 32], BF16)
    invt = sb("invt", [1, 8], F32)
    invt_col = sb("invt_col", [128, 1], F32)
    qch_sb = ht1b[:, 1824:3872]
    Qch = [qch_sb[:, i * 512:(i + 1) * 512].rearrange("p (h n) -> p h n",
                                                      h=4, n=128)
           for i in range(4)]
    QchF = [qch_sb[:, i * 512:(i + 1) * 512] for i in range(4)]
    sfb = s_fin[:].bitcast(BF16)
    Gsb = [sfb[:, i * 512:(i + 1) * 512].rearrange("p (h n) -> p h n",
                                                   h=4, n=128)
           for i in range(2)]
    GsbF = [sfb[:, i * 512:(i + 1) * 512] for i in range(2)]
    attnT = [sfb[:, 1024 + i * 512:1024 + (i + 1) * 512]
             .rearrange("p (h n) -> p h n", h=4, n=128)
             for i in range(2)]
    ub_sb = gfin
    ubuf = ub_sb.rearrange("p (h n) -> p h n", h=4, n=256)
    rowb = wbigb[0:1, 29728:30752]
    eye_bf = ht2b[:, 3936:4064]

    fgw1_bhbm = nc.dram_tensor("fgw1_bhbm", [1792, 1024], BF16).ap()
    pre_hbm = nc.dram_tensor("pre_hbm", [512, 304], BF16).ap()
    invt_hbm = nc.dram_tensor("invt_hbm", [1, 1], F32).ap()

    def VG(v, n=3):
        for i in range(n):
            v.tensor_copy(vgscr[0:1, i:i + 1], vgscr[0:1, i:i + 1])

    def _wq_loads(g, fin):
        for kc in range(8):
            fin(g.dma_start(out=wqkvT[:, kc, :],
                            in_=wqkv_in[kc * 128:(kc + 1) * 128, :]))
        fin(g.dma_start(out=fgb1_sb[:], in_=fgb1_in[:]))
        fin(g.dma_start(out=fgb2_sb[:], in_=fgb2_in[:]))
        for kc in range(8):
            fin(g.dma_start(out=fgw2_sb[:, kc * 4:kc * 4 + 4],
                            in_=fgw2_in[kc * 128:(kc + 1) * 128, :]))
        fin(g.dma_start(out=ones2[:], in_=ones2_in[:]))
    t_wq = step(sGQ, B_DONE, _wq_loads, ndma=19, key="wld")

    def _consts(sy, fin):
        fin(sy.dma_start(out=cst128[:, 0:128], in_=umask_in[:]))
        fin(sy.dma_start(out=cst128[:, 128:256], in_=lmask_in[:]))
        fin(sy.dma_start(out=cst128[:, 256:384], in_=amask_in[:]))
        fin(sy.dma_start(out=eye128[:], in_=eye_in[:]))
        fin(sy.dma_start(out=dmask_sb[:], in_=dmask_in[:]))
        fin(sy.dma_start(out=ccols[:, 0:96], in_=c4cols_in[:]))
        fin(sy.dma_start(out=ccols[:, 96:352], in_=f31cols_in[:]))
        fin(sy.dma_start(out=ccols[:, 352:384], in_=f3cols_in[:]))
        fin(sy.dma_start(out=onw_sb[:], in_=onw_in[:]))
        fin(sy.dma_start(out=invt[0:1, 0:1], in_=ltemp_in[:]))
        fin(sy.dma_start(out=xn_tm[0:64, :],
                         in_=zeros_in[0:64, 0:512].bitcast(BF16)))
        zb = zeros_in[0:128, :].bitcast(BF16)
        VROW = XROWS + 8
        for ch in range(8):
            i_ap = bass.AP(zb.tensor, 0, [[2048, 128], [0, 5], [1, 1640]])
            o_ap = bass.AP(v_hbm.tensor, ch * 128 * VROW,
                           [[VROW, 128], [1640, 5], [1, 1640]])
            fin(sy.dma_start(out=o_ap, in_=i_ap))
    t_cst = step(sY, B_DONE, _consts, ndma=19, key="cst")

    def _convst(v, fin):
        for i in range(96):
            v.tensor_scalar(c4st[:, i, :], dmask_sb[:], ccols[:, i:i + 1], None, A.mult)
        for i in range(256):
            v.tensor_scalar(f31st[:, i, :], dmask_sb[:], ccols[:, 96 + i:97 + i],
                            None, A.mult)
        for i in range(32):
            v.tensor_scalar(f3st[:, i, :], dmask_sb[:], ccols[:, 352 + i:353 + i],
                            None, A.mult)
        v.tensor_copy(eye_bf[:], eye128[:])
        v.memset(zrof[:], 0.0)
        VG(v)
        for h in range(4):
            v.tensor_copy(Ssb[:, h, :], zrof[:, 0:512])
        fin(v.tensor_copy(vgscr[0:1, 7:8], vgscr[0:1, 7:8]))
    t_cvst = step(sV, [t_cst] + B_DONE, _convst)

    t_pv, t_pg = t_cvst, t_wq
    for i in range(14):
        t_rd = step(sY, [t_pv, t_pg],
                    lambda sy, fin, i=i: fin(sy.dma_start(
                        out=ofs32[:], in_=fgw1_in[i * 128:(i + 1) * 128, :])),
                    ndma=1, key="fgrd")
        t_pv = step(sV, [t_rd],
                    lambda v, fin: fin(v.tensor_copy(wstage[:, 0:992] if False
                                                     else gfin[:, 0:1024], ofs32[:])))
        t_pg = step(sGQ, [t_pv],
                    lambda g, fin, i=i: fin(g.dma_start(
                        out=fgw1_bhbm[i * 128:(i + 1) * 128, :],
                        in_=gfin[:, 0:1024])), ndma=1, key="fgst")
    t_fgw1 = t_pg

    for kc in range(8):
        t_rd = step(sY, [t_pv, t_pg],
                    lambda sy, fin, kc=kc: fin(sy.dma_start(
                        out=ofs32[:], in_=oprj_in[kc * 128:(kc + 1) * 128, :])),
                    ndma=1, key="oprd")
        t_pv = step(sV, [t_rd],
                    lambda v, fin, kc=kc: fin(v.tensor_scalar(
                        oprjT[kc][:], ofs32[:], onw_sb[:, kc:kc + 1], None, A.mult)))
    t_oprep = t_pv

    def _sp1(sc, fin):
        def CG():
            for _ in range(3):
                sc.activation(rowb[0:1, 0:1], invt[0:1, 0:1], AF.Copy)
        sc.activation(invt[0:1, 1:2], invt[0:1, 0:1], AF.Abs)
        sc.activation(invt[0:1, 4:5], invt[0:1, 0:1], AF.Relu)
        CG()
        sc.activation(invt[0:1, 2:3], invt[0:1, 1:2], AF.Exp, scale=-1.0)
        CG()
        fin(sc.activation(invt[0:1, 3:4], invt[0:1, 2:3], AF.Ln, bias=1.0))
    t_sp1 = step(sC, [t_cst], _sp1)
    def _sp2(v, fin):
        v.tensor_tensor(invt[0:1, 5:6], invt[0:1, 3:4], invt[0:1, 4:5], A.add)
        VG(v)
        v.tensor_single_scalar(invt[0:1, 6:7], invt[0:1, 5:6], 1e-4, A.add)
        VG(v)
        v.reciprocal(invt[0:1, 7:8], invt[0:1, 6:7])
        VG(v)
        fin(v.tensor_copy(vgscr[0:1, 6:7], vgscr[0:1, 6:7]))
    t_sp2 = step(sV, [t_sp1], _sp2)
    def _sp3(g, fin):
        fin(g.dma_start(out=invt_hbm[:], in_=invt[0:1, 7:8]))
    t_sp3a = step(sGQ, [t_sp2], _sp3, ndma=1, key="invt")

    def _sp4(g, fin):
        in_ap = bass.AP(invt_hbm.tensor, 0, [[0, 128], [1, 1]])
        fin(g.dma_start(out=invt_col[:], in_=in_ap))
    t_sp3 = step(sGQ, [t_sp3a], _sp4, ndma=1, key="invt")

    NOWN = NAT // 2
    t_tile_prev = t_cvst
    t_last_out = t_oprep

    for t in range(NAT):
        own = t >= NOWN
        tok0 = t * ATILE
        ot0 = tok0 - PRE

        def _xnrd(sc, fin, tok0=tok0):
            for c in range(8):
                fin(sc.dma_start_transpose(
                    xnfm[:, c, 0:XW],
                    xn_tm[tok0 + 16:tok0 + 16 + XW, c * 128:(c + 1) * 128]))
        t_xn = step(sCQ, B_DONE[0:2] + [t_tile_prev, t_cvst], _xnrd, ndma=8, key="xnr")

        chunk_list = ([("q", c) for c in range(8)] if own else []) + \
                     [("k", c) for c in range(8)] + [("v", c) for c in range(8)]
        t_sil = {}
        t_evac = {}
        t_conv = {}
        t_w = {}
        pi = 0
        for (kind, c) in chunk_list:
            mc = {"q": 0, "k": 8, "v": 16}[kind] + c
            bank = 6 + pi % 2
            ring = pi % 4
            par = pi % 2

            def _proj(te, fin, mc=mc, bank=bank):
                ins = None
                for kc in range(8):
                    ins = te.matmul(psum[bank][:, 0:XW],
                                    wqkvT[:, kc, mc * 128:(mc + 1) * 128],
                                    xnfm[:, kc, 0:XW],
                                    start=(kc == 0), stop=(kc == 7))
                fin(ins)
            deps = [t_xn, t_wq]
            if pi - 2 in t_sil:
                deps.append(t_sil[pi - 2])
            t_p = step(sT, deps, _proj)

            def _evpre(sc, fin, bank=bank, ring=ring):
                fin(sc.activation(pre_ring[:, ring, :], psum[bank][:, 0:XW], AF.Copy))
            t_e = step(sC, [t_p], _evpre)
            t_evac[pi] = t_e

            def _prew(g, fin, ring=ring):
                fin(g.dma_start(out=pre_hbm[ring * 128:(ring + 1) * 128, :],
                                in_=pre_ring[:, ring, :]))
            deps_pw = [t_e]
            if pi - 4 in t_conv:
                deps_pw.append(t_conv[pi - 4])
            t_pw = step(sY, deps_pw, _prew, ndma=1, key=f"prew{ring}")

            def _reprd(g, fin, kind=kind, ring=ring, par=par):
                off = 1 if kind == "v" else 45
                W = 300 if kind == "v" else 256
                for g4 in range(4):
                    in_ap = bass.AP(pre_hbm.tensor,
                                    (ring * 128 + 32 * g4) * 304 + off,
                                    [[1, 4], [304, 32], [1, W]])
                    fin(g.dma_start(out=rep4[:, par, g4, 0:W], in_=in_ap))
            deps_rr = [t_pw]
            if pi - 2 in t_conv:
                deps_rr.append(t_conv[pi - 2])
            t_rr = step(sY, deps_rr, _reprd, ndma=4, key=f"rep{par}")

            def _conv(te, fin, kind=kind, c=c, par=par, bank=bank):
                base = {"q": 0, "k": 32, "v": 64}[kind]
                W = 300 if kind == "v" else 256
                ins = None
                for g4 in range(4):
                    ins = te.matmul(psum[bank][32 * g4:32 * g4 + 32, 0:W],
                                    c4st[:, base + c * 4 + g4, :],
                                    rep4[:, par, g4, 0:W],
                                    start=True, stop=True,
                                    tile_position=(0, 32 * g4))
                fin(ins)
            t_c = step(sT, [t_e, t_rr], _conv)
            t_conv[pi] = t_c

            def _silu(sc, fin, kind=kind, c=c, bank=bank, par=par):
                if kind == "q":
                    fin(sc.activation(qfm[:, c, :], psum[bank][:, 0:256], AF.Silu))
                elif kind == "k":
                    fin(sc.activation(kfm[:, c, :], psum[bank][:, 0:256], AF.Silu))
                else:
                    fin(sc.activation(vst[:, par, :], psum[bank][:, 0:300], AF.Silu))
            deps_s = [t_c]
            if kind == "v" and pi - 2 in t_w:
                deps_s.append(t_w[pi - 2])
            t_s = step(sC, deps_s, _silu)
            t_sil[pi] = t_s

            def _whbm(g, fin, kind=kind, c=c, tok0=tok0, par=par, t=t):
                if kind == "v":
                    if t == 0:
                        fin(g.dma_start(out=v_hbm[c * 128:(c + 1) * 128, 0:256],
                                        in_=vst[:, par, 44:300]))
                    else:
                        fin(g.dma_start(
                            out=v_hbm[c * 128:(c + 1) * 128, tok0 - 44:tok0 + 256],
                            in_=vst[:, par, :]))
                else:
                    src = qfm if kind == "q" else kfm
                    hb = q_hbm if kind == "q" else k_hbm
                    fin(g.dma_start(out=hb[c * 128:(c + 1) * 128, tok0:tok0 + 256],
                                    in_=src[:, c, :]))
            wkey = f"vw{c % 2}" if kind == "v" else "whbm"
            t_wr = step(sY, [t_s], _whbm, ndma=1, key=wkey)
            t_w[pi] = t_wr
            pi += 1

        t_beta = {}
        for sub in range(2):
            bank = 6 + sub

            def _bproj(te, fin, sub=sub, bank=bank):
                ins = None
                for kc in range(8):
                    ins = te.matmul(psum[bank][:, 0:4],
                                    xnfm[:, kc, 48 + sub * 128:48 + sub * 128 + 128],
                                    wqkvT[:, kc, 3072:3076],
                                    start=(kc == 0), stop=(kc == 7))
                fin(ins)
            t_bp = step(sT, [t_xn, t_wq, t_sil[pi - 2 + sub]], _bproj)

            def _bev(sc, fin, sub=sub, bank=bank):
                fin(sc.activation(arsn[:, sub * 24 + 8:sub * 24 + 12],
                                  psum[bank][:, 0:4], AF.Sigmoid))
            t_beta[sub] = step(sC, [t_bp], _bev)

        t_fir = {}
        if own:
            for c in range(8):
                def _repf(g, fin, c=c, tok0=tok0):
                    VROW = XROWS + 8
                    for g4 in range(4):
                        in_ap = bass.AP(v_hbm.tensor,
                                        (c * 128 + 32 * g4) * VROW + tok0 - 30,
                                        [[1, 4], [VROW, 32], [1, 284]])
                        fin(g.dma_start(out=repf[:, g4, :], in_=in_ap))
                deps_rf = [t_w[16 + c]]
                if ("mm", c - 1) in t_fir:
                    deps_rf.append(t_fir[("mm", c - 1)])
                t_rf = step(sY, deps_rf, _repf, ndma=4, key="repf")

                def _firmm(te, fin, c=c):
                    ins = None
                    for g4 in range(4):
                        te.matmul(psum[6][32 * g4:32 * g4 + 32, 0:256],
                                  f3st[:, c * 4 + g4, :], repf[:, g4, 28:284],
                                  start=True, stop=True, tile_position=(0, 32 * g4))
                    for tb in range(8):
                        for g4 in range(4):
                            ins = te.matmul(psum[7][32 * g4:32 * g4 + 32, 0:256],
                                            f31st[:, (c * 4 + g4) * 8 + tb, :],
                                            repf[:, g4, 4 * tb:4 * tb + 256],
                                            start=(tb == 0), stop=(tb == 7),
                                            tile_position=(0, 32 * g4))
                    fin(ins)
                deps_fm = [t_rf, t_beta[1]]
                if ("ev", c - 1) in t_fir:
                    deps_fm.append(t_fir[("ev", c - 1)])
                t_fm = step(sT, deps_fm, _firmm)
                t_fir[("mm", c)] = t_fm

                def _firev(sc, fin, c=c):
                    sc.activation(slevv[:, (2 * c) % 4, :], psum[6][:, 0:256], AF.Copy)
                    fin(sc.activation(slevv[:, (2 * c + 1) % 4, :], psum[7][:, 0:256],
                                      AF.Copy))
                deps_fe = [t_fm]
                if ("w", c - 2) in t_fir:
                    deps_fe.append(t_fir[("w", c - 2)])
                t_fe = step(sC, deps_fe, _firev)
                t_fir[("ev", c)] = t_fe

                def _firw(g, fin, c=c, ot0=ot0):
                    fin(g.dma_start(out=sh_hbm[c * 128:(c + 1) * 128, ot0:ot0 + 256],
                                    in_=slevv[:, (2 * c) % 4, :]))
                    fin(g.dma_start(out=lo_hbm[c * 128:(c + 1) * 128, ot0:ot0 + 256],
                                    in_=slevv[:, (2 * c + 1) % 4, :]))
                t_fw = step(sY, [t_fe], _firw, ndma=2, key="firw")
                t_fir[("w", c)] = t_fw

        t_sub = {}
        for sub in range(2):
            def _tmrd(sc, fin, sub=sub, tok0=tok0, own=own):
                t0 = tok0 + sub * 128
                fin(sc.dma_start_transpose(ktm[:, sub, :], k_hbm[:, t0:t0 + 128]))
                fin(sc.dma_start_transpose(vtm[:, sub, :], v_hbm[:, t0:t0 + 128]))
                if own:
                    fin(sc.dma_start_transpose(qtm[:, sub, :], q_hbm[:, t0:t0 + 128]))
            t_rd = step(sCQ, [t_w[pi - 9], t_w[pi - 2], t_w[pi - 1]], _tmrd,
                        ndma=3 if own else 2, key="tmrd")

            def _l2a(v, fin, sub=sub, own=own):
                base = sub * 24
                v.tensor_tensor(o2buf[:].bitcast(BF16)[:, 0:1024], ktm[:, sub, :],
                                ktm[:, sub, :], A.mult)
                VG(v)
                v.tensor_reduce(arsn[:, base + 4:base + 8].unsqueeze(2),
                                o2buf[:].bitcast(BF16)[:, 0:1024]
                                .rearrange("p (h n) -> p h n", h=4, n=256),
                                AX.X, A.add)
                VG(v)
                v.tensor_scalar(arsn[:, base + 4:base + 8],
                                arsn[:, base + 4:base + 8], 1e-12, None, A.add)
                if own:
                    v.tensor_tensor(o2buf[:].bitcast(BF16)[:, 0:1024], qtm[:, sub, :],
                                    qtm[:, sub, :], A.mult)
                    VG(v)
                    v.tensor_reduce(arsn[:, base + 0:base + 4].unsqueeze(2),
                                    o2buf[:].bitcast(BF16)[:, 0:1024]
                                    .rearrange("p (h n) -> p h n", h=4, n=256),
                                    AX.X, A.add)
                    VG(v)
                    v.tensor_scalar(arsn[:, base + 0:base + 4],
                                    arsn[:, base + 0:base + 4], 1e-12, None, A.add)
                fin(v.tensor_copy(vgscr[0:1, 4:5], vgscr[0:1, 4:5]))
            t_l1 = step(sV, [t_rd, t_beta[sub]], _l2a)

            def _sqr(sc, fin, sub=sub, own=own):
                base = sub * 24
                ins = sc.activation(arsn[:, base + 4:base + 8],
                                    arsn[:, base + 4:base + 8], AF.Sqrt)
                if own:
                    ins = sc.activation(arsn[:, base + 0:base + 4],
                                        arsn[:, base + 0:base + 4], AF.Sqrt)
                fin(ins)
            t_l2 = step(sC, [t_l1], _sqr)

            def _l3(v, fin, sub=sub, own=own):
                base = sub * 24
                v.reciprocal(arsn[:, base + 4:base + 8], arsn[:, base + 4:base + 8])
                if own:
                    v.reciprocal(arsn[:, base + 0:base + 4], arsn[:, base + 0:base + 4])
                VG(v)
                v.tensor_tensor(arsn[:, base + 12:base + 16],
                                arsn[:, base + 8:base + 12],
                                arsn[:, base + 4:base + 8], A.mult)
                VG(v)
                v.tensor_scalar(arsn[:, base + 16:base + 20],
                                arsn[:, base + 12:base + 16], -1.0, None, A.mult)
                VG(v)
                for h in range(4):
                    v.tensor_scalar(ktm[:, sub, h * 256:(h + 1) * 256],
                                    ktm[:, sub, h * 256:(h + 1) * 256],
                                    arsn[:, base + 4 + h:base + 5 + h], None, A.mult)
                for h in range(4):
                    v.tensor_scalar(vbt[:, sub, h * 256:(h + 1) * 256],
                                    vtm[:, sub, h * 256:(h + 1) * 256],
                                    arsn[:, base + 8 + h:base + 9 + h], None, A.mult)
                fin(v.tensor_copy(vgscr[0:1, 3:4], vgscr[0:1, 3:4]))
            t_sub[sub] = step(sV, [t_l2], _l3)

        t_sev = t_tile_prev
        t_de = {}
        t_pre_done = t_tile_prev
        for sub in range(2):
            c0 = sub * 128
            base = sub * 24
            par = sub
            first = (t == 0 and sub == 0)

            def _trp(te, fin, base=base):
                ins = None
                for h in range(4):
                    te.transpose(psum[4][0:1, h * 128:(h + 1) * 128],
                                 arsn[:, base + 12 + h:base + 13 + h], eye128[:])
                    ins = te.transpose(psum[5][0:1, h * 128:(h + 1) * 128],
                                       arsn[:, base + 4 + h:base + 5 + h], eye128[:])
                fin(ins)
            t_tr = step(sT, [t_sub[sub], t_pre_done], _trp)

            def _re(sc, fin):
                sc.activation(rowb[0:1, 0:512], psum[4][0:1, 0:512], AF.Copy)
                fin(sc.activation(rowb[0:1, 512:1024], psum[5][0:1, 0:512], AF.Copy))
            t_rw = step(sC, [t_tr], _re)

            def _bc(te, fin):
                ins = None
                for h in range(4):
                    te.matmul(psum[4][:, h * 128:(h + 1) * 128], ones[0:1, 0:128],
                              rowb[0:1, h * 128:(h + 1) * 128], start=True, stop=True)
                    ins = te.matmul(psum[5][:, h * 128:(h + 1) * 128],
                                    ones[0:1, 0:128],
                                    rowb[0:1, 512 + h * 128:512 + (h + 1) * 128],
                                    start=True, stop=True)
                fin(ins)
            t_b2 = step(sT, [t_rw], _bc)

            def _cmk(v, fin):
                for h in range(4):
                    v.tensor_tensor(cmt[:, h, :], cst128[:, 0:128],
                                    psum[4][:, h * 128:(h + 1) * 128], A.mult)
                    v.tensor_tensor(cmt[:, 4 + h, :], cst128[:, 128:256],
                                    psum[5][:, h * 128:(h + 1) * 128], A.mult)
                fin(v.tensor_copy(vgscr[0:1, 5:6], vgscr[0:1, 5:6]))
            t_cm = step(sV, [t_b2], _cmk)

            def _a0(te, fin, c0=c0, own=own):
                ins = None
                for h in range(4):
                    for kk in range(2):
                        ins = te.matmul(psum[4][:, h * 128:(h + 1) * 128],
                                        kfm[:, 2 * h + kk, c0:c0 + 128],
                                        kfm[:, 2 * h + kk, c0:c0 + 128],
                                        start=(kk == 0), stop=(kk == 1))
                if own:
                    for h in range(4):
                        for kk in range(2):
                            ins = te.matmul(psum[5][:, h * 128:(h + 1) * 128],
                                            kfm[:, 2 * h + kk, c0:c0 + 128],
                                            qfm[:, 2 * h + kk, c0:c0 + 128],
                                            start=(kk == 0), stop=(kk == 1))
                fin(ins)
            t_a0 = step(sT, [t_cm], _a0)

            def _am(v, fin, base=base, par=par, own=own):
                for h in range(4):
                    v.scalar_tensor_tensor(Qch[0][:, h, :],
                                           psum[4][:, h * 128:(h + 1) * 128],
                                           arsn[:, base + 4 + h:base + 5 + h],
                                           cmt[:, h, :], A.mult, A.mult)
                for h in range(4):
                    v.scalar_tensor_tensor(Qch[1][:, h, :],
                                           psum[4][:, h * 128:(h + 1) * 128],
                                           arsn[:, base + 12 + h:base + 13 + h],
                                           cmt[:, 4 + h, :], A.mult, A.mult)
                for h in range(4):
                    v.tensor_tensor(Gsb[par][:, h, :], eye_bf[:], Qch[0][:, h, :],
                                    A.subtract)
                if own:
                    for h in range(4):
                        v.scalar_tensor_tensor(attnT[par][:, h, :],
                                               psum[5][:, h * 128:(h + 1) * 128],
                                               arsn[:, base + 4 + h:base + 5 + h],
                                               cst128[:, 256:384], A.mult, A.mult)
                fin(v.tensor_copy(vgscr[0:1, 5:6], vgscr[0:1, 5:6]))
            t_pw = step(sV, [t_a0], _am)

            for lvl in range(6):
                qs = lvl % 2
                def _la(te, fin, qs=qs):
                    ins = None
                    for h in range(4):
                        te.matmul(psum[4][:, h * 128:(h + 1) * 128],
                                  Qch[2 * qs + 1][:, h, :], Qch[2 * qs][:, h, :],
                                  start=True, stop=True)
                        ins = te.matmul(psum[5][:, h * 128:(h + 1) * 128],
                                        Qch[2 * qs][:, h, :], Qch[2 * qs + 1][:, h, :],
                                        start=True, stop=True)
                    fin(ins)
                t_la = step(sT, [t_pw], _la)

                def _lb(sc, fin, qs=qs):
                    sc.activation(QchF[2 - 2 * qs][:], psum[4][:, 0:512], AF.Copy)
                    fin(sc.activation(QchF[3 - 2 * qs][:], psum[5][:, 0:512],
                                      AF.Copy))
                t_lb = step(sC, [t_la], _lb)

                def _lc(te, fin, qs=qs, par=par):
                    ins = None
                    for h in range(4):
                        ins = te.matmul(psum[4][:, h * 128:(h + 1) * 128],
                                        Qch[3 - 2 * qs][:, h, :], Gsb[par][:, h, :],
                                        start=True, stop=True)
                    fin(ins)
                t_lc = step(sT, [t_lb], _lc)

                def _ld(v, fin, par=par):
                    fin(v.tensor_tensor(GsbF[par][:], GsbF[par][:],
                                        psum[4][:, 0:512], A.add))
                t_pw = step(sV, [t_lc], _ld)
            t_pre_done = t_pw

            bdeps = [t_pw, t_sev]
            if own:
                bdeps.append(t_fir[("ev", 7)])

            def _zm(te, fin, c0=c0):
                ins = None
                for h in range(4):
                    for kk in range(2):
                        ins = te.matmul(psum[6 + h // 2][:, (h % 2) * 256:
                                                         (h % 2) * 256 + 256],
                                        kfm[:, 2 * h + kk, c0:c0 + 128],
                                        Ssb[:, h, kk * 256:(kk + 1) * 256],
                                        start=(kk == 0), stop=(kk == 1))
                fin(ins)
            t_zm = step(sT, bdeps, _zm)

            def _ze(v, fin, base=base, sub=sub):
                for h in range(4):
                    v.scalar_tensor_tensor(zbuf[:, h, :],
                                           psum[6 + h // 2][:, (h % 2) * 256:
                                                            (h % 2) * 256 + 256],
                                           arsn[:, base + 16 + h:base + 17 + h],
                                           vbt[:, sub, h * 256:(h + 1) * 256],
                                           A.mult, A.add)
                fin(v.tensor_copy(vgscr[0:1, 5:6], vgscr[0:1, 5:6]))
            t_ze = step(sV, [t_zm], _ze)

            def _umm(te, fin, par=par):
                ins = None
                for h in range(4):
                    ins = te.matmul(psum[6 + h // 2][:, (h % 2) * 256:
                                                     (h % 2) * 256 + 256],
                                    Gsb[par][:, h, :], zbuf[:, h, :],
                                    start=True, stop=True)
                fin(ins)
            t_um = step(sT, [t_ze], _umm)

            def _ue(sc, fin):
                sc.activation(ub_sb[:, 0:512], psum[6][:, 0:512], AF.Copy)
                fin(sc.activation(ub_sb[:, 512:1024], psum[7][:, 0:512],
                                  AF.Copy))
            t_ue = step(sC, [t_um], _ue)

            def _sm(te, fin, sub=sub, c0=c0, par=par, own=own, first=first):
                ins = None
                for h in range(4):
                    for kk in range(2):
                        ins = te.matmul(psum[h][:, kk * 256:(kk + 1) * 256],
                                        ktm[:, sub, h * 256 + kk * 128:
                                            h * 256 + kk * 128 + 128],
                                        ubuf[:, h, :], start=first, stop=True)
                if own:
                    for h in range(4):
                        for kk in range(2):
                            te.matmul(psum[6 + h // 2][:, (h % 2) * 256:
                                                       (h % 2) * 256 + 256],
                                      qfm[:, 2 * h + kk, c0:c0 + 128],
                                      Ssb[:, h, kk * 256:(kk + 1) * 256],
                                      start=(kk == 0), stop=False)
                        ins = te.matmul(psum[6 + h // 2][:, (h % 2) * 256:
                                                         (h % 2) * 256 + 256],
                                        attnT[par][:, h, :], ubuf[:, h, :],
                                        start=False, stop=True)
                fin(ins)
            t_su = step(sT, [t_ue], _sm)

            def _se(sc, fin):
                ins = None
                for h in range(4):
                    ins = sc.activation(Ssb[:, h, :], psum[h][:, 0:512], AF.Copy)
                fin(ins)
            t_sev = step(sC, [t_su], _se)

            if own:
                def _de(v, fin, base=base, sub=sub):
                    for h in range(4):
                        v.tensor_scalar(dtm[:, sub, h * 256:(h + 1) * 256],
                                        psum[6 + h // 2][:, (h % 2) * 256:
                                                         (h % 2) * 256 + 256],
                                        arsn[:, base + h:base + h + 1],
                                        None, A.mult)
                    fin(v.tensor_copy(vgscr[0:1, 5:6], vgscr[0:1, 5:6]))
                t_de[sub] = step(sV, [t_su], _de)

        if own:
            t_m = {}
            for sub in range(2):
                def _slrd(sc, fin, sub=sub, ot0=ot0):
                    t0 = ot0 + sub * 128
                    fin(sc.dma_start_transpose(sltm[:, 0, :], sh_hbm[:, t0:t0 + 128]))
                    fin(sc.dma_start_transpose(sltm[:, 1, :], lo_hbm[:, t0:t0 + 128]))
                deps_sr = [t_fir[("w", 7)], t_sev]
                if sub == 1:
                    deps_sr.append(t_m[0])
                t_sr = step(sCQ, deps_sr, _slrd, ndma=2, key="slrd")

                def _mn(v, fin, sub=sub):
                    st = mstg[sub]
                    for (src, col) in ((sltm[:, 0, :], 0), (sltm[:, 1, :], 256),
                                       (dtm[:, sub, :], 512)):
                        v.tensor_tensor(st[:, 2 * col:2 * col + 512]
                                        if False else st[:, col * 2:col * 2 + 2][:, 0:0]
                                        if False else st[:, col:col + 256],
                                        src[:, 0:256], src[:, 256:512], A.add)
                        v.tensor_tensor(o2buf[:].bitcast(BF16)[:, 0:256],
                                        src[:, 512:768], src[:, 768:1024], A.add)
                        VG(v)
                        v.tensor_tensor(st[:, col:col + 256], st[:, col:col + 256],
                                        o2buf[:].bitcast(BF16)[:, 0:256], A.add)
                        VG(v)
                        v.tensor_scalar(st[:, col:col + 256], st[:, col:col + 256],
                                        0.25, None, A.mult)
                    fin(v.tensor_copy(vgscr[0:1, 1:2], vgscr[0:1, 1:2]))
                t_mv = step(sV, [t_sr, t_de[sub]], _mn)

                def _mw(g, fin, sub=sub, ot0=ot0):
                    fin(g.dma_start(
                        out=mn_hbm[ot0 + sub * 128:ot0 + sub * 128 + 128, :],
                        in_=mstg[sub][:, 0:768]))
                t_m[sub] = step(sY, [t_mv], _mw, ndma=1, key="mnw")

            t_gmm = {}
            t_gr = {}
            for kc in range(14):
                def _fgr(g, fin, kc=kc):
                    fin(g.dma_start(out=fgw1r[2 * (kc % 4)][:],
                                    in_=fgw1_bhbm[kc * 128:(kc + 1) * 128, 0:512]))
                    fin(g.dma_start(out=fgw1r[2 * (kc % 4) + 1][:],
                                    in_=fgw1_bhbm[kc * 128:(kc + 1) * 128, 512:1024]))
                depf = [t_fgw1]
                if kc - 4 in t_gmm:
                    depf.append(t_gmm[kc - 4])
                t_fg = step(sY, depf, _fgr, ndma=2, key=f"fgw1{kc % 4}")

                if kc >= 8:
                    def _mr(sc, fin, kc=kc, ot0=ot0):
                        for sub in range(2):
                            fin(sc.dma_start_transpose(
                                grd[:, 2 * (kc % 2) + sub, :],
                                mn_hbm[ot0 + sub * 128:ot0 + sub * 128 + 128,
                                       (kc - 8) * 128:(kc - 7) * 128]))
                    depm = [t_m[0], t_m[1]]
                    if kc - 2 in t_gmm:
                        depm.append(t_gmm[kc - 2])
                    t_gr[kc] = step(sCQ, depm, _mr, ndma=2, key=f"mnr{kc % 2}")

                def _gm(te, fin, kc=kc):
                    ins = None
                    for sub in range(2):
                        if kc < 8:
                            lhsT = xnfm[:, kc, 48 + sub * 128:48 + sub * 128 + 128]
                        else:
                            lhsT = grd[:, 2 * (kc % 2) + sub, :]
                        for nb in range(2):
                            bank = 4 + sub * 2 + nb
                            if kc == 0:
                                te.matmul(psum[bank][:, 0:512], ones[0:1, 0:128],
                                          fgb1_sb[0:1, nb * 512:(nb + 1) * 512],
                                          start=True, stop=False)
                            ins = te.matmul(psum[bank][:, 0:512], lhsT,
                                            fgw1r[2 * (kc % 4) + nb][:],
                                            start=False, stop=(kc == 13))
                    fin(ins)
                depsg = [t_fg, t_sev]
                if kc >= 8:
                    depsg.append(t_gr[kc])
                t_gmm[kc] = step(sT, depsg, _gm)

            t_gh = {}
            for sub in range(2):
                def _ge(sc, fin, sub=sub):
                    ins = None
                    for nb in range(2):
                        bank = 4 + sub * 2 + nb
                        ins = sc.activation(ghst[:, sub, nb * 512:(nb + 1) * 512],
                                            psum[bank][:, 0:512], AF.Gelu)
                    fin(ins)
                t_ge = step(sC, [t_gmm[13], sV.tok()], _ge)

                def _gw(g, fin, sub=sub, ot0=ot0):
                    fin(g.dma_start(
                        out=gh_hbm[ot0 + sub * 128:ot0 + sub * 128 + 128, :],
                        in_=ghst[:, sub, :]))
                t_gh[sub] = step(sY, [t_ge], _gw, ndma=1, key="ghw")

            t_ow = {}
            for sub in range(2):
                def _gr2(sc, fin, sub=sub, ot0=ot0):
                    for kc in range(8):
                        fin(sc.dma_start_transpose(
                            grd[:, kc, :],
                            gh_hbm[ot0 + sub * 128:ot0 + sub * 128 + 128,
                                   kc * 128:(kc + 1) * 128]))
                t_g2 = step(sCQ, [t_gh[sub], sT.tok()], _gr2, ndma=8, key="ghr")

                def _lm(te, fin, sub=sub):
                    te.matmul(psum[4][:, 0:4], ones[0:1, 0:128], fgb2_sb[0:1, 0:4],
                              start=True, stop=False)
                    ins = None
                    for kc in range(8):
                        ins = te.matmul(psum[4][:, 0:4], grd[:, kc, :],
                                        fgw2_sb[:, kc * 4:kc * 4 + 4],
                                        start=False, stop=(kc == 7))
                    fin(ins)
                t_lg = step(sT, [t_g2, t_sp3], _lm)

                def _s1(v, fin):
                    v.tensor_copy(wsm[:, 0:4], psum[4][:, 0:4])
                    VG(v)
                    v.tensor_reduce(wsm[:, 4:5].unsqueeze(2),
                                    wsm[:, 0:4].unsqueeze(1), AX.X, A.max)
                    VG(v)
                    fin(v.tensor_scalar(wsm[:, 5:9], wsm[:, 0:4], wsm[:, 4:5], None,
                                        A.subtract))
                t_s1 = step(sV, [t_lg], _s1)

                def _s2(sc, fin):
                    fin(sc.activation(wsm[:, 9:13], wsm[:, 5:9], AF.Exp,
                                      scale=invt_col[:, 0:1]))
                t_s2 = step(sC, [t_s1], _s2)

                def _s3(v, fin):
                    v.tensor_reduce(wsm[:, 13:14].unsqueeze(2),
                                    wsm[:, 9:13].unsqueeze(1), AX.X, A.add)
                    VG(v)
                    v.reciprocal(wsm[:, 14:15], wsm[:, 13:14])
                    VG(v)
                    v.tensor_scalar(wsm[:, 15:19], wsm[:, 9:13], wsm[:, 14:15],
                                    None, A.mult)
                    VG(v)
                    v.tensor_scalar(wsm[:, 19:20], wsm[:, 18:19], 0.05, None, A.mult)
                    VG(v)
                    v.tensor_scalar(wsm[:, 15:17], wsm[:, 15:17], wsm[:, 19:20],
                                    None, A.max)
                    VG(v)
                    v.tensor_reduce(wsm[:, 20:21].unsqueeze(2),
                                    wsm[:, 15:19].unsqueeze(1), AX.X, A.add)
                    VG(v)
                    v.reciprocal(wsm[:, 21:22], wsm[:, 20:21])
                    VG(v)
                    fin(v.tensor_scalar(wsm[:, 15:19], wsm[:, 15:19], wsm[:, 21:22],
                                        None, A.mult))
                t_s3 = step(sV, [t_s2], _s3)

                def _sl2(sc, fin, sub=sub, ot0=ot0):
                    t0 = ot0 + sub * 128
                    fin(sc.dma_start_transpose(sltm[:, 0, :], sh_hbm[:, t0:t0 + 128]))
                    fin(sc.dma_start_transpose(sltm[:, 1, :], lo_hbm[:, t0:t0 + 128]))
                t_sl2 = step(sCQ, [sV.tok(), t_lg], _sl2, ndma=2, key="slrd")

                def _cb(v, fin, sub=sub):
                    VG(v)
                    v.tensor_scalar(comb[0][:], sltm[:, 1, :], wsm[:, 16:17], None,
                                    A.mult)
                    VG(v)
                    v.scalar_tensor_tensor(comb[1][:], sltm[:, 0, :], wsm[:, 15:16],
                                           comb[0][:], A.mult, A.add)
                    VG(v)
                    v.scalar_tensor_tensor(comb[0][:], dtm[:, sub, :], wsm[:, 17:18],
                                           comb[1][:], A.mult, A.add)
                    VG(v)
                    v.scalar_tensor_tensor(comb[1][:], vtm[:, sub, :], wsm[:, 18:19],
                                           comb[0][:], A.mult, A.add)
                    v.tensor_tensor(comb[0][:], comb[1][:], comb[1][:], A.mult)
                    VG(v)
                    v.tensor_reduce(wsm[:, 4:8].unsqueeze(2),
                                    comb[0][:].rearrange("p (h n) -> p h n",
                                                         h=4, n=256),
                                    AX.X, A.add)
                    VG(v)
                    fin(v.tensor_scalar(wsm[:, 4:8], wsm[:, 4:8], 1.0 / 256.0, 1e-5,
                                        A.mult, A.add))
                t_c1 = step(sV, [t_s3, t_sl2, t_de[sub]], _cb)

                def _rq(sc, fin):
                    fin(sc.activation(wsm[:, 8:12], wsm[:, 4:8], AF.Sqrt))
                t_c2 = step(sC, [t_c1], _rq)

                def _c3(v, fin, sub=sub):
                    v.reciprocal(wsm[:, 8:12], wsm[:, 8:12])
                    VG(v)
                    for h in range(4):
                        v.tensor_scalar(o_tmb[:, sub, h * 256:(h + 1) * 256],
                                        comb[1][:, h * 256:(h + 1) * 256],
                                        wsm[:, 8 + h:9 + h], None, A.mult)
                    fin(v.tensor_copy(vgscr[0:1, 0:1], vgscr[0:1, 0:1]))
                t_c3 = step(sV, [t_c2], _c3)

                def _ow2(g, fin, sub=sub, ot0=ot0):
                    fin(g.dma_start(
                        out=o_hbm[ot0 + sub * 128:ot0 + sub * 128 + 128, :],
                        in_=o_tmb[:, sub, :]))
                t_ow[sub] = step(sY, [t_c3], _ow2, ndma=1, key="ow")

            for sub in range(2):
                def _or(sc, fin, sub=sub, ot0=ot0):
                    for kc in range(8):
                        fin(sc.dma_start_transpose(
                            grd[:, kc, :],
                            o_hbm[ot0 + sub * 128:ot0 + sub * 128 + 128,
                                  kc * 128:(kc + 1) * 128]))
                t_or = step(sCQ, [t_ow[sub], sT.tok()], _or, ndma=8, key="ghr")

                def _op(te, fin, sub=sub):
                    ins = None
                    for kc in range(8):
                        for nb in range(2):
                            ins = te.matmul(psum[4 + nb][:, 0:512], grd[:, kc, :],
                                            oprjT[kc][:, nb * 512:(nb + 1) * 512],
                                            start=(kc == 0), stop=(kc == 7))
                    fin(ins)
                t_op = step(sT, [t_or, t_oprep, t_last_out], _op)

                def _sv(sc, fin):
                    sc.activation(s_fin[:, 0:512], psum[4][:, 0:512], AF.Copy)
                    fin(sc.activation(s_fin[:, 512:1024], psum[5][:, 0:512], AF.Copy))
                t_se2 = step(sC, [t_op], _sv)

                def _xr(sy, fin, sub=sub, tok0=tok0):
                    fin(sy.dma_start(
                        out=x_fin[:],
                        in_=x_in[tok0 + sub * 128:tok0 + sub * 128 + 128, :]))
                t_xf = step(sY, [t_se2], _xr, ndma=1, key="xr")

                def _gr3(g, fin, sub=sub, ot0=ot0):
                    fin(g.dma_start(out=gfin[:, 0:1024],
                                    in_=g_tm[ot0 + sub * 128:ot0 + sub * 128 + 128, :]))
                t_gf = step(sY, [t_se2], _gr3, ndma=1, key="gr3")

                def _fo(v, fin):
                    v.tensor_tensor(outst[:], s_fin[:], gfin[:, 0:1024], A.mult)
                    VG(v)
                    fin(v.tensor_tensor(outst[:], outst[:], x_fin[:], A.add))
                t_fo = step(sV, [t_se2, t_xf, t_gf], _fo)

                def _out(g, fin, sub=sub, ot0=ot0):
                    fin(g.dma_start(
                        out=out[ot0 + sub * 128:ot0 + sub * 128 + 128, :],
                        in_=outst[:]))
                t_last_out = step(sY, [t_fo], _out, ndma=1, key="outw")

            t_tile_prev = t_last_out
        else:
            t_tile_prev = t_sev

    def _fbar(g, fin):
        fin(g.nop())
    _all_d = [(ps.h, ps.n) for ps in _psem.values()]
    step(sG, [t_tile_prev] + B_DONE + _all_d, _fbar)

    pr.emit()
    return nc


_PROGRAM_CACHE = {}


def _get_program():
    if "nc" not in _PROGRAM_CACHE:
        _PROGRAM_CACHE["nc"] = build_program()
    return _PROGRAM_CACHE["nc"]


def make_in_maps(inputs):
    x = np.asarray(inputs["L0_attn_hidden_states"], dtype=np.float32)
    w_fc = np.ascontiguousarray(np.asarray(inputs["L0_ffn_w_fc"], dtype=np.float32))
    b_fc = np.ascontiguousarray(
        np.asarray(inputs["L0_ffn_b_fc"], dtype=np.float32).reshape(1, HID))
    w_pj = np.ascontiguousarray(np.asarray(inputs["L0_ffn_w_proj"], dtype=np.float32))
    b_pj = np.ascontiguousarray(
        np.asarray(inputs["L0_ffn_b_proj"], dtype=np.float32).reshape(1, D))
    ones = np.ones((1, 128), np.float32)
    qw = np.asarray(inputs["L0_attn_q_proj_w"], np.float32)
    kw = np.asarray(inputs["L0_attn_k_proj_w"], np.float32)
    vw = np.asarray(inputs["L0_attn_v_proj_w"], np.float32)
    bw = np.asarray(inputs["L0_attn_b_proj_w"], np.float32)
    wqkvT = np.ascontiguousarray(
        np.concatenate([qw.T, kw.T, vw.T, bw.T], axis=1))
    qc = np.asarray(inputs["L0_attn_q_conv_w"], np.float32)
    kc_ = np.asarray(inputs["L0_attn_k_conv_w"], np.float32)
    vc = np.asarray(inputs["L0_attn_v_conv_w"], np.float32)
    firs = np.asarray(inputs["L0_attn_fir_short_w"], np.float32).reshape(1024, 3)
    firl = np.asarray(inputs["L0_attn_fir_long_w"], np.float32).reshape(1024, 31)

    def convcols(w, ntap):
        outc = np.zeros((128, 32), np.float32)
        for i in range(32):
            for a in range(min(4, ntap)):
                outc[32 * a:32 * a + 32, i] = w[i * 32:i * 32 + 32, a]
        return outc
    c4cols = np.concatenate([convcols(qc, 4), convcols(kc_, 4), convcols(vc, 4)],
                            axis=1)
    f31cols = np.zeros((128, 256), np.float32)
    for i in range(32):
        for tb in range(8):
            for a in range(4):
                j = 4 * tb + a
                if j <= 30:
                    f31cols[32 * a:32 * a + 32, i * 8 + tb] =                         firl[i * 32:i * 32 + 32, j]
    f3cols = convcols(firs, 3)
    dmask = np.zeros((128, 32), np.float32)
    for a in range(4):
        dmask[32 * a:32 * a + 32, :] = np.eye(32, dtype=np.float32)
    idx = np.arange(128)
    umask = (idx[:, None] < idx[None, :]).astype(np.float32)
    lmask = (idx[:, None] > idx[None, :]).astype(np.float32)
    amask = (idx[:, None] <= idx[None, :]).astype(np.float32)
    eye128 = np.eye(128, dtype=np.float32)
    ones2 = np.ones((128, 64), np.float32)
    fgw1T = np.ascontiguousarray(np.asarray(inputs["L0_attn_fg_w1"], np.float32).T)
    fgb1r = np.ascontiguousarray(
        np.asarray(inputs["L0_attn_fg_b1"], np.float32).reshape(1, 1024))
    fgw2T = np.ascontiguousarray(np.asarray(inputs["L0_attn_fg_w2"], np.float32).T)
    fgb2r = np.ascontiguousarray(
        np.asarray(inputs["L0_attn_fg_b2"], np.float32).reshape(1, 4))
    ltemp = np.asarray(inputs["L0_attn_logit_temp"], np.float32).reshape(1, 1)
    onw = np.asarray(inputs["L0_attn_o_norm_w"], np.float32)
    onwcols = np.zeros((128, 8), np.float32)
    for kc2 in range(8):
        onwcols[:, kc2] = onw[(kc2 * 128 + np.arange(128)) % 256]
    oprojT = np.ascontiguousarray(np.asarray(inputs["L0_attn_o_proj_w"], np.float32).T)
    zeros128 = np.zeros((128, 1024), np.float32)
    attn_ins = {"wqkvT": wqkvT, "c4cols": c4cols, "f31cols": f31cols,
                "f3cols": f3cols, "dmask": dmask, "umask": umask, "lmask": lmask,
                "amask": amask, "eye128": eye128, "ones2": ones2, "fgw1T": fgw1T,
                "fgb1r": fgb1r, "fgw2T": fgw2T, "fgb2r": fgb2r, "ltemp": ltemp,
                "onwcols": onwcols, "oprojT": oprojT, "zeros128": zeros128}
    in_maps = []
    for c in range(NCORES):
        b, h = c // 2, c % 2
        if h == 0:
            xs = np.concatenate([np.zeros((PRE, D), np.float32), x[b, :TPC]], axis=0)
        else:
            xs = np.ascontiguousarray(x[b])
        m = {"x": xs, "w_fc": w_fc, "b_fc": b_fc,
             "w_proj": w_pj, "b_proj": b_pj, "ones_row": ones}
        m.update(attn_ins)
        in_maps.append(m)
    return in_maps


def kernel(**inputs):
    nc = _get_program()
    in_maps = make_in_maps(inputs)
    res = run_bass_kernel_spmd(nc, in_maps, core_ids=list(range(NCORES)))
    outp = np.empty((B, L, D), np.float32)
    for c in range(NCORES):
        b, h = c // 2, c % 2
        outp[b, h * TPC:(h + 1) * TPC] = res.results[c]["out"]
    return outp


# version 12
# speedup vs baseline: 1.8682x; 1.3795x over previous
import numpy as np
import concourse.bass as bass
import concourse.mybir as mybir
from concourse.bass_utils import run_bass_kernel_spmd

F32 = mybir.dt.float32
BF16 = mybir.dt.bfloat16
A = mybir.AluOpType
AF = mybir.ActivationFunctionType
AX = mybir.AxisListType

B, L, D = 4, 8192, 1024
NCORES = 8
TPC = L // 2
PRE = TPC
XROWS = PRE + TPC
NLN = XROWS // 128
HID = 4096
NTILE = TPC // 128
GTILES = 8
NGRP = NTILE // GTILES
SUBT = 4
ATILE = 256
NAT = XROWS // ATILE
CS = 64
NCH = ATILE // CS
BK = 32
LN_EPS = 1e-5
FP64_GADGET = True

MAGIC23 = float(1.5 * 2 ** 23)
MAGIC33 = float(1.5 * 2 ** 33)
MAGIC52 = float(1.5 * 2 ** 52)


class Sem:
    def __init__(self, nc, name):
        self.h = nc.alloc_semaphore(name)
        self.n = 0


class Prog:
    ENGINES = ("sync", "scalar", "vector", "tensor", "gpsimd")

    def __init__(self, nc):
        self.nc = nc
        self.streams = {e: [] for e in self.ENGINES}

    def on(self, eng):
        def deco(fn):
            self.streams[eng].append(fn)
            return None
        return deco

    def emit(self):
        with self.nc.Block() as blk:
            for eng in self.ENGINES:
                thunks = self.streams[eng]
                getattr(blk, eng)(lambda e, th=thunks: [f(e) for f in th])


def emit_ladder_m(v, am, p, scr, m23, m33, m52, nblk, lanes=4):
    nb4 = nblk // lanes
    sl = [slice(l * nb4, (l + 1) * nb4) for l in range(lanes)]
    for l in range(lanes):
        v.memset(p[:, sl[l]], float(2.0 ** -34))
    for k in (32, 16, 8, 4, 2, 1):
        for l in range(lanes):
            v.tensor_single_scalar(scr[:, sl[l]], p[:, sl[l]], float(2.0 ** k), A.mult)
        for l in range(lanes):
            v.tensor_tensor(scr[:, sl[l]], am[:, sl[l]], scr[:, sl[l]], A.is_ge)
        for l in range(lanes):
            v.tensor_scalar(scr[:, sl[l]], scr[:, sl[l]], float(2.0 ** k - 1.0), 1.0,
                            A.mult, A.add)
        for l in range(lanes):
            v.tensor_tensor(p[:, sl[l]], p[:, sl[l]], scr[:, sl[l]], A.mult)
    for l in range(lanes):
        v.tensor_single_scalar(m23[:, sl[l]], p[:, sl[l]], MAGIC23 / 8.0, A.mult)
    for l in range(lanes):
        v.tensor_single_scalar(m33[:, sl[l]], p[:, sl[l]], MAGIC33 / 8.0, A.mult)
    for l in range(lanes):
        v.tensor_single_scalar(m52[:, sl[l]], p[:, sl[l]], MAGIC52 / 8.0, A.mult)


def emit_quant(v, src_ap, dst_ap, m23, m33, m52, tmp, tmp2, nblk, P=128):
    s3 = src_ap.rearrange("p (b c) -> p b c", b=nblk, c=BK)
    d3 = dst_ap.rearrange("p (b c) -> p b c", b=nblk, c=BK)
    t3 = tmp.rearrange("p (b c) -> p b c", b=nblk, c=BK)
    u3 = tmp2.rearrange("p (b c) -> p b c", b=nblk, c=BK)
    mb23 = m23[0:P, 0:nblk].unsqueeze(2).broadcast_to([P, nblk, BK])
    v.tensor_tensor(t3, s3, mb23, A.add)
    if not FP64_GADGET:
        v.tensor_tensor(d3, t3, mb23, A.subtract)
        return
    v.tensor_tensor(t3, t3, mb23, A.subtract)
    mb52 = m52[0:P, 0:nblk].unsqueeze(2).broadcast_to([P, nblk, BK])
    v.tensor_tensor(u3, s3, mb52, A.add)
    v.tensor_tensor(u3, u3, mb52, A.subtract)
    v.tensor_tensor(t3, t3, u3, A.add)
    mb33 = m33[0:P, 0:nblk].unsqueeze(2).broadcast_to([P, nblk, BK])
    v.tensor_tensor(u3, s3, mb33, A.add)
    v.tensor_tensor(u3, u3, mb33, A.subtract)
    v.tensor_tensor(d3, t3, u3, A.subtract)


def build_program():
    nc = bass.Bass("TRN2", target_bir_lowering=False)
    pr = Prog(nc)

    x_in = nc.dram_tensor("x", [XROWS, D], F32, kind="ExternalInput").ap()
    w_fc = nc.dram_tensor("w_fc", [HID, D], F32, kind="ExternalInput").ap()
    b_fc = nc.dram_tensor("b_fc", [1, HID], F32, kind="ExternalInput").ap()
    w_pj = nc.dram_tensor("w_proj", [D, HID], F32, kind="ExternalInput").ap()
    b_pj = nc.dram_tensor("b_proj", [1, D], F32, kind="ExternalInput").ap()
    ones_in = nc.dram_tensor("ones_row", [1, 128], F32, kind="ExternalInput").ap()
    wqkv_in = nc.dram_tensor("wqkvT", [D, 3076], F32, kind="ExternalInput").ap()
    cvw_in = nc.dram_tensor("cvw", [128, 96], F32, kind="ExternalInput").ap()
    f31cols_in = nc.dram_tensor("f31cols", [128, 256], F32, kind="ExternalInput").ap()
    f3cols_in = nc.dram_tensor("f3cols", [128, 32], F32, kind="ExternalInput").ap()
    dmask_in = nc.dram_tensor("dmask", [128, 32], F32, kind="ExternalInput").ap()
    umask_in = nc.dram_tensor("umask", [128, 128], F32, kind="ExternalInput").ap()
    lmask_in = nc.dram_tensor("lmask", [128, 128], F32, kind="ExternalInput").ap()
    amask_in = nc.dram_tensor("amask", [128, 128], F32, kind="ExternalInput").ap()
    eye_in = nc.dram_tensor("eye128", [128, 128], F32, kind="ExternalInput").ap()
    ones2_in = nc.dram_tensor("ones2", [128, 64], F32, kind="ExternalInput").ap()
    fgw1_in = nc.dram_tensor("fgw1T", [1792, 1024], F32, kind="ExternalInput").ap()
    fgb1_in = nc.dram_tensor("fgb1r", [1, 1024], F32, kind="ExternalInput").ap()
    fgw2_in = nc.dram_tensor("fgw2T", [1024, 4], F32, kind="ExternalInput").ap()
    fgb2_in = nc.dram_tensor("fgb2r", [1, 4], F32, kind="ExternalInput").ap()
    ltemp_in = nc.dram_tensor("ltemp", [1, 1], F32, kind="ExternalInput").ap()
    onw_in = nc.dram_tensor("onwcols", [128, 8], F32, kind="ExternalInput").ap()
    oprj_in = nc.dram_tensor("oprojT", [D, D], F32, kind="ExternalInput").ap()
    zeros_in = nc.dram_tensor("zeros128", [128, 1024], F32, kind="ExternalInput").ap()
    out = nc.dram_tensor("out", [TPC, D], F32, kind="ExternalOutput").ap()
    q_hbm = nc.dram_tensor("q_hbm", [D, XROWS], BF16).ap()
    k_hbm = nc.dram_tensor("k_hbm", [D, XROWS], BF16).ap()
    v_hbm = nc.dram_tensor("v_hbm", [D, XROWS + 8], BF16).ap()
    sh_hbm = nc.dram_tensor("sh_hbm", [D, TPC], BF16).ap()
    lo_hbm = nc.dram_tensor("lo_hbm", [D, TPC], BF16).ap()
    mn_hbm = nc.dram_tensor("mn_hbm", [TPC, 768], BF16).ap()
    gh_hbm = nc.dram_tensor("gh_hbm", [TPC, 1024], BF16).ap()
    o_hbm = nc.dram_tensor("o_hbm", [TPC, D], BF16).ap()

    xn_tm = nc.dram_tensor("xn_tm", [64 + XROWS, D], BF16).ap()
    g_tm = nc.dram_tensor("g_tm", [TPC, D], BF16).ap()
    xnq_tm = nc.dram_tensor("xnq_tm", [TPC, D], BF16).ap()
    hq_tm = nc.dram_tensor("hq_tm", [TPC, HID], BF16).ap()
    wqfc_hbm = nc.dram_tensor("wqfc_hbm", [HID, D], BF16).ap()
    wqpj_hbm = nc.dram_tensor("wqpj_hbm", [D, HID], BF16).ap()

    sb = nc.alloc_sbuf_tensor
    wbig = sb("wbig", [128, 8 * HID], BF16)
    bfc_row = sb("bfc_row", [1, HID], BF16)
    bpj_row = sb("bpj_row", [1, D], BF16)
    ones = sb("ones", [1, 128], BF16)

    NX = 2
    xt = [sb(f"xt{i}", [128, D], F32) for i in range(NX)]
    xnt = [sb(f"xnt{i}", [128, D], F32) for i in range(NX)]
    xnb = [sb(f"xnb{i}", [128, D], BF16) for i in range(NX)]
    xqb = [sb(f"xqb{i}", [128, D], BF16) for i in range(NX)]
    lnscr = sb("lnscr", [128, D], F32)
    qt1 = sb("qt1", [128, D], F32)
    qt2 = sb("qt2", [128, D], F32)
    stat = [sb(f"stat{i}", [128, 12], F32) for i in range(NX)]
    xam = [sb(f"xam{i}", [128, 32], F32) for i in range(NX)]
    xp = [sb(f"xp{i}", [128, 32], F32) for i in range(NX)]
    xscr = sb("xscr", [128, 32], F32)
    xm23 = [sb(f"xm23{i}", [128, 32], F32) for i in range(NX)]
    xm33 = [sb(f"xm33{i}", [128, 32], F32) for i in range(NX)]
    xm52 = [sb(f"xm52{i}", [128, 32], F32) for i in range(NX)]

    xnqf = sb("xnqf", [128, 8 * 1024], BF16)
    hbuf = sb("hbuf", [128, 2048], F32)
    hqb = sb("hqb", [128, 2048], BF16)
    ht1 = sb("ht1", [128, 2048], F32)
    ht2 = sb("ht2", [128, 2048], F32)
    NH = 2
    ham = [sb(f"ham{i}", [128, 64], F32) for i in range(NH)]
    hp = [sb(f"hp{i}", [128, 64], F32) for i in range(NH)]
    hscr = sb("hscr", [128, 64], F32)
    hm23 = [sb(f"hm23{i}", [128, 64], F32) for i in range(NH)]
    hm33 = [sb(f"hm33{i}", [128, 64], F32) for i in range(NH)]
    hm52 = [sb(f"hm52{i}", [128, 64], F32) for i in range(NH)]

    NHQ = 8
    hqT = [sb(f"hqT{i}", [128, SUBT * 128], BF16) for i in range(NHQ)]
    gbuf = [sb(f"gbuf{i}", [128, D], F32) for i in range(SUBT)]
    x2 = [sb(f"x2_{i}", [128, D], F32) for i in range(2)]
    obuf = [sb(f"obuf{i}", [128, D], F32) for i in range(2)]

    psum = [nc.alloc_psum_tensor(f"ps{i}", [128, 512], F32) for i in range(8)]

    wsrc = ht1
    wt1 = ht2
    wt2 = hbuf
    wout = hqb
    wam, wp_, wscr = ham[0], hp[0], hscr
    wm23, wm33, wm52 = hm23[0], hm33[0], hm52[0]

    wfcT_v = wbig[:].rearrange("p (c n) -> p c n", c=8, n=HID)
    wpjT_v = wbig[:].rearrange("p (c n) -> p c n", c=32, n=D)
    xnqf_v = xnqf[:].rearrange("p (c n) -> p c n", c=8, n=1024)

    S = lambda name: Sem(nc, name)
    s_cst = S("s_cst")
    s_wld = S("s_wld"); s_wam = S("s_wam"); s_wpool = S("s_wpool")
    s_wqd = S("s_wqd"); s_wst = S("s_wst"); s_wtr = S("s_wtr"); s_wtr2 = S("s_wtr2")
    s_xp = [S("s_xp0"), S("s_xp1")]
    s_xstp = [S("s_xstp0"), S("s_xstp1")]
    s_htr8 = [S(f"s_htr8_{i}") for i in range(8)]
    s_x = S("s_x"); s_st1 = S("s_st1"); s_pmath = S("s_pmath"); s_sqrt = S("s_sqrt")
    s_lnv = S("s_lnv"); s_xpool = S("s_xpool"); s_xq = S("s_xq"); s_xst = S("s_xst")
    s_xtr = S("s_xtr"); s_fc = S("s_fc"); s_gel = S("s_gel")
    s_ham = S("s_ham"); s_hpool = S("s_hpool"); s_hq = S("s_hq"); s_hst = S("s_hst")
    s_htr = S("s_htr"); s_pjk = S("s_pjk"); s_sil = S("s_sil")
    s_x2 = S("s_x2"); s_ores = S("s_ores"); s_out = S("s_out")

    @pr.on("gpsimd")
    def _(g):
        g.dma_start(out=ones[:], in_=ones_in[:]).then_inc(s_cst.h, 16)
        g.dma_start(out=bfc_row[:], in_=b_fc[:]).then_inc(s_cst.h, 16)
        g.dma_start(out=bpj_row[:], in_=b_pj[:]).then_inc(s_cst.h, 16)
    s_cst.n += 48

    WTILES = [("fc", i, 0, 1024, 32) for i in range(32)] + \
             [("pj", i, hh, 2048, 64) for i in range(8) for hh in range(2)]
    wqd_done = {}

    for wi, (kind, i, hh, ncol, nblk) in enumerate(WTILES):
        src = w_fc if kind == "fc" else w_pj
        dst = wqfc_hbm if kind == "fc" else wqpj_hbm
        col0 = hh * 2048

        @pr.on("sync")
        def _(sy, src=src, i=i, col0=col0, ncol=ncol, wi=wi):
            if wi >= 1:
                sy.wait_ge(s_wqd.h, wqd_done[wi - 1])
            sy.dma_start(out=wsrc[:, 0:ncol],
                         in_=src[i * 128:(i + 1) * 128, col0:col0 + ncol]).then_inc(s_wld.h, 16)
        s_wld.n += 16

        @pr.on("vector")
        def _(v, wld_n=s_wld.n, nblk=nblk, ncol=ncol):
            v.wait_ge(s_wld.h, wld_n)
            v.tensor_reduce(wam[:, 0:nblk].unsqueeze(2),
                            wsrc[:, 0:ncol].rearrange("p (b c) -> p b c", b=nblk, c=BK),
                            AX.X, A.max, apply_absolute_value=True)
            emit_ladder_m(v, wam, wp_, wscr, wm23, wm33, wm52, nblk)
            emit_quant(v, wsrc[:, 0:ncol], wout[:, 0:ncol],
                       wm23, wm33, wm52, wt1[:, 0:ncol], wt2[:, 0:ncol], nblk)
            v.tensor_copy(wscr[0:1, 1:2], wscr[0:1, 1:2]).then_inc(s_wqd.h, 1)
        s_wqd.n += 1
        wqd_done[wi] = s_wqd.n

        @pr.on("gpsimd")
        def _(g, dst=dst, i=i, col0=col0, ncol=ncol, wqd_n=s_wqd.n):
            g.wait_ge(s_wqd.h, wqd_n)
            g.dma_start(out=dst[i * 128:(i + 1) * 128, col0:col0 + ncol],
                        in_=wout[:, 0:ncol]).then_inc(s_wst.h, 16)
        s_wst.n += 16

    @pr.on("scalar")
    def _(sc, wst_n=s_wst.n):
        sc.wait_ge(s_wst.h, wst_n)
        for fc_ in range(8):
            sc.dma_start_transpose(wfcT_v[:, fc_, :],
                                   wqfc_hbm[:, fc_ * 128:(fc_ + 1) * 128]).then_inc(s_wtr.h, 16)
    s_wtr.n += 8 * 16

    lnv_done = {}
    xq_done = {}
    xst_done = {}

    for t in range(NLN):
        bi = t % NX
        row0 = t * 128
        own = t >= NLN - NTILE
        orow0 = (t - (NLN - NTILE)) * 128

        @pr.on("sync")
        def _(sy, t=t, bi=bi, row0=row0):
            if t >= NX:
                sy.wait_ge(s_xq.h, xq_done[t - NX])
            sy.dma_start(out=xt[bi][:],
                         in_=x_in[row0:row0 + 128, :]).then_inc(s_xp[bi].h, 16)
        s_xp[bi].n += 16

        @pr.on("scalar")
        def _(sc, t=t, bi=bi, x_n=s_xp[bi].n):
            sc.wait_ge(s_xp[bi].h, x_n)
            if t >= NX:
                sc.wait_ge(s_lnv.h, lnv_done[t - NX])
            sc.activation(lnscr[:], xt[bi][:], AF.Copy, accum_out=stat[bi][:, 0:1])
            sc.activation(lnscr[:], xt[bi][:], AF.Square,
                          accum_out=stat[bi][:, 1:2]).then_inc(s_st1.h, 1)
        s_st1.n += 1

        @pr.on("gpsimd")
        def _(g, bi=bi, st1_n=s_st1.n):
            g.wait_ge(s_st1.h, st1_n)
            g.tensor_single_scalar(stat[bi][:, 2:3], stat[bi][:, 0:1], 1.0 / D, A.mult)
            g.tensor_single_scalar(stat[bi][:, 3:4], stat[bi][:, 1:2], 1.0 / D, A.mult)
            g.tensor_tensor(stat[bi][:, 4:5], stat[bi][:, 2:3], stat[bi][:, 2:3], A.mult)
            g.tensor_tensor(stat[bi][:, 3:4], stat[bi][:, 3:4], stat[bi][:, 4:5], A.subtract)
            g.tensor_single_scalar(stat[bi][:, 3:4], stat[bi][:, 3:4], LN_EPS,
                                   A.add).then_inc(s_pmath.h, 1)
        s_pmath.n += 1

        @pr.on("scalar")
        def _(sc, bi=bi, pmath_n=s_pmath.n):
            sc.wait_ge(s_pmath.h, pmath_n)
            sc.activation(stat[bi][:, 5:6], stat[bi][:, 3:4], AF.Sqrt).then_inc(s_sqrt.h, 1)
        s_sqrt.n += 1

        @pr.on("vector")
        def _(v, t=t, bi=bi, sqrt_n=s_sqrt.n):
            v.wait_ge(s_sqrt.h, sqrt_n)
            v.reciprocal(stat[bi][:, 6:7], stat[bi][:, 5:6])
            v.tensor_copy(xscr[0:1, 0:1], xscr[0:1, 0:1])
            v.tensor_copy(xscr[0:1, 1:2], xscr[0:1, 1:2])
            v.tensor_copy(xscr[0:1, 2:3], xscr[0:1, 2:3])
            v.tensor_scalar(xnt[bi][:], xt[bi][:], stat[bi][:, 2:3], stat[bi][:, 6:7],
                            A.subtract, A.mult)
            v.tensor_copy(xnb[bi][:], xnt[bi][:])
            v.tensor_reduce(xam[bi][:, 0:32].unsqueeze(2),
                            xnt[bi][:].rearrange("p (b c) -> p b c", b=32, c=BK),
                            AX.X, A.max, apply_absolute_value=True).then_inc(s_lnv.h, 1)
        s_lnv.n += 1
        lnv_done[t] = s_lnv.n

        @pr.on("vector")
        def _(v, t=t, bi=bi):
            if t >= NX:
                _xd = xst_done[t - NX]
                v.wait_ge(s_xstp[0].h, _xd[0])
                v.wait_ge(s_xstp[1].h, _xd[1])
            emit_ladder_m(v, xam[bi], xp[bi], xscr, xm23[bi], xm33[bi], xm52[bi], 32)
            emit_quant(v, xnt[bi][:], xqb[bi][:], xm23[bi], xm33[bi], xm52[bi],
                       qt1[:], qt2[:], 32)
            v.tensor_copy(xscr[0:1, 4:5], xscr[0:1, 4:5]).then_inc(s_xq.h, 1)
        s_xq.n += 1
        xq_done[t] = s_xq.n

        @pr.on("gpsimd")
        def _(g, bi=bi, row0=row0, own=own, orow0=orow0, xq_n=s_xq.n):
            g.wait_ge(s_xq.h, xq_n)
            g.dma_start(out=xn_tm[64 + row0:64 + row0 + 128, :],
                        in_=xnb[bi][:]).then_inc(s_xstp[bi].h, 16)
            if own:
                g.dma_start(out=xnq_tm[orow0:orow0 + 128, :],
                            in_=xqb[bi][:]).then_inc(s_xstp[bi].h, 16)
        s_xstp[bi].n += 16 + (16 if own else 0)
        xst_done[t] = (s_xstp[0].n, s_xstp[1].n)

    fc_done = {}
    gel_done = {}
    hq_done = {}
    hst_done = {}
    xtr_done = {}
    hpool_done = {}

    def prev_half(key):
        grp, tl, half = key
        if half:
            return (grp, tl, 0)
        if tl:
            return (grp, tl - 1, 1)
        if grp:
            return (grp - 1, GTILES - 1, 1)
        return None

    for grp in range(NGRP):
        @pr.on("scalar")
        def _(sc, grp=grp, wtr_n=s_wtr.n, xst_n=None):
            _xd = xst_done[(NLN - NTILE) + grp * GTILES + GTILES - 1]
            sc.wait_ge(s_xstp[0].h, _xd[0])
            sc.wait_ge(s_xstp[1].h, _xd[1])
            sc.wait_ge(s_wtr.h, wtr_n)
            if grp >= 1:
                sc.wait_ge(s_fc.h, fc_done[(grp - 1, GTILES - 1, 1)])
            for fc_ in range(8):
                sc.dma_start_transpose(
                    xnqf_v[:, fc_, :],
                    xnq_tm[grp * 1024:(grp + 1) * 1024,
                           fc_ * 128:(fc_ + 1) * 128]).then_inc(s_xtr.h, 16)
        s_xtr.n += 8 * 16
        xtr_done[grp] = s_xtr.n

        for tl in range(GTILES):
            tok = tl * 128
            for half in range(2):
                key = (grp, tl, half)

                @pr.on("tensor")
                def _(te, key=key, tok=tok, half=half, xtr_n=xtr_done[grp]):
                    te.wait_ge(s_xtr.h, xtr_n)
                    te.wait_ge(s_cst.h, 48)
                    pk = prev_half(key)
                    if pk in gel_done:
                        te.wait_ge(s_gel.h, gel_done[pk])
                    for nb in range(4):
                        hid0 = half * 2048 + nb * 512
                        te.matmul(psum[nb][:], ones[:],
                                  bfc_row[0:1, hid0:hid0 + 512], start=True, stop=False)
                    ins = None
                    for kc in range(8):
                        for nb in range(4):
                            hid0 = half * 2048 + nb * 512
                            ins = te.matmul(psum[nb][:],
                                            xnqf_v[:, kc, tok:tok + 128],
                                            wfcT_v[:, kc, hid0:hid0 + 512],
                                            start=False, stop=(kc == 7))
                    ins.then_inc(s_fc.h, 1)
                s_fc.n += 1
                fc_done[key] = s_fc.n

                @pr.on("scalar")
                def _(sc, key=key, fc_n=s_fc.n):
                    sc.wait_ge(s_fc.h, fc_n)
                    pk = prev_half(key)
                    if pk in hq_done:
                        sc.wait_ge(s_hq.h, hq_done[pk])
                    for nb in range(4):
                        sc.activation(hbuf[:, nb * 512:(nb + 1) * 512], psum[nb][:],
                                      AF.Gelu).then_inc(s_gel.h, 1)
                s_gel.n += 4
                gel_done[key] = s_gel.n

                @pr.on("vector")
                def _(v, key=key, half=half, gel_n=s_gel.n):
                    v.wait_ge(s_gel.h, gel_n)
                    pk = prev_half(key)
                    if pk in hst_done:
                        v.wait_ge(s_hst.h, hst_done[pk])
                    v.tensor_reduce(ham[half][:, 0:64].unsqueeze(2),
                                    hbuf[:].rearrange("p (b c) -> p b c", b=64, c=BK),
                                    AX.X, A.max, apply_absolute_value=True)
                    emit_ladder_m(v, ham[half], hp[half], hscr,
                                  hm23[half], hm33[half], hm52[half], 64)
                    emit_quant(v, hbuf[:], hqb[:], hm23[half], hm33[half], hm52[half],
                               ht1[:], ht2[:], 64)
                    v.tensor_copy(hscr[0:1, 1:2], hscr[0:1, 1:2]).then_inc(s_hq.h, 1)
                s_hq.n += 1
                hq_done[key] = s_hq.n

                @pr.on("gpsimd")
                def _(g, key=key, half=half, hq_n=s_hq.n):
                    grp_, tl_, _ = key
                    g.wait_ge(s_hq.h, hq_n)
                    r0 = grp_ * 1024 + tl_ * 128
                    g.dma_start(out=hq_tm[r0:r0 + 128, half * 2048:(half + 1) * 2048],
                                in_=hqb[:]).then_inc(s_hst.h, 16)
                s_hst.n += 16
                hst_done[key] = s_hst.n

    last_fc = fc_done[(NGRP - 1, GTILES - 1, 1)]
    last_gel = gel_done[(NGRP - 1, GTILES - 1, 1)]

    @pr.on("scalar")
    def _(sc, last_fc=last_fc):
        sc.wait_ge(s_fc.h, last_fc)
        for hc in range(32):
            sc.dma_start_transpose(wpjT_v[:, hc, :],
                                   wqpj_hbm[:, hc * 128:(hc + 1) * 128]).then_inc(s_wtr2.h, 16)
    s_wtr2.n += 32 * 16

    htr_done = {}
    pjk_done = {}
    sil_done = {}
    ores_done = {}
    sgi = 0
    ckidx = 0

    for grp in range(NGRP):
        for sub in range(GTILES // SUBT):
            tok0 = sub * SUBT * 128
            ck0 = ckidx
            for kc in range(32):
                @pr.on("scalar")
                def _(sc, grp=grp, kc=kc, tok0=tok0, ck=ckidx,
                      hst_n=hst_done[(grp, GTILES - 1, 1)]):
                    sc.wait_ge(s_hst.h, hst_n)
                    if ck >= NHQ:
                        sc.wait_ge(s_pjk.h, pjk_done[ck - NHQ])
                    sc.dma_start_transpose(
                        hqT[ck % NHQ][:],
                        hq_tm[grp * 1024 + tok0:grp * 1024 + tok0 + SUBT * 128,
                              kc * 128:(kc + 1) * 128]).then_inc(s_htr8[ck % 8].h, 16)
                s_htr8[ckidx % 8].n += 16
                htr_done[ckidx] = (ckidx % 8, s_htr8[ckidx % 8].n)
                ckidx += 1

            @pr.on("tensor")
            def _(te, sgi=sgi, ck0=ck0, last_gel=last_gel, wtr2_n=s_wtr2.n):
                te.wait_ge(s_wtr2.h, wtr2_n)
                if sgi == 0:
                    te.wait_ge(s_gel.h, last_gel)
                else:
                    te.wait_ge(s_sil.h, sil_done[sgi - 1])
                for st_ in range(SUBT):
                    for pb in range(2):
                        te.matmul(psum[st_ * 2 + pb][:], ones[:],
                                  bpj_row[0:1, pb * 512:(pb + 1) * 512],
                                  start=True, stop=False)
                for kc in range(32):
                    _hs, _hn = htr_done[ck0 + kc]
                    te.wait_ge(s_htr8[_hs].h, _hn)
                    ins = None
                    for st_ in range(SUBT):
                        for pb in range(2):
                            ins = te.matmul(psum[st_ * 2 + pb][:],
                                            hqT[(ck0 + kc) % NHQ][:, st_ * 128:st_ * 128 + 128],
                                            wpjT_v[:, kc, pb * 512:(pb + 1) * 512],
                                            start=False, stop=(kc == 31))
                    ins.then_inc(s_pjk.h, 1)
            for kc in range(32):
                pjk_done[ck0 + kc] = s_pjk.n + kc + 1
            s_pjk.n += 32

            @pr.on("scalar")
            def _(sc, sgi=sgi, pjk_n=s_pjk.n):
                sc.wait_ge(s_pjk.h, pjk_n)
                if sgi >= 1:
                    sc.wait_ge(s_ores.h, ores_done[sgi * SUBT - 1])
                for st_ in range(SUBT):
                    for pb in range(2):
                        sc.activation(gbuf[st_][:, pb * 512:(pb + 1) * 512],
                                      psum[st_ * 2 + pb][:], AF.Silu).then_inc(s_sil.h, 1)
            s_sil.n += 2 * SUBT
            sil_done[sgi] = s_sil.n

            for st_ in range(SUBT):
                t_own = grp * GTILES + sub * SUBT + st_

                @pr.on("vector")
                def _(v, sgi=sgi, st_=st_, t_own=t_own, sil_n=sil_done[sgi]):
                    v.wait_ge(s_sil.h, sil_n)
                    if t_own >= 2:
                        v.wait_ge(s_out.h, (t_own - 1) * 16)
                    v.tensor_copy(obuf[st_ % 2][:].bitcast(BF16)[:, 0:D],
                                  gbuf[st_][:]).then_inc(s_ores.h, 1)
                s_ores.n += 1
                ores_done[t_own] = s_ores.n

                @pr.on("gpsimd")
                def _(g, t_own=t_own, st_=st_, ores_n=s_ores.n):
                    g.wait_ge(s_ores.h, ores_n)
                    r0 = t_own * 128
                    g.dma_start(out=g_tm[r0:r0 + 128, :],
                                in_=obuf[st_ % 2][:].bitcast(BF16)[:, 0:D]).then_inc(s_out.h, 16)
                s_out.n += 16
            sgi += 1

    class St:
        def __init__(self, name, eng, dma):
            self.name, self.eng, self.dma = name, eng, dma
            self.sem = nc.alloc_semaphore("tk_" + name)
            if dma:
                self.dsem = nc.alloc_semaphore("tkd_" + name)
            self.n = 0
            self.q = 0

        def tok(self):
            return (self.sem, self.n)

    sT = St("T", "tensor", False)
    sV = St("V", "vector", False)
    sC = St("C", "scalar", False)
    sCQ = St("CQ", "scalar", True)
    sG = St("G", "gpsimd", False)
    sGQ = St("GQ", "gpsimd", True)
    sY = St("Y", "sync", True)

    _psem = {}

    def step(stream, deps, body, ndma=0, key=None):
        if stream.dma:
            key = key or (stream.name + "_d")
            kk = (stream.name, key)
            if kk not in _psem:
                _psem[kk] = Sem(nc, "pd_" + stream.name + "_" + key)
            ps = _psem[kk]
            ps.n += 16 * ndma
            post = ps.n
            sem = ps.h
        else:
            stream.n += 1
            post = stream.n
            sem = stream.sem
        isdma = stream.dma

        def thunk(e, deps=tuple(deps), body=body, sem=sem):
            for (s, c) in deps:
                if c > 0:
                    e.wait_ge(s, c)

            def fin(instr):
                instr.then_inc(sem, 16 if isdma else 1)
                return instr
            body(e, fin)
        pr.streams[stream.eng].append(thunk)
        return (sem, post)

    B_DONE = [(s_xstp[0].h, s_xstp[0].n), (s_xstp[1].h, s_xstp[1].n),
              (s_out.h, s_out.n)]

    wbigb = wbig[:]
    wqkvT = wbigb[:, 0:24608].rearrange("p (c n) -> p c n", c=8, n=3076)
    f3st = wbigb[:, 27680:28704].rearrange("p (i n) -> p i n", i=32, n=32)
    gfin = wbigb[:, 28704:29728]
    o_tmb = wbigb[:, 29728:31776].rearrange("p (s n) -> p s n", s=2, n=1024)
    wstage = wbigb[:, 31776:32768]
    f31st = xnqf[:].rearrange("p (i n) -> p i n", i=256, n=32)
    XW = 304
    xnfm = hbuf[:].bitcast(BF16).rearrange("p (c n) -> p c n", c=8, n=512)
    ht1b = ht1[:].bitcast(BF16)
    vst = ht1b[:, 1216:1816].rearrange("p (r n) -> p r n", r=2, n=300)
    ht2b = ht2[:].bitcast(BF16)
    cta = ht2[:, 0:304]
    ctb = ht2[:, 304:608]
    cvo = [ht2[:, 608:912], ht2[:, 912:1216]]
    repf = ht2b[:, 2432:3568].rearrange("p (g n) -> p g n", g=4, n=284)
    qfm = xt[0][:].bitcast(BF16).rearrange("p (c n) -> p c n", c=8, n=256)
    kfm = xt[1][:].bitcast(BF16).rearrange("p (c n) -> p c n", c=8, n=256)
    qtm = qt1[:].bitcast(BF16).rearrange("p (s n) -> p s n", s=2, n=1024)
    ktm = qt2[:].bitcast(BF16).rearrange("p (s n) -> p s n", s=2, n=1024)
    vtm = obuf[0][:].bitcast(BF16).rearrange("p (s n) -> p s n", s=2, n=1024)
    dtm = obuf[1][:].bitcast(BF16).rearrange("p (s n) -> p s n", s=2, n=1024)
    vbt = lnscr[:].bitcast(BF16).rearrange("p (s n) -> p s n", s=2, n=1024)
    ghst = vbt
    Ssb = x2[0][:].bitcast(BF16).rearrange("p (h n) -> p h n", h=4, n=512)
    scb = x2[1][:].bitcast(BF16)
    zbuf = scb[:, 0:1024].rearrange("p (h n) -> p h n", h=4, n=256)
    cmt = scb[:, 1024:2048].rearrange("p (i n) -> p i n", i=8, n=128)
    slevv = xnb[1][:].rearrange("p (r n) -> p r n", r=4, n=256)
    mstg = [xqb[0][:], xqb[1][:]]
    grd = xnb[0][:].rearrange("p (r n) -> p r n", r=8, n=128)
    sltm = hqb[:].rearrange("p (i n) -> p i n", i=2, n=1024)
    x_fin = xnt[1][:]
    comb = [xnt[0][:].bitcast(BF16)[:, 0:1024],
            xnt[0][:].bitcast(BF16)[:, 1024:2048]]
    o2buf = xqb[0][:]
    oprjT = [gbuf[i // 2][:].bitcast(BF16)[:, (i % 2) * 1024:(i % 2 + 1) * 1024]
             for i in range(8)]
    fgw1r = [hqT[i][:] for i in range(8)]
    ofs32 = qt1[:]
    zrof = qt2[:]

    arsn = sb("arsn", [128, 48], F32)
    wsm = sb("wsm", [128, 24], F32)
    s_fin = sb("s_fin", [128, D], F32)
    outst = lnscr[:]
    vgscr = sb("vgscr", [128, 8], F32)
    cst128 = sb("cst128", [128, 384], F32)
    eye128 = sb("eye128_sb", [128, 128], F32)
    ones2 = sb("ones2_sb", [128, 64], BF16)
    dmask_sb = sb("dmask_s2", [128, 32], F32)
    cvw = sb("cvw_sb", [128, 96], F32)
    ccols = s_fin[:, 0:384]
    onw_sb = sb("onw_sb", [128, 8], F32)
    fgb1_sb = sb("fgb1_sb", [1, 1024], BF16)
    fgb2_sb = sb("fgb2_sb", [1, 4], BF16)
    fgw2_sb = sb("fgw2_sb", [128, 32], BF16)
    invt = sb("invt", [1, 8], F32)
    invt_col = sb("invt_col", [128, 1], F32)
    qch_sb = ht1b[:, 1824:3872]
    Qch = [qch_sb[:, i * 512:(i + 1) * 512].rearrange("p (h n) -> p h n",
                                                      h=4, n=128)
           for i in range(4)]
    QchF = [qch_sb[:, i * 512:(i + 1) * 512] for i in range(4)]
    sfb = s_fin[:].bitcast(BF16)
    Gsb = [sfb[:, i * 512:(i + 1) * 512].rearrange("p (h n) -> p h n",
                                                   h=4, n=128)
           for i in range(2)]
    GsbF = [sfb[:, i * 512:(i + 1) * 512] for i in range(2)]
    attnT = [sfb[:, 1024 + i * 512:1024 + (i + 1) * 512]
             .rearrange("p (h n) -> p h n", h=4, n=128)
             for i in range(2)]
    ub_sb = gfin
    ubuf = ub_sb.rearrange("p (h n) -> p h n", h=4, n=256)
    rowb = wbigb[0:1, 29728:30752]
    eye_bf = ht2b[:, 3936:4064]

    fgw1_bhbm = nc.dram_tensor("fgw1_bhbm", [1792, 1024], BF16).ap()
    pre_hbm = nc.dram_tensor("pre_hbm", [512, 304], BF16).ap()
    invt_hbm = nc.dram_tensor("invt_hbm", [1, 1], F32).ap()

    def VG(v, n=3):
        for i in range(n):
            v.tensor_copy(vgscr[0:1, i:i + 1], vgscr[0:1, i:i + 1])

    def _wq_loads(g, fin):
        for kc in range(8):
            fin(g.dma_start(out=wqkvT[:, kc, :],
                            in_=wqkv_in[kc * 128:(kc + 1) * 128, :]))
        fin(g.dma_start(out=fgb1_sb[:], in_=fgb1_in[:]))
        fin(g.dma_start(out=fgb2_sb[:], in_=fgb2_in[:]))
        for kc in range(8):
            fin(g.dma_start(out=fgw2_sb[:, kc * 4:kc * 4 + 4],
                            in_=fgw2_in[kc * 128:(kc + 1) * 128, :]))
        fin(g.dma_start(out=ones2[:], in_=ones2_in[:]))
    t_wq = step(sGQ, B_DONE, _wq_loads, ndma=19, key="wld")

    def _consts(sy, fin):
        fin(sy.dma_start(out=cst128[:, 0:128], in_=umask_in[:]))
        fin(sy.dma_start(out=cst128[:, 128:256], in_=lmask_in[:]))
        fin(sy.dma_start(out=cst128[:, 256:384], in_=amask_in[:]))
        fin(sy.dma_start(out=eye128[:], in_=eye_in[:]))
        fin(sy.dma_start(out=dmask_sb[:], in_=dmask_in[:]))
        fin(sy.dma_start(out=cvw[:], in_=cvw_in[:]))
        fin(sy.dma_start(out=ccols[:, 96:352], in_=f31cols_in[:]))
        fin(sy.dma_start(out=ccols[:, 352:384], in_=f3cols_in[:]))
        fin(sy.dma_start(out=onw_sb[:], in_=onw_in[:]))
        fin(sy.dma_start(out=invt[0:1, 0:1], in_=ltemp_in[:]))
        fin(sy.dma_start(out=xn_tm[0:64, :],
                         in_=zeros_in[0:64, 0:512].bitcast(BF16)))
        zb = zeros_in[0:128, :].bitcast(BF16)
        VROW = XROWS + 8
        for ch in range(8):
            i_ap = bass.AP(zb.tensor, 0, [[2048, 128], [0, 5], [1, 1640]])
            o_ap = bass.AP(v_hbm.tensor, ch * 128 * VROW,
                           [[VROW, 128], [1640, 5], [1, 1640]])
            fin(sy.dma_start(out=o_ap, in_=i_ap))
    t_cst = step(sY, B_DONE, _consts, ndma=19, key="cst")

    def _convst(v, fin):
        for i in range(256):
            v.tensor_scalar(f31st[:, i, :], dmask_sb[:], ccols[:, 96 + i:97 + i],
                            None, A.mult)
        for i in range(32):
            v.tensor_scalar(f3st[:, i, :], dmask_sb[:], ccols[:, 352 + i:353 + i],
                            None, A.mult)
        v.tensor_copy(eye_bf[:], eye128[:])
        v.memset(ht1[:, 908:912], 0.0)
        v.memset(zrof[:], 0.0)
        VG(v)
        for h in range(4):
            v.tensor_copy(Ssb[:, h, :], zrof[:, 0:512])
        fin(v.tensor_copy(vgscr[0:1, 7:8], vgscr[0:1, 7:8]))
    t_cvst = step(sV, [t_cst] + B_DONE, _convst)

    t_pv, t_pg = t_cvst, t_wq
    for i in range(14):
        t_rd = step(sY, [t_pv, t_pg],
                    lambda sy, fin, i=i: fin(sy.dma_start(
                        out=ofs32[:], in_=fgw1_in[i * 128:(i + 1) * 128, :])),
                    ndma=1, key="fgrd")
        t_pv = step(sV, [t_rd],
                    lambda v, fin: fin(v.tensor_copy(wstage[:, 0:992] if False
                                                     else gfin[:, 0:1024], ofs32[:])))
        t_pg = step(sGQ, [t_pv],
                    lambda g, fin, i=i: fin(g.dma_start(
                        out=fgw1_bhbm[i * 128:(i + 1) * 128, :],
                        in_=gfin[:, 0:1024])), ndma=1, key="fgst")
    t_fgw1 = t_pg

    for kc in range(8):
        t_rd = step(sY, [t_pv, t_pg],
                    lambda sy, fin, kc=kc: fin(sy.dma_start(
                        out=ofs32[:], in_=oprj_in[kc * 128:(kc + 1) * 128, :])),
                    ndma=1, key="oprd")
        t_pv = step(sV, [t_rd],
                    lambda v, fin, kc=kc: fin(v.tensor_scalar(
                        oprjT[kc][:], ofs32[:], onw_sb[:, kc:kc + 1], None, A.mult)))
    t_oprep = t_pv

    def _sp1(sc, fin):
        def CG():
            for _ in range(3):
                sc.activation(rowb[0:1, 0:1], invt[0:1, 0:1], AF.Copy)
        sc.activation(invt[0:1, 1:2], invt[0:1, 0:1], AF.Abs)
        sc.activation(invt[0:1, 4:5], invt[0:1, 0:1], AF.Relu)
        CG()
        sc.activation(invt[0:1, 2:3], invt[0:1, 1:2], AF.Exp, scale=-1.0)
        CG()
        fin(sc.activation(invt[0:1, 3:4], invt[0:1, 2:3], AF.Ln, bias=1.0))
    t_sp1 = step(sC, [t_cst], _sp1)
    def _sp2(v, fin):
        v.tensor_tensor(invt[0:1, 5:6], invt[0:1, 3:4], invt[0:1, 4:5], A.add)
        VG(v)
        v.tensor_single_scalar(invt[0:1, 6:7], invt[0:1, 5:6], 1e-4, A.add)
        VG(v)
        v.reciprocal(invt[0:1, 7:8], invt[0:1, 6:7])
        VG(v)
        fin(v.tensor_copy(vgscr[0:1, 6:7], vgscr[0:1, 6:7]))
    t_sp2 = step(sV, [t_sp1], _sp2)
    def _sp3(g, fin):
        fin(g.dma_start(out=invt_hbm[:], in_=invt[0:1, 7:8]))
    t_sp3a = step(sGQ, [t_sp2], _sp3, ndma=1, key="invt")

    def _sp4(g, fin):
        in_ap = bass.AP(invt_hbm.tensor, 0, [[0, 128], [1, 1]])
        fin(g.dma_start(out=invt_col[:], in_=in_ap))
    t_sp3 = step(sGQ, [t_sp3a], _sp4, ndma=1, key="invt")

    NOWN = NAT // 2
    t_tile_prev = t_cvst
    t_last_out = t_oprep

    for t in range(NAT):
        own = t >= NOWN
        tok0 = t * ATILE
        ot0 = tok0 - PRE

        def _xnrd(sc, fin, tok0=tok0):
            for c in range(8):
                fin(sc.dma_start_transpose(
                    xnfm[:, c, 0:XW],
                    xn_tm[tok0 + 16:tok0 + 16 + XW, c * 128:(c + 1) * 128]))
        t_xn = step(sCQ, B_DONE[0:2] + [t_tile_prev, t_cvst], _xnrd, ndma=8, key="xnr")

        chunk_list = []
        for c in range(8):
            chunk_list.append(("v", c))
            if own:
                chunk_list.append(("q", c))
            chunk_list.append(("k", c))
        t_sil = {}
        t_cv = {}
        t_w = {}
        tv = {}
        t_fir = {}
        twqk_last = None
        pi = 0
        for (kind, c) in chunk_list:
            mc = {"q": 0, "k": 8, "v": 16}[kind] + c
            bank = 6 + pi % 2
            par = pi % 2

            def _proj(te, fin, mc=mc, bank=bank):
                ins = None
                for kc in range(8):
                    ins = te.matmul(psum[bank][:, 0:XW],
                                    wqkvT[:, kc, mc * 128:(mc + 1) * 128],
                                    xnfm[:, kc, 0:XW],
                                    start=(kc == 0), stop=(kc == 7))
                fin(ins)
            deps = [t_xn, t_wq]
            if pi - 2 in t_cv:
                deps.append(t_cv[pi - 2])
            t_p = step(sT, deps, _proj)

            def _conv(v, fin, kind=kind, c=c, bank=bank, par=par):
                off = 1 if kind == "v" else 45
                W = 300 if kind == "v" else 256
                wc = ({"q": 0, "k": 1, "v": 2}[kind] * 8 + c) * 4
                v.tensor_scalar(cta[:, 0:W], psum[bank][:, off:off + W],
                                cvw[:, wc:wc + 1], None, A.mult)
                v.tensor_scalar(ctb[:, 0:W], psum[bank][:, off + 1:off + 1 + W],
                                cvw[:, wc + 1:wc + 2], None, A.mult)
                v.scalar_tensor_tensor(cta[:, 0:W], psum[bank][:, off + 2:off + 2 + W],
                                       cvw[:, wc + 2:wc + 3], cta[:, 0:W],
                                       A.mult, A.add)
                v.scalar_tensor_tensor(ctb[:, 0:W], psum[bank][:, off + 3:off + 3 + W],
                                       cvw[:, wc + 3:wc + 4], ctb[:, 0:W],
                                       A.mult, A.add)
                VG(v, 2)
                fin(v.tensor_tensor(cvo[par][:, 0:W], cta[:, 0:W], ctb[:, 0:W],
                                    A.add))
            deps_c = [t_p]
            if pi - 2 in t_sil:
                deps_c.append(t_sil[pi - 2])
            t_c = step(sV, deps_c, _conv)
            t_cv[pi] = t_c

            def _silu(sc, fin, kind=kind, c=c, par=par):
                if kind == "q":
                    fin(sc.activation(qfm[:, c, :], cvo[par][:, 0:256], AF.Silu))
                elif kind == "k":
                    fin(sc.activation(kfm[:, c, :], cvo[par][:, 0:256], AF.Silu))
                else:
                    fin(sc.activation(vst[:, c % 2, :], cvo[par][:, 0:300], AF.Silu))
            deps_s = [t_c]
            if kind == "v" and c - 2 in tv:
                deps_s.append(tv[c - 2])
                if ("rf", c - 2) in t_fir:
                    deps_s.append(t_fir[("rf", c - 2)])
            t_s = step(sC, deps_s, _silu)
            t_sil[pi] = t_s

            def _whbm(g, fin, kind=kind, c=c, tok0=tok0, t=t):
                if kind == "v":
                    if t == 0:
                        fin(g.dma_start(out=v_hbm[c * 128:(c + 1) * 128, 0:256],
                                        in_=vst[:, c % 2, 44:300]))
                    else:
                        fin(g.dma_start(
                            out=v_hbm[c * 128:(c + 1) * 128, tok0 - 44:tok0 + 256],
                            in_=vst[:, c % 2, :]))
                else:
                    src2 = qfm if kind == "q" else kfm
                    hb = q_hbm if kind == "q" else k_hbm
                    fin(g.dma_start(out=hb[c * 128:(c + 1) * 128, tok0:tok0 + 256],
                                    in_=src2[:, c, :]))
            wkey = f"vw{c % 2}" if kind == "v" else "whbm"
            t_wr = step(sY, [t_s], _whbm, ndma=1, key=wkey)
            t_w[pi] = t_wr
            if kind == "v":
                tv[c] = t_wr
            else:
                twqk_last = t_wr
            pi += 1

            if own and kind == "v":
                def _repf(g, fin, c=c, par=par):
                    vsb = 1216 + (c % 2) * 300
                    for g4 in range(4):
                        in_ap = bass.AP(ht1b.tensor,
                                        32 * g4 * 4096 + vsb + 14,
                                        [[4096, 32], [1, 4], [1, 284]])
                        fin(g.dma_start(out=repf[:, g4, :], in_=in_ap))
                deps_rf = [t_sil[pi - 1]]
                if ("mm", c - 1) in t_fir:
                    deps_rf.append(t_fir[("mm", c - 1)])
                t_rf = step(sY, deps_rf, _repf, ndma=4, key="repf")
                t_fir[("rf", c)] = t_rf

                def _firmm(te, fin, c=c):
                    ins = None
                    for g4 in range(4):
                        te.matmul(psum[4][32 * g4:32 * g4 + 32, 0:256],
                                  f3st[:, c * 4 + g4, :], repf[:, g4, 28:284],
                                  start=True, stop=True, tile_position=(0, 32 * g4))
                    for tb in range(8):
                        for g4 in range(4):
                            ins = te.matmul(psum[5][32 * g4:32 * g4 + 32, 0:256],
                                            f31st[:, (c * 4 + g4) * 8 + tb, :],
                                            repf[:, g4, 4 * tb:4 * tb + 256],
                                            start=(tb == 0), stop=(tb == 7),
                                            tile_position=(0, 32 * g4))
                    fin(ins)
                deps_fm = [t_rf]
                if ("ev", c - 1) in t_fir:
                    deps_fm.append(t_fir[("ev", c - 1)])
                t_fm = step(sT, deps_fm, _firmm)
                t_fir[("mm", c)] = t_fm

                def _firev(sc, fin, c=c):
                    sc.activation(slevv[:, (2 * c) % 4, :], psum[4][:, 0:256],
                                  AF.Copy)
                    fin(sc.activation(slevv[:, (2 * c + 1) % 4, :],
                                      psum[5][:, 0:256], AF.Copy))
                deps_fe = [t_fm]
                if ("w", c - 2) in t_fir:
                    deps_fe.append(t_fir[("w", c - 2)])
                t_fe = step(sC, deps_fe, _firev)
                t_fir[("ev", c)] = t_fe

                def _firw(g, fin, c=c, ot0=ot0):
                    fin(g.dma_start(out=sh_hbm[c * 128:(c + 1) * 128, ot0:ot0 + 256],
                                    in_=slevv[:, (2 * c) % 4, :]))
                    fin(g.dma_start(out=lo_hbm[c * 128:(c + 1) * 128, ot0:ot0 + 256],
                                    in_=slevv[:, (2 * c + 1) % 4, :]))
                t_fw = step(sY, [t_fe], _firw, ndma=2, key="firw")
                t_fir[("w", c)] = t_fw

        t_beta = {}
        for sub in range(2):
            bank = 6 + sub

            def _bproj(te, fin, sub=sub, bank=bank):
                ins = None
                for kc in range(8):
                    ins = te.matmul(psum[bank][:, 0:4],
                                    xnfm[:, kc, 48 + sub * 128:48 + sub * 128 + 128],
                                    wqkvT[:, kc, 3072:3076],
                                    start=(kc == 0), stop=(kc == 7))
                fin(ins)
            t_bp = step(sT, [t_xn, t_wq, t_cv[pi - 2 + sub]], _bproj)

            def _bev(sc, fin, sub=sub, bank=bank):
                fin(sc.activation(arsn[:, sub * 24 + 8:sub * 24 + 12],
                                  psum[bank][:, 0:4], AF.Sigmoid))
            t_beta[sub] = step(sC, [t_bp], _bev)

        t_sub = {}
        for sub in range(2):
            def _tmrd(sc, fin, sub=sub, tok0=tok0, own=own):
                t0 = tok0 + sub * 128
                fin(sc.dma_start_transpose(ktm[:, sub, :], k_hbm[:, t0:t0 + 128]))
                fin(sc.dma_start_transpose(vtm[:, sub, :], v_hbm[:, t0:t0 + 128]))
                if own:
                    fin(sc.dma_start_transpose(qtm[:, sub, :], q_hbm[:, t0:t0 + 128]))
            t_rd = step(sCQ, [twqk_last, tv[6], tv[7]], _tmrd,
                        ndma=3 if own else 2, key="tmrd")

            def _l2a(v, fin, sub=sub, own=own):
                base = sub * 24
                v.tensor_tensor(o2buf[:].bitcast(BF16)[:, 0:1024], ktm[:, sub, :],
                                ktm[:, sub, :], A.mult)
                VG(v)
                v.tensor_reduce(arsn[:, base + 4:base + 8].unsqueeze(2),
                                o2buf[:].bitcast(BF16)[:, 0:1024]
                                .rearrange("p (h n) -> p h n", h=4, n=256),
                                AX.X, A.add)
                VG(v)
                v.tensor_scalar(arsn[:, base + 4:base + 8],
                                arsn[:, base + 4:base + 8], 1e-12, None, A.add)
                if own:
                    v.tensor_tensor(o2buf[:].bitcast(BF16)[:, 0:1024], qtm[:, sub, :],
                                    qtm[:, sub, :], A.mult)
                    VG(v)
                    v.tensor_reduce(arsn[:, base + 0:base + 4].unsqueeze(2),
                                    o2buf[:].bitcast(BF16)[:, 0:1024]
                                    .rearrange("p (h n) -> p h n", h=4, n=256),
                                    AX.X, A.add)
                    VG(v)
                    v.tensor_scalar(arsn[:, base + 0:base + 4],
                                    arsn[:, base + 0:base + 4], 1e-12, None, A.add)
                fin(v.tensor_copy(vgscr[0:1, 4:5], vgscr[0:1, 4:5]))
            t_l1 = step(sV, [t_rd, t_beta[sub]], _l2a)

            def _sqr(sc, fin, sub=sub, own=own):
                base = sub * 24
                ins = sc.activation(arsn[:, base + 4:base + 8],
                                    arsn[:, base + 4:base + 8], AF.Sqrt)
                if own:
                    ins = sc.activation(arsn[:, base + 0:base + 4],
                                        arsn[:, base + 0:base + 4], AF.Sqrt)
                fin(ins)
            t_l2 = step(sC, [t_l1], _sqr)

            def _l3(v, fin, sub=sub, own=own):
                base = sub * 24
                v.reciprocal(arsn[:, base + 4:base + 8], arsn[:, base + 4:base + 8])
                if own:
                    v.reciprocal(arsn[:, base + 0:base + 4], arsn[:, base + 0:base + 4])
                VG(v)
                v.tensor_tensor(arsn[:, base + 12:base + 16],
                                arsn[:, base + 8:base + 12],
                                arsn[:, base + 4:base + 8], A.mult)
                VG(v)
                v.tensor_scalar(arsn[:, base + 16:base + 20],
                                arsn[:, base + 12:base + 16], -1.0, None, A.mult)
                VG(v)
                for h in range(4):
                    v.tensor_scalar(ktm[:, sub, h * 256:(h + 1) * 256],
                                    ktm[:, sub, h * 256:(h + 1) * 256],
                                    arsn[:, base + 4 + h:base + 5 + h], None, A.mult)
                for h in range(4):
                    v.tensor_scalar(vbt[:, sub, h * 256:(h + 1) * 256],
                                    vtm[:, sub, h * 256:(h + 1) * 256],
                                    arsn[:, base + 8 + h:base + 9 + h], None, A.mult)
                fin(v.tensor_copy(vgscr[0:1, 3:4], vgscr[0:1, 3:4]))
            t_sub[sub] = step(sV, [t_l2], _l3)

        t_sev = t_tile_prev
        t_de = {}
        t_pre_done = t_tile_prev
        for sub in range(2):
            c0 = sub * 128
            base = sub * 24
            par = sub
            first = (t == 0 and sub == 0)

            def _trp(te, fin, base=base):
                ins = None
                for h in range(4):
                    te.transpose(psum[4][0:1, h * 128:(h + 1) * 128],
                                 arsn[:, base + 12 + h:base + 13 + h], eye128[:])
                    ins = te.transpose(psum[5][0:1, h * 128:(h + 1) * 128],
                                       arsn[:, base + 4 + h:base + 5 + h], eye128[:])
                fin(ins)
            deps_tr = [t_sub[sub], t_pre_done]
            if own:
                deps_tr.append(t_fir[("ev", 7)])
            t_tr = step(sT, deps_tr, _trp)

            def _re(sc, fin):
                sc.activation(rowb[0:1, 0:512], psum[4][0:1, 0:512], AF.Copy)
                fin(sc.activation(rowb[0:1, 512:1024], psum[5][0:1, 0:512], AF.Copy))
            t_rw = step(sC, [t_tr], _re)

            def _bc(te, fin):
                ins = None
                for h in range(4):
                    te.matmul(psum[4][:, h * 128:(h + 1) * 128], ones[0:1, 0:128],
                              rowb[0:1, h * 128:(h + 1) * 128], start=True, stop=True)
                    ins = te.matmul(psum[5][:, h * 128:(h + 1) * 128],
                                    ones[0:1, 0:128],
                                    rowb[0:1, 512 + h * 128:512 + (h + 1) * 128],
                                    start=True, stop=True)
                fin(ins)
            t_b2 = step(sT, [t_rw], _bc)

            def _cmk(v, fin):
                for h in range(4):
                    v.tensor_tensor(cmt[:, h, :], cst128[:, 0:128],
                                    psum[4][:, h * 128:(h + 1) * 128], A.mult)
                    v.tensor_tensor(cmt[:, 4 + h, :], cst128[:, 128:256],
                                    psum[5][:, h * 128:(h + 1) * 128], A.mult)
                fin(v.tensor_copy(vgscr[0:1, 5:6], vgscr[0:1, 5:6]))
            t_cm = step(sV, [t_b2], _cmk)

            def _a0(te, fin, c0=c0, own=own):
                ins = None
                for h in range(4):
                    for kk in range(2):
                        ins = te.matmul(psum[4][:, h * 128:(h + 1) * 128],
                                        kfm[:, 2 * h + kk, c0:c0 + 128],
                                        kfm[:, 2 * h + kk, c0:c0 + 128],
                                        start=(kk == 0), stop=(kk == 1))
                if own:
                    for h in range(4):
                        for kk in range(2):
                            ins = te.matmul(psum[5][:, h * 128:(h + 1) * 128],
                                            kfm[:, 2 * h + kk, c0:c0 + 128],
                                            qfm[:, 2 * h + kk, c0:c0 + 128],
                                            start=(kk == 0), stop=(kk == 1))
                fin(ins)
            t_a0 = step(sT, [t_cm], _a0)

            def _am(v, fin, base=base, par=par, own=own):
                for h in range(4):
                    v.scalar_tensor_tensor(Qch[0][:, h, :],
                                           psum[4][:, h * 128:(h + 1) * 128],
                                           arsn[:, base + 4 + h:base + 5 + h],
                                           cmt[:, h, :], A.mult, A.mult)
                for h in range(4):
                    v.scalar_tensor_tensor(Qch[1][:, h, :],
                                           psum[4][:, h * 128:(h + 1) * 128],
                                           arsn[:, base + 12 + h:base + 13 + h],
                                           cmt[:, 4 + h, :], A.mult, A.mult)
                for h in range(4):
                    v.tensor_tensor(Gsb[par][:, h, :], eye_bf[:], Qch[0][:, h, :],
                                    A.subtract)
                if own:
                    for h in range(4):
                        v.scalar_tensor_tensor(attnT[par][:, h, :],
                                               psum[5][:, h * 128:(h + 1) * 128],
                                               arsn[:, base + 4 + h:base + 5 + h],
                                               cst128[:, 256:384], A.mult, A.mult)
                fin(v.tensor_copy(vgscr[0:1, 5:6], vgscr[0:1, 5:6]))
            t_pw = step(sV, [t_a0], _am)

            for lvl in range(6):
                qs = lvl % 2
                def _la(te, fin, qs=qs):
                    ins = None
                    for h in range(4):
                        te.matmul(psum[4][:, h * 128:(h + 1) * 128],
                                  Qch[2 * qs + 1][:, h, :], Qch[2 * qs][:, h, :],
                                  start=True, stop=True)
                        ins = te.matmul(psum[5][:, h * 128:(h + 1) * 128],
                                        Qch[2 * qs][:, h, :], Qch[2 * qs + 1][:, h, :],
                                        start=True, stop=True)
                    fin(ins)
                t_la = step(sT, [t_pw], _la)

                def _lb(sc, fin, qs=qs):
                    sc.activation(QchF[2 - 2 * qs][:], psum[4][:, 0:512], AF.Copy)
                    fin(sc.activation(QchF[3 - 2 * qs][:], psum[5][:, 0:512],
                                      AF.Copy))
                t_lb = step(sC, [t_la], _lb)

                def _lc(te, fin, qs=qs, par=par):
                    ins = None
                    for h in range(4):
                        ins = te.matmul(psum[4][:, h * 128:(h + 1) * 128],
                                        Qch[3 - 2 * qs][:, h, :], Gsb[par][:, h, :],
                                        start=True, stop=True)
                    fin(ins)
                t_lc = step(sT, [t_lb], _lc)

                def _ld(v, fin, par=par):
                    fin(v.tensor_tensor(GsbF[par][:], GsbF[par][:],
                                        psum[4][:, 0:512], A.add))
                t_pw = step(sV, [t_lc], _ld)
            t_pre_done = t_pw

            bdeps = [t_pw, t_sev]
            if own:
                bdeps.append(t_fir[("ev", 7)])

            def _zm(te, fin, c0=c0):
                ins = None
                for h in range(4):
                    for kk in range(2):
                        ins = te.matmul(psum[6 + h // 2][:, (h % 2) * 256:
                                                         (h % 2) * 256 + 256],
                                        kfm[:, 2 * h + kk, c0:c0 + 128],
                                        Ssb[:, h, kk * 256:(kk + 1) * 256],
                                        start=(kk == 0), stop=(kk == 1))
                fin(ins)
            t_zm = step(sT, bdeps, _zm)

            def _ze(v, fin, base=base, sub=sub):
                for h in range(4):
                    v.scalar_tensor_tensor(zbuf[:, h, :],
                                           psum[6 + h // 2][:, (h % 2) * 256:
                                                            (h % 2) * 256 + 256],
                                           arsn[:, base + 16 + h:base + 17 + h],
                                           vbt[:, sub, h * 256:(h + 1) * 256],
                                           A.mult, A.add)
                fin(v.tensor_copy(vgscr[0:1, 5:6], vgscr[0:1, 5:6]))
            t_ze = step(sV, [t_zm], _ze)

            def _umm(te, fin, par=par):
                ins = None
                for h in range(4):
                    ins = te.matmul(psum[6 + h // 2][:, (h % 2) * 256:
                                                     (h % 2) * 256 + 256],
                                    Gsb[par][:, h, :], zbuf[:, h, :],
                                    start=True, stop=True)
                fin(ins)
            t_um = step(sT, [t_ze], _umm)

            def _ue(sc, fin):
                sc.activation(ub_sb[:, 0:512], psum[6][:, 0:512], AF.Copy)
                fin(sc.activation(ub_sb[:, 512:1024], psum[7][:, 0:512],
                                  AF.Copy))
            t_ue = step(sC, [t_um], _ue)

            def _sm(te, fin, sub=sub, c0=c0, par=par, own=own, first=first):
                ins = None
                for h in range(4):
                    for kk in range(2):
                        ins = te.matmul(psum[h][:, kk * 256:(kk + 1) * 256],
                                        ktm[:, sub, h * 256 + kk * 128:
                                            h * 256 + kk * 128 + 128],
                                        ubuf[:, h, :], start=first, stop=True)
                if own:
                    for h in range(4):
                        for kk in range(2):
                            te.matmul(psum[6 + h // 2][:, (h % 2) * 256:
                                                       (h % 2) * 256 + 256],
                                      qfm[:, 2 * h + kk, c0:c0 + 128],
                                      Ssb[:, h, kk * 256:(kk + 1) * 256],
                                      start=(kk == 0), stop=False)
                        ins = te.matmul(psum[6 + h // 2][:, (h % 2) * 256:
                                                         (h % 2) * 256 + 256],
                                        attnT[par][:, h, :], ubuf[:, h, :],
                                        start=False, stop=True)
                fin(ins)
            t_su = step(sT, [t_ue], _sm)

            def _se(sc, fin):
                ins = None
                for h in range(4):
                    ins = sc.activation(Ssb[:, h, :], psum[h][:, 0:512], AF.Copy)
                fin(ins)
            t_sev = step(sC, [t_su], _se)

            if own:
                def _de(v, fin, base=base, sub=sub):
                    for h in range(4):
                        v.tensor_scalar(dtm[:, sub, h * 256:(h + 1) * 256],
                                        psum[6 + h // 2][:, (h % 2) * 256:
                                                         (h % 2) * 256 + 256],
                                        arsn[:, base + h:base + h + 1],
                                        None, A.mult)
                    fin(v.tensor_copy(vgscr[0:1, 5:6], vgscr[0:1, 5:6]))
                t_de[sub] = step(sV, [t_su], _de)

        if own:
            t_m = {}
            for sub in range(2):
                def _slrd(sc, fin, sub=sub, ot0=ot0):
                    t0 = ot0 + sub * 128
                    fin(sc.dma_start_transpose(sltm[:, 0, :], sh_hbm[:, t0:t0 + 128]))
                    fin(sc.dma_start_transpose(sltm[:, 1, :], lo_hbm[:, t0:t0 + 128]))
                deps_sr = [t_fir[("w", 7)], t_sev]
                if sub == 1:
                    deps_sr.append(t_m[0])
                t_sr = step(sCQ, deps_sr, _slrd, ndma=2, key="slrd")

                def _mn(v, fin, sub=sub):
                    st = mstg[sub]
                    for (src, col) in ((sltm[:, 0, :], 0), (sltm[:, 1, :], 256),
                                       (dtm[:, sub, :], 512)):
                        v.tensor_tensor(st[:, 2 * col:2 * col + 512]
                                        if False else st[:, col * 2:col * 2 + 2][:, 0:0]
                                        if False else st[:, col:col + 256],
                                        src[:, 0:256], src[:, 256:512], A.add)
                        v.tensor_tensor(o2buf[:].bitcast(BF16)[:, 0:256],
                                        src[:, 512:768], src[:, 768:1024], A.add)
                        VG(v)
                        v.tensor_tensor(st[:, col:col + 256], st[:, col:col + 256],
                                        o2buf[:].bitcast(BF16)[:, 0:256], A.add)
                        VG(v)
                        v.tensor_scalar(st[:, col:col + 256], st[:, col:col + 256],
                                        0.25, None, A.mult)
                    fin(v.tensor_copy(vgscr[0:1, 1:2], vgscr[0:1, 1:2]))
                t_mv = step(sV, [t_sr, t_de[sub]], _mn)

                def _mw(g, fin, sub=sub, ot0=ot0):
                    fin(g.dma_start(
                        out=mn_hbm[ot0 + sub * 128:ot0 + sub * 128 + 128, :],
                        in_=mstg[sub][:, 0:768]))
                t_m[sub] = step(sY, [t_mv], _mw, ndma=1, key="mnw")

            t_gmm = {}
            t_gr = {}
            for kc in range(14):
                def _fgr(g, fin, kc=kc):
                    fin(g.dma_start(out=fgw1r[2 * (kc % 4)][:],
                                    in_=fgw1_bhbm[kc * 128:(kc + 1) * 128, 0:512]))
                    fin(g.dma_start(out=fgw1r[2 * (kc % 4) + 1][:],
                                    in_=fgw1_bhbm[kc * 128:(kc + 1) * 128, 512:1024]))
                depf = [t_fgw1]
                if kc - 4 in t_gmm:
                    depf.append(t_gmm[kc - 4])
                t_fg = step(sY, depf, _fgr, ndma=2, key=f"fgw1{kc % 4}")

                if kc >= 8:
                    def _mr(sc, fin, kc=kc, ot0=ot0):
                        for sub in range(2):
                            fin(sc.dma_start_transpose(
                                grd[:, 2 * (kc % 2) + sub, :],
                                mn_hbm[ot0 + sub * 128:ot0 + sub * 128 + 128,
                                       (kc - 8) * 128:(kc - 7) * 128]))
                    depm = [t_m[0], t_m[1]]
                    if kc - 2 in t_gmm:
                        depm.append(t_gmm[kc - 2])
                    t_gr[kc] = step(sCQ, depm, _mr, ndma=2, key=f"mnr{kc % 2}")

                def _gm(te, fin, kc=kc):
                    ins = None
                    for sub in range(2):
                        if kc < 8:
                            lhsT = xnfm[:, kc, 48 + sub * 128:48 + sub * 128 + 128]
                        else:
                            lhsT = grd[:, 2 * (kc % 2) + sub, :]
                        for nb in range(2):
                            bank = 4 + sub * 2 + nb
                            if kc == 0:
                                te.matmul(psum[bank][:, 0:512], ones[0:1, 0:128],
                                          fgb1_sb[0:1, nb * 512:(nb + 1) * 512],
                                          start=True, stop=False)
                            ins = te.matmul(psum[bank][:, 0:512], lhsT,
                                            fgw1r[2 * (kc % 4) + nb][:],
                                            start=False, stop=(kc == 13))
                    fin(ins)
                depsg = [t_fg, t_sev]
                if kc >= 8:
                    depsg.append(t_gr[kc])
                t_gmm[kc] = step(sT, depsg, _gm)

            t_gh = {}
            for sub in range(2):
                def _ge(sc, fin, sub=sub):
                    ins = None
                    for nb in range(2):
                        bank = 4 + sub * 2 + nb
                        ins = sc.activation(ghst[:, sub, nb * 512:(nb + 1) * 512],
                                            psum[bank][:, 0:512], AF.Gelu)
                    fin(ins)
                t_ge = step(sC, [t_gmm[13], sV.tok()], _ge)

                def _gw(g, fin, sub=sub, ot0=ot0):
                    fin(g.dma_start(
                        out=gh_hbm[ot0 + sub * 128:ot0 + sub * 128 + 128, :],
                        in_=ghst[:, sub, :]))
                t_gh[sub] = step(sY, [t_ge], _gw, ndma=1, key="ghw")

            t_ow = {}
            for sub in range(2):
                def _gr2(sc, fin, sub=sub, ot0=ot0):
                    for kc in range(8):
                        fin(sc.dma_start_transpose(
                            grd[:, kc, :],
                            gh_hbm[ot0 + sub * 128:ot0 + sub * 128 + 128,
                                   kc * 128:(kc + 1) * 128]))
                t_g2 = step(sCQ, [t_gh[sub], sT.tok()], _gr2, ndma=8, key="ghr")

                def _lm(te, fin, sub=sub):
                    te.matmul(psum[4][:, 0:4], ones[0:1, 0:128], fgb2_sb[0:1, 0:4],
                              start=True, stop=False)
                    ins = None
                    for kc in range(8):
                        ins = te.matmul(psum[4][:, 0:4], grd[:, kc, :],
                                        fgw2_sb[:, kc * 4:kc * 4 + 4],
                                        start=False, stop=(kc == 7))
                    fin(ins)
                t_lg = step(sT, [t_g2, t_sp3], _lm)

                def _s1(v, fin):
                    v.tensor_copy(wsm[:, 0:4], psum[4][:, 0:4])
                    VG(v)
                    v.tensor_reduce(wsm[:, 4:5].unsqueeze(2),
                                    wsm[:, 0:4].unsqueeze(1), AX.X, A.max)
                    VG(v)
                    fin(v.tensor_scalar(wsm[:, 5:9], wsm[:, 0:4], wsm[:, 4:5], None,
                                        A.subtract))
                t_s1 = step(sV, [t_lg], _s1)

                def _s2(sc, fin):
                    fin(sc.activation(wsm[:, 9:13], wsm[:, 5:9], AF.Exp,
                                      scale=invt_col[:, 0:1]))
                t_s2 = step(sC, [t_s1], _s2)

                def _s3(v, fin):
                    v.tensor_reduce(wsm[:, 13:14].unsqueeze(2),
                                    wsm[:, 9:13].unsqueeze(1), AX.X, A.add)
                    VG(v)
                    v.reciprocal(wsm[:, 14:15], wsm[:, 13:14])
                    VG(v)
                    v.tensor_scalar(wsm[:, 15:19], wsm[:, 9:13], wsm[:, 14:15],
                                    None, A.mult)
                    VG(v)
                    v.tensor_scalar(wsm[:, 19:20], wsm[:, 18:19], 0.05, None, A.mult)
                    VG(v)
                    v.tensor_scalar(wsm[:, 15:17], wsm[:, 15:17], wsm[:, 19:20],
                                    None, A.max)
                    VG(v)
                    v.tensor_reduce(wsm[:, 20:21].unsqueeze(2),
                                    wsm[:, 15:19].unsqueeze(1), AX.X, A.add)
                    VG(v)
                    v.reciprocal(wsm[:, 21:22], wsm[:, 20:21])
                    VG(v)
                    fin(v.tensor_scalar(wsm[:, 15:19], wsm[:, 15:19], wsm[:, 21:22],
                                        None, A.mult))
                t_s3 = step(sV, [t_s2], _s3)

                def _sl2(sc, fin, sub=sub, ot0=ot0):
                    t0 = ot0 + sub * 128
                    fin(sc.dma_start_transpose(sltm[:, 0, :], sh_hbm[:, t0:t0 + 128]))
                    fin(sc.dma_start_transpose(sltm[:, 1, :], lo_hbm[:, t0:t0 + 128]))
                t_sl2 = step(sCQ, [sV.tok(), t_lg], _sl2, ndma=2, key="slrd")

                def _cb(v, fin, sub=sub):
                    VG(v)
                    v.tensor_scalar(comb[0][:], sltm[:, 1, :], wsm[:, 16:17], None,
                                    A.mult)
                    VG(v)
                    v.scalar_tensor_tensor(comb[1][:], sltm[:, 0, :], wsm[:, 15:16],
                                           comb[0][:], A.mult, A.add)
                    VG(v)
                    v.scalar_tensor_tensor(comb[0][:], dtm[:, sub, :], wsm[:, 17:18],
                                           comb[1][:], A.mult, A.add)
                    VG(v)
                    v.scalar_tensor_tensor(comb[1][:], vtm[:, sub, :], wsm[:, 18:19],
                                           comb[0][:], A.mult, A.add)
                    v.tensor_tensor(comb[0][:], comb[1][:], comb[1][:], A.mult)
                    VG(v)
                    v.tensor_reduce(wsm[:, 4:8].unsqueeze(2),
                                    comb[0][:].rearrange("p (h n) -> p h n",
                                                         h=4, n=256),
                                    AX.X, A.add)
                    VG(v)
                    fin(v.tensor_scalar(wsm[:, 4:8], wsm[:, 4:8], 1.0 / 256.0, 1e-5,
                                        A.mult, A.add))
                t_c1 = step(sV, [t_s3, t_sl2, t_de[sub]], _cb)

                def _rq(sc, fin):
                    fin(sc.activation(wsm[:, 8:12], wsm[:, 4:8], AF.Sqrt))
                t_c2 = step(sC, [t_c1], _rq)

                def _c3(v, fin, sub=sub):
                    v.reciprocal(wsm[:, 8:12], wsm[:, 8:12])
                    VG(v)
                    for h in range(4):
                        v.tensor_scalar(o_tmb[:, sub, h * 256:(h + 1) * 256],
                                        comb[1][:, h * 256:(h + 1) * 256],
                                        wsm[:, 8 + h:9 + h], None, A.mult)
                    fin(v.tensor_copy(vgscr[0:1, 0:1], vgscr[0:1, 0:1]))
                t_c3 = step(sV, [t_c2], _c3)

                def _ow2(g, fin, sub=sub, ot0=ot0):
                    fin(g.dma_start(
                        out=o_hbm[ot0 + sub * 128:ot0 + sub * 128 + 128, :],
                        in_=o_tmb[:, sub, :]))
                t_ow[sub] = step(sY, [t_c3], _ow2, ndma=1, key="ow")

            for sub in range(2):
                def _or(sc, fin, sub=sub, ot0=ot0):
                    for kc in range(8):
                        fin(sc.dma_start_transpose(
                            grd[:, kc, :],
                            o_hbm[ot0 + sub * 128:ot0 + sub * 128 + 128,
                                  kc * 128:(kc + 1) * 128]))
                t_or = step(sCQ, [t_ow[sub], sT.tok()], _or, ndma=8, key="ghr")

                def _op(te, fin, sub=sub):
                    ins = None
                    for kc in range(8):
                        for nb in range(2):
                            ins = te.matmul(psum[4 + nb][:, 0:512], grd[:, kc, :],
                                            oprjT[kc][:, nb * 512:(nb + 1) * 512],
                                            start=(kc == 0), stop=(kc == 7))
                    fin(ins)
                t_op = step(sT, [t_or, t_oprep, t_last_out], _op)

                def _sv(sc, fin):
                    sc.activation(s_fin[:, 0:512], psum[4][:, 0:512], AF.Copy)
                    fin(sc.activation(s_fin[:, 512:1024], psum[5][:, 0:512], AF.Copy))
                t_se2 = step(sC, [t_op], _sv)

                def _xr(sy, fin, sub=sub, tok0=tok0):
                    fin(sy.dma_start(
                        out=x_fin[:],
                        in_=x_in[tok0 + sub * 128:tok0 + sub * 128 + 128, :]))
                t_xf = step(sY, [t_se2], _xr, ndma=1, key="xr")

                def _gr3(g, fin, sub=sub, ot0=ot0):
                    fin(g.dma_start(out=gfin[:, 0:1024],
                                    in_=g_tm[ot0 + sub * 128:ot0 + sub * 128 + 128, :]))
                t_gf = step(sY, [t_se2], _gr3, ndma=1, key="gr3")

                def _fo(v, fin):
                    v.tensor_tensor(outst[:], s_fin[:], gfin[:, 0:1024], A.mult)
                    VG(v)
                    fin(v.tensor_tensor(outst[:], outst[:], x_fin[:], A.add))
                t_fo = step(sV, [t_se2, t_xf, t_gf], _fo)

                def _out(g, fin, sub=sub, ot0=ot0):
                    fin(g.dma_start(
                        out=out[ot0 + sub * 128:ot0 + sub * 128 + 128, :],
                        in_=outst[:]))
                t_last_out = step(sY, [t_fo], _out, ndma=1, key="outw")

            t_tile_prev = t_last_out
        else:
            t_tile_prev = t_sev

    def _fbar(g, fin):
        fin(g.nop())
    _all_d = [(ps.h, ps.n) for ps in _psem.values()]
    step(sG, [t_tile_prev] + B_DONE + _all_d, _fbar)

    pr.emit()
    return nc


_PROGRAM_CACHE = {}


def _get_program():
    if "nc" not in _PROGRAM_CACHE:
        _PROGRAM_CACHE["nc"] = build_program()
    return _PROGRAM_CACHE["nc"]


def make_in_maps(inputs):
    x = np.asarray(inputs["L0_attn_hidden_states"], dtype=np.float32)
    w_fc = np.ascontiguousarray(np.asarray(inputs["L0_ffn_w_fc"], dtype=np.float32))
    b_fc = np.ascontiguousarray(
        np.asarray(inputs["L0_ffn_b_fc"], dtype=np.float32).reshape(1, HID))
    w_pj = np.ascontiguousarray(np.asarray(inputs["L0_ffn_w_proj"], dtype=np.float32))
    b_pj = np.ascontiguousarray(
        np.asarray(inputs["L0_ffn_b_proj"], dtype=np.float32).reshape(1, D))
    ones = np.ones((1, 128), np.float32)
    qw = np.asarray(inputs["L0_attn_q_proj_w"], np.float32)
    kw = np.asarray(inputs["L0_attn_k_proj_w"], np.float32)
    vw = np.asarray(inputs["L0_attn_v_proj_w"], np.float32)
    bw = np.asarray(inputs["L0_attn_b_proj_w"], np.float32)
    wqkvT = np.ascontiguousarray(
        np.concatenate([qw.T, kw.T, vw.T, bw.T], axis=1))
    qc = np.asarray(inputs["L0_attn_q_conv_w"], np.float32)
    kc_ = np.asarray(inputs["L0_attn_k_conv_w"], np.float32)
    vc = np.asarray(inputs["L0_attn_v_conv_w"], np.float32)
    firs = np.asarray(inputs["L0_attn_fir_short_w"], np.float32).reshape(1024, 3)
    firl = np.asarray(inputs["L0_attn_fir_long_w"], np.float32).reshape(1024, 31)

    def convcols(w, ntap):
        outc = np.zeros((128, 32), np.float32)
        for i in range(32):
            for a in range(min(4, ntap)):
                outc[4 * np.arange(32) + a, i] = w[i * 32:i * 32 + 32, a]
        return outc
    cvw = np.zeros((128, 96), np.float32)
    for ci, w in enumerate([qc, kc_, vc]):
        for c in range(8):
            for j in range(4):
                cvw[:, (ci * 8 + c) * 4 + j] = w[c * 128:(c + 1) * 128, j]
    f31cols = np.zeros((128, 256), np.float32)
    for i in range(32):
        for tb in range(8):
            for a in range(4):
                j = 4 * tb + a
                if j <= 30:
                    f31cols[4 * np.arange(32) + a, i * 8 + tb] =                         firl[i * 32:i * 32 + 32, j]
    f3cols = convcols(firs, 3)
    dmask = np.zeros((128, 32), np.float32)
    for a in range(4):
        dmask[4 * np.arange(32) + a, np.arange(32)] = 1.0
    idx = np.arange(128)
    umask = (idx[:, None] < idx[None, :]).astype(np.float32)
    lmask = (idx[:, None] > idx[None, :]).astype(np.float32)
    amask = (idx[:, None] <= idx[None, :]).astype(np.float32)
    eye128 = np.eye(128, dtype=np.float32)
    ones2 = np.ones((128, 64), np.float32)
    fgw1T = np.ascontiguousarray(np.asarray(inputs["L0_attn_fg_w1"], np.float32).T)
    fgb1r = np.ascontiguousarray(
        np.asarray(inputs["L0_attn_fg_b1"], np.float32).reshape(1, 1024))
    fgw2T = np.ascontiguousarray(np.asarray(inputs["L0_attn_fg_w2"], np.float32).T)
    fgb2r = np.ascontiguousarray(
        np.asarray(inputs["L0_attn_fg_b2"], np.float32).reshape(1, 4))
    ltemp = np.asarray(inputs["L0_attn_logit_temp"], np.float32).reshape(1, 1)
    onw = np.asarray(inputs["L0_attn_o_norm_w"], np.float32)
    onwcols = np.zeros((128, 8), np.float32)
    for kc2 in range(8):
        onwcols[:, kc2] = onw[(kc2 * 128 + np.arange(128)) % 256]
    oprojT = np.ascontiguousarray(np.asarray(inputs["L0_attn_o_proj_w"], np.float32).T)
    zeros128 = np.zeros((128, 1024), np.float32)
    attn_ins = {"wqkvT": wqkvT, "cvw": cvw, "f31cols": f31cols,
                "f3cols": f3cols, "dmask": dmask, "umask": umask, "lmask": lmask,
                "amask": amask, "eye128": eye128, "ones2": ones2, "fgw1T": fgw1T,
                "fgb1r": fgb1r, "fgw2T": fgw2T, "fgb2r": fgb2r, "ltemp": ltemp,
                "onwcols": onwcols, "oprojT": oprojT, "zeros128": zeros128}
    in_maps = []
    for c in range(NCORES):
        b, h = c // 2, c % 2
        if h == 0:
            xs = np.concatenate([np.zeros((PRE, D), np.float32), x[b, :TPC]], axis=0)
        else:
            xs = np.ascontiguousarray(x[b])
        m = {"x": xs, "w_fc": w_fc, "b_fc": b_fc,
             "w_proj": w_pj, "b_proj": b_pj, "ones_row": ones}
        m.update(attn_ins)
        in_maps.append(m)
    return in_maps


def kernel(**inputs):
    nc = _get_program()
    in_maps = make_in_maps(inputs)
    res = run_bass_kernel_spmd(nc, in_maps, core_ids=list(range(NCORES)))
    outp = np.empty((B, L, D), np.float32)
    for c in range(NCORES):
        b, h = c // 2, c % 2
        outp[b, h * TPC:(h + 1) * TPC] = res.results[c]["out"]
    return outp


# version 13
# speedup vs baseline: 2.0422x; 1.0931x over previous
import numpy as np
import concourse.bass as bass
import concourse.mybir as mybir
from concourse.bass_utils import run_bass_kernel_spmd

F32 = mybir.dt.float32
BF16 = mybir.dt.bfloat16
A = mybir.AluOpType
AF = mybir.ActivationFunctionType
AX = mybir.AxisListType

B, L, D = 4, 8192, 1024
NCORES = 8
TPC = L // 2
PRE = TPC
XROWS = PRE + TPC
NLN = XROWS // 128
HID = 4096
NTILE = TPC // 128
GTILES = 8
NGRP = NTILE // GTILES
SUBT = 4
ATILE = 256
NAT = XROWS // ATILE
CS = 64
NCH = ATILE // CS
BK = 32
LN_EPS = 1e-5
FP64_GADGET = True

MAGIC23 = float(1.5 * 2 ** 23)
MAGIC33 = float(1.5 * 2 ** 33)
MAGIC52 = float(1.5 * 2 ** 52)


class Sem:
    def __init__(self, nc, name):
        self.h = nc.alloc_semaphore(name)
        self.n = 0


class Prog:
    ENGINES = ("sync", "scalar", "vector", "tensor", "gpsimd")

    def __init__(self, nc):
        self.nc = nc
        self.streams = {e: [] for e in self.ENGINES}

    def on(self, eng):
        def deco(fn):
            self.streams[eng].append(fn)
            return None
        return deco

    def emit(self):
        with self.nc.Block() as blk:
            for eng in self.ENGINES:
                thunks = self.streams[eng]
                getattr(blk, eng)(lambda e, th=thunks: [f(e) for f in th])


def emit_ladder_m(v, am, p, scr, m23, m33, m52, nblk, lanes=4):
    nb4 = nblk // lanes
    sl = [slice(l * nb4, (l + 1) * nb4) for l in range(lanes)]
    for l in range(lanes):
        v.memset(p[:, sl[l]], float(2.0 ** -34))
    for k in (32, 16, 8, 4, 2, 1):
        for l in range(lanes):
            v.tensor_single_scalar(scr[:, sl[l]], p[:, sl[l]], float(2.0 ** k), A.mult)
        for l in range(lanes):
            v.tensor_tensor(scr[:, sl[l]], am[:, sl[l]], scr[:, sl[l]], A.is_ge)
        for l in range(lanes):
            v.tensor_scalar(scr[:, sl[l]], scr[:, sl[l]], float(2.0 ** k - 1.0), 1.0,
                            A.mult, A.add)
        for l in range(lanes):
            v.tensor_tensor(p[:, sl[l]], p[:, sl[l]], scr[:, sl[l]], A.mult)
    for l in range(lanes):
        v.tensor_single_scalar(m23[:, sl[l]], p[:, sl[l]], MAGIC23 / 8.0, A.mult)
    for l in range(lanes):
        v.tensor_single_scalar(m33[:, sl[l]], p[:, sl[l]], MAGIC33 / 8.0, A.mult)
    for l in range(lanes):
        v.tensor_single_scalar(m52[:, sl[l]], p[:, sl[l]], MAGIC52 / 8.0, A.mult)


def emit_quant(v, src_ap, dst_ap, m23, m33, m52, tmp, tmp2, nblk, P=128):
    s3 = src_ap.rearrange("p (b c) -> p b c", b=nblk, c=BK)
    d3 = dst_ap.rearrange("p (b c) -> p b c", b=nblk, c=BK)
    t3 = tmp.rearrange("p (b c) -> p b c", b=nblk, c=BK)
    u3 = tmp2.rearrange("p (b c) -> p b c", b=nblk, c=BK)
    mb23 = m23[0:P, 0:nblk].unsqueeze(2).broadcast_to([P, nblk, BK])
    v.tensor_tensor(t3, s3, mb23, A.add)
    if not FP64_GADGET:
        v.tensor_tensor(d3, t3, mb23, A.subtract)
        return
    v.tensor_tensor(t3, t3, mb23, A.subtract)
    mb52 = m52[0:P, 0:nblk].unsqueeze(2).broadcast_to([P, nblk, BK])
    v.tensor_tensor(u3, s3, mb52, A.add)
    v.tensor_tensor(u3, u3, mb52, A.subtract)
    v.tensor_tensor(t3, t3, u3, A.add)
    mb33 = m33[0:P, 0:nblk].unsqueeze(2).broadcast_to([P, nblk, BK])
    v.tensor_tensor(u3, s3, mb33, A.add)
    v.tensor_tensor(u3, u3, mb33, A.subtract)
    v.tensor_tensor(d3, t3, u3, A.subtract)


def build_program():
    nc = bass.Bass("TRN2", target_bir_lowering=False)
    pr = Prog(nc)

    x_in = nc.dram_tensor("x", [XROWS, D], F32, kind="ExternalInput").ap()
    w_fc = nc.dram_tensor("w_fc", [HID, D], F32, kind="ExternalInput").ap()
    b_fc = nc.dram_tensor("b_fc", [1, HID], F32, kind="ExternalInput").ap()
    w_pj = nc.dram_tensor("w_proj", [D, HID], F32, kind="ExternalInput").ap()
    b_pj = nc.dram_tensor("b_proj", [1, D], F32, kind="ExternalInput").ap()
    ones_in = nc.dram_tensor("ones_row", [1, 128], F32, kind="ExternalInput").ap()
    wqkv_in = nc.dram_tensor("wqkvT", [D, 3076], F32, kind="ExternalInput").ap()
    cvw_in = nc.dram_tensor("cvw", [128, 96], F32, kind="ExternalInput").ap()
    f31cols_in = nc.dram_tensor("f31cols", [128, 256], F32, kind="ExternalInput").ap()
    f3cols_in = nc.dram_tensor("f3cols", [128, 32], F32, kind="ExternalInput").ap()
    dmask_in = nc.dram_tensor("dmask", [128, 32], F32, kind="ExternalInput").ap()
    umask_in = nc.dram_tensor("umask", [128, 128], F32, kind="ExternalInput").ap()
    lmask_in = nc.dram_tensor("lmask", [128, 128], F32, kind="ExternalInput").ap()
    amask_in = nc.dram_tensor("amask", [128, 128], F32, kind="ExternalInput").ap()
    eye_in = nc.dram_tensor("eye128", [128, 128], F32, kind="ExternalInput").ap()
    ones2_in = nc.dram_tensor("ones2", [128, 64], F32, kind="ExternalInput").ap()
    fgw1_in = nc.dram_tensor("fgw1T", [1792, 1024], F32, kind="ExternalInput").ap()
    fgb1_in = nc.dram_tensor("fgb1r", [1, 1024], F32, kind="ExternalInput").ap()
    fgw2_in = nc.dram_tensor("fgw2T", [1024, 4], F32, kind="ExternalInput").ap()
    fgb2_in = nc.dram_tensor("fgb2r", [1, 4], F32, kind="ExternalInput").ap()
    ltemp_in = nc.dram_tensor("ltemp", [1, 1], F32, kind="ExternalInput").ap()
    onw_in = nc.dram_tensor("onwcols", [128, 8], F32, kind="ExternalInput").ap()
    oprj_in = nc.dram_tensor("oprojT", [D, D], F32, kind="ExternalInput").ap()
    zeros_in = nc.dram_tensor("zeros128", [128, 1024], F32, kind="ExternalInput").ap()
    out = nc.dram_tensor("out", [TPC, D], F32, kind="ExternalOutput").ap()
    q_hbm = nc.dram_tensor("q_hbm", [D, XROWS], BF16).ap()
    k_hbm = nc.dram_tensor("k_hbm", [D, XROWS], BF16).ap()
    v_hbm = nc.dram_tensor("v_hbm", [D, XROWS + 8], BF16).ap()
    sh_hbm = nc.dram_tensor("sh_hbm", [D, TPC], BF16).ap()
    lo_hbm = nc.dram_tensor("lo_hbm", [D, TPC], BF16).ap()
    mn_hbm = nc.dram_tensor("mn_hbm", [TPC, 768], BF16).ap()
    gh_hbm = nc.dram_tensor("gh_hbm", [TPC, 1024], BF16).ap()
    o_hbm = nc.dram_tensor("o_hbm", [TPC, D], BF16).ap()

    xn_tm = nc.dram_tensor("xn_tm", [64 + XROWS, D], BF16).ap()
    g_tm = nc.dram_tensor("g_tm", [TPC, D], BF16).ap()
    xnq_tm = nc.dram_tensor("xnq_tm", [TPC, D], BF16).ap()
    hq_tm = nc.dram_tensor("hq_tm", [TPC, HID], BF16).ap()
    wqfc_hbm = nc.dram_tensor("wqfc_hbm", [HID, D], BF16).ap()
    wqpj_hbm = nc.dram_tensor("wqpj_hbm", [D, HID], BF16).ap()

    sb = nc.alloc_sbuf_tensor
    wbig = sb("wbig", [128, 8 * HID], BF16)
    bfc_row = sb("bfc_row", [1, HID], BF16)
    bpj_row = sb("bpj_row", [1, D], BF16)
    ones = sb("ones", [1, 128], BF16)

    NX = 2
    xt = [sb(f"xt{i}", [128, D], F32) for i in range(NX)]
    xnt = [sb(f"xnt{i}", [128, D], F32) for i in range(NX)]
    xnb = [sb(f"xnb{i}", [128, D], BF16) for i in range(NX)]
    xqb = [sb(f"xqb{i}", [128, D], BF16) for i in range(NX)]
    lnscr = sb("lnscr", [128, D], F32)
    qt1 = sb("qt1", [128, D], F32)
    qt2 = sb("qt2", [128, D], F32)
    stat = [sb(f"stat{i}", [128, 12], F32) for i in range(NX)]
    xam = [sb(f"xam{i}", [128, 32], F32) for i in range(NX)]
    xp = [sb(f"xp{i}", [128, 32], F32) for i in range(NX)]
    xscr = sb("xscr", [128, 32], F32)
    xm23 = [sb(f"xm23{i}", [128, 32], F32) for i in range(NX)]
    xm33 = [sb(f"xm33{i}", [128, 32], F32) for i in range(NX)]
    xm52 = [sb(f"xm52{i}", [128, 32], F32) for i in range(NX)]

    xnqf = sb("xnqf", [128, 8 * 1024], BF16)
    hbuf = sb("hbuf", [128, 2048], F32)
    hqb = sb("hqb", [128, 2048], BF16)
    ht1 = sb("ht1", [128, 2048], F32)
    ht2 = sb("ht2", [128, 2048], F32)
    NH = 2
    ham = [sb(f"ham{i}", [128, 64], F32) for i in range(NH)]
    hp = [sb(f"hp{i}", [128, 64], F32) for i in range(NH)]
    hscr = sb("hscr", [128, 64], F32)
    hm23 = [sb(f"hm23{i}", [128, 64], F32) for i in range(NH)]
    hm33 = [sb(f"hm33{i}", [128, 64], F32) for i in range(NH)]
    hm52 = [sb(f"hm52{i}", [128, 64], F32) for i in range(NH)]

    NHQ = 8
    hqT = [sb(f"hqT{i}", [128, SUBT * 128], BF16) for i in range(NHQ)]
    gbuf = [sb(f"gbuf{i}", [128, D], F32) for i in range(SUBT)]
    x2 = [sb(f"x2_{i}", [128, D], F32) for i in range(2)]
    obuf = [sb(f"obuf{i}", [128, D], F32) for i in range(2)]

    psum = [nc.alloc_psum_tensor(f"ps{i}", [128, 512], F32) for i in range(8)]

    wsrc = ht1
    wt1 = ht2
    wt2 = hbuf
    wout = hqb
    wam, wp_, wscr = ham[0], hp[0], hscr
    wm23, wm33, wm52 = hm23[0], hm33[0], hm52[0]

    wfcT_v = wbig[:].rearrange("p (c n) -> p c n", c=8, n=HID)
    wpjT_v = wbig[:].rearrange("p (c n) -> p c n", c=32, n=D)
    xnqf_v = xnqf[:].rearrange("p (c n) -> p c n", c=8, n=1024)

    S = lambda name: Sem(nc, name)
    s_cst = S("s_cst")
    s_wld = S("s_wld"); s_wam = S("s_wam"); s_wpool = S("s_wpool")
    s_wqd = S("s_wqd"); s_wst = S("s_wst"); s_wtr = S("s_wtr"); s_wtr2 = S("s_wtr2")
    s_xp = [S("s_xp0"), S("s_xp1")]
    s_xstp = [S("s_xstp0"), S("s_xstp1")]
    s_htr8 = [S(f"s_htr8_{i}") for i in range(8)]
    s_x = S("s_x"); s_st1 = S("s_st1"); s_pmath = S("s_pmath"); s_sqrt = S("s_sqrt")
    s_lnv = S("s_lnv"); s_xpool = S("s_xpool"); s_xq = S("s_xq"); s_xst = S("s_xst")
    s_xtr = S("s_xtr"); s_fc = S("s_fc"); s_gel = S("s_gel")
    s_ham = S("s_ham"); s_hpool = S("s_hpool"); s_hq = S("s_hq"); s_hst = S("s_hst")
    s_htr = S("s_htr"); s_pjk = S("s_pjk"); s_sil = S("s_sil")
    s_x2 = S("s_x2"); s_ores = S("s_ores"); s_out = S("s_out")

    @pr.on("gpsimd")
    def _(g):
        g.dma_start(out=ones[:], in_=ones_in[:]).then_inc(s_cst.h, 16)
        g.dma_start(out=bfc_row[:], in_=b_fc[:]).then_inc(s_cst.h, 16)
        g.dma_start(out=bpj_row[:], in_=b_pj[:]).then_inc(s_cst.h, 16)
    s_cst.n += 48

    WTILES = [("fc", i, 0, 1024, 32) for i in range(32)] + \
             [("pj", i, hh, 2048, 64) for i in range(8) for hh in range(2)]
    wqd_done = {}

    for wi, (kind, i, hh, ncol, nblk) in enumerate(WTILES):
        src = w_fc if kind == "fc" else w_pj
        dst = wqfc_hbm if kind == "fc" else wqpj_hbm
        col0 = hh * 2048

        @pr.on("sync")
        def _(sy, src=src, i=i, col0=col0, ncol=ncol, wi=wi):
            if wi >= 1:
                sy.wait_ge(s_wqd.h, wqd_done[wi - 1])
            sy.dma_start(out=wsrc[:, 0:ncol],
                         in_=src[i * 128:(i + 1) * 128, col0:col0 + ncol]).then_inc(s_wld.h, 16)
        s_wld.n += 16

        @pr.on("vector")
        def _(v, wld_n=s_wld.n, nblk=nblk, ncol=ncol):
            v.wait_ge(s_wld.h, wld_n)
            v.tensor_reduce(wam[:, 0:nblk].unsqueeze(2),
                            wsrc[:, 0:ncol].rearrange("p (b c) -> p b c", b=nblk, c=BK),
                            AX.X, A.max, apply_absolute_value=True)
            emit_ladder_m(v, wam, wp_, wscr, wm23, wm33, wm52, nblk)
            emit_quant(v, wsrc[:, 0:ncol], wout[:, 0:ncol],
                       wm23, wm33, wm52, wt1[:, 0:ncol], wt2[:, 0:ncol], nblk)
            v.tensor_copy(wscr[0:1, 1:2], wscr[0:1, 1:2]).then_inc(s_wqd.h, 1)
        s_wqd.n += 1
        wqd_done[wi] = s_wqd.n

        @pr.on("gpsimd")
        def _(g, dst=dst, i=i, col0=col0, ncol=ncol, wqd_n=s_wqd.n):
            g.wait_ge(s_wqd.h, wqd_n)
            g.dma_start(out=dst[i * 128:(i + 1) * 128, col0:col0 + ncol],
                        in_=wout[:, 0:ncol]).then_inc(s_wst.h, 16)
        s_wst.n += 16

    @pr.on("scalar")
    def _(sc, wst_n=s_wst.n):
        sc.wait_ge(s_wst.h, wst_n)
        for fc_ in range(8):
            sc.dma_start_transpose(wfcT_v[:, fc_, :],
                                   wqfc_hbm[:, fc_ * 128:(fc_ + 1) * 128]).then_inc(s_wtr.h, 16)
    s_wtr.n += 8 * 16

    lnv_done = {}
    xq_done = {}
    xst_done = {}

    for t in range(NLN):
        bi = t % NX
        row0 = t * 128
        own = t >= NLN - NTILE
        orow0 = (t - (NLN - NTILE)) * 128

        @pr.on("sync")
        def _(sy, t=t, bi=bi, row0=row0):
            if t >= NX:
                sy.wait_ge(s_xq.h, xq_done[t - NX])
            sy.dma_start(out=xt[bi][:],
                         in_=x_in[row0:row0 + 128, :]).then_inc(s_xp[bi].h, 16)
        s_xp[bi].n += 16

        @pr.on("scalar")
        def _(sc, t=t, bi=bi, x_n=s_xp[bi].n):
            sc.wait_ge(s_xp[bi].h, x_n)
            if t >= NX:
                sc.wait_ge(s_lnv.h, lnv_done[t - NX])
            sc.activation(lnscr[:], xt[bi][:], AF.Copy, accum_out=stat[bi][:, 0:1])
            sc.activation(lnscr[:], xt[bi][:], AF.Square,
                          accum_out=stat[bi][:, 1:2]).then_inc(s_st1.h, 1)
        s_st1.n += 1

        @pr.on("gpsimd")
        def _(g, bi=bi, st1_n=s_st1.n):
            g.wait_ge(s_st1.h, st1_n)
            g.tensor_single_scalar(stat[bi][:, 2:3], stat[bi][:, 0:1], 1.0 / D, A.mult)
            g.tensor_single_scalar(stat[bi][:, 3:4], stat[bi][:, 1:2], 1.0 / D, A.mult)
            g.tensor_tensor(stat[bi][:, 4:5], stat[bi][:, 2:3], stat[bi][:, 2:3], A.mult)
            g.tensor_tensor(stat[bi][:, 3:4], stat[bi][:, 3:4], stat[bi][:, 4:5], A.subtract)
            g.tensor_single_scalar(stat[bi][:, 3:4], stat[bi][:, 3:4], LN_EPS,
                                   A.add).then_inc(s_pmath.h, 1)
        s_pmath.n += 1

        @pr.on("scalar")
        def _(sc, bi=bi, pmath_n=s_pmath.n):
            sc.wait_ge(s_pmath.h, pmath_n)
            sc.activation(stat[bi][:, 5:6], stat[bi][:, 3:4], AF.Sqrt).then_inc(s_sqrt.h, 1)
        s_sqrt.n += 1

        @pr.on("vector")
        def _(v, t=t, bi=bi, sqrt_n=s_sqrt.n):
            v.wait_ge(s_sqrt.h, sqrt_n)
            v.reciprocal(stat[bi][:, 6:7], stat[bi][:, 5:6])
            v.tensor_copy(xscr[0:1, 0:1], xscr[0:1, 0:1])
            v.tensor_copy(xscr[0:1, 1:2], xscr[0:1, 1:2])
            v.tensor_copy(xscr[0:1, 2:3], xscr[0:1, 2:3])
            v.tensor_scalar(xnt[bi][:], xt[bi][:], stat[bi][:, 2:3], stat[bi][:, 6:7],
                            A.subtract, A.mult)
            v.tensor_copy(xnb[bi][:], xnt[bi][:])
            v.tensor_reduce(xam[bi][:, 0:32].unsqueeze(2),
                            xnt[bi][:].rearrange("p (b c) -> p b c", b=32, c=BK),
                            AX.X, A.max, apply_absolute_value=True).then_inc(s_lnv.h, 1)
        s_lnv.n += 1
        lnv_done[t] = s_lnv.n

        @pr.on("vector")
        def _(v, t=t, bi=bi):
            if t >= NX:
                _xd = xst_done[t - NX]
                v.wait_ge(s_xstp[0].h, _xd[0])
                v.wait_ge(s_xstp[1].h, _xd[1])
            emit_ladder_m(v, xam[bi], xp[bi], xscr, xm23[bi], xm33[bi], xm52[bi], 32)
            emit_quant(v, xnt[bi][:], xqb[bi][:], xm23[bi], xm33[bi], xm52[bi],
                       qt1[:], qt2[:], 32)
            v.tensor_copy(xscr[0:1, 4:5], xscr[0:1, 4:5]).then_inc(s_xq.h, 1)
        s_xq.n += 1
        xq_done[t] = s_xq.n

        @pr.on("gpsimd")
        def _(g, bi=bi, row0=row0, own=own, orow0=orow0, xq_n=s_xq.n):
            g.wait_ge(s_xq.h, xq_n)
            g.dma_start(out=xn_tm[64 + row0:64 + row0 + 128, :],
                        in_=xnb[bi][:]).then_inc(s_xstp[bi].h, 16)
            if own:
                g.dma_start(out=xnq_tm[orow0:orow0 + 128, :],
                            in_=xqb[bi][:]).then_inc(s_xstp[bi].h, 16)
        s_xstp[bi].n += 16 + (16 if own else 0)
        xst_done[t] = (s_xstp[0].n, s_xstp[1].n)

    fc_done = {}
    gel_done = {}
    hq_done = {}
    hst_done = {}
    xtr_done = {}
    hpool_done = {}

    def prev_half(key):
        grp, tl, half = key
        if half:
            return (grp, tl, 0)
        if tl:
            return (grp, tl - 1, 1)
        if grp:
            return (grp - 1, GTILES - 1, 1)
        return None

    for grp in range(NGRP):
        @pr.on("scalar")
        def _(sc, grp=grp, wtr_n=s_wtr.n, xst_n=None):
            _xd = xst_done[(NLN - NTILE) + grp * GTILES + GTILES - 1]
            sc.wait_ge(s_xstp[0].h, _xd[0])
            sc.wait_ge(s_xstp[1].h, _xd[1])
            sc.wait_ge(s_wtr.h, wtr_n)
            if grp >= 1:
                sc.wait_ge(s_fc.h, fc_done[(grp - 1, GTILES - 1, 1)])
            for fc_ in range(8):
                sc.dma_start_transpose(
                    xnqf_v[:, fc_, :],
                    xnq_tm[grp * 1024:(grp + 1) * 1024,
                           fc_ * 128:(fc_ + 1) * 128]).then_inc(s_xtr.h, 16)
        s_xtr.n += 8 * 16
        xtr_done[grp] = s_xtr.n

        for tl in range(GTILES):
            tok = tl * 128
            for half in range(2):
                key = (grp, tl, half)

                @pr.on("tensor")
                def _(te, key=key, tok=tok, half=half, xtr_n=xtr_done[grp]):
                    te.wait_ge(s_xtr.h, xtr_n)
                    te.wait_ge(s_cst.h, 48)
                    pk = prev_half(key)
                    if pk in gel_done:
                        te.wait_ge(s_gel.h, gel_done[pk])
                    for nb in range(4):
                        hid0 = half * 2048 + nb * 512
                        te.matmul(psum[nb][:], ones[:],
                                  bfc_row[0:1, hid0:hid0 + 512], start=True, stop=False)
                    ins = None
                    for kc in range(8):
                        for nb in range(4):
                            hid0 = half * 2048 + nb * 512
                            ins = te.matmul(psum[nb][:],
                                            xnqf_v[:, kc, tok:tok + 128],
                                            wfcT_v[:, kc, hid0:hid0 + 512],
                                            start=False, stop=(kc == 7))
                    ins.then_inc(s_fc.h, 1)
                s_fc.n += 1
                fc_done[key] = s_fc.n

                @pr.on("scalar")
                def _(sc, key=key, fc_n=s_fc.n):
                    sc.wait_ge(s_fc.h, fc_n)
                    pk = prev_half(key)
                    if pk in hq_done:
                        sc.wait_ge(s_hq.h, hq_done[pk])
                    for nb in range(4):
                        sc.activation(hbuf[:, nb * 512:(nb + 1) * 512], psum[nb][:],
                                      AF.Gelu).then_inc(s_gel.h, 1)
                s_gel.n += 4
                gel_done[key] = s_gel.n

                @pr.on("vector")
                def _(v, key=key, half=half, gel_n=s_gel.n):
                    v.wait_ge(s_gel.h, gel_n)
                    pk = prev_half(key)
                    if pk in hst_done:
                        v.wait_ge(s_hst.h, hst_done[pk])
                    v.tensor_reduce(ham[half][:, 0:64].unsqueeze(2),
                                    hbuf[:].rearrange("p (b c) -> p b c", b=64, c=BK),
                                    AX.X, A.max, apply_absolute_value=True)
                    emit_ladder_m(v, ham[half], hp[half], hscr,
                                  hm23[half], hm33[half], hm52[half], 64)
                    emit_quant(v, hbuf[:], hqb[:], hm23[half], hm33[half], hm52[half],
                               ht1[:], ht2[:], 64)
                    v.tensor_copy(hscr[0:1, 1:2], hscr[0:1, 1:2]).then_inc(s_hq.h, 1)
                s_hq.n += 1
                hq_done[key] = s_hq.n

                @pr.on("gpsimd")
                def _(g, key=key, half=half, hq_n=s_hq.n):
                    grp_, tl_, _ = key
                    g.wait_ge(s_hq.h, hq_n)
                    r0 = grp_ * 1024 + tl_ * 128
                    g.dma_start(out=hq_tm[r0:r0 + 128, half * 2048:(half + 1) * 2048],
                                in_=hqb[:]).then_inc(s_hst.h, 16)
                s_hst.n += 16
                hst_done[key] = s_hst.n

    last_fc = fc_done[(NGRP - 1, GTILES - 1, 1)]
    last_gel = gel_done[(NGRP - 1, GTILES - 1, 1)]

    @pr.on("scalar")
    def _(sc, last_fc=last_fc):
        sc.wait_ge(s_fc.h, last_fc)
        for hc in range(32):
            sc.dma_start_transpose(wpjT_v[:, hc, :],
                                   wqpj_hbm[:, hc * 128:(hc + 1) * 128]).then_inc(s_wtr2.h, 16)
    s_wtr2.n += 32 * 16

    htr_done = {}
    pjk_done = {}
    sil_done = {}
    ores_done = {}
    sgi = 0
    ckidx = 0

    for grp in range(NGRP):
        for sub in range(GTILES // SUBT):
            tok0 = sub * SUBT * 128
            ck0 = ckidx
            for kc in range(32):
                @pr.on("scalar")
                def _(sc, grp=grp, kc=kc, tok0=tok0, ck=ckidx,
                      hst_n=hst_done[(grp, GTILES - 1, 1)]):
                    sc.wait_ge(s_hst.h, hst_n)
                    if ck >= NHQ:
                        sc.wait_ge(s_pjk.h, pjk_done[ck - NHQ])
                    sc.dma_start_transpose(
                        hqT[ck % NHQ][:],
                        hq_tm[grp * 1024 + tok0:grp * 1024 + tok0 + SUBT * 128,
                              kc * 128:(kc + 1) * 128]).then_inc(s_htr8[ck % 8].h, 16)
                s_htr8[ckidx % 8].n += 16
                htr_done[ckidx] = (ckidx % 8, s_htr8[ckidx % 8].n)
                ckidx += 1

            @pr.on("tensor")
            def _(te, sgi=sgi, ck0=ck0, last_gel=last_gel, wtr2_n=s_wtr2.n):
                te.wait_ge(s_wtr2.h, wtr2_n)
                if sgi == 0:
                    te.wait_ge(s_gel.h, last_gel)
                else:
                    te.wait_ge(s_sil.h, sil_done[sgi - 1])
                for st_ in range(SUBT):
                    for pb in range(2):
                        te.matmul(psum[st_ * 2 + pb][:], ones[:],
                                  bpj_row[0:1, pb * 512:(pb + 1) * 512],
                                  start=True, stop=False)
                for kc in range(32):
                    _hs, _hn = htr_done[ck0 + kc]
                    te.wait_ge(s_htr8[_hs].h, _hn)
                    ins = None
                    for st_ in range(SUBT):
                        for pb in range(2):
                            ins = te.matmul(psum[st_ * 2 + pb][:],
                                            hqT[(ck0 + kc) % NHQ][:, st_ * 128:st_ * 128 + 128],
                                            wpjT_v[:, kc, pb * 512:(pb + 1) * 512],
                                            start=False, stop=(kc == 31))
                    ins.then_inc(s_pjk.h, 1)
            for kc in range(32):
                pjk_done[ck0 + kc] = s_pjk.n + kc + 1
            s_pjk.n += 32

            @pr.on("scalar")
            def _(sc, sgi=sgi, pjk_n=s_pjk.n):
                sc.wait_ge(s_pjk.h, pjk_n)
                if sgi >= 1:
                    sc.wait_ge(s_ores.h, ores_done[sgi * SUBT - 1])
                for st_ in range(SUBT):
                    for pb in range(2):
                        sc.activation(gbuf[st_][:, pb * 512:(pb + 1) * 512],
                                      psum[st_ * 2 + pb][:], AF.Silu).then_inc(s_sil.h, 1)
            s_sil.n += 2 * SUBT
            sil_done[sgi] = s_sil.n

            for st_ in range(SUBT):
                t_own = grp * GTILES + sub * SUBT + st_

                @pr.on("vector")
                def _(v, sgi=sgi, st_=st_, t_own=t_own, sil_n=sil_done[sgi]):
                    v.wait_ge(s_sil.h, sil_n)
                    if t_own >= 2:
                        v.wait_ge(s_out.h, (t_own - 1) * 16)
                    v.tensor_copy(obuf[st_ % 2][:].bitcast(BF16)[:, 0:D],
                                  gbuf[st_][:]).then_inc(s_ores.h, 1)
                s_ores.n += 1
                ores_done[t_own] = s_ores.n

                @pr.on("gpsimd")
                def _(g, t_own=t_own, st_=st_, ores_n=s_ores.n):
                    g.wait_ge(s_ores.h, ores_n)
                    r0 = t_own * 128
                    g.dma_start(out=g_tm[r0:r0 + 128, :],
                                in_=obuf[st_ % 2][:].bitcast(BF16)[:, 0:D]).then_inc(s_out.h, 16)
                s_out.n += 16
            sgi += 1

    class St:
        def __init__(self, name, eng, dma):
            self.name, self.eng, self.dma = name, eng, dma
            self.sem = nc.alloc_semaphore("tk_" + name)
            if dma:
                self.dsem = nc.alloc_semaphore("tkd_" + name)
            self.n = 0
            self.q = 0

        def tok(self):
            return (self.sem, self.n)

    sT = St("T", "tensor", False)
    sV = St("V", "vector", False)
    sC = St("C", "scalar", False)
    sCQ = St("CQ", "scalar", True)
    sG = St("G", "gpsimd", False)
    sGQ = St("GQ", "gpsimd", True)
    sY = St("Y", "sync", True)

    _psem = {}

    def step(stream, deps, body, ndma=0, key=None):
        if stream.dma:
            key = key or (stream.name + "_d")
            kk = (stream.name, key)
            if kk not in _psem:
                _psem[kk] = Sem(nc, "pd_" + stream.name + "_" + key)
            ps = _psem[kk]
            ps.n += 16 * ndma
            post = ps.n
            sem = ps.h
        else:
            stream.n += 1
            post = stream.n
            sem = stream.sem
        isdma = stream.dma

        def thunk(e, deps=tuple(deps), body=body, sem=sem):
            for (s, c) in deps:
                if c > 0:
                    e.wait_ge(s, c)

            def fin(instr):
                instr.then_inc(sem, 16 if isdma else 1)
                return instr
            body(e, fin)
        pr.streams[stream.eng].append(thunk)
        return (sem, post)

    B_DONE = [(s_xstp[0].h, s_xstp[0].n), (s_xstp[1].h, s_xstp[1].n),
              (s_out.h, s_out.n)]

    wbigb = wbig[:]
    wqkvT = wbigb[:, 0:24608].rearrange("p (c n) -> p c n", c=8, n=3076)
    f3st = wbigb[:, 27680:28704].rearrange("p (i n) -> p i n", i=32, n=32)
    gfin = wbigb[:, 28704:29728]
    o_tmb = wbigb[:, 29728:31776].rearrange("p (s n) -> p s n", s=2, n=1024)
    wstage = wbigb[:, 31776:32768]
    f31st = xnqf[:].rearrange("p (i n) -> p i n", i=256, n=32)
    XW = 304
    xnfm = hbuf[:].bitcast(BF16).rearrange("p (c n) -> p c n", c=8, n=512)
    ht1b = ht1[:].bitcast(BF16)
    vst = ht1b[:, 1216:1816].rearrange("p (r n) -> p r n", r=2, n=300)
    ht2b = ht2[:].bitcast(BF16)
    cta = ht2[:, 0:304]
    ctb = ht2[:, 304:608]
    cvo = [ht2[:, 608:912], ht2[:, 912:1216]]
    repf = ht2b[:, 2432:3568].rearrange("p (g n) -> p g n", g=4, n=284)
    qfm = xt[0][:].bitcast(BF16).rearrange("p (c n) -> p c n", c=8, n=256)
    kfm = xt[1][:].bitcast(BF16).rearrange("p (c n) -> p c n", c=8, n=256)
    qtm = qt1[:].bitcast(BF16).rearrange("p (s n) -> p s n", s=2, n=1024)
    ktm = qt2[:].bitcast(BF16).rearrange("p (s n) -> p s n", s=2, n=1024)
    vtm = obuf[0][:].bitcast(BF16).rearrange("p (s n) -> p s n", s=2, n=1024)
    dtm = obuf[1][:].bitcast(BF16).rearrange("p (s n) -> p s n", s=2, n=1024)
    vbt = lnscr[:].bitcast(BF16).rearrange("p (s n) -> p s n", s=2, n=1024)
    ghst = vbt
    Ssb = x2[0][:].bitcast(BF16).rearrange("p (h n) -> p h n", h=4, n=512)
    scb = x2[1][:].bitcast(BF16)
    zbuf = scb[:, 0:1024].rearrange("p (h n) -> p h n", h=4, n=256)
    cmt = scb[:, 1024:2048].rearrange("p (i n) -> p i n", i=8, n=128)
    slevv = xnb[1][:].rearrange("p (r n) -> p r n", r=4, n=256)
    mstg = [xqb[0][:], xqb[1][:]]
    grd = xnb[0][:].rearrange("p (r n) -> p r n", r=8, n=128)
    sltm = hqb[:].rearrange("p (i n) -> p i n", i=2, n=1024)
    x_fin = xnt[1][:]
    comb = [xnt[0][:].bitcast(BF16)[:, 0:1024],
            xnt[0][:].bitcast(BF16)[:, 1024:2048]]
    o2buf = xqb[0][:]
    oprjT = [gbuf[i // 2][:].bitcast(BF16)[:, (i % 2) * 1024:(i % 2 + 1) * 1024]
             for i in range(8)]
    fgw1r = [hqT[i][:] for i in range(8)]
    ofs32 = qt1[:]
    zrof = qt2[:]

    arsn = sb("arsn", [128, 48], F32)
    wsm = sb("wsm", [128, 24], F32)
    s_fin = sb("s_fin", [128, D], F32)
    outst = lnscr[:]
    vgscr = sb("vgscr", [128, 8], F32)
    cst128 = sb("cst128", [128, 384], F32)
    eye128 = sb("eye128_sb", [128, 128], F32)
    ones2 = sb("ones2_sb", [128, 64], BF16)
    dmask_sb = sb("dmask_s2", [128, 32], F32)
    cvw = sb("cvw_sb", [128, 96], F32)
    ccols = s_fin[:, 0:384]
    onw_sb = sb("onw_sb", [128, 8], F32)
    fgb1_sb = sb("fgb1_sb", [1, 1024], BF16)
    fgb2_sb = sb("fgb2_sb", [1, 4], BF16)
    fgw2_sb = sb("fgw2_sb", [128, 32], BF16)
    invt = sb("invt", [1, 8], F32)
    invt_col = sb("invt_col", [128, 1], F32)
    qch_sb = ht1b[:, 1824:3872]
    Qch = [qch_sb[:, i * 512:(i + 1) * 512].rearrange("p (h n) -> p h n",
                                                      h=4, n=128)
           for i in range(4)]
    QchF = [qch_sb[:, i * 512:(i + 1) * 512] for i in range(4)]
    sfb = s_fin[:].bitcast(BF16)
    Gsb = [sfb[:, i * 512:(i + 1) * 512].rearrange("p (h n) -> p h n",
                                                   h=4, n=128)
           for i in range(2)]
    GsbF = [sfb[:, i * 512:(i + 1) * 512] for i in range(2)]
    attnT = [sfb[:, 1024 + i * 512:1024 + (i + 1) * 512]
             .rearrange("p (h n) -> p h n", h=4, n=128)
             for i in range(2)]
    ub_sb = gfin
    ubuf = ub_sb.rearrange("p (h n) -> p h n", h=4, n=256)
    rowb = wbigb[0:1, 29728:30752]
    eye_bf = ht2b[:, 3936:4064]

    fgw1_bhbm = nc.dram_tensor("fgw1_bhbm", [1792, 1024], BF16).ap()
    pre_hbm = nc.dram_tensor("pre_hbm", [512, 304], BF16).ap()
    invt_hbm = nc.dram_tensor("invt_hbm", [1, 1], F32).ap()

    def VG(v, n=3):
        for i in range(n):
            v.tensor_copy(vgscr[0:1, i:i + 1], vgscr[0:1, i:i + 1])

    def _wq_loads(g, fin):
        for kc in range(8):
            fin(g.dma_start(out=wqkvT[:, kc, :],
                            in_=wqkv_in[kc * 128:(kc + 1) * 128, :]))
        fin(g.dma_start(out=fgb1_sb[:], in_=fgb1_in[:]))
        fin(g.dma_start(out=fgb2_sb[:], in_=fgb2_in[:]))
        for kc in range(8):
            fin(g.dma_start(out=fgw2_sb[:, kc * 4:kc * 4 + 4],
                            in_=fgw2_in[kc * 128:(kc + 1) * 128, :]))
        fin(g.dma_start(out=ones2[:], in_=ones2_in[:]))
    t_wq = step(sGQ, B_DONE, _wq_loads, ndma=19, key="wld")

    def _consts(sy, fin):
        fin(sy.dma_start(out=cst128[:, 0:128], in_=umask_in[:]))
        fin(sy.dma_start(out=cst128[:, 128:256], in_=lmask_in[:]))
        fin(sy.dma_start(out=cst128[:, 256:384], in_=amask_in[:]))
        fin(sy.dma_start(out=eye128[:], in_=eye_in[:]))
        fin(sy.dma_start(out=dmask_sb[:], in_=dmask_in[:]))
        fin(sy.dma_start(out=cvw[:], in_=cvw_in[:]))
        fin(sy.dma_start(out=ccols[:, 96:352], in_=f31cols_in[:]))
        fin(sy.dma_start(out=ccols[:, 352:384], in_=f3cols_in[:]))
        fin(sy.dma_start(out=onw_sb[:], in_=onw_in[:]))
        fin(sy.dma_start(out=invt[0:1, 0:1], in_=ltemp_in[:]))
        fin(sy.dma_start(out=xn_tm[0:64, :],
                         in_=zeros_in[0:64, 0:512].bitcast(BF16)))
        zb = zeros_in[0:128, :].bitcast(BF16)
        VROW = XROWS + 8
        for ch in range(8):
            i_ap = bass.AP(zb.tensor, 0, [[2048, 128], [0, 5], [1, 1640]])
            o_ap = bass.AP(v_hbm.tensor, ch * 128 * VROW,
                           [[VROW, 128], [1640, 5], [1, 1640]])
            fin(sy.dma_start(out=o_ap, in_=i_ap))
    t_cst = step(sY, B_DONE, _consts, ndma=19, key="cst")

    def _convst(v, fin):
        for i in range(256):
            v.tensor_scalar(f31st[:, i, :], dmask_sb[:], ccols[:, 96 + i:97 + i],
                            None, A.mult)
        for i in range(32):
            v.tensor_scalar(f3st[:, i, :], dmask_sb[:], ccols[:, 352 + i:353 + i],
                            None, A.mult)
        v.tensor_copy(eye_bf[:], eye128[:])
        v.memset(ht1[:, 908:912], 0.0)
        v.memset(zrof[:], 0.0)
        VG(v)
        for h in range(4):
            v.tensor_copy(Ssb[:, h, :], zrof[:, 0:512])
        fin(v.tensor_copy(vgscr[0:1, 7:8], vgscr[0:1, 7:8]))
    t_cvst = step(sV, [t_cst] + B_DONE, _convst)

    t_pv, t_pg = t_cvst, t_wq
    for i in range(14):
        t_rd = step(sY, [t_pv, t_pg],
                    lambda sy, fin, i=i: fin(sy.dma_start(
                        out=ofs32[:], in_=fgw1_in[i * 128:(i + 1) * 128, :])),
                    ndma=1, key="fgrd")
        t_pv = step(sV, [t_rd],
                    lambda v, fin: fin(v.tensor_copy(wstage[:, 0:992] if False
                                                     else gfin[:, 0:1024], ofs32[:])))
        t_pg = step(sGQ, [t_pv],
                    lambda g, fin, i=i: fin(g.dma_start(
                        out=fgw1_bhbm[i * 128:(i + 1) * 128, :],
                        in_=gfin[:, 0:1024])), ndma=1, key="fgst")
    t_fgw1 = t_pg

    for kc in range(8):
        t_rd = step(sY, [t_pv, t_pg],
                    lambda sy, fin, kc=kc: fin(sy.dma_start(
                        out=ofs32[:], in_=oprj_in[kc * 128:(kc + 1) * 128, :])),
                    ndma=1, key="oprd")
        t_pv = step(sV, [t_rd],
                    lambda v, fin, kc=kc: fin(v.tensor_scalar(
                        oprjT[kc][:], ofs32[:], onw_sb[:, kc:kc + 1], None, A.mult)))
    t_oprep = t_pv

    def _sp1(sc, fin):
        def CG():
            for _ in range(3):
                sc.activation(rowb[0:1, 0:1], invt[0:1, 0:1], AF.Copy)
        sc.activation(invt[0:1, 1:2], invt[0:1, 0:1], AF.Abs)
        sc.activation(invt[0:1, 4:5], invt[0:1, 0:1], AF.Relu)
        CG()
        sc.activation(invt[0:1, 2:3], invt[0:1, 1:2], AF.Exp, scale=-1.0)
        CG()
        fin(sc.activation(invt[0:1, 3:4], invt[0:1, 2:3], AF.Ln, bias=1.0))
    t_sp1 = step(sC, [t_cst], _sp1)
    def _sp2(v, fin):
        v.tensor_tensor(invt[0:1, 5:6], invt[0:1, 3:4], invt[0:1, 4:5], A.add)
        VG(v)
        v.tensor_single_scalar(invt[0:1, 6:7], invt[0:1, 5:6], 1e-4, A.add)
        VG(v)
        v.reciprocal(invt[0:1, 7:8], invt[0:1, 6:7])
        VG(v)
        fin(v.tensor_copy(vgscr[0:1, 6:7], vgscr[0:1, 6:7]))
    t_sp2 = step(sV, [t_sp1], _sp2)
    def _sp3(g, fin):
        fin(g.dma_start(out=invt_hbm[:], in_=invt[0:1, 7:8]))
    t_sp3a = step(sGQ, [t_sp2], _sp3, ndma=1, key="invt")

    def _sp4(g, fin):
        in_ap = bass.AP(invt_hbm.tensor, 0, [[0, 128], [1, 1]])
        fin(g.dma_start(out=invt_col[:], in_=in_ap))
    t_sp3 = step(sGQ, [t_sp3a], _sp4, ndma=1, key="invt")

    NOWN = NAT // 2
    t_tile_prev = t_cvst
    t_last_out = t_oprep

    for t in range(NAT):
        own = t >= NOWN
        tok0 = t * ATILE
        ot0 = tok0 - PRE

        def _xnrd(sc, fin, tok0=tok0):
            for c in range(8):
                fin(sc.dma_start_transpose(
                    xnfm[:, c, 0:XW],
                    xn_tm[tok0 + 16:tok0 + 16 + XW, c * 128:(c + 1) * 128]))
        t_xn = step(sCQ, B_DONE[0:2] + [t_tile_prev, t_cvst], _xnrd, ndma=8, key="xnr")

        chunk_list = []
        for c in range(8):
            chunk_list.append(("v", c))
            if own:
                chunk_list.append(("q", c))
            chunk_list.append(("k", c))
        t_sil = {}
        t_cv = {}
        t_w = {}
        tv = {}
        t_fir = {}
        twqk_last = None
        pi = 0
        fir_pending = {}
        for (kind, c) in chunk_list:
            mc = {"q": 0, "k": 8, "v": 16}[kind] + c
            bank = 5 + pi % 3
            par = pi % 2

            def _proj(te, fin, mc=mc, bank=bank):
                ins = None
                for kc in range(8):
                    ins = te.matmul(psum[bank][:, 0:XW],
                                    wqkvT[:, kc, mc * 128:(mc + 1) * 128],
                                    xnfm[:, kc, 0:XW],
                                    start=(kc == 0), stop=(kc == 7))
                fin(ins)
            deps = [t_xn, t_wq]
            if pi - 3 in t_cv:
                deps.append(t_cv[pi - 3])
            t_p = step(sT, deps, _proj)

            def _conv(v, fin, kind=kind, c=c, bank=bank, par=par):
                off = 1 if kind == "v" else 45
                W = 300 if kind == "v" else 256
                wc = ({"q": 0, "k": 1, "v": 2}[kind] * 8 + c) * 4
                v.tensor_scalar(cta[:, 0:W], psum[bank][:, off:off + W],
                                cvw[:, wc:wc + 1], None, A.mult)
                v.tensor_scalar(ctb[:, 0:W], psum[bank][:, off + 1:off + 1 + W],
                                cvw[:, wc + 1:wc + 2], None, A.mult)
                v.scalar_tensor_tensor(cta[:, 0:W], psum[bank][:, off + 2:off + 2 + W],
                                       cvw[:, wc + 2:wc + 3], cta[:, 0:W],
                                       A.mult, A.add)
                v.scalar_tensor_tensor(ctb[:, 0:W], psum[bank][:, off + 3:off + 3 + W],
                                       cvw[:, wc + 3:wc + 4], ctb[:, 0:W],
                                       A.mult, A.add)
                VG(v, 2)
                fin(v.tensor_tensor(cvo[par][:, 0:W], cta[:, 0:W], ctb[:, 0:W],
                                    A.add))
            deps_c = [t_p]
            if pi - 2 in t_sil:
                deps_c.append(t_sil[pi - 2])
            t_c = step(sV, deps_c, _conv)
            t_cv[pi] = t_c

            def _silu(sc, fin, kind=kind, c=c, par=par):
                if kind == "q":
                    fin(sc.activation(qfm[:, c, :], cvo[par][:, 0:256], AF.Silu))
                elif kind == "k":
                    fin(sc.activation(kfm[:, c, :], cvo[par][:, 0:256], AF.Silu))
                else:
                    fin(sc.activation(vst[:, c % 2, :], cvo[par][:, 0:300], AF.Silu))
            deps_s = [t_c]
            if kind == "v" and c - 2 in tv:
                deps_s.append(tv[c - 2])
                if ("rf", c - 2) in t_fir:
                    deps_s.append(t_fir[("rf", c - 2)])
            t_s = step(sC, deps_s, _silu)
            t_sil[pi] = t_s

            def _whbm(g, fin, kind=kind, c=c, tok0=tok0, t=t):
                if kind == "v":
                    if t == 0:
                        fin(g.dma_start(out=v_hbm[c * 128:(c + 1) * 128, 0:256],
                                        in_=vst[:, c % 2, 44:300]))
                    else:
                        fin(g.dma_start(
                            out=v_hbm[c * 128:(c + 1) * 128, tok0 - 44:tok0 + 256],
                            in_=vst[:, c % 2, :]))
                else:
                    src2 = qfm if kind == "q" else kfm
                    hb = q_hbm if kind == "q" else k_hbm
                    fin(g.dma_start(out=hb[c * 128:(c + 1) * 128, tok0:tok0 + 256],
                                    in_=src2[:, c, :]))
            wkey = f"vw{c % 2}" if kind == "v" else "whbm"
            t_wr = step(sY, [t_s], _whbm, ndma=1, key=wkey)
            t_w[pi] = t_wr
            if kind == "v":
                tv[c] = t_wr
            else:
                twqk_last = t_wr
            pi += 1
            if kind == "k" and own and (c - 1) in fir_pending:
                fir_pending.pop(c - 1)()

            if own and kind == "v":
                def _repf(g, fin, c=c, par=par):
                    vsb = 1216 + (c % 2) * 300
                    for g4 in range(4):
                        in_ap = bass.AP(ht1b.tensor,
                                        32 * g4 * 4096 + vsb + 14,
                                        [[4096, 32], [1, 4], [1, 284]])
                        fin(g.dma_start(out=repf[:, g4, :], in_=in_ap))
                deps_rf = [t_sil[pi - 1]]
                if ("mm", c - 1) in t_fir:
                    deps_rf.append(t_fir[("mm", c - 1)])
                t_rf = step(sY, deps_rf, _repf, ndma=4, key="repf")
                t_fir[("rf", c)] = t_rf

                def _emit_fir(c=c, t_rf=t_rf, ot0=ot0):
                    def _firmm(te, fin, c=c):
                        ins = None
                        for g4 in range(4):
                            te.matmul(psum[4][32 * g4:32 * g4 + 32, 0:256],
                                      f3st[:, c * 4 + g4, :], repf[:, g4, 28:284],
                                      start=True, stop=True,
                                      tile_position=(0, 32 * g4))
                        for tb in range(8):
                            for g4 in range(4):
                                ins = te.matmul(
                                    psum[4][32 * g4:32 * g4 + 32, 256:512],
                                    f31st[:, (c * 4 + g4) * 8 + tb, :],
                                    repf[:, g4, 4 * tb:4 * tb + 256],
                                    start=(tb == 0), stop=(tb == 7),
                                    tile_position=(0, 32 * g4))
                        fin(ins)
                    deps_fm = [t_rf]
                    if ("ev", c - 1) in t_fir:
                        deps_fm.append(t_fir[("ev", c - 1)])
                    t_fm = step(sT, deps_fm, _firmm)
                    t_fir[("mm", c)] = t_fm

                    def _firev(sc, fin, c=c):
                        sc.activation(slevv[:, (2 * c) % 4, :], psum[4][:, 0:256],
                                      AF.Copy)
                        fin(sc.activation(slevv[:, (2 * c + 1) % 4, :],
                                          psum[4][:, 256:512], AF.Copy))
                    deps_fe = [t_fm]
                    if ("w", c - 2) in t_fir:
                        deps_fe.append(t_fir[("w", c - 2)])
                    t_fe = step(sC, deps_fe, _firev)
                    t_fir[("ev", c)] = t_fe

                    def _firw(g, fin, c=c, ot0=ot0):
                        fin(g.dma_start(
                            out=sh_hbm[c * 128:(c + 1) * 128, ot0:ot0 + 256],
                            in_=slevv[:, (2 * c) % 4, :]))
                        fin(g.dma_start(
                            out=lo_hbm[c * 128:(c + 1) * 128, ot0:ot0 + 256],
                            in_=slevv[:, (2 * c + 1) % 4, :]))
                    t_fw = step(sY, [t_fe], _firw, ndma=2, key="firw")
                    t_fir[("w", c)] = t_fw
                fir_pending[c] = _emit_fir

        for c2 in sorted(fir_pending):
            fir_pending.pop(c2)()

        t_beta = {}
        for sub in range(2):
            bank = 6 + sub

            def _bproj(te, fin, sub=sub, bank=bank):
                ins = None
                for kc in range(8):
                    ins = te.matmul(psum[bank][:, 0:4],
                                    xnfm[:, kc, 48 + sub * 128:48 + sub * 128 + 128],
                                    wqkvT[:, kc, 3072:3076],
                                    start=(kc == 0), stop=(kc == 7))
                fin(ins)
            t_bp = step(sT, [t_xn, t_wq, t_cv[pi - 3], t_cv[pi - 2],
                             t_cv[pi - 1]], _bproj)

            def _bev(sc, fin, sub=sub, bank=bank):
                fin(sc.activation(arsn[:, sub * 24 + 8:sub * 24 + 12],
                                  psum[bank][:, 0:4], AF.Sigmoid))
            t_beta[sub] = step(sC, [t_bp], _bev)

        t_sub = {}
        for sub in range(2):
            def _tmrd(sc, fin, sub=sub, tok0=tok0, own=own):
                t0 = tok0 + sub * 128
                fin(sc.dma_start_transpose(ktm[:, sub, :], k_hbm[:, t0:t0 + 128]))
                fin(sc.dma_start_transpose(vtm[:, sub, :], v_hbm[:, t0:t0 + 128]))
                if own:
                    fin(sc.dma_start_transpose(qtm[:, sub, :], q_hbm[:, t0:t0 + 128]))
            t_rd = step(sCQ, [twqk_last, tv[6], tv[7]], _tmrd,
                        ndma=3 if own else 2, key="tmrd")

            def _l2a(v, fin, sub=sub, own=own):
                base = sub * 24
                v.tensor_tensor(o2buf[:].bitcast(BF16)[:, 0:1024], ktm[:, sub, :],
                                ktm[:, sub, :], A.mult)
                VG(v)
                v.tensor_reduce(arsn[:, base + 4:base + 8].unsqueeze(2),
                                o2buf[:].bitcast(BF16)[:, 0:1024]
                                .rearrange("p (h n) -> p h n", h=4, n=256),
                                AX.X, A.add)
                VG(v)
                v.tensor_scalar(arsn[:, base + 4:base + 8],
                                arsn[:, base + 4:base + 8], 1e-12, None, A.add)
                if own:
                    v.tensor_tensor(o2buf[:].bitcast(BF16)[:, 0:1024], qtm[:, sub, :],
                                    qtm[:, sub, :], A.mult)
                    VG(v)
                    v.tensor_reduce(arsn[:, base + 0:base + 4].unsqueeze(2),
                                    o2buf[:].bitcast(BF16)[:, 0:1024]
                                    .rearrange("p (h n) -> p h n", h=4, n=256),
                                    AX.X, A.add)
                    VG(v)
                    v.tensor_scalar(arsn[:, base + 0:base + 4],
                                    arsn[:, base + 0:base + 4], 1e-12, None, A.add)
                fin(v.tensor_copy(vgscr[0:1, 4:5], vgscr[0:1, 4:5]))
            t_l1 = step(sV, [t_rd, t_beta[sub]], _l2a)

            def _sqr(sc, fin, sub=sub, own=own):
                base = sub * 24
                ins = sc.activation(arsn[:, base + 4:base + 8],
                                    arsn[:, base + 4:base + 8], AF.Sqrt)
                if own:
                    ins = sc.activation(arsn[:, base + 0:base + 4],
                                        arsn[:, base + 0:base + 4], AF.Sqrt)
                fin(ins)
            t_l2 = step(sC, [t_l1], _sqr)

            def _l3(v, fin, sub=sub, own=own):
                base = sub * 24
                v.reciprocal(arsn[:, base + 4:base + 8], arsn[:, base + 4:base + 8])
                if own:
                    v.reciprocal(arsn[:, base + 0:base + 4], arsn[:, base + 0:base + 4])
                VG(v)
                v.tensor_tensor(arsn[:, base + 12:base + 16],
                                arsn[:, base + 8:base + 12],
                                arsn[:, base + 4:base + 8], A.mult)
                VG(v)
                v.tensor_scalar(arsn[:, base + 16:base + 20],
                                arsn[:, base + 12:base + 16], -1.0, None, A.mult)
                VG(v)
                for h in range(4):
                    v.tensor_scalar(ktm[:, sub, h * 256:(h + 1) * 256],
                                    ktm[:, sub, h * 256:(h + 1) * 256],
                                    arsn[:, base + 4 + h:base + 5 + h], None, A.mult)
                for h in range(4):
                    v.tensor_scalar(vbt[:, sub, h * 256:(h + 1) * 256],
                                    vtm[:, sub, h * 256:(h + 1) * 256],
                                    arsn[:, base + 8 + h:base + 9 + h], None, A.mult)
                fin(v.tensor_copy(vgscr[0:1, 3:4], vgscr[0:1, 3:4]))
            t_sub[sub] = step(sV, [t_l2], _l3)

        t_sev = t_tile_prev
        t_de = {}
        t_pre_done = t_tile_prev
        for sub in range(2):
            c0 = sub * 128
            base = sub * 24
            par = sub
            first = (t == 0 and sub == 0)

            def _trp(te, fin, base=base):
                ins = None
                for h in range(4):
                    te.transpose(psum[4][0:1, h * 128:(h + 1) * 128],
                                 arsn[:, base + 12 + h:base + 13 + h], eye128[:])
                    ins = te.transpose(psum[5][0:1, h * 128:(h + 1) * 128],
                                       arsn[:, base + 4 + h:base + 5 + h], eye128[:])
                fin(ins)
            deps_tr = [t_sub[sub], t_pre_done]
            if own:
                deps_tr.append(t_fir[("ev", 7)])
            t_tr = step(sT, deps_tr, _trp)

            def _re(sc, fin):
                sc.activation(rowb[0:1, 0:512], psum[4][0:1, 0:512], AF.Copy)
                fin(sc.activation(rowb[0:1, 512:1024], psum[5][0:1, 0:512], AF.Copy))
            t_rw = step(sC, [t_tr], _re)

            def _bc(te, fin):
                ins = None
                for h in range(4):
                    te.matmul(psum[4][:, h * 128:(h + 1) * 128], ones[0:1, 0:128],
                              rowb[0:1, h * 128:(h + 1) * 128], start=True, stop=True)
                    ins = te.matmul(psum[5][:, h * 128:(h + 1) * 128],
                                    ones[0:1, 0:128],
                                    rowb[0:1, 512 + h * 128:512 + (h + 1) * 128],
                                    start=True, stop=True)
                fin(ins)
            t_b2 = step(sT, [t_rw], _bc)

            def _cmk(v, fin):
                for h in range(4):
                    v.tensor_tensor(cmt[:, h, :], cst128[:, 0:128],
                                    psum[4][:, h * 128:(h + 1) * 128], A.mult)
                    v.tensor_tensor(cmt[:, 4 + h, :], cst128[:, 128:256],
                                    psum[5][:, h * 128:(h + 1) * 128], A.mult)
                fin(v.tensor_copy(vgscr[0:1, 5:6], vgscr[0:1, 5:6]))
            t_cm = step(sV, [t_b2], _cmk)

            def _a0(te, fin, c0=c0, own=own):
                ins = None
                for h in range(4):
                    for kk in range(2):
                        ins = te.matmul(psum[4][:, h * 128:(h + 1) * 128],
                                        kfm[:, 2 * h + kk, c0:c0 + 128],
                                        kfm[:, 2 * h + kk, c0:c0 + 128],
                                        start=(kk == 0), stop=(kk == 1))
                if own:
                    for h in range(4):
                        for kk in range(2):
                            ins = te.matmul(psum[5][:, h * 128:(h + 1) * 128],
                                            kfm[:, 2 * h + kk, c0:c0 + 128],
                                            qfm[:, 2 * h + kk, c0:c0 + 128],
                                            start=(kk == 0), stop=(kk == 1))
                fin(ins)
            t_a0 = step(sT, [t_cm], _a0)

            def _am(v, fin, base=base, par=par, own=own):
                for h in range(4):
                    v.scalar_tensor_tensor(Qch[0][:, h, :],
                                           psum[4][:, h * 128:(h + 1) * 128],
                                           arsn[:, base + 4 + h:base + 5 + h],
                                           cmt[:, h, :], A.mult, A.mult)
                for h in range(4):
                    v.scalar_tensor_tensor(Qch[1][:, h, :],
                                           psum[4][:, h * 128:(h + 1) * 128],
                                           arsn[:, base + 12 + h:base + 13 + h],
                                           cmt[:, 4 + h, :], A.mult, A.mult)
                for h in range(4):
                    v.tensor_tensor(Gsb[par][:, h, :], eye_bf[:], Qch[0][:, h, :],
                                    A.subtract)
                if own:
                    for h in range(4):
                        v.scalar_tensor_tensor(attnT[par][:, h, :],
                                               psum[5][:, h * 128:(h + 1) * 128],
                                               arsn[:, base + 4 + h:base + 5 + h],
                                               cst128[:, 256:384], A.mult, A.mult)
                fin(v.tensor_copy(vgscr[0:1, 5:6], vgscr[0:1, 5:6]))
            t_pw = step(sV, [t_a0], _am)

            for lvl in range(6):
                qs = lvl % 2
                def _la(te, fin, qs=qs):
                    ins = None
                    for h in range(4):
                        te.matmul(psum[4][:, h * 128:(h + 1) * 128],
                                  Qch[2 * qs + 1][:, h, :], Qch[2 * qs][:, h, :],
                                  start=True, stop=True)
                        ins = te.matmul(psum[5][:, h * 128:(h + 1) * 128],
                                        Qch[2 * qs][:, h, :], Qch[2 * qs + 1][:, h, :],
                                        start=True, stop=True)
                    fin(ins)
                t_la = step(sT, [t_pw], _la)

                def _lb(sc, fin, qs=qs):
                    sc.activation(QchF[2 - 2 * qs][:], psum[4][:, 0:512], AF.Copy)
                    fin(sc.activation(QchF[3 - 2 * qs][:], psum[5][:, 0:512],
                                      AF.Copy))
                t_lb = step(sC, [t_la], _lb)

                def _lc(te, fin, qs=qs, par=par):
                    ins = None
                    for h in range(4):
                        ins = te.matmul(psum[4][:, h * 128:(h + 1) * 128],
                                        Qch[3 - 2 * qs][:, h, :], Gsb[par][:, h, :],
                                        start=True, stop=True)
                    fin(ins)
                t_lc = step(sT, [t_lb], _lc)

                def _ld(v, fin, par=par):
                    fin(v.tensor_tensor(GsbF[par][:], GsbF[par][:],
                                        psum[4][:, 0:512], A.add))
                t_pw = step(sV, [t_lc], _ld)
            t_pre_done = t_pw

            bdeps = [t_pw, t_sev]
            if own:
                bdeps.append(t_fir[("ev", 7)])

            def _zm(te, fin, c0=c0):
                ins = None
                for h in range(4):
                    for kk in range(2):
                        ins = te.matmul(psum[6 + h // 2][:, (h % 2) * 256:
                                                         (h % 2) * 256 + 256],
                                        kfm[:, 2 * h + kk, c0:c0 + 128],
                                        Ssb[:, h, kk * 256:(kk + 1) * 256],
                                        start=(kk == 0), stop=(kk == 1))
                fin(ins)
            t_zm = step(sT, bdeps, _zm)

            def _ze(v, fin, base=base, sub=sub):
                for h in range(4):
                    v.scalar_tensor_tensor(zbuf[:, h, :],
                                           psum[6 + h // 2][:, (h % 2) * 256:
                                                            (h % 2) * 256 + 256],
                                           arsn[:, base + 16 + h:base + 17 + h],
                                           vbt[:, sub, h * 256:(h + 1) * 256],
                                           A.mult, A.add)
                fin(v.tensor_copy(vgscr[0:1, 5:6], vgscr[0:1, 5:6]))
            t_ze = step(sV, [t_zm], _ze)

            def _umm(te, fin, par=par):
                ins = None
                for h in range(4):
                    ins = te.matmul(psum[6 + h // 2][:, (h % 2) * 256:
                                                     (h % 2) * 256 + 256],
                                    Gsb[par][:, h, :], zbuf[:, h, :],
                                    start=True, stop=True)
                fin(ins)
            t_um = step(sT, [t_ze], _umm)

            def _ue(sc, fin):
                sc.activation(ub_sb[:, 0:512], psum[6][:, 0:512], AF.Copy)
                fin(sc.activation(ub_sb[:, 512:1024], psum[7][:, 0:512],
                                  AF.Copy))
            t_ue = step(sC, [t_um], _ue)

            def _sm(te, fin, sub=sub, c0=c0, par=par, own=own, first=first):
                ins = None
                for h in range(4):
                    for kk in range(2):
                        ins = te.matmul(psum[h][:, kk * 256:(kk + 1) * 256],
                                        ktm[:, sub, h * 256 + kk * 128:
                                            h * 256 + kk * 128 + 128],
                                        ubuf[:, h, :], start=first, stop=True)
                if own:
                    for h in range(4):
                        for kk in range(2):
                            te.matmul(psum[6 + h // 2][:, (h % 2) * 256:
                                                       (h % 2) * 256 + 256],
                                      qfm[:, 2 * h + kk, c0:c0 + 128],
                                      Ssb[:, h, kk * 256:(kk + 1) * 256],
                                      start=(kk == 0), stop=False)
                        ins = te.matmul(psum[6 + h // 2][:, (h % 2) * 256:
                                                         (h % 2) * 256 + 256],
                                        attnT[par][:, h, :], ubuf[:, h, :],
                                        start=False, stop=True)
                fin(ins)
            t_su = step(sT, [t_ue], _sm)

            def _se(sc, fin):
                ins = None
                for h in range(4):
                    ins = sc.activation(Ssb[:, h, :], psum[h][:, 0:512], AF.Copy)
                fin(ins)
            t_sev = step(sC, [t_su], _se)

            if own:
                def _de(v, fin, base=base, sub=sub):
                    for h in range(4):
                        v.tensor_scalar(dtm[:, sub, h * 256:(h + 1) * 256],
                                        psum[6 + h // 2][:, (h % 2) * 256:
                                                         (h % 2) * 256 + 256],
                                        arsn[:, base + h:base + h + 1],
                                        None, A.mult)
                    fin(v.tensor_copy(vgscr[0:1, 5:6], vgscr[0:1, 5:6]))
                t_de[sub] = step(sV, [t_su], _de)

        if own:
            t_m = {}
            for sub in range(2):
                def _slrd(sc, fin, sub=sub, ot0=ot0):
                    t0 = ot0 + sub * 128
                    fin(sc.dma_start_transpose(sltm[:, 0, :], sh_hbm[:, t0:t0 + 128]))
                    fin(sc.dma_start_transpose(sltm[:, 1, :], lo_hbm[:, t0:t0 + 128]))
                deps_sr = [t_fir[("w", 7)], t_sev]
                if sub == 1:
                    deps_sr.append(t_m[0])
                t_sr = step(sCQ, deps_sr, _slrd, ndma=2, key="slrd")

                def _mn(v, fin, sub=sub):
                    st = mstg[sub]
                    for (src, col) in ((sltm[:, 0, :], 0), (sltm[:, 1, :], 256),
                                       (dtm[:, sub, :], 512)):
                        v.tensor_tensor(st[:, 2 * col:2 * col + 512]
                                        if False else st[:, col * 2:col * 2 + 2][:, 0:0]
                                        if False else st[:, col:col + 256],
                                        src[:, 0:256], src[:, 256:512], A.add)
                        v.tensor_tensor(o2buf[:].bitcast(BF16)[:, 0:256],
                                        src[:, 512:768], src[:, 768:1024], A.add)
                        VG(v)
                        v.tensor_tensor(st[:, col:col + 256], st[:, col:col + 256],
                                        o2buf[:].bitcast(BF16)[:, 0:256], A.add)
                        VG(v)
                        v.tensor_scalar(st[:, col:col + 256], st[:, col:col + 256],
                                        0.25, None, A.mult)
                    fin(v.tensor_copy(vgscr[0:1, 1:2], vgscr[0:1, 1:2]))
                t_mv = step(sV, [t_sr, t_de[sub]], _mn)

                def _mw(g, fin, sub=sub, ot0=ot0):
                    fin(g.dma_start(
                        out=mn_hbm[ot0 + sub * 128:ot0 + sub * 128 + 128, :],
                        in_=mstg[sub][:, 0:768]))
                t_m[sub] = step(sY, [t_mv], _mw, ndma=1, key="mnw")

            t_gmm = {}
            t_gr = {}
            for kc in range(14):
                def _fgr(g, fin, kc=kc):
                    fin(g.dma_start(out=fgw1r[2 * (kc % 4)][:],
                                    in_=fgw1_bhbm[kc * 128:(kc + 1) * 128, 0:512]))
                    fin(g.dma_start(out=fgw1r[2 * (kc % 4) + 1][:],
                                    in_=fgw1_bhbm[kc * 128:(kc + 1) * 128, 512:1024]))
                depf = [t_fgw1]
                if kc - 4 in t_gmm:
                    depf.append(t_gmm[kc - 4])
                t_fg = step(sY, depf, _fgr, ndma=2, key=f"fgw1{kc % 4}")

                if kc >= 8:
                    def _mr(sc, fin, kc=kc, ot0=ot0):
                        for sub in range(2):
                            fin(sc.dma_start_transpose(
                                grd[:, 2 * (kc % 2) + sub, :],
                                mn_hbm[ot0 + sub * 128:ot0 + sub * 128 + 128,
                                       (kc - 8) * 128:(kc - 7) * 128]))
                    depm = [t_m[0], t_m[1]]
                    if kc - 2 in t_gmm:
                        depm.append(t_gmm[kc - 2])
                    t_gr[kc] = step(sCQ, depm, _mr, ndma=2, key=f"mnr{kc % 2}")

                def _gm(te, fin, kc=kc):
                    ins = None
                    for sub in range(2):
                        if kc < 8:
                            lhsT = xnfm[:, kc, 48 + sub * 128:48 + sub * 128 + 128]
                        else:
                            lhsT = grd[:, 2 * (kc % 2) + sub, :]
                        for nb in range(2):
                            bank = 4 + sub * 2 + nb
                            if kc == 0:
                                te.matmul(psum[bank][:, 0:512], ones[0:1, 0:128],
                                          fgb1_sb[0:1, nb * 512:(nb + 1) * 512],
                                          start=True, stop=False)
                            ins = te.matmul(psum[bank][:, 0:512], lhsT,
                                            fgw1r[2 * (kc % 4) + nb][:],
                                            start=False, stop=(kc == 13))
                    fin(ins)
                depsg = [t_fg, t_sev]
                if kc >= 8:
                    depsg.append(t_gr[kc])
                t_gmm[kc] = step(sT, depsg, _gm)

            t_gh = {}
            for sub in range(2):
                def _ge(sc, fin, sub=sub):
                    ins = None
                    for nb in range(2):
                        bank = 4 + sub * 2 + nb
                        ins = sc.activation(ghst[:, sub, nb * 512:(nb + 1) * 512],
                                            psum[bank][:, 0:512], AF.Gelu)
                    fin(ins)
                t_ge = step(sC, [t_gmm[13], sV.tok()], _ge)

                def _gw(g, fin, sub=sub, ot0=ot0):
                    fin(g.dma_start(
                        out=gh_hbm[ot0 + sub * 128:ot0 + sub * 128 + 128, :],
                        in_=ghst[:, sub, :]))
                t_gh[sub] = step(sY, [t_ge], _gw, ndma=1, key="ghw")

            t_ow = {}
            for sub in range(2):
                def _gr2(sc, fin, sub=sub, ot0=ot0):
                    for kc in range(8):
                        fin(sc.dma_start_transpose(
                            grd[:, kc, :],
                            gh_hbm[ot0 + sub * 128:ot0 + sub * 128 + 128,
                                   kc * 128:(kc + 1) * 128]))
                t_g2 = step(sCQ, [t_gh[sub], sT.tok()], _gr2, ndma=8, key="ghr")

                def _lm(te, fin, sub=sub):
                    te.matmul(psum[4][:, 0:4], ones[0:1, 0:128], fgb2_sb[0:1, 0:4],
                              start=True, stop=False)
                    ins = None
                    for kc in range(8):
                        ins = te.matmul(psum[4][:, 0:4], grd[:, kc, :],
                                        fgw2_sb[:, kc * 4:kc * 4 + 4],
                                        start=False, stop=(kc == 7))
                    fin(ins)
                t_lg = step(sT, [t_g2, t_sp3], _lm)

                def _s1(v, fin):
                    v.tensor_copy(wsm[:, 0:4], psum[4][:, 0:4])
                    VG(v)
                    v.tensor_reduce(wsm[:, 4:5].unsqueeze(2),
                                    wsm[:, 0:4].unsqueeze(1), AX.X, A.max)
                    VG(v)
                    fin(v.tensor_scalar(wsm[:, 5:9], wsm[:, 0:4], wsm[:, 4:5], None,
                                        A.subtract))
                t_s1 = step(sV, [t_lg], _s1)

                def _s2(sc, fin):
                    fin(sc.activation(wsm[:, 9:13], wsm[:, 5:9], AF.Exp,
                                      scale=invt_col[:, 0:1]))
                t_s2 = step(sC, [t_s1], _s2)

                def _s3(v, fin):
                    v.tensor_reduce(wsm[:, 13:14].unsqueeze(2),
                                    wsm[:, 9:13].unsqueeze(1), AX.X, A.add)
                    VG(v)
                    v.reciprocal(wsm[:, 14:15], wsm[:, 13:14])
                    VG(v)
                    v.tensor_scalar(wsm[:, 15:19], wsm[:, 9:13], wsm[:, 14:15],
                                    None, A.mult)
                    VG(v)
                    v.tensor_scalar(wsm[:, 19:20], wsm[:, 18:19], 0.05, None, A.mult)
                    VG(v)
                    v.tensor_scalar(wsm[:, 15:17], wsm[:, 15:17], wsm[:, 19:20],
                                    None, A.max)
                    VG(v)
                    v.tensor_reduce(wsm[:, 20:21].unsqueeze(2),
                                    wsm[:, 15:19].unsqueeze(1), AX.X, A.add)
                    VG(v)
                    v.reciprocal(wsm[:, 21:22], wsm[:, 20:21])
                    VG(v)
                    fin(v.tensor_scalar(wsm[:, 15:19], wsm[:, 15:19], wsm[:, 21:22],
                                        None, A.mult))
                t_s3 = step(sV, [t_s2], _s3)

                def _sl2(sc, fin, sub=sub, ot0=ot0):
                    t0 = ot0 + sub * 128
                    fin(sc.dma_start_transpose(sltm[:, 0, :], sh_hbm[:, t0:t0 + 128]))
                    fin(sc.dma_start_transpose(sltm[:, 1, :], lo_hbm[:, t0:t0 + 128]))
                t_sl2 = step(sCQ, [sV.tok(), t_lg], _sl2, ndma=2, key="slrd")

                def _cb(v, fin, sub=sub):
                    VG(v)
                    v.tensor_scalar(comb[0][:], sltm[:, 1, :], wsm[:, 16:17], None,
                                    A.mult)
                    VG(v)
                    v.scalar_tensor_tensor(comb[1][:], sltm[:, 0, :], wsm[:, 15:16],
                                           comb[0][:], A.mult, A.add)
                    VG(v)
                    v.scalar_tensor_tensor(comb[0][:], dtm[:, sub, :], wsm[:, 17:18],
                                           comb[1][:], A.mult, A.add)
                    VG(v)
                    v.scalar_tensor_tensor(comb[1][:], vtm[:, sub, :], wsm[:, 18:19],
                                           comb[0][:], A.mult, A.add)
                    v.tensor_tensor(comb[0][:], comb[1][:], comb[1][:], A.mult)
                    VG(v)
                    v.tensor_reduce(wsm[:, 4:8].unsqueeze(2),
                                    comb[0][:].rearrange("p (h n) -> p h n",
                                                         h=4, n=256),
                                    AX.X, A.add)
                    VG(v)
                    fin(v.tensor_scalar(wsm[:, 4:8], wsm[:, 4:8], 1.0 / 256.0, 1e-5,
                                        A.mult, A.add))
                t_c1 = step(sV, [t_s3, t_sl2, t_de[sub]], _cb)

                def _rq(sc, fin):
                    fin(sc.activation(wsm[:, 8:12], wsm[:, 4:8], AF.Sqrt))
                t_c2 = step(sC, [t_c1], _rq)

                def _c3(v, fin, sub=sub):
                    v.reciprocal(wsm[:, 8:12], wsm[:, 8:12])
                    VG(v)
                    for h in range(4):
                        v.tensor_scalar(o_tmb[:, sub, h * 256:(h + 1) * 256],
                                        comb[1][:, h * 256:(h + 1) * 256],
                                        wsm[:, 8 + h:9 + h], None, A.mult)
                    fin(v.tensor_copy(vgscr[0:1, 0:1], vgscr[0:1, 0:1]))
                t_c3 = step(sV, [t_c2], _c3)

                def _ow2(g, fin, sub=sub, ot0=ot0):
                    fin(g.dma_start(
                        out=o_hbm[ot0 + sub * 128:ot0 + sub * 128 + 128, :],
                        in_=o_tmb[:, sub, :]))
                t_ow[sub] = step(sY, [t_c3], _ow2, ndma=1, key="ow")

            for sub in range(2):
                def _or(sc, fin, sub=sub, ot0=ot0):
                    for kc in range(8):
                        fin(sc.dma_start_transpose(
                            grd[:, kc, :],
                            o_hbm[ot0 + sub * 128:ot0 + sub * 128 + 128,
                                  kc * 128:(kc + 1) * 128]))
                t_or = step(sCQ, [t_ow[sub], sT.tok()], _or, ndma=8, key="ghr")

                def _op(te, fin, sub=sub):
                    ins = None
                    for kc in range(8):
                        for nb in range(2):
                            ins = te.matmul(psum[4 + nb][:, 0:512], grd[:, kc, :],
                                            oprjT[kc][:, nb * 512:(nb + 1) * 512],
                                            start=(kc == 0), stop=(kc == 7))
                    fin(ins)
                t_op = step(sT, [t_or, t_oprep, t_last_out], _op)

                def _sv(sc, fin):
                    sc.activation(s_fin[:, 0:512], psum[4][:, 0:512], AF.Copy)
                    fin(sc.activation(s_fin[:, 512:1024], psum[5][:, 0:512], AF.Copy))
                t_se2 = step(sC, [t_op], _sv)

                def _xr(sy, fin, sub=sub, tok0=tok0):
                    fin(sy.dma_start(
                        out=x_fin[:],
                        in_=x_in[tok0 + sub * 128:tok0 + sub * 128 + 128, :]))
                t_xf = step(sY, [t_se2], _xr, ndma=1, key="xr")

                def _gr3(g, fin, sub=sub, ot0=ot0):
                    fin(g.dma_start(out=gfin[:, 0:1024],
                                    in_=g_tm[ot0 + sub * 128:ot0 + sub * 128 + 128, :]))
                t_gf = step(sY, [t_se2], _gr3, ndma=1, key="gr3")

                def _fo(v, fin):
                    v.tensor_tensor(outst[:], s_fin[:], gfin[:, 0:1024], A.mult)
                    VG(v)
                    fin(v.tensor_tensor(outst[:], outst[:], x_fin[:], A.add))
                t_fo = step(sV, [t_se2, t_xf, t_gf], _fo)

                def _out(g, fin, sub=sub, ot0=ot0):
                    fin(g.dma_start(
                        out=out[ot0 + sub * 128:ot0 + sub * 128 + 128, :],
                        in_=outst[:]))
                t_last_out = step(sY, [t_fo], _out, ndma=1, key="outw")

            t_tile_prev = t_last_out
        else:
            t_tile_prev = t_sev

    def _fbar(g, fin):
        fin(g.nop())
    _all_d = [(ps.h, ps.n) for ps in _psem.values()]
    step(sG, [t_tile_prev] + B_DONE + _all_d, _fbar)

    pr.emit()
    return nc


_PROGRAM_CACHE = {}


def _get_program():
    if "nc" not in _PROGRAM_CACHE:
        _PROGRAM_CACHE["nc"] = build_program()
    return _PROGRAM_CACHE["nc"]


def make_in_maps(inputs):
    x = np.asarray(inputs["L0_attn_hidden_states"], dtype=np.float32)
    w_fc = np.ascontiguousarray(np.asarray(inputs["L0_ffn_w_fc"], dtype=np.float32))
    b_fc = np.ascontiguousarray(
        np.asarray(inputs["L0_ffn_b_fc"], dtype=np.float32).reshape(1, HID))
    w_pj = np.ascontiguousarray(np.asarray(inputs["L0_ffn_w_proj"], dtype=np.float32))
    b_pj = np.ascontiguousarray(
        np.asarray(inputs["L0_ffn_b_proj"], dtype=np.float32).reshape(1, D))
    ones = np.ones((1, 128), np.float32)
    qw = np.asarray(inputs["L0_attn_q_proj_w"], np.float32)
    kw = np.asarray(inputs["L0_attn_k_proj_w"], np.float32)
    vw = np.asarray(inputs["L0_attn_v_proj_w"], np.float32)
    bw = np.asarray(inputs["L0_attn_b_proj_w"], np.float32)
    wqkvT = np.ascontiguousarray(
        np.concatenate([qw.T, kw.T, vw.T, bw.T], axis=1))
    qc = np.asarray(inputs["L0_attn_q_conv_w"], np.float32)
    kc_ = np.asarray(inputs["L0_attn_k_conv_w"], np.float32)
    vc = np.asarray(inputs["L0_attn_v_conv_w"], np.float32)
    firs = np.asarray(inputs["L0_attn_fir_short_w"], np.float32).reshape(1024, 3)
    firl = np.asarray(inputs["L0_attn_fir_long_w"], np.float32).reshape(1024, 31)

    def convcols(w, ntap):
        outc = np.zeros((128, 32), np.float32)
        for i in range(32):
            for a in range(min(4, ntap)):
                outc[4 * np.arange(32) + a, i] = w[i * 32:i * 32 + 32, a]
        return outc
    cvw = np.zeros((128, 96), np.float32)
    for ci, w in enumerate([qc, kc_, vc]):
        for c in range(8):
            for j in range(4):
                cvw[:, (ci * 8 + c) * 4 + j] = w[c * 128:(c + 1) * 128, j]
    f31cols = np.zeros((128, 256), np.float32)
    for i in range(32):
        for tb in range(8):
            for a in range(4):
                j = 4 * tb + a
                if j <= 30:
                    f31cols[4 * np.arange(32) + a, i * 8 + tb] =                         firl[i * 32:i * 32 + 32, j]
    f3cols = convcols(firs, 3)
    dmask = np.zeros((128, 32), np.float32)
    for a in range(4):
        dmask[4 * np.arange(32) + a, np.arange(32)] = 1.0
    idx = np.arange(128)
    umask = (idx[:, None] < idx[None, :]).astype(np.float32)
    lmask = (idx[:, None] > idx[None, :]).astype(np.float32)
    amask = (idx[:, None] <= idx[None, :]).astype(np.float32)
    eye128 = np.eye(128, dtype=np.float32)
    ones2 = np.ones((128, 64), np.float32)
    fgw1T = np.ascontiguousarray(np.asarray(inputs["L0_attn_fg_w1"], np.float32).T)
    fgb1r = np.ascontiguousarray(
        np.asarray(inputs["L0_attn_fg_b1"], np.float32).reshape(1, 1024))
    fgw2T = np.ascontiguousarray(np.asarray(inputs["L0_attn_fg_w2"], np.float32).T)
    fgb2r = np.ascontiguousarray(
        np.asarray(inputs["L0_attn_fg_b2"], np.float32).reshape(1, 4))
    ltemp = np.asarray(inputs["L0_attn_logit_temp"], np.float32).reshape(1, 1)
    onw = np.asarray(inputs["L0_attn_o_norm_w"], np.float32)
    onwcols = np.zeros((128, 8), np.float32)
    for kc2 in range(8):
        onwcols[:, kc2] = onw[(kc2 * 128 + np.arange(128)) % 256]
    oprojT = np.ascontiguousarray(np.asarray(inputs["L0_attn_o_proj_w"], np.float32).T)
    zeros128 = np.zeros((128, 1024), np.float32)
    attn_ins = {"wqkvT": wqkvT, "cvw": cvw, "f31cols": f31cols,
                "f3cols": f3cols, "dmask": dmask, "umask": umask, "lmask": lmask,
                "amask": amask, "eye128": eye128, "ones2": ones2, "fgw1T": fgw1T,
                "fgb1r": fgb1r, "fgw2T": fgw2T, "fgb2r": fgb2r, "ltemp": ltemp,
                "onwcols": onwcols, "oprojT": oprojT, "zeros128": zeros128}
    in_maps = []
    for c in range(NCORES):
        b, h = c // 2, c % 2
        if h == 0:
            xs = np.concatenate([np.zeros((PRE, D), np.float32), x[b, :TPC]], axis=0)
        else:
            xs = np.ascontiguousarray(x[b])
        m = {"x": xs, "w_fc": w_fc, "b_fc": b_fc,
             "w_proj": w_pj, "b_proj": b_pj, "ones_row": ones}
        m.update(attn_ins)
        in_maps.append(m)
    return in_maps


def kernel(**inputs):
    nc = _get_program()
    in_maps = make_in_maps(inputs)
    res = run_bass_kernel_spmd(nc, in_maps, core_ids=list(range(NCORES)))
    outp = np.empty((B, L, D), np.float32)
    for c in range(NCORES):
        b, h = c // 2, c % 2
        outp[b, h * TPC:(h + 1) * TPC] = res.results[c]["out"]
    return outp
